# version 10
# baseline (speedup 1.0000x reference)
import numpy as np
import jax
jax.config.update('jax_compilation_cache_dir', '/root/.bass_jax_cache')
jax.config.update('jax_persistent_cache_min_compile_time_secs', 1.0)
jax.config.update('jax_persistent_cache_min_entry_size_bytes', 0)
import concourse.bass as bass
import concourse.mybir as mybir
from concourse.bass_utils import run_bass_kernel_spmd
from concourse.bass_isa import ReduceOp
from concourse import library_config
from concourse.bass_core_rust import IndirectOffsetOnAxis

F32 = mybir.dt.float32
ALU = mybir.AluOpType
AF = mybir.ActivationFunctionType
AX = mybir.AxisListType

D = 1024
G = 8
T = 2048
TH = 1024
NE = 8
F = 1024
CAP = 2048.0
THR = 0.9
EPS = 1e-5
ENGS = ("sync", "gpsimd", "scalar", "vector", "tensor")

BIAS_ORDER = [("fr_b", 8), ("dp_b", 8), ("bp_b", 8), ("cp_b", 8),
              ("m1_b", 16), ("m2_b", 8), ("gate_b", 8), ("ffn1_b", 32),
              ("ffn2_b", 8), ("norm_g", 8), ("norm_b", 8), ("ffn_g", 8),
              ("ffn_bn", 8), ("skip", 8)]
NBIAS = sum(n for _, n in BIAS_ORDER)


class Sem:
    __slots__ = ("name", "n", "h")
    def __init__(self, name):
        self.name, self.n, self.h = name, 0, None


def wmerge(*dicts):
    out = {}
    for d in dicts:
        if not d:
            continue
        for s, v in d.items():
            if v and out.get(s, 0) < v:
                out[s] = v
    return out


class Buf:
    __slots__ = ("ready", "last_use")
    def __init__(self):
        self.ready = {}
        self.last_use = {}

    def set_ready(self, sem, val):
        if val and self.ready.get(sem, 0) < val:
            self.ready[sem] = val

    def set_ready_d(self, d):
        self.ready = wmerge(self.ready, d)

    def used(self, sem, val):
        if val and self.last_use.get(sem, 0) < val:
            self.last_use[sem] = val

    def used_d(self, d):
        self.last_use = wmerge(self.last_use, d)

    def begin_write(self):
        w = wmerge(self.ready, self.last_use)
        self.ready = {}
        self.last_use = {}
        return w


class Slot:
    __slots__ = ("ap", "last_evac", "mm_done")
    def __init__(self, ap):
        self.ap = ap
        self.last_evac = {}
        self.mm_done = 0


class KB:
    def __init__(self, nc):
        self.nc = nc
        self.ops = {e: [] for e in ENGS}
        self.LWs = [Sem("LW%d" % i) for i in range(6)]
        self.SPs = [Sem("SP%d" % i) for i in range(4)]
        self.MM = Sem("MM")
        self.GP = Sem("GP")
        self.SC = Sem("SC")
        self.VE = Sem("VE")
        self.CC = Sem("CC")
        self.sems = self.LWs + self.SPs + [self.MM, self.GP, self.SC,
                                           self.VE, self.CC]
        self.lw_rot = 0
        self.sp_rot = 0
        self.slots = []
        self.wring_lastmm = [{}, {}]
        self.wring_ctr = 0

    def lw_all(self):
        return {s: s.n for s in self.LWs if s.n}

    def sp_all(self):
        return {s: s.n for s in self.SPs if s.n}

    def op(self, eng, fn, waits=None, inc=None, add=1):
        w = dict(waits) if waits else {}
        v = None
        if inc is not None:
            inc.n += add
            v = inc.n
        self.ops[eng].append((w, fn, inc, add))
        return v

    def load(self, out_ap, in_ap, waits=None):
        sem = self.LWs[self.lw_rot % len(self.LWs)]
        self.lw_rot += 1
        w = wmerge(waits, {sem: sem.n})
        v = self.op("sync", lambda e: e.dma_start(out=out_ap, in_=in_ap),
                    w, sem, 16)
        return {sem: v}

    def store(self, out_ap, in_ap, waits=None, accum=False, sem_i=None):
        if accum:
            fn = lambda e: e.dma_start(out=out_ap, in_=in_ap, accum_op=ALU.add)
        else:
            fn = lambda e: e.dma_start(out=out_ap, in_=in_ap)
        if sem_i is None:
            sem = self.SPs[self.sp_rot % len(self.SPs)]
            self.sp_rot += 1
        else:
            sem = self.SPs[sem_i]
        w = wmerge(waits, {sem: sem.n})
        v = self.op("gpsimd", fn, w, sem, 16)
        return {sem: v}

    def gdma(self, out_ap, in_ap, waits=None):
        return self.store(out_ap, in_ap, waits=waits)

    def mm(self, out, lhsT, rhs, start, stop, waits=None, inc=False):
        return self.op(
            "tensor",
            lambda e: e.matmul(out, lhsT, rhs, start=start, stop=stop),
            waits, self.MM if inc else None)

    def act(self, out, in_, func, bias=0.0, scale=1.0, waits=None):
        w = wmerge(waits, {self.SC: self.SC.n})
        return self.op(
            "scalar",
            lambda e: e.activation(out=out, in_=in_, func=func, bias=bias,
                                   scale=scale),
            w, self.SC)

    def scop(self, fn, waits=None):
        w = wmerge(waits, {self.SC: self.SC.n})
        return self.op("scalar", fn, w, self.SC)

    def vec(self, fn, waits=None):
        w = wmerge(waits, {self.VE: self.VE.n})
        return self.op("vector", fn, w, self.VE)


TEST_SLOTS4 = False


def build_nc(taps=()):
    nc = bass.Bass("TRN2", target_bir_lowering=False)
    kb = KB(nc)

    def dinp(name, shape):
        return nc.dram_tensor(name, shape, F32, kind="ExternalInput").ap()

    x_fm = dinp("x_fm", [D, T])
    fr_w = dinp("fr_w", [D, D]); dp_w = dinp("dp_w", [D, D])
    bp_w = dinp("bp_w", [D, D]); cp_w = dinp("cp_w", [D, D])
    m1_w = dinp("m1_w", [D, 2 * D]); m2_w = dinp("m2_w", [2 * D, D])
    gate_w = dinp("gate_w", [2 * D, D])
    ffn1_w = dinp("ffn1_w", [D, 4 * D]); ffn2_w = dinp("ffn2_w", [4 * D, D])
    gwT = dinp("gwT", [D, NE])
    w1T = dinp("w1T", [NE, D, F])
    w2T = dinp("w2T", [NE, F, D])
    biases_in = dinp("biases_in", [128, NBIAS])
    misc_in = dinp("misc_in", [128, 20])
    sel8_in = dinp("sel8_in", [8, 1024])
    ident_in = dinp("ident_in", [128, 128])
    sel_in = dinp("sel_in", [128, 8])

    out_tm = nc.dram_tensor("out_tm", [T, D], F32, kind="ExternalOutput").ap()
    tap_out = {}
    small_taps = {}
    for tname in taps:
        if tname.startswith("sm_"):
            shp = {"sm_probs": [128, 128], "sm_s": [128, 128],
                   "sm_R": [128, 128], "sm_bis": [128, 96],
                   "sm_ymoe": [T, D], "sm_tok": [8, 513],
                   "sm_cnt": [1, 8]}[tname]
            small_taps[tname] = nc.dram_tensor("tap_" + tname, shp, F32,
                                               kind="ExternalOutput").ap()
        else:
            tap_out[tname] = nc.dram_tensor("tap_" + tname, [D, T], F32,
                                            kind="ExternalOutput").ap()

    c_hbm = nc.dram_tensor("c_hbm", [D, T], F32).ap()
    sk_hbm = nc.dram_tensor("sk_hbm", [D, T], F32).ap()
    st_hbm = nc.dram_tensor("st_hbm", [D, T], F32).ap()
    A_hbm = nc.dram_tensor("A_hbm", [D, T], F32).ap()
    mo_hbm = nc.dram_tensor("mo_hbm", [D, T], F32).ap()
    x2tm = nc.dram_tensor("x2tm", [T + 1, 1032], F32).ap()
    ymtm = nc.dram_tensor("ymtm", [T + 1, D], F32).ap()
    gin_h = nc.dram_tensor("gin_h", [128, 8], F32)
    gout_h = nc.dram_tensor("gout_h", [8 * 128, 8], F32)
    rin_h = nc.dram_tensor("rin_h", [128, 128], F32)
    rout_h = nc.dram_tensor("rout_h", [8 * 128, 128], F32)

    from contextlib import ExitStack
    es = ExitStack()
    e = es.enter_context

    pool1 = e(nc.sbuf_tensor("pool1", [128, 8192], F32))
    pool2 = e(nc.sbuf_tensor("pool2", [128, 8192], F32))
    pool3 = e(nc.sbuf_tensor("pool3", [128, 8192], F32))
    pool4 = e(nc.sbuf_tensor("pool4", [128, 8192], F32))
    wring = e(nc.sbuf_tensor("wring", [128, 4096], F32))
    scr = e(nc.sbuf_tensor("scr", [128, 5280], F32))
    mbt = e(nc.sbuf_tensor("mbt", [128, 1024], F32))
    rbt = e(nc.sbuf_tensor("rbt", [128, 1024], F32))
    stats_m = e(nc.sbuf_tensor("stats_m", [1, 1024], F32))
    stats_r = e(nc.sbuf_tensor("stats_r", [1, 1024], F32))
    stats_t = e(nc.sbuf_tensor("stats_t", [1, 2048], F32))
    sbias = e(nc.sbuf_tensor("sbias", [128, NBIAS], F32))
    ident = e(nc.sbuf_tensor("ident", [128, 128], F32))
    selt = e(nc.sbuf_tensor("selt", [128, 8], F32))
    gwTt = e(nc.sbuf_tensor("gwTt", [128, 64], F32))
    onesc = e(nc.sbuf_tensor("onesc", [128, 1], F32))
    onesk = e(nc.sbuf_tensor("onesk", [1, 128], F32))
    ones128 = e(nc.sbuf_tensor("ones128", [128, 128], F32))
    Sinit = e(nc.sbuf_tensor("Sinit", [128, 8], F32))
    Ainit = e(nc.sbuf_tensor("Ainit", [128, 8], F32))
    gath = e(nc.sbuf_tensor("gath", [128, 64], F32))
    carry = e(nc.sbuf_tensor("carry", [128, 8], F32))
    probs = e(nc.sbuf_tensor("probs", [128, 128], F32))
    assn = e(nc.sbuf_tensor("assn", [128, 128], F32))
    s_sb = e(nc.sbuf_tensor("s_sb", [128, 128], F32))
    sbc = e(nc.sbuf_tensor("sbc", [128, 2048], F32))
    bis = e(nc.sbuf_tensor("bis", [128, 96], F32))
    epsc = e(nc.sbuf_tensor("epsc", [1, 1], F32))
    epsc128 = e(nc.sbuf_tensor("epsc128", [128, 1], F32))
    ones11 = e(nc.sbuf_tensor("ones11", [1, 1], F32))
    misct = e(nc.sbuf_tensor("misct", [128, 20], F32))
    sel8t = e(nc.sbuf_tensor("sel8t", [8, 1024], F32))
    tokcf = e(nc.sbuf_tensor("tokcf", [128, 32], F32))
    tokci = e(nc.sbuf_tensor("tokci", [128, 32], mybir.dt.int32))
    neg9 = e(nc.sbuf_tensor("neg9", [128, 128], F32))

    PA = e(nc.psum_tensor("PA", [128, 2048], F32))
    PB = e(nc.psum_tensor("PB", [128, 2048], F32))

    for s in kb.sems:
        s.h = e(nc.semaphore(s.name))
    block = e(nc.Block())

    for i in range(4):
        kb.slots.append(Slot(PA[:, i * 512:(i + 1) * 512]))
    for i in range(4):
        kb.slots.append(Slot(PB[:, i * 512:(i + 1) * 512]))

    tb = scr[:, 0:1024]
    tbv = [scr[:, 0:512], scr[:, 512:1024]]
    stdb = [scr[:, 1024:2048], scr[:, 2048:3072]]
    udb = [scr[:, 3072:4096], scr[:, 4096:5120]]
    B_tb = Buf()
    stdb_use = [{}, {}]
    udb_use = [{}, {}]

    bofs = {}
    cur = 0
    for nme, n in BIAS_ORDER:
        bofs[nme] = cur
        cur += n
    def bv(nme, g):
        return sbias[:, bofs[nme] + g: bofs[nme] + g + 1]

    B_p1, B_p2, B_p3, B_p4 = Buf(), Buf(), Buf(), Buf()
    B_chbm, B_skhbm, B_sthbm, B_Ahbm = Buf(), Buf(), Buf(), Buf()
    B_mohbm, B_x2tm, B_ymtm = Buf(), Buf(), Buf()
    B_mb, B_rb, B_sbc, B_probs = Buf(), Buf(), Buf(), Buf()

    kb.load(sbias[:], biases_in)
    kb.load(ident[:], ident_in)
    kb.load(selt[:], sel_in)
    kb.load(gwTt[:], gwT.rearrange("(k p) e -> p k e", p=128))
    kb.load(misct[:], misc_in)
    kb.load(sel8t[:], sel8_in)
    const_lw = kb.lw_all()
    kb.vec(lambda e: e.memset(onesc[:], 1.0))
    kb.vec(lambda e: e.memset(onesk[:], 1.0))
    kb.vec(lambda e: e.memset(ones128[:], 1.0))
    kb.vec(lambda e: e.memset(Sinit[:], 0.0))
    kb.vec(lambda e: e.memset(Ainit[:], 1.0))
    kb.vec(lambda e: e.memset(neg9[:], -1e9))
    kb.vec(lambda e: e.memset(epsc[:], EPS))
    kb.vec(lambda e: e.memset(epsc128[:], EPS))
    kb.vec(lambda e: e.memset(ones11[:], 1.0))
    kb.vec(lambda e: e.memset(bis[:], 0.0))
    const_ve = kb.VE.n
    CONSTS = wmerge({kb.VE: const_ve}, const_lw)

    def proj(w_hbm, Kc, Nout, rhs_fn, rhs_bufs, evac_pair_fn, ntt,
             kpass=4, first_waits=None, rhs_ready_k=None, korder=None,
             slots4=False):
        nkp = (Kc + kpass - 1) // kpass
        rhs_ready = wmerge(*(b.ready for b in rhs_bufs))
        ks_all = korder if korder is not None else list(range(Kc))
        passes = []
        for jb in range(Nout // 512):
            for kp in range(nkp):
                passes.append((jb, kp,
                               ks_all[kp * kpass:
                                      min((kp + 1) * kpass, Kc)]))
        planned = {}
        k_seen = set()

        def plan_load(idx):
            jb, kp, ks = passes[idx]
            setid = kb.wring_ctr % 2
            kb.wring_ctr += 1
            i = 0
            lwv = {}
            first_load = True
            while i < len(ks):
                npair = 2 if (i + 1 < len(ks)
                              and ks[i + 1] == ks[i] + 1) else 1
                k0 = ks[i]
                dst = wring[:, (setid * 4 + i) * 512:
                            (setid * 4 + i + npair) * 512]
                if npair == 2:
                    ap = bass.AP(w_hbm.tensor,
                                 w_hbm.offset + k0 * 128 * Nout + jb * 512,
                                 [[Nout, 128], [128 * Nout, 2], [1, 512]])
                else:
                    ap = bass.AP(w_hbm.tensor,
                                 w_hbm.offset + k0 * 128 * Nout + jb * 512,
                                 [[Nout, 128], [1, 512]])
                w = kb.wring_lastmm[setid] if first_load else None
                lwv = wmerge(lwv, kb.load(dst, ap, waits=w))
                first_load = False
                i += npair
            planned[idx] = (setid, lwv)

        plan_load(0)
        if len(passes) > 1:
            plan_load(1)
        first_mm_of_proj = True
        for idx, (jb, kp, ks) in enumerate(passes):
            setid, lwv = planned.pop(idx)
            first_of_pass = True
            lastv = None
            for j_local in range(4):
                for tt in range(ntt):
                    s = (j_local * 2 + tt) if ntt == 2 \
                        else (jb % 2) * 4 + j_local
                    slot = kb.slots[s]
                    for i, k in enumerate(ks):
                        w = {}
                        if kp == 0 and i == 0:
                            w = wmerge(w, slot.last_evac)
                        if first_of_pass:
                            w = wmerge(w, lwv)
                            if first_mm_of_proj:
                                w = wmerge(w,
                                           None if rhs_ready_k else rhs_ready,
                                           first_waits)
                        if rhs_ready_k is not None and k not in k_seen:
                            k_seen.add(k)
                            w = wmerge(w, rhs_ready_k[k])
                        start = (kp == 0 and i == 0)
                        stop = (kp == nkp - 1 and i == len(ks) - 1)
                        last_of_pass = (i == len(ks) - 1 and
                                        j_local == 3 and tt == ntt - 1)
                        lhsT = wring[:, (setid * 4 + i) * 512
                                     + j_local * 128:
                                     (setid * 4 + i) * 512
                                     + (j_local + 1) * 128]
                        v = kb.mm(slot.ap, lhsT, rhs_fn(k, tt), start,
                                  stop, waits=w, inc=(stop or last_of_pass))
                        first_of_pass = False
                        first_mm_of_proj = False
                        if v:
                            lastv = v
                        if stop:
                            slot.mm_done = v
            if lastv:
                kb.wring_lastmm[setid] = {kb.MM: lastv}
            if idx + 2 < len(passes):
                plan_load(idx + 2)
            if kp == nkp - 1:
                if ntt == 2:
                    for jl in range(4):
                        j = jb * 4 + jl
                        base = jl * 2
                        sl0, sl1 = kb.slots[base], kb.slots[base + 1]
                        pt = PA if base < 4 else PB
                        pap = pt[:, (base % 4) * 512:(base % 4) * 512 + 1024]
                        mmw = {kb.MM: max(sl0.mm_done, sl1.mm_done)}
                        done = evac_pair_fn(j, None, pap, mmw)
                        sl0.last_evac = done
                        sl1.last_evac = done
                else:
                    for jl in (0, 2):
                        base = jl if slots4 else (jb % 2) * 4 + jl
                        sl0, sl1 = kb.slots[base], kb.slots[base + 1]
                        pt = PA if base < 4 else PB
                        pap = pt[:, (base % 4) * 512:(base % 4) * 512 + 1024]
                        mmw = {kb.MM: max(sl0.mm_done, sl1.mm_done)}
                        done = evac_pair_fn(jb * 4 + jl, jb * 4 + jl + 1,
                                            pap, mmw)
                        sl0.last_evac = done
                        sl1.last_evac = done
        for b in rhs_bufs:
            b.used(kb.MM, kb.MM.n)

    def ln_stats(x_ap_fn, x_buf, th, sqp, sq_gate, x_ready_g=None):
        ntt = th // 512
        sqv = 0
        for tt in range(ntt):
            for g in range(G):
                idx = (g * ntt + tt) * 512
                w = wmerge(x_ready_g[g] if x_ready_g else x_buf.ready,
                           CONSTS,
                           sq_gate if (tt == 0 and g == 0) else None)
                sqv = kb.act(sqp[:, idx:idx + 512], x_ap_fn(g, tt),
                             AF.Square, waits=w)
        sx, sq = kb.slots[4], kb.slots[5]
        for tt in range(ntt):
            for g in range(G):
                idx = (g * ntt + tt) * 512
                w0 = wmerge(x_buf.ready, sx.last_evac if g == 0 else None,
                            CONSTS)
                kb.mm(sx.ap[0:1, :], onesc[:], x_ap_fn(g, tt),
                      start=(g == 0), stop=(g == G - 1), waits=w0, inc=True)
                w1 = wmerge({kb.SC: sqv} if (tt == 0 and g == 0) else None,
                            sq.last_evac if g == 0 else None)
                v = kb.mm(sq.ap[0:1, :], onesc[:], sqp[:, idx:idx + 512],
                          start=(g == 0), stop=(g == G - 1), waits=w1,
                          inc=True)
                sq.mm_done = v
            sxv = kb.MM.n - 1
            c = slice(tt * 512, (tt + 1) * 512)
            v1 = kb.scop(lambda e, c=c: e.mul(stats_m[0:1, c], sx.ap[0:1, :],
                                              1.0 / D), waits={kb.MM: sxv})
            v2 = kb.scop(lambda e, c=c: e.mul(stats_t[0:1, c], sq.ap[0:1, :],
                                              1.0 / D), waits={kb.MM: kb.MM.n})
            sx.last_evac = {kb.SC: v1}
            sq.last_evac = {kb.SC: v2}
            c2 = slice(1024 + c.start, 1024 + c.stop)
            kb.vec(lambda e, c=c, c2=c2: e.tensor_tensor(
                stats_t[0:1, c2], stats_m[0:1, c], stats_m[0:1, c], ALU.mult),
                waits={kb.SC: v2})
            v4 = kb.vec(lambda e, c=c, c2=c2: e.tensor_tensor(
                stats_t[0:1, c], stats_t[0:1, c], stats_t[0:1, c2],
                ALU.subtract))
            v5 = kb.scop(lambda e, c=c: e.activation(
                out=stats_t[0:1, c], in_=stats_t[0:1, c], func=AF.Sqrt,
                bias=epsc[0:1, :]), waits={kb.VE: v4})
            kb.vec(lambda e, c=c: e.reciprocal(stats_r[0:1, c],
                                               stats_t[0:1, c]),
                   waits={kb.SC: v5})
        x_buf.used(kb.MM, kb.MM.n)
        for row, srct, dstt, bufb in ((0, stats_m, mbt, B_mb),
                                      (1, stats_r, rbt, B_rb)):
            bw = bufb.begin_write()
            for tt in range(ntt):
                sl = kb.slots[6 + row]
                c = slice(tt * 512, (tt + 1) * 512)
                w = wmerge({kb.VE: kb.VE.n, kb.SC: kb.SC.n}, sl.last_evac,
                           bw, CONSTS)
                kb.mm(sl.ap, onesk[:], srct[0:1, c], True, True,
                      waits=w, inc=True)
                vv = kb.scop(lambda e, sl=sl, dstt=dstt, c=c:
                             e.copy(dstt[:, c], sl.ap),
                             waits={kb.MM: kb.MM.n})
                sl.last_evac = {kb.SC: vv}
                bufb.set_ready(kb.SC, vv)

    def pslice(pool, g, tt=None):
        if tt is None:
            return pool[:, g * TH:(g + 1) * TH]
        return pool[:, g * TH + tt * 512: g * TH + (tt + 1) * 512]

    def hslab(hb, g, h, tt=None):
        off = hb.offset + g * 128 * T + h * TH
        if tt is None:
            return bass.AP(hb.tensor, off, [[T, 128], [1, TH]])
        return bass.AP(hb.tensor, off + tt * 512, [[T, 128], [1, 512]])

    def tap(name, src_pool, h, buf):
        if name in tap_out:
            for g in range(G):
                v = kb.store(hslab(tap_out[name], g, h),
                             pslice(src_pool, g), waits=buf.ready)
                buf.used_d(v)

    def plan_ln1(h, xr, sqp, sqb):
        ln_stats(lambda g, tt: pslice(pool1, g, tt), B_p1, TH,
                 sqp, sqb.begin_write(), x_ready_g=xr)
        sqb.set_ready(kb.MM, kb.MM.n)
        xn_rg = []
        for g in range(G):
            w = wmerge(B_mb.ready, B_rb.ready, CONSTS, B_p1.last_use,
                       {kb.MM: kb.MM.n} if g == 0 else None)
            kb.vec(lambda e, g=g: e.tensor_tensor(
                pslice(pool1, g), pslice(pool1, g), mbt[:], ALU.subtract),
                waits=w)
            kb.vec(lambda e, g=g: e.tensor_tensor(
                pslice(pool1, g), pslice(pool1, g), rbt[:], ALU.mult))
            vv = kb.vec(lambda e, g=g: e.tensor_scalar(
                pslice(pool1, g), pslice(pool1, g),
                bv("norm_g", g), bv("norm_b", g), ALU.mult, ALU.add))
            xn_rg.append({kb.VE: vv})
        B_p1.ready = {kb.VE: kb.VE.n}
        B_mb.used(kb.VE, kb.VE.n)
        B_rb.used(kb.VE, kb.VE.n)
        tap("xn", pool1, h, B_p1)
        return xn_rg

    xg_ready = {}
    prefetched = [None]
    pre_ln1 = [None]
    for h in range(2):
        if h == 1 and prefetched[0] is not None:
            B_p1.ready = prefetched[0]
            xr = xg_ready[1]
        else:
            gate = B_p1.begin_write()
            xr = []
            for g in range(G):
                lv = kb.load(pslice(pool1, g), hslab(x_fm, g, h),
                             waits=gate if g == 0 else None)
                xr.append(lv)
            B_p1.set_ready_d(kb.lw_all())
            xg_ready[h] = xr

        if h == 1 and pre_ln1[0] is not None:
            xn_rg = pre_ln1[0]
        else:
            xn_rg = plan_ln1(h, xr, pool4, B_p4)

        p2gate = B_p2.begin_write()
        p3gate = B_p3.begin_write()

        low_rg, high_rg = [None] * G, [None] * G

        def fr_evac(j, _u, pap, mmw):
            w = wmerge(mmw, B_tb.last_use)
            sv = kb.act(tb, pap, AF.Sigmoid, bias=bv("fr_b", j), waits=w)
            B_tb.set_ready(kb.SC, sv)
            v1 = kb.vec(lambda e, j=j: e.tensor_tensor(
                pslice(pool2, j), pslice(pool1, j), tb, ALU.mult),
                waits=wmerge({kb.SC: sv}, p2gate if j == 0 else None))
            v2 = kb.vec(lambda e, j=j: e.tensor_tensor(
                pslice(pool3, j), pslice(pool1, j), pslice(pool2, j),
                ALU.subtract),
                waits=p3gate if j == 0 else None)
            B_tb.used(kb.VE, v1)
            B_p2.set_ready(kb.VE, v2)
            B_p3.set_ready(kb.VE, v2)
            low_rg[j] = {kb.VE: v1}
            high_rg[j] = {kb.VE: v2}
            return {kb.VE: v1}

        proj(fr_w, 8, 1024, lambda k, tt: pslice(pool1, k, tt), [B_p1],
             fr_evac, ntt=2, rhs_ready_k=xn_rg)
        tap("low", pool2, h, B_p2)
        tap("high", pool3, h, B_p3)

        def cp_evac(j, _u, pap, mmw):
            i = j % 2
            w = wmerge(mmw, stdb_use[i])
            sv = kb.act(stdb[i], pap, AF.Tanh, bias=bv("cp_b", j), waits=w)
            spv = kb.store(hslab(c_hbm, j, h), stdb[i], waits={kb.SC: sv})
            stdb_use[i] = spv
            B_chbm.set_ready_d(spv)
            return {kb.SC: sv}

        proj(cp_w, 8, 1024, lambda k, tt: pslice(pool3, k, tt), [B_p3],
             cp_evac, ntt=2, rhs_ready_k=high_rg)

        p3gate2 = B_p3.begin_write()

        def dp_evac(j, _u, pap, mmw):
            w = wmerge(mmw, p3gate2 if j == 0 else None)
            sv = kb.act(pslice(pool3, j), pap, AF.Sigmoid,
                        bias=bv("dp_b", j), waits=w)
            B_p3.set_ready(kb.SC, sv)
            return {kb.SC: sv}

        proj(dp_w, 8, 1024, lambda k, tt: pslice(pool2, k, tt), [B_p2],
             dp_evac, ntt=2, rhs_ready_k=low_rg)
        tap("a", pool3, h, B_p3)

        def bp_evac(g, _u, pap, mmw, h=h):
            i = g % 2
            w = wmerge(mmw, B_tb.last_use)
            sv = kb.act(tb, pap, AF.Tanh, bias=bv("bp_b", g), waits=w)
            B_tb.set_ready(kb.SC, sv)
            w2 = wmerge({kb.SC: sv}, udb_use[i], B_p3.ready)
            uv = kb.vec(lambda e, g=g, i=i: e.tensor_tensor(
                udb[i], tb, pslice(pool1, g), ALU.mult), waits=w2)
            B_tb.used(kb.VE, uv)
            w3 = wmerge({kb.VE: uv}, stdb_use[i])
            stv = kb.vec(lambda e, g=g, i=i: e.tensor_tensor_scan(
                stdb[i], pslice(pool3, g), udb[i], Sinit[:, g:g + 1],
                ALU.mult, ALU.add), waits=w3)
            kb.vec(lambda e, g=g, i=i: e.tensor_copy(
                Sinit[:, g:g + 1], stdb[i][:, TH - 1:TH]))
            spv = kb.store(hslab(st_hbm, g, h), stdb[i],
                           waits={kb.VE: kb.VE.n})
            stdb_use[i] = spv
            B_sthbm.set_ready_d(spv)
            zv = kb.vec(lambda e: e.memset(tb, 0.0),
                        waits=wmerge({kb.VE: stv}, B_tb.last_use))
            B_tb.set_ready(kb.VE, zv)
            av = kb.vec(lambda e, g=g, i=i: e.tensor_tensor_scan(
                udb[i], pslice(pool3, g), tb, Ainit[:, g:g + 1],
                ALU.mult, ALU.add), waits={kb.VE: zv})
            B_p3.used(kb.VE, av)
            B_tb.used(kb.VE, av)
            kb.vec(lambda e, g=g, i=i: e.tensor_copy(
                Ainit[:, g:g + 1], udb[i][:, TH - 1:TH]))
            spv2 = kb.store(hslab(A_hbm, g, h), udb[i],
                            waits={kb.VE: kb.VE.n})
            udb_use[i] = spv2
            B_Ahbm.set_ready_d(spv2)
            skv = kb.vec(lambda e, g=g: e.tensor_single_scalar(
                pslice(pool1, g), pslice(pool1, g), bv("skip", g), ALU.mult),
                waits=wmerge({kb.VE: uv}, B_p1.last_use))
            spv3 = kb.store(hslab(sk_hbm, g, h), pslice(pool1, g),
                            waits={kb.VE: skv})
            B_skhbm.set_ready_d(spv3)
            B_p1.used_d(spv3)
            return {kb.VE: av}

        proj(bp_w, 8, 1024, lambda k, tt: pslice(pool2, k, tt), [B_p2],
             bp_evac, ntt=2, rhs_ready_k=low_rg)

        if h == 0:
            gate = B_p1.begin_write()
            xg_ready[1] = []
            for g in range(G):
                lv = kb.load(pslice(pool1, g), hslab(x_fm, g, 1),
                             waits=gate if g == 0 else None)
                xg_ready[1].append(lv)
            B_p1.set_ready_d(kb.lw_all())
            prefetched[0] = dict(B_p1.ready)
            pre_ln1[0] = plan_ln1(1, xg_ready[1], pool3, B_p3)

        for tt in range(2):
            h1_rg = [None] * 16
            p4gate = B_p4.begin_write()

            def m1_evac(j0, j1, pap, mmw, p4gate=p4gate):
                sv = None
                for jj, off in ((j0, 0), (j1, 512)):
                    w = wmerge(mmw if off == 0 else None,
                               p4gate if jj == 0 else None)
                    sv = kb.act(pool4[:, jj * 512:(jj + 1) * 512],
                                pap[:, off:off + 512], AF.Gelu,
                                bias=bv("m1_b", jj), waits=w)
                    h1_rg[jj] = {kb.SC: sv}
                B_p4.set_ready(kb.SC, sv)
                return {kb.SC: sv}

            proj(m1_w, 8, 2048, lambda k, _t, tt=tt: pslice(pool2, k, tt),
                 [B_p2], m1_evac, ntt=1, rhs_ready_k=low_rg, slots4=TEST_SLOTS4)

            def m2_evac(j0, j1, pap, mmw, tt=tt, h=h):
                svs = []
                for jj, off in ((j0, 0), (j1, 512)):
                    i = jj % 2
                    w = wmerge(mmw if off == 0 else None, B_tb.last_use)
                    sv = kb.scop(lambda e, jj=jj, off=off: e.add(
                        tbv[jj % 2], pap[:, off:off + 512], bv("m2_b", jj)),
                        waits=w)
                    B_tb.set_ready(kb.SC, sv)
                    spv = kb.store(hslab(mo_hbm, jj, h, tt), tbv[jj % 2],
                                   waits={kb.SC: sv})
                    B_tb.used_d(spv)
                    svs.append(sv)
                B_mohbm.set_ready_d(kb.sp_all())
                return {kb.SC: max(svs)}

            proj(m2_w, 16, 1024,
                 lambda k, _t: pool4[:, k * 512:(k + 1) * 512], [B_p4],
                 m2_evac, ntt=1, rhs_ready_k=h1_rg, slots4=TEST_SLOTS4)

    gv = kb.store(gin_h.ap(), Sinit[:], waits={kb.VE: kb.VE.n})
    ccv = kb.op("gpsimd", lambda e: e.collective_compute(
        "AllGather", ALU.bypass, replica_groups=[list(range(8))],
        ins=[gin_h.ap().opt()], outs=[gout_h.ap().opt()]),
        waits=gv, inc=kb.CC)
    glv = kb.gdma(gath[:],
                  bass.AP(gout_h, 0, [[8, 128], [1024, 8], [1, 8]]),
                  waits={kb.CC: ccv})
    cv1 = kb.vec(lambda e: e.tensor_tensor(
        bass.AP(gath, 0, [[64, 128], [1, 8], [8, 8]]),
        bass.AP(gath, 0, [[64, 128], [1, 8], [8, 8]]),
        bass.AP(selt, 0, [[8, 128], [0, 8], [1, 8]]), ALU.mult),
        waits=wmerge(glv, CONSTS))
    cv2 = kb.vec(lambda e: e.tensor_reduce(
        carry[:], bass.AP(gath, 0, [[64, 128], [1, 8], [8, 8]]),
        AX.X, ALU.add), waits={kb.VE: cv1})
    carry_ready = {kb.VE: cv2}

    defer_x2tm = []
    for h in range(2):
        g2 = B_p2.begin_write()
        g3 = B_p3.begin_write()
        g4 = B_p4.begin_write()
        mo_rg = []
        for g in range(G):
            lvm = kb.load(pslice(pool2, g), hslab(mo_hbm, g, h),
                          waits=wmerge(B_mohbm.ready, g2) if g == 0 else None)
            mo_rg.append(lvm)
            kb.load(pslice(pool3, g), hslab(c_hbm, g, h),
                    waits=wmerge(B_chbm.ready, g3) if g == 0 else None)
            kb.load(pslice(pool4, g), hslab(sk_hbm, g, h),
                    waits=wmerge(B_skhbm.ready, g4) if g == 0 else None)
        if h == 0:
            B_chbm.ready = {}
            B_skhbm.ready = {}
            B_mohbm.ready = {}
        B_p2.set_ready_d(kb.lw_all())
        B_p3.set_ready_d(kb.lw_all())
        B_p4.set_ready_d(kb.lw_all())

        ssd_rg = []
        g1 = B_p1.begin_write()
        for g in range(G):
            i = g % 2
            l1 = kb.load(stdb[i], hslab(st_hbm, g, h),
                         waits=wmerge(B_sthbm.ready, stdb_use[i],
                                      g1 if g == 0 else None))
            l2 = kb.load(udb[i], hslab(A_hbm, g, h),
                         waits=wmerge(B_Ahbm.ready, udb_use[i]))
            fv = kb.vec(lambda e, g=g, i=i: e.scalar_tensor_tensor(
                stdb[i], udb[i], carry[:, g:g + 1], stdb[i],
                ALU.mult, ALU.add),
                waits=wmerge(l1, l2, carry_ready))
            udb_use[i] = {kb.VE: fv}
            kb.vec(lambda e, g=g, i=i: e.tensor_tensor(
                stdb[i], stdb[i], pslice(pool3, g), ALU.mult),
                waits=B_p3.ready)
            sv = kb.vec(lambda e, g=g, i=i: e.tensor_tensor(
                pslice(pool1, g), stdb[i], pslice(pool4, g), ALU.add),
                waits=B_p4.ready)
            stdb_use[i] = {kb.VE: sv}
            ssd_rg.append({kb.VE: sv})
            B_p1.set_ready(kb.VE, sv)
        tap("ssd", pool1, h, B_p1)

        g3b = B_p3.begin_write()

        def gate_evac(j, _u, pap, mmw, g3b=g3b):
            w = wmerge(mmw, g3b if j == 0 else None)
            sv = kb.act(pslice(pool3, j), pap, AF.Sigmoid,
                        bias=bv("gate_b", j), waits=w)
            B_p3.set_ready(kb.SC, sv)
            return {kb.SC: sv}

        gate_rk = [ssd_rg[k] if k < 8 else mo_rg[k - 8] for k in range(16)]
        proj(gate_w, 16, 1024,
             lambda k, tt: pslice(pool1, k, tt) if k < 8
             else pslice(pool2, k - 8, tt),
             [B_p1, B_p2], gate_evac, ntt=2, rhs_ready_k=gate_rk,
             korder=[8, 9, 10, 11, 12, 13, 14, 15, 0, 1, 2, 3, 4, 5, 6, 7])

        for g in range(G):
            w = wmerge(B_p3.ready, B_tb.last_use,
                       B_p1.last_use if g == 0 else None)
            v0 = kb.vec(lambda e, g=g: e.tensor_tensor(
                tb, pslice(pool1, g), pslice(pool2, g), ALU.subtract),
                waits=w)
            B_tb.set_ready(kb.VE, v0)
            kb.vec(lambda e, g=g: e.tensor_tensor(
                tb, tb, pslice(pool3, g), ALU.mult))
            v = kb.vec(lambda e, g=g: e.tensor_tensor(
                pslice(pool1, g), tb, pslice(pool2, g), ALU.add))
            B_tb.used(kb.VE, v)
        B_p1.ready = {kb.VE: kb.VE.n}
        B_p2.used(kb.VE, kb.VE.n)
        B_p3.used(kb.VE, kb.VE.n)

        for g in range(G):
            i = g % 2
            lv = kb.load(stdb[i], hslab(x_fm, g, h), waits=stdb_use[i])
            v = kb.vec(lambda e, g=g, i=i: e.tensor_tensor(
                pslice(pool1, g), pslice(pool1, g), stdb[i], ALU.add),
                waits=wmerge(lv, B_p1.last_use if g == 0 else None))
            stdb_use[i] = {kb.VE: v}
        B_p1.ready = {kb.VE: kb.VE.n}
        tap("y", pool1, h, B_p1)

        ln_stats(lambda g, tt: pslice(pool1, g, tt), B_p1, TH,
                 pool4, B_p4.begin_write())
        B_p4.set_ready(kb.MM, kb.MM.n)
        p2g = B_p2.begin_write()
        yn_rg = []
        for g in range(G):
            w = wmerge(B_mb.ready, B_rb.ready, p2g if g == 0 else None)
            kb.vec(lambda e, g=g: e.tensor_tensor(
                pslice(pool2, g), pslice(pool1, g), mbt[:], ALU.subtract),
                waits=w)
            kb.vec(lambda e, g=g: e.tensor_tensor(
                pslice(pool2, g), pslice(pool2, g), rbt[:], ALU.mult))
            vv = kb.vec(lambda e, g=g: e.tensor_scalar(
                pslice(pool2, g), pslice(pool2, g),
                bv("ffn_g", g), bv("ffn_bn", g), ALU.mult, ALU.add))
            yn_rg.append({kb.VE: vv})
        B_p2.ready = {kb.VE: kb.VE.n}
        B_mb.used(kb.VE, kb.VE.n)
        B_rb.used(kb.VE, kb.VE.n)

        for tt in range(2):
            h2_rg = [None] * 32
            p34g = wmerge(B_p3.begin_write(), B_p4.begin_write())

            def f1_evac(j0, j1, pap, mmw, p34g=p34g):
                sv = None
                for jj, off in ((j0, 0), (j1, 512)):
                    dst = (pool3[:, jj * 512:(jj + 1) * 512] if jj < 16
                           else pool4[:, (jj - 16) * 512:(jj - 15) * 512])
                    w = wmerge(mmw if off == 0 else None,
                               p34g if jj == 0 else None)
                    sv = kb.act(dst, pap[:, off:off + 512], AF.Gelu,
                                bias=bv("ffn1_b", jj), waits=w)
                    h2_rg[jj] = {kb.SC: sv}
                B_p3.set_ready(kb.SC, sv)
                B_p4.set_ready(kb.SC, sv)
                return {kb.SC: sv}

            proj(ffn1_w, 8, 4096, lambda k, _t, tt=tt: pslice(pool2, k, tt),
                 [B_p2], f1_evac, ntt=1, rhs_ready_k=yn_rg)

            def f2_evac(j0, j1, pap, mmw, tt=tt):
                vv = None
                for jj, off in ((j0, 0), (j1, 512)):
                    w = wmerge(mmw, B_p1.last_use) if off == 0 else None
                    vv = kb.vec(lambda e, jj=jj, off=off, tt=tt:
                                e.scalar_tensor_tensor(
                                    pslice(pool1, jj, tt),
                                    pap[:, off:off + 512], bv("ffn2_b", jj),
                                    pslice(pool1, jj, tt), ALU.add, ALU.add),
                                waits=w)
                return {kb.VE: vv}

            proj(ffn2_w, 32, 1024,
                 lambda k, _t: (pool3[:, k * 512:(k + 1) * 512] if k < 16
                                else pool4[:, (k - 16) * 512:(k - 15) * 512]),
                 [B_p3, B_p4], f2_evac, ntt=1, rhs_ready_k=h2_rg)
        B_p1.ready = {kb.VE: kb.VE.n}

        for g in range(G):
            i = g % 2
            lv = kb.load(stdb[i], hslab(x_fm, g, h), waits=stdb_use[i])
            v = kb.vec(lambda e, g=g, i=i: e.tensor_tensor(
                pslice(pool1, g), pslice(pool1, g), stdb[i], ALU.add),
                waits=wmerge(lv, B_p1.last_use if g == 0 else None))
            stdb_use[i] = {kb.VE: v}
        B_p1.ready = {kb.VE: kb.VE.n}
        tap("y3", pool1, h, B_p1)

        ln_stats(lambda g, tt: pslice(pool1, g, tt), B_p1, TH,
                 pool4, B_p4.begin_write())
        B_p4.set_ready(kb.MM, kb.MM.n)
        for g in range(G):
            w = wmerge(B_mb.ready, B_rb.ready,
                       B_p1.last_use if g == 0 else None)
            kb.vec(lambda e, g=g: e.tensor_tensor(
                pslice(pool1, g), pslice(pool1, g), mbt[:], ALU.subtract),
                waits=w)
            v = kb.vec(lambda e, g=g: e.tensor_tensor(
                pslice(pool1, g), pslice(pool1, g), rbt[:], ALU.mult))
        B_p1.ready = {kb.VE: kb.VE.n}
        B_mb.used(kb.VE, kb.VE.n)
        B_rb.used(kb.VE, kb.VE.n)
        tap("x2", pool1, h, B_p1)
        p4g2 = B_p4.begin_write()
        st_prev = [{} for _ in range(4)]
        def x2tm_transposes(h=h, p4g2=p4g2, st_prev=st_prev):
          for g in range(G):
            for tc in range(8):
                tsl = kb.slots[6 + ((g * 8 + tc) % 2)]
                w = wmerge(B_p1.ready if (g == 0 and tc == 0) else None,
                           tsl.last_evac, CONSTS)
                mv = kb.op("tensor", lambda e, g=g, tc=tc, tsl=tsl:
                           e.transpose(tsl.ap[:, 0:128],
                                       pool1[:, g * TH + tc * 128:
                                             g * TH + (tc + 1) * 128],
                                       ident[:]),
                           waits=w, inc=kb.MM)
                b = (g * 8 + tc) % 4
                cv = kb.scop(lambda e, tsl=tsl, b=b: e.copy(
                    pool4[:, b * 128:(b + 1) * 128], tsl.ap[:, 0:128]),
                    waits=wmerge({kb.MM: mv}, st_prev[b],
                                 p4g2 if (g == 0 and tc == 0) else None))
                tsl.last_evac = {kb.SC: cv}
                spv = kb.store(
                    bass.AP(x2tm.tensor,
                            x2tm.offset + (h * 1024 + tc * 128) * 1032
                            + g * 128,
                            [[1032, 128], [1, 128]]),
                    pool4[:, b * 128:(b + 1) * 128], waits={kb.SC: cv})
                st_prev[b] = spv
                B_p4.used_d(spv)
                B_x2tm.set_ready_d(spv)
          B_p4.set_ready(kb.SC, kb.SC.n)
          B_p1.used(kb.MM, kb.MM.n)
        if h == 0:
            x2tm_transposes()
        else:
            defer_x2tm.append(x2tm_transposes)

        lsl = kb.slots[4]
        for tc in range(8):
            for k in range(G):
                w = wmerge(B_p1.ready if (tc == 0 and k == 0) else None,
                           lsl.last_evac if k == 0 else None, CONSTS)
                lhsT = pool1[:, k * TH + tc * 128: k * TH + (tc + 1) * 128]
                kb.mm(lsl.ap[0:128, 0:8], lhsT, gwTt[:, k * 8:(k + 1) * 8],
                      start=(k == 0), stop=(k == G - 1), waits=w,
                      inc=(k == G - 1))
            j = h * 8 + tc
            sv = kb.scop(lambda e, j=j: e.copy(probs[:, j * 8:(j + 1) * 8],
                                               lsl.ap[0:128, 0:8]),
                         waits={kb.MM: kb.MM.n})
            lsl.last_evac = {kb.SC: sv}
        B_p1.used(kb.MM, kb.MM.n)
    B_probs.set_ready(kb.SC, kb.SC.n)

    pv = bass.AP(probs, 0, [[128, 128], [8, 16], [1, 8]])
    SCRSTR = 5280
    gt4 = lambda: bass.AP(scr, 0, [[SCRSTR, 128], [64, 16], [8, 8], [1, 8]])
    w44 = lambda: bass.AP(scr, 1024, [[SCRSTR, 128], [64, 16], [8, 8], [1, 8]])
    gt128 = scr[:, 0:128]
    w128 = scr[:, 1024:1152]
    cnt8 = scr[:, 1152:1160]

    v = kb.vec(lambda e: e.tensor_reduce(
        bass.AP(bis, 32, [[96, 128], [1, 16]]), pv, AX.X, ALU.max),
        waits=wmerge({kb.SC: kb.SC.n}, B_probs.ready))
    v = kb.vec(lambda e: e.tensor_tensor(
        pv, pv, bass.AP(bis, 32, [[96, 128], [1, 16], [0, 8]]),
        ALU.subtract))
    v = kb.scop(lambda e: e.activation(out=probs[:], in_=probs[:],
                                       func=AF.Exp),
                waits={kb.VE: kb.VE.n})
    v = kb.vec(lambda e: e.tensor_reduce(
        bass.AP(bis, 32, [[96, 128], [1, 16]]), pv, AX.X, ALU.add),
        waits={kb.SC: v})
    v = kb.vec(lambda e: e.reciprocal(bass.AP(bis, 48, [[96, 128], [1, 16]]),
                                      bass.AP(bis, 32, [[96, 128], [1, 16]])))
    v = kb.vec(lambda e: e.tensor_tensor(
        pv, pv, bass.AP(bis, 48, [[96, 128], [1, 16], [0, 8]]), ALU.mult))
    v = kb.vec(lambda e: e.tensor_tensor(
        gt4(),
        bass.AP(probs, 0, [[128, 128], [8, 16], [0, 8], [1, 8]]),
        bass.AP(probs, 0, [[128, 128], [8, 16], [1, 8], [0, 8]]),
        ALU.is_gt), waits=wmerge({kb.VE: v}, B_tb.last_use, stdb_use[0],
                                 stdb_use[1], udb_use[0], udb_use[1]))
    v = kb.vec(lambda e: e.tensor_tensor(
        w44(), gt4(),
        bass.AP(probs, 0, [[128, 128], [8, 16], [0, 8], [1, 8]]), ALU.mult))
    v = kb.vec(lambda e: e.tensor_reduce(
        bass.AP(s_sb, 0, [[128, 128], [8, 16], [1, 8]]), w44(), AX.X,
        ALU.add))
    v = kb.vec(lambda e: e.tensor_reduce(
        bass.AP(scr, 1024, [[SCRSTR, 128], [8, 16], [1, 8]]), gt4(), AX.X,
        ALU.add))
    v = kb.vec(lambda e: e.tensor_single_scalar(assn[:], s_sb[:], THR,
                                                ALU.is_lt))
    v = kb.vec(lambda e: e.tensor_single_scalar(gt128, probs[:], 1.0,
                                                ALU.subtract))
    v = kb.vec(lambda e: e.tensor_tensor(gt128, gt128, w128, ALU.subtract))
    kb.vec(lambda e: e.tensor_scalar(w128, assn[:], 1e9, -1e9,
                                     ALU.mult, ALU.add))
    kb.vec(lambda e: e.tensor_tensor(gt128, gt128, assn[:], ALU.mult))
    rv = kb.vec(lambda e: e.tensor_tensor(s_sb[:], gt128, w128, ALU.add))
    rtapv = {}
    if "sm_R" in small_taps:
        rtapv = kb.store(small_taps["sm_R"], s_sb[:], waits={kb.VE: rv})
    rsp = kb.store(rin_h.ap(), s_sb[:], waits={kb.VE: rv})
    ccv2 = kb.op("gpsimd", lambda e: e.collective_compute(
        "AllGather", ALU.bypass, replica_groups=[list(range(8))],
        ins=[rin_h.ap().opt()], outs=[rout_h.ap().opt()]),
        waits=wmerge(rsp, {kb.CC: 1}), inc=kb.CC)
    for fn_ in defer_x2tm:
        fn_()
    sbcg = B_sbc.begin_write()
    rlv = kb.gdma(sbc[:, 0:1024],
                  bass.AP(rout_h, 0, [[128, 128], [16384, 8], [1, 128]]),
                  waits=wmerge({kb.CC: ccv2}, sbcg))
    B_sbc.set_ready_d(rlv)

    kb.vec(lambda e: e.memset(bis[:, 24:32], -8.5), waits=rlv)
    kb.vec(lambda e: e.memset(bis[:, 32:40], 0.5))
    kb.vec(lambda e: e.memset(bis[:, 40:48], -2e9))
    kb.vec(lambda e: e.memset(bis[:, 48:56], -9e8))
    kb.vec(lambda e: e.tensor_copy(bis[:, 16:24], bis[:, 24:32]))

    Rall = lambda: bass.AP(sbc, 0, [[2048, 128], [1, 8], [128, 8], [8, 16]])
    gtb = lambda: bass.AP(scr, 0, [[SCRSTR, 128], [1, 8], [128, 8], [8, 16]])
    midb = lambda: bass.AP(bis, 16, [[96, 128], [1, 8], [0, 8], [0, 16]])
    bsl = kb.slots[6]

    def bis_round(first):
        v1 = kb.vec(lambda e: e.tensor_tensor(gtb(), Rall(), midb(),
                                              ALU.is_ge),
                    waits=wmerge({kb.VE: kb.VE.n}, B_sbc.ready))
        v2 = kb.vec(lambda e: e.tensor_reduce(
            bass.AP(scr, 0, [[SCRSTR, 128], [1, 8], [128, 8]]), gtb(),
            AX.X, ALU.add))
        v3 = kb.vec(lambda e: e.tensor_reduce(
            bass.AP(scr, 1152, [[SCRSTR, 128], [1, 8]]),
            bass.AP(scr, 0, [[SCRSTR, 128], [1, 8], [128, 8]]),
            AX.X, ALU.add))
        mv = kb.mm(bsl.ap[0:128, 0:8], ones128[:], cnt8, True, True,
                   waits=wmerge({kb.VE: v3}, bsl.last_evac), inc=True)
        v4 = kb.vec(lambda e: e.tensor_single_scalar(
            bis[:, 56:64], bsl.ap[0:128, 0:8], CAP, ALU.is_ge),
            waits={kb.MM: mv})
        bsl.last_evac = {kb.VE: v4}
        kb.vec(lambda e: e.tensor_scalar(bis[:, 64:72], bis[:, 56:64],
                                         -1.0, 1.0, ALU.mult, ALU.add))
        if first:
            kb.vec(lambda e: e.tensor_single_scalar(bis[:, 72:80],
                                                    bis[:, 56:64], -8.5,
                                                    ALU.mult))
            kb.vec(lambda e: e.tensor_single_scalar(bis[:, 80:88],
                                                    bis[:, 64:72], -2e9,
                                                    ALU.mult))
            kb.vec(lambda e: e.tensor_tensor(bis[:, 0:8], bis[:, 72:80],
                                             bis[:, 80:88], ALU.add))
            kb.vec(lambda e: e.tensor_single_scalar(bis[:, 72:80],
                                                    bis[:, 56:64], 0.5,
                                                    ALU.mult))
            kb.vec(lambda e: e.tensor_single_scalar(bis[:, 80:88],
                                                    bis[:, 64:72], -9e8,
                                                    ALU.mult))
            kb.vec(lambda e: e.tensor_tensor(bis[:, 8:16], bis[:, 72:80],
                                             bis[:, 80:88], ALU.add))
        else:
            kb.vec(lambda e: e.tensor_tensor(bis[:, 72:80], bis[:, 16:24],
                                             bis[:, 56:64], ALU.mult))
            kb.vec(lambda e: e.tensor_tensor(bis[:, 80:88], bis[:, 0:8],
                                             bis[:, 64:72], ALU.mult))
            kb.vec(lambda e: e.tensor_tensor(bis[:, 0:8], bis[:, 72:80],
                                             bis[:, 80:88], ALU.add))
            kb.vec(lambda e: e.tensor_tensor(bis[:, 72:80], bis[:, 8:16],
                                             bis[:, 56:64], ALU.mult))
            kb.vec(lambda e: e.tensor_tensor(bis[:, 80:88], bis[:, 16:24],
                                             bis[:, 64:72], ALU.mult))
            kb.vec(lambda e: e.tensor_tensor(bis[:, 8:16], bis[:, 72:80],
                                             bis[:, 80:88], ALU.add))
        kb.vec(lambda e: e.tensor_tensor(bis[:, 16:24], bis[:, 0:8],
                                         bis[:, 8:16], ALU.add))
        kb.vec(lambda e: e.tensor_single_scalar(bis[:, 16:24], bis[:, 16:24],
                                                0.5, ALU.mult))

    bis_round(True)
    for _ in range(41):
        bis_round(False)
    B_sbc.used(kb.VE, kb.VE.n)

    v = kb.vec(lambda e: e.tensor_tensor(
        bass.AP(scr, 0, [[SCRSTR, 128], [8, 16], [1, 8]]),
        bass.AP(s_sb, 0, [[128, 128], [8, 16], [1, 8]]),
        bass.AP(bis, 0, [[96, 128], [0, 16], [1, 8]]), ALU.is_ge),
        waits={kb.VE: kb.VE.n})
    v = kb.vec(lambda e: e.tensor_tensor(s_sb[:], probs[:], gt128,
                                         ALU.mult),
               waits=wmerge(rtapv, rsp))
    sv_done = kb.vec(lambda e: e.tensor_tensor(s_sb[:], s_sb[:], assn[:],
                                               ALU.mult))
    ssp = kb.store(bass.AP(x2tm.tensor, x2tm.offset + 1024,
                           [[1032, 128], [1032 * 128, 16], [1, 8]]),
                   s_sb[:], waits={kb.VE: sv_done})
    B_x2tm.set_ready_d(ssp)

    m8 = scr[0:8, 1024:3072]
    p8 = scr[0:8, 3072:5120]
    for j in range(16):
        tsl = kb.slots[6 + (j % 2)]
        w = wmerge({kb.VE: sv_done} if j == 0 else None, tsl.last_evac,
                   CONSTS)
        mv = kb.op("tensor", lambda e, j=j, tsl=tsl: e.transpose(
            tsl.ap[0:8, 0:128], s_sb[:, j * 8:(j + 1) * 8], ident[:]),
            waits=w, inc=kb.MM)
        cv = kb.scop(lambda e, j=j, tsl=tsl: e.copy(
            m8[:, j * 128:(j + 1) * 128], tsl.ap[0:8, 0:128]),
            waits={kb.MM: mv})
        tsl.last_evac = {kb.SC: cv}
    v = kb.vec(lambda e: e.tensor_single_scalar(m8, m8, 0.0, ALU.is_gt),
               waits={kb.SC: kb.SC.n})
    p8v = kb.vec(lambda e: e.tensor_tensor_scan(p8, m8, m8, 0.0,
                                                ALU.add, ALU.bypass))
    kb.vec(lambda e: e.tensor_copy(mbt[0:8, :], p8[:, 0:1024]),
           waits=wmerge(B_mb.begin_write(), {kb.VE: p8v}))
    p8v = kb.vec(lambda e: e.tensor_copy(rbt[0:8, :], p8[:, 1024:2048]),
                 waits=B_rb.begin_write())
    pos_ready = {kb.VE: p8v}

    zv2 = kb.vec(lambda e: e.memset(tb, 0.0), waits=B_tb.last_use)
    B_tb.set_ready(kb.VE, zv2)
    for tc in range(16):
        sp = kb.store(bass.AP(ymtm.tensor, ymtm.offset + tc * 128 * D,
                              [[D, 128], [1, D]]), tb, waits={kb.VE: zv2})
        B_tb.used_d(sp)
        B_ymtm.set_ready_d(sp)
    sp = kb.store(bass.AP(ymtm.tensor, ymtm.offset + T * D, [[D, 1], [1, D]]),
                  tb[0:1, :], waits={kb.VE: zv2})
    B_tb.used_d(sp)
    B_ymtm.set_ready_d(sp)
    ymtm_zero = dict(B_ymtm.ready)
    sp = kb.store(bass.AP(x2tm.tensor, x2tm.offset + T * 1032,
                          [[1032, 1], [1, 1024]]), tb[0:1, :],
                  waits={kb.VE: zv2})
    B_tb.used_d(sp)
    B_x2tm.set_ready_d(sp)
    sp = kb.store(bass.AP(x2tm.tensor, x2tm.offset + T * 1032 + 1024,
                          [[8, 1], [1, 8]]), tb[0:1, 0:8], waits={kb.VE: zv2})
    B_tb.used_d(sp)
    B_x2tm.set_ready_d(sp)

    if "sm_probs" in small_taps:
        kb.store(small_taps["sm_probs"], probs[:], waits={kb.VE: kb.VE.n})
    if "sm_s" in small_taps:
        kb.store(small_taps["sm_s"], s_sb[:], waits={kb.VE: kb.VE.n})
    if "sm_bis" in small_taps:
        kb.store(small_taps["sm_bis"], bis[:], waits={kb.VE: kb.VE.n})

    B_p4lo, B_p4hi = Buf(), Buf()
    scrS = [Buf() for _ in range(4)]
    for b_ in scrS:
        b_.used(kb.VE, kb.VE.n)
        b_.used(kb.MM, kb.MM.n)
    w1bufs = {}
    w2bufs = {}
    tokr = {}
    xgb = {}

    def plan_w1_loads(ee):
        w1b = Buf()
        g1 = B_p1.begin_write()
        g2 = B_p2.begin_write()
        for k in range(G):
            pool = pool1 if k < 4 else pool2
            w = (g1 if k == 0 else (g2 if k == 4 else None))
            kb.load(pool[:, (k % 4) * 1024:(k % 4 + 1) * 1024],
                    bass.AP(w1T.tensor, w1T.offset + ee * D * F + k * 128 * F,
                            [[F, 128], [1, F]]), waits=w)
        w1b.set_ready_d(kb.lw_all())
        w1bufs[ee] = w1b

    def plan_w2_loads(ee):
        w2b = Buf()
        g3 = B_p3.begin_write()
        for k in range(G):
            kb.load(pool3[:, k * 1024:(k + 1) * 1024],
                    bass.AP(w2T.tensor, w2T.offset + ee * F * D + k * 128 * D,
                            [[D, 128], [1, D]]), waits=g3 if k == 0 else None)
        w2b.set_ready_d(kb.lw_all())
        w2bufs[ee] = w2b

    def plan_lists(ee):
        sgate = B_sbc.begin_write()
        for tt2 in range(4):
            tsl = kb.slots[6 + (tt2 % 2)]
            w = wmerge(pos_ready if tt2 == 0 else None, tsl.last_evac,
                       CONSTS)
            psrc = mbt if tt2 < 2 else rbt
            mv = kb.mm(tsl.ap, sel8t[0:8, ee * 128:(ee + 1) * 128],
                       psrc[0:8, (tt2 % 2) * 512:(tt2 % 2 + 1) * 512],
                       True, True, waits=w, inc=True)
            cv = kb.scop(lambda e, tt2=tt2, tsl=tsl: e.copy(
                sbc[:, tt2 * 512:(tt2 + 1) * 512], tsl.ap),
                waits=wmerge({kb.MM: mv}, sgate if tt2 == 0 else None))
            tsl.last_evac = {kb.SC: cv}
        posb_ready = {kb.SC: kb.SC.n}
        cmp = tb
        for sg in range(4):
            w = wmerge(posb_ready if sg == 0 else None, B_tb.last_use)
            v = kb.vec(lambda e, sg=sg: e.tensor_scalar(
                scr[0:128, 0:1024], sbc[:, 0:1024], misct[:, sg:sg + 1],
                None, ALU.is_le, ALU.bypass), waits=w)
            B_tb.set_ready(kb.VE, v)
            v = kb.vec(lambda e, sg=sg, ee=ee: e.tensor_reduce(
                tokcf[:, ee * 4 + sg:ee * 4 + sg + 1], scr[0:128, 0:1024],
                AX.X, ALU.add))
            v = kb.vec(lambda e, sg=sg: e.tensor_scalar(
                scr[0:128, 0:1024], sbc[:, 1024:2048], misct[:, sg:sg + 1],
                None, ALU.is_le, ALU.bypass))
            v = kb.vec(lambda e, sg=sg, ee=ee: e.tensor_reduce(
                bis[:, 8:9], scr[0:128, 0:1024], AX.X, ALU.add))
            v = kb.vec(lambda e, sg=sg, ee=ee: e.tensor_tensor(
                tokcf[:, ee * 4 + sg:ee * 4 + sg + 1],
                tokcf[:, ee * 4 + sg:ee * 4 + sg + 1],
                bis[:, 8:9], ALU.add))
            B_tb.used(kb.VE, v)
        vtok = kb.vec(lambda e, ee=ee: e.tensor_copy(
            tokci[:, ee * 4:(ee + 1) * 4], tokcf[:, ee * 4:(ee + 1) * 4]))
        tokr[ee] = {kb.VE: vtok}
        B_sbc.used(kb.VE, vtok)

    def plan_gathers(ee, first):
        b = Buf()
        for sg in range(4):
            w = wmerge(B_x2tm.ready if first and sg == 0 else None,
                       tokr[ee] if sg == 0 else None,
                       scrS[sg].begin_write())
            gv = kb.store(scr[:, 1024 + sg * 1032:1024 + (sg + 1) * 1032],
                          misct[:, 0:1], waits=w)
            wd, fn, inc, add = kb.ops["gpsimd"].pop()
            idxap = tokci[:, ee * 4 + sg:ee * 4 + sg + 1]
            fn2 = (lambda e, sg=sg, idxap=idxap: e.indirect_dma_start(
                scr[:, 1024 + sg * 1032:1024 + (sg + 1) * 1032], None,
                x2tm, IndirectOffsetOnAxis(idxap, 0)))
            kb.ops["gpsimd"].append((wd, fn2, inc, add))
            scrS[sg].set_ready_d(gv)
        xgb[ee] = b

    plan_w1_loads(0)
    plan_w2_loads(0)
    plan_lists(0)
    plan_gathers(0, True)
    for ee in range(NE):
        w1b, w2b = w1bufs[ee], w2bufs[ee]
        p4lo_gate = B_p4lo.begin_write()
        p4lo_g = [None] * G
        for g in range(G):
            for sg in range(4):
                tsl = kb.slots[(g * 4 + sg) % 4]
                w = wmerge(scrS[sg].ready if g == 0 else None,
                           p4lo_gate if (sg == 0 and g == 0) else None,
                           tsl.last_evac)
                mv = kb.op("tensor", lambda e, sg=sg, g=g, tsl=tsl:
                           e.transpose(tsl.ap[:, 0:128],
                                       scr[:, 1024 + sg * 1032 + g * 128:
                                           1024 + sg * 1032 + (g + 1) * 128],
                                       ident[:]), waits=w, inc=kb.MM)
                cv = kb.scop(lambda e, sg=sg, g=g, tsl=tsl: e.copy(
                    pool4[:, g * 512 + sg * 128:g * 512 + (sg + 1) * 128],
                    tsl.ap[:, 0:128]), waits={kb.MM: mv})
                tsl.last_evac = {kb.SC: cv}
                scrS[sg].used(kb.MM, mv)
            p4lo_g[g] = {kb.SC: kb.SC.n}
        B_p4lo.set_ready(kb.SC, kb.SC.n)
        srow = sbc[0:1, 1536:2048]
        sbc_gate = B_sbc.begin_write()
        for sg in range(4):
            tsl = kb.slots[6 + (sg % 2)]
            w = wmerge(scrS[sg].ready, tsl.last_evac,
                       sbc_gate if sg == 0 else None)
            mv = kb.op("tensor", lambda e, sg=sg, tsl=tsl, ee=ee:
                       e.transpose(tsl.ap[0:1, 0:128],
                                   scr[:, 1024 + sg * 1032 + 1024 + ee:
                                       1024 + sg * 1032 + 1024 + ee + 1],
                                   ident[:]), waits=w, inc=kb.MM)
            cv = kb.scop(lambda e, sg=sg, tsl=tsl: e.copy(
                srow[0:1, sg * 128:(sg + 1) * 128], tsl.ap[0:1, 0:128]),
                waits={kb.MM: mv})
            tsl.last_evac = {kb.SC: cv}
            scrS[sg].used(kb.MM, mv)
        bsl2 = kb.slots[6]
        mv = kb.mm(bsl2.ap, onesk[:], srow, True, True,
                   waits=wmerge({kb.SC: kb.SC.n}, bsl2.last_evac), inc=True)
        scv0 = kb.scop(lambda e, bsl2=bsl2: e.copy(sbc[:, 0:512], bsl2.ap),
                       waits={kb.MM: mv})
        bsl2.last_evac = {kb.SC: scv0}
        B_sbc.set_ready(kb.SC, scv0)

        hsl = [kb.slots[0], kb.slots[1]]
        p4hi_gate = B_p4hi.begin_write()
        gelu_done = []
        for jf in range(8):
            sl = hsl[jf % 2]
            mv = None
            for k in range(G):
                pool = pool1 if k < 4 else pool2
                w = wmerge(w1b.ready if (jf == 0 and k == 0) else None,
                           p4lo_g[k] if jf == 0 else None,
                           sl.last_evac if k == 0 else None)
                lhsT = pool[:, (k % 4) * 1024 + jf * 128:
                            (k % 4) * 1024 + (jf + 1) * 128]
                mv = kb.mm(sl.ap, lhsT, pool4[:, k * 512:(k + 1) * 512],
                           start=(k == 0), stop=(k == G - 1), waits=w,
                           inc=(k == G - 1))
                B_p4lo.used(kb.MM, mv)
            gvv = kb.act(pool4[:, 4096 + jf * 512:4096 + (jf + 1) * 512],
                         sl.ap, AF.Gelu,
                         waits=wmerge({kb.MM: mv},
                                      p4hi_gate if jf == 0 else None))
            gelu_done.append(gvv)
            sl.last_evac = {kb.SC: gvv}
        hready = 0
        for jf in range(8):
            scv = kb.vec(lambda e, jf=jf: e.tensor_tensor(
                pool4[:, 4096 + jf * 512:4096 + (jf + 1) * 512],
                pool4[:, 4096 + jf * 512:4096 + (jf + 1) * 512],
                sbc[:, 0:512], ALU.mult),
                waits=wmerge({kb.SC: gelu_done[jf]},
                             B_sbc.ready if jf == 0 else None))
            hready = scv
            kb.slots[jf % 2].last_evac = wmerge(
                kb.slots[jf % 2].last_evac, {kb.VE: scv})
        B_sbc.used(kb.VE, hready)

        B_p1.used(kb.MM, kb.MM.n)
        B_p2.used(kb.MM, kb.MM.n)
        if ee + 1 < NE:
            plan_w1_loads(ee + 1)
            plan_lists(ee + 1)

        ysl = [kb.slots[4], kb.slots[5]]
        for j2 in range(8):
            sl = ysl[j2 % 2]
            mv = None
            for k in range(G):
                w = wmerge(w2b.ready if (j2 == 0 and k == 0) else None,
                           {kb.VE: hready} if (j2 == 0 and k == 0) else None,
                           sl.last_evac if k == 0 else None)
                lhsT = pool3[:, k * 1024 + j2 * 128:
                             k * 1024 + (j2 + 1) * 128]
                mv = kb.mm(sl.ap, lhsT,
                           pool4[:, 4096 + k * 512:4096 + (k + 1) * 512],
                           start=(k == 0), stop=(k == G - 1), waits=w,
                           inc=(k == G - 1))
                B_p4hi.used(kb.MM, mv)
            cv = kb.scop(lambda e, j2=j2, sl=sl: e.copy(tbv[j2 % 2], sl.ap),
                         waits=wmerge({kb.MM: mv}, B_tb.last_use))
            B_tb.set_ready(kb.SC, cv)
            sl.last_evac = {kb.SC: cv}
            for sg in range(4):
                tsl = kb.slots[2 + (sg % 2)]
                w = wmerge({kb.SC: cv}, tsl.last_evac,
                           B_p4lo.begin_write() if j2 == 0 and sg == 0
                           else None)
                mv2 = kb.op("tensor", lambda e, j2=j2, sg=sg, tsl=tsl:
                            e.transpose(tsl.ap[:, 0:128],
                                        tb[:, (j2 % 2) * 512 + sg * 128:
                                           (j2 % 2) * 512 + (sg + 1) * 128],
                                        ident[:]), waits=w, inc=kb.MM)
                cv2 = kb.scop(lambda e, j2=j2, sg=sg, tsl=tsl: e.copy(
                    pool4[:, sg * 1024 + j2 * 128:sg * 1024 + (j2 + 1) * 128],
                    tsl.ap[:, 0:128]), waits={kb.MM: mv2})
                tsl.last_evac = {kb.SC: cv2}
                B_tb.used(kb.MM, mv2)
        for sg in range(4):
            w = wmerge({kb.SC: kb.SC.n},
                       ymtm_zero if ee == 0 and sg == 0 else None)
            sp = kb.store(pool4[:, sg * 1024:(sg + 1) * 1024],
                          misct[:, 0:1], waits=w, sem_i=3)
            wd, fn, inc, add = kb.ops["gpsimd"].pop()
            idxap = tokci[:, ee * 4 + sg:ee * 4 + sg + 1]
            fn2 = (lambda e, sg=sg, idxap=idxap: e.indirect_dma_start(
                ymtm, IndirectOffsetOnAxis(idxap, 0),
                pool4[:, sg * 1024:(sg + 1) * 1024], None,
                compute_op=ALU.add))
            kb.ops["gpsimd"].append((wd, fn2, inc, add))
            B_p4lo.used_d(sp)
            B_ymtm.set_ready_d(sp)
        B_p3.used(kb.MM, kb.MM.n)
        if ee + 1 < NE:
            plan_w2_loads(ee + 1)
            plan_gathers(ee + 1, False)

    if "sm_ymoe" in small_taps:
        for tc in range(16):
            kb.store(bass.AP(small_taps["sm_ymoe"].tensor,
                             small_taps["sm_ymoe"].offset + tc * 128 * D,
                             [[D, 128], [1, D]]),
                     misct[:, 0:1], waits=kb.sp_all())
            wd, fn, inc, add = kb.ops["gpsimd"].pop()
            fn2 = (lambda e, tc=tc: e.dma_start(
                out=bass.AP(small_taps["sm_ymoe"].tensor,
                            small_taps["sm_ymoe"].offset + tc * 128 * D,
                            [[D, 128], [1, D]]),
                in_=bass.AP(ymtm.tensor, ymtm.offset + tc * 128 * D,
                            [[D, 128], [1, D]])))
            kb.ops["gpsimd"].append((wd, fn2, inc, add))

    g4f = wmerge(B_p4.begin_write(), B_p4lo.begin_write(),
                 B_p4hi.begin_write())
    ch_use = [{}, {}]
    sq_use = [{}, {}]
    for tc in range(16):
        q = tc % 2
        x2o, ymo = q * 2048, q * 2048 + 1024
        w = wmerge(g4f if tc == 0 else None, ch_use[q],
                   B_x2tm.ready if tc == 0 else None)
        l1 = kb.load(pool4[:, x2o:x2o + 1024],
                     bass.AP(x2tm.tensor, x2tm.offset + tc * 128 * 1032,
                             [[1032, 128], [1, 1024]]), waits=w)
        l2 = kb.load(pool4[:, ymo:ymo + 1024],
                     bass.AP(ymtm.tensor, ymtm.offset + tc * 128 * D,
                             [[D, 128], [1, D]]),
                     waits=wmerge(B_ymtm.ready if tc == 0 else None))
        v = kb.vec(lambda e, x2o=x2o, ymo=ymo: e.tensor_tensor(
            pool4[:, x2o:x2o + 1024], pool4[:, x2o:x2o + 1024],
            pool4[:, ymo:ymo + 1024], ALU.add), waits=wmerge(l1, l2))
        c0 = q * 8
        sqo = 1024 + q * 1024
        s1 = kb.op("scalar", lambda e, x2o=x2o, c0=c0, sqo=sqo:
                   e.activation(out=scr[:, sqo:sqo + 1024],
                                in_=pool4[:, x2o:x2o + 1024], func=AF.Square,
                                accum_out=bis[:, c0 + 1:c0 + 2]),
                   waits=wmerge({kb.VE: v}, sq_use[q],
                                {kb.SC: kb.SC.n}), inc=kb.SC)
        s2 = kb.op("scalar", lambda e, x2o=x2o, c0=c0, sqo=sqo:
                   e.activation(out=scr[:, sqo:sqo + 1024],
                                in_=pool4[:, x2o:x2o + 1024], func=AF.Copy,
                                accum_out=bis[:, c0:c0 + 1]),
                   waits={kb.SC: kb.SC.n}, inc=kb.SC)
        sq_use[q] = {kb.SC: s2}
        v = kb.vec(lambda e, c0=c0: e.tensor_single_scalar(
            bis[:, c0:c0 + 1], bis[:, c0:c0 + 1], 1.0 / D, ALU.mult),
            waits={kb.SC: s2})
        v = kb.vec(lambda e, c0=c0: e.tensor_single_scalar(
            bis[:, c0 + 1:c0 + 2], bis[:, c0 + 1:c0 + 2], 1.0 / D, ALU.mult))
        v = kb.vec(lambda e, c0=c0: e.tensor_tensor(
            bis[:, c0 + 2:c0 + 3], bis[:, c0:c0 + 1], bis[:, c0:c0 + 1],
            ALU.mult))
        v = kb.vec(lambda e, c0=c0: e.tensor_tensor(
            bis[:, c0 + 1:c0 + 2], bis[:, c0 + 1:c0 + 2],
            bis[:, c0 + 2:c0 + 3], ALU.subtract))
        v5 = kb.scop(lambda e, c0=c0: e.activation(
            out=bis[:, c0 + 1:c0 + 2], in_=bis[:, c0 + 1:c0 + 2],
            func=AF.Sqrt, bias=epsc128[:]), waits={kb.VE: v})
        v6 = kb.vec(lambda e, c0=c0: e.reciprocal(
            bis[:, c0 + 2:c0 + 3], bis[:, c0 + 1:c0 + 2]),
            waits={kb.SC: v5})
        v = kb.vec(lambda e, x2o=x2o, c0=c0: e.tensor_scalar(
            pool4[:, x2o:x2o + 1024], pool4[:, x2o:x2o + 1024],
            bis[:, c0:c0 + 1], None, ALU.subtract, ALU.bypass))
        v = kb.vec(lambda e, x2o=x2o, c0=c0: e.tensor_scalar(
            pool4[:, x2o:x2o + 1024], pool4[:, x2o:x2o + 1024],
            bis[:, c0 + 2:c0 + 3], None, ALU.mult, ALU.bypass))
        sp = kb.store(bass.AP(out_tm.tensor, out_tm.offset + tc * 128 * D,
                              [[D, 128], [1, D]]), pool4[:, x2o:x2o + 1024],
                      waits={kb.VE: v})
        ch_use[q] = sp

    kb.op("gpsimd", lambda e: e.engine_nop(), waits=kb.sp_all())

    def emit(engine, name):
        seen = {}
        for w, fn, inc, add in kb.ops[name]:
            for s, v in w.items():
                if v and seen.get(s, 0) < v:
                    engine.wait_ge(s.h, v)
                    seen[s] = v
            instr = fn(engine)
            if inc is not None:
                instr.then_inc(inc.h, add)

    @block.sync
    def _(sync):
        emit(sync, "sync")

    @block.gpsimd
    def _(gpsimd):
        emit(gpsimd, "gpsimd")

    @block.scalar
    def _(scalar):
        emit(scalar, "scalar")

    @block.vector
    def _(vector):
        emit(vector, "vector")

    @block.tensor
    def _(tensor):
        emit(tensor, "tensor")

    es.close()
    return nc


def _pack_biases(kw):
    src = {"fr_b": "L0_attn_fr_b", "dp_b": "L0_attn_dp_b",
           "bp_b": "L0_attn_bp_b", "cp_b": "L0_attn_cp_b",
           "m1_b": "L0_attn_m1_b", "m2_b": "L0_attn_m2_b",
           "gate_b": "L0_attn_gate_b", "ffn1_b": "L0_attn_ffn1_b",
           "ffn2_b": "L0_attn_ffn2_b", "norm_g": "L0_attn_norm_g",
           "norm_b": "L0_attn_norm_b", "ffn_g": "L0_attn_ffn_g",
           "ffn_bn": "L0_attn_ffn_bn", "skip": "L0_attn_skip"}
    cols = []
    for name, _n in BIAS_ORDER:
        v = np.asarray(kw[src[name]], dtype=np.float32)
        cols.append(v.reshape(-1, 128).T)
    return np.ascontiguousarray(np.concatenate(cols, axis=1))


def kernel(**kw):
    x = np.asarray(kw["L0_attn_x"], dtype=np.float32)
    B, L, _ = x.shape
    biases = _pack_biases(kw)
    ident = np.eye(128, dtype=np.float32)
    misc = np.zeros((128, 20), dtype=np.float32)
    for tc in range(16):
        misc[:, tc] = tc * 128 + np.arange(128)
    misc[0:8, 16] = np.arange(8) * 513.0
    sel8 = np.zeros((8, 1024), dtype=np.float32)
    for ee in range(8):
        sel8[ee, ee * 128:(ee + 1) * 128] = 1.0
    w1T = np.ascontiguousarray(
        np.asarray(kw["L0_ffn_w1"], np.float32).transpose(0, 2, 1))
    w2T = np.ascontiguousarray(
        np.asarray(kw["L0_ffn_w2"], np.float32).transpose(0, 2, 1))
    gwT = np.ascontiguousarray(np.asarray(kw["L0_ffn_gate_w"], np.float32).T)

    shared = {
        "fr_w": np.ascontiguousarray(kw["L0_attn_fr_w"], np.float32),
        "dp_w": np.ascontiguousarray(kw["L0_attn_dp_w"], np.float32),
        "bp_w": np.ascontiguousarray(kw["L0_attn_bp_w"], np.float32),
        "cp_w": np.ascontiguousarray(kw["L0_attn_cp_w"], np.float32),
        "m1_w": np.ascontiguousarray(kw["L0_attn_m1_w"], np.float32),
        "m2_w": np.ascontiguousarray(kw["L0_attn_m2_w"], np.float32),
        "gate_w": np.ascontiguousarray(kw["L0_attn_gate_w"], np.float32),
        "ffn1_w": np.ascontiguousarray(kw["L0_attn_ffn1_w"], np.float32),
        "ffn2_w": np.ascontiguousarray(kw["L0_attn_ffn2_w"], np.float32),
        "gwT": gwT, "w1T": w1T, "w2T": w2T,
        "biases_in": biases, "ident_in": ident,
        "misc_in": misc, "sel8_in": sel8,
    }
    in_maps = []
    for c in range(8):
        b, h = c // 2, c % 2
        xs = np.ascontiguousarray(x[b, h * T:(h + 1) * T, :].T)
        sel = np.zeros((128, 8), dtype=np.float32)
        if h == 1:
            sel[:, c - 1] = 1.0
        m = dict(shared)
        m["x_fm"] = xs
        m["sel_in"] = sel
        in_maps.append(m)

    nc = build_nc()
    res = run_bass_kernel_spmd(nc, in_maps, core_ids=list(range(8)))
    out = np.empty((B, L, D), dtype=np.float32)
    for c in range(8):
        b, h = c // 2, c % 2
        out[b, h * T:(h + 1) * T, :] = res.results[c]["out_tm"]
    return out


# version 11
# speedup vs baseline: 1.0070x; 1.0070x over previous
import numpy as np
import jax
jax.config.update('jax_compilation_cache_dir', '/root/.bass_jax_cache')
jax.config.update('jax_persistent_cache_min_compile_time_secs', 1.0)
jax.config.update('jax_persistent_cache_min_entry_size_bytes', 0)
import concourse.bass as bass
import concourse.mybir as mybir
from concourse.bass_utils import run_bass_kernel_spmd
from concourse.bass_isa import ReduceOp
from concourse import library_config
from concourse.bass_core_rust import IndirectOffsetOnAxis

F32 = mybir.dt.float32
ALU = mybir.AluOpType
AF = mybir.ActivationFunctionType
AX = mybir.AxisListType

D = 1024
G = 8
T = 2048
TH = 1024
NE = 8
F = 1024
CAP = 2048.0
THR = 0.9
EPS = 1e-5
ENGS = ("sync", "gpsimd", "scalar", "vector", "tensor")

BIAS_ORDER = [("fr_b", 8), ("dp_b", 8), ("bp_b", 8), ("cp_b", 8),
              ("m1_b", 16), ("m2_b", 8), ("gate_b", 8), ("ffn1_b", 32),
              ("ffn2_b", 8), ("norm_g", 8), ("norm_b", 8), ("ffn_g", 8),
              ("ffn_bn", 8), ("skip", 8)]
NBIAS = sum(n for _, n in BIAS_ORDER)


class Sem:
    __slots__ = ("name", "n", "h")
    def __init__(self, name):
        self.name, self.n, self.h = name, 0, None


def wmerge(*dicts):
    out = {}
    for d in dicts:
        if not d:
            continue
        for s, v in d.items():
            if v and out.get(s, 0) < v:
                out[s] = v
    return out


class Buf:
    __slots__ = ("ready", "last_use")
    def __init__(self):
        self.ready = {}
        self.last_use = {}

    def set_ready(self, sem, val):
        if val and self.ready.get(sem, 0) < val:
            self.ready[sem] = val

    def set_ready_d(self, d):
        self.ready = wmerge(self.ready, d)

    def used(self, sem, val):
        if val and self.last_use.get(sem, 0) < val:
            self.last_use[sem] = val

    def used_d(self, d):
        self.last_use = wmerge(self.last_use, d)

    def begin_write(self):
        w = wmerge(self.ready, self.last_use)
        self.ready = {}
        self.last_use = {}
        return w


class Slot:
    __slots__ = ("ap", "last_evac", "mm_done")
    def __init__(self, ap):
        self.ap = ap
        self.last_evac = {}
        self.mm_done = 0


class KB:
    def __init__(self, nc):
        self.nc = nc
        self.ops = {e: [] for e in ENGS}
        self.LWs = [Sem("LW%d" % i) for i in range(6)]
        self.SPs = [Sem("SP%d" % i) for i in range(4)]
        self.MM = Sem("MM")
        self.GP = Sem("GP")
        self.SC = Sem("SC")
        self.VE = Sem("VE")
        self.CC = Sem("CC")
        self.sems = self.LWs + self.SPs + [self.MM, self.GP, self.SC,
                                           self.VE, self.CC]
        self.lw_rot = 0
        self.sp_rot = 0
        self.slots = []
        self.wring_lastmm = [{}, {}]
        self.wring_ctr = 0

    def lw_all(self):
        return {s: s.n for s in self.LWs if s.n}

    def sp_all(self):
        return {s: s.n for s in self.SPs if s.n}

    def op(self, eng, fn, waits=None, inc=None, add=1):
        w = dict(waits) if waits else {}
        v = None
        if inc is not None:
            inc.n += add
            v = inc.n
        self.ops[eng].append((w, fn, inc, add))
        return v

    def load(self, out_ap, in_ap, waits=None):
        sem = self.LWs[self.lw_rot % len(self.LWs)]
        self.lw_rot += 1
        w = wmerge(waits, {sem: sem.n})
        v = self.op("sync", lambda e: e.dma_start(out=out_ap, in_=in_ap),
                    w, sem, 16)
        return {sem: v}

    def store(self, out_ap, in_ap, waits=None, accum=False, sem_i=None):
        if accum:
            fn = lambda e: e.dma_start(out=out_ap, in_=in_ap, accum_op=ALU.add)
        else:
            fn = lambda e: e.dma_start(out=out_ap, in_=in_ap)
        if sem_i is None:
            sem = self.SPs[self.sp_rot % len(self.SPs)]
            self.sp_rot += 1
        else:
            sem = self.SPs[sem_i]
        w = wmerge(waits, {sem: sem.n})
        v = self.op("gpsimd", fn, w, sem, 16)
        return {sem: v}

    def gdma(self, out_ap, in_ap, waits=None):
        return self.store(out_ap, in_ap, waits=waits)

    def mm(self, out, lhsT, rhs, start, stop, waits=None, inc=False):
        return self.op(
            "tensor",
            lambda e: e.matmul(out, lhsT, rhs, start=start, stop=stop),
            waits, self.MM if inc else None)

    def act(self, out, in_, func, bias=0.0, scale=1.0, waits=None):
        w = wmerge(waits, {self.SC: self.SC.n})
        return self.op(
            "scalar",
            lambda e: e.activation(out=out, in_=in_, func=func, bias=bias,
                                   scale=scale),
            w, self.SC)

    def scop(self, fn, waits=None):
        w = wmerge(waits, {self.SC: self.SC.n})
        return self.op("scalar", fn, w, self.SC)

    def vec(self, fn, waits=None):
        w = wmerge(waits, {self.VE: self.VE.n})
        return self.op("vector", fn, w, self.VE)


TEST_SLOTS4 = False


def build_nc(taps=()):
    nc = bass.Bass("TRN2", target_bir_lowering=False)
    kb = KB(nc)

    def dinp(name, shape):
        return nc.dram_tensor(name, shape, F32, kind="ExternalInput").ap()

    x_fm = dinp("x_fm", [D, T])
    fr_w = dinp("fr_w", [D, D]); dp_w = dinp("dp_w", [D, D])
    bp_w = dinp("bp_w", [D, D]); cp_w = dinp("cp_w", [D, D])
    m1_w = dinp("m1_w", [D, 2 * D]); m2_w = dinp("m2_w", [2 * D, D])
    gate_w = dinp("gate_w", [2 * D, D])
    ffn1_w = dinp("ffn1_w", [D, 4 * D]); ffn2_w = dinp("ffn2_w", [4 * D, D])
    gwT = dinp("gwT", [D, NE])
    w1T = dinp("w1T", [NE, D, F])
    w2T = dinp("w2T", [NE, F, D])
    biases_in = dinp("biases_in", [128, NBIAS])
    misc_in = dinp("misc_in", [128, 20])
    sel8_in = dinp("sel8_in", [8, 1024])
    ident_in = dinp("ident_in", [128, 128])
    sel_in = dinp("sel_in", [128, 8])

    out_tm = nc.dram_tensor("out_tm", [T, D], F32, kind="ExternalOutput").ap()
    tap_out = {}
    small_taps = {}
    for tname in taps:
        if tname.startswith("sm_"):
            shp = {"sm_probs": [128, 128], "sm_s": [128, 128],
                   "sm_R": [128, 128], "sm_bis": [128, 96],
                   "sm_ymoe": [T, D], "sm_tok": [8, 513],
                   "sm_cnt": [1, 8]}[tname]
            small_taps[tname] = nc.dram_tensor("tap_" + tname, shp, F32,
                                               kind="ExternalOutput").ap()
        else:
            tap_out[tname] = nc.dram_tensor("tap_" + tname, [D, T], F32,
                                            kind="ExternalOutput").ap()

    c_hbm = nc.dram_tensor("c_hbm", [D, T], F32).ap()
    sk_hbm = nc.dram_tensor("sk_hbm", [D, T], F32).ap()
    st_hbm = nc.dram_tensor("st_hbm", [D, T], F32).ap()
    A_hbm = nc.dram_tensor("A_hbm", [D, T], F32).ap()
    mo_hbm = nc.dram_tensor("mo_hbm", [D, T], F32).ap()
    x2tm = nc.dram_tensor("x2tm", [T + 1, 1032], F32).ap()
    ymtm = nc.dram_tensor("ymtm", [T + 1, D], F32).ap()
    gin_h = nc.dram_tensor("gin_h", [128, 8], F32)
    gout_h = nc.dram_tensor("gout_h", [8 * 128, 8], F32)
    rin_h = nc.dram_tensor("rin_h", [128, 128], F32)
    rout_h = nc.dram_tensor("rout_h", [8 * 128, 128], F32)

    from contextlib import ExitStack
    es = ExitStack()
    e = es.enter_context

    pool1 = e(nc.sbuf_tensor("pool1", [128, 8192], F32))
    pool2 = e(nc.sbuf_tensor("pool2", [128, 8192], F32))
    pool3 = e(nc.sbuf_tensor("pool3", [128, 8192], F32))
    pool4 = e(nc.sbuf_tensor("pool4", [128, 8192], F32))
    wring = e(nc.sbuf_tensor("wring", [128, 4096], F32))
    scr = e(nc.sbuf_tensor("scr", [128, 5280], F32))
    mbt = e(nc.sbuf_tensor("mbt", [128, 1024], F32))
    rbt = e(nc.sbuf_tensor("rbt", [128, 1024], F32))
    stats_m = e(nc.sbuf_tensor("stats_m", [1, 1024], F32))
    stats_r = e(nc.sbuf_tensor("stats_r", [1, 1024], F32))
    stats_t = e(nc.sbuf_tensor("stats_t", [1, 2048], F32))
    sbias = e(nc.sbuf_tensor("sbias", [128, NBIAS], F32))
    ident = e(nc.sbuf_tensor("ident", [128, 128], F32))
    selt = e(nc.sbuf_tensor("selt", [128, 8], F32))
    gwTt = e(nc.sbuf_tensor("gwTt", [128, 64], F32))
    onesc = e(nc.sbuf_tensor("onesc", [128, 1], F32))
    onesk = e(nc.sbuf_tensor("onesk", [1, 128], F32))
    ones128 = e(nc.sbuf_tensor("ones128", [128, 128], F32))
    Sinit = e(nc.sbuf_tensor("Sinit", [128, 8], F32))
    Ainit = e(nc.sbuf_tensor("Ainit", [128, 8], F32))
    gath = e(nc.sbuf_tensor("gath", [128, 64], F32))
    carry = e(nc.sbuf_tensor("carry", [128, 8], F32))
    probs = e(nc.sbuf_tensor("probs", [128, 128], F32))
    assn = e(nc.sbuf_tensor("assn", [128, 128], F32))
    s_sb = e(nc.sbuf_tensor("s_sb", [128, 128], F32))
    sbc = e(nc.sbuf_tensor("sbc", [128, 2048], F32))
    bis = e(nc.sbuf_tensor("bis", [128, 96], F32))
    epsc = e(nc.sbuf_tensor("epsc", [1, 1], F32))
    epsc128 = e(nc.sbuf_tensor("epsc128", [128, 1], F32))
    ones11 = e(nc.sbuf_tensor("ones11", [1, 1], F32))
    misct = e(nc.sbuf_tensor("misct", [128, 20], F32))
    sel8t = e(nc.sbuf_tensor("sel8t", [8, 1024], F32))
    tokcf = e(nc.sbuf_tensor("tokcf", [128, 32], F32))
    tokci = e(nc.sbuf_tensor("tokci", [128, 32], mybir.dt.int32))
    neg9 = e(nc.sbuf_tensor("neg9", [128, 128], F32))

    PA = e(nc.psum_tensor("PA", [128, 2048], F32))
    PB = e(nc.psum_tensor("PB", [128, 2048], F32))

    for s in kb.sems:
        s.h = e(nc.semaphore(s.name))
    block = e(nc.Block())

    for i in range(4):
        kb.slots.append(Slot(PA[:, i * 512:(i + 1) * 512]))
    for i in range(4):
        kb.slots.append(Slot(PB[:, i * 512:(i + 1) * 512]))

    tb = scr[:, 0:1024]
    tbv = [scr[:, 0:512], scr[:, 512:1024]]
    stdb = [scr[:, 1024:2048], scr[:, 2048:3072]]
    udb = [scr[:, 3072:4096], scr[:, 4096:5120]]
    B_tb = Buf()
    stdb_use = [{}, {}]
    udb_use = [{}, {}]

    bofs = {}
    cur = 0
    for nme, n in BIAS_ORDER:
        bofs[nme] = cur
        cur += n
    def bv(nme, g):
        return sbias[:, bofs[nme] + g: bofs[nme] + g + 1]

    B_p1, B_p2, B_p3, B_p4 = Buf(), Buf(), Buf(), Buf()
    B_chbm, B_skhbm, B_sthbm, B_Ahbm = Buf(), Buf(), Buf(), Buf()
    B_mohbm, B_x2tm, B_ymtm = Buf(), Buf(), Buf()
    B_mb, B_rb, B_sbc, B_probs = Buf(), Buf(), Buf(), Buf()

    kb.load(sbias[:], biases_in)
    kb.load(ident[:], ident_in)
    kb.load(selt[:], sel_in)
    kb.load(gwTt[:], gwT.rearrange("(k p) e -> p k e", p=128))
    kb.load(misct[:], misc_in)
    kb.load(sel8t[:], sel8_in)
    const_lw = kb.lw_all()
    kb.vec(lambda e: e.memset(onesc[:], 1.0))
    kb.vec(lambda e: e.memset(onesk[:], 1.0))
    kb.vec(lambda e: e.memset(ones128[:], 1.0))
    kb.vec(lambda e: e.memset(Sinit[:], 0.0))
    kb.vec(lambda e: e.memset(Ainit[:], 1.0))
    kb.vec(lambda e: e.memset(neg9[:], -1e9))
    kb.vec(lambda e: e.memset(epsc[:], EPS))
    kb.vec(lambda e: e.memset(epsc128[:], EPS))
    kb.vec(lambda e: e.memset(ones11[:], 1.0))
    kb.vec(lambda e: e.memset(bis[:], 0.0))
    const_ve = kb.VE.n
    CONSTS = wmerge({kb.VE: const_ve}, const_lw)

    def proj(w_hbm, Kc, Nout, rhs_fn, rhs_bufs, evac_pair_fn, ntt,
             kpass=4, first_waits=None, rhs_ready_k=None, korder=None,
             slots4=False):
        nkp = (Kc + kpass - 1) // kpass
        rhs_ready = wmerge(*(b.ready for b in rhs_bufs))
        ks_all = korder if korder is not None else list(range(Kc))
        passes = []
        for jb in range(Nout // 512):
            for kp in range(nkp):
                passes.append((jb, kp,
                               ks_all[kp * kpass:
                                      min((kp + 1) * kpass, Kc)]))
        planned = {}
        k_seen = set()

        def plan_load(idx):
            jb, kp, ks = passes[idx]
            setid = kb.wring_ctr % 2
            kb.wring_ctr += 1
            i = 0
            lwv = {}
            first_load = True
            while i < len(ks):
                npair = 2 if (i + 1 < len(ks)
                              and ks[i + 1] == ks[i] + 1) else 1
                k0 = ks[i]
                dst = wring[:, (setid * 4 + i) * 512:
                            (setid * 4 + i + npair) * 512]
                if npair == 2:
                    ap = bass.AP(w_hbm.tensor,
                                 w_hbm.offset + k0 * 128 * Nout + jb * 512,
                                 [[Nout, 128], [128 * Nout, 2], [1, 512]])
                else:
                    ap = bass.AP(w_hbm.tensor,
                                 w_hbm.offset + k0 * 128 * Nout + jb * 512,
                                 [[Nout, 128], [1, 512]])
                w = kb.wring_lastmm[setid] if first_load else None
                lwv = wmerge(lwv, kb.load(dst, ap, waits=w))
                first_load = False
                i += npair
            planned[idx] = (setid, lwv)

        plan_load(0)
        if len(passes) > 1:
            plan_load(1)
        first_mm_of_proj = True
        for idx, (jb, kp, ks) in enumerate(passes):
            setid, lwv = planned.pop(idx)
            first_of_pass = True
            lastv = None
            for j_local in range(4):
                for tt in range(ntt):
                    s = (j_local * 2 + tt) if ntt == 2 \
                        else (jb % 2) * 4 + j_local
                    slot = kb.slots[s]
                    for i, k in enumerate(ks):
                        w = {}
                        if kp == 0 and i == 0:
                            w = wmerge(w, slot.last_evac)
                        if first_of_pass:
                            w = wmerge(w, lwv)
                            if first_mm_of_proj:
                                w = wmerge(w,
                                           None if rhs_ready_k else rhs_ready,
                                           first_waits)
                        if rhs_ready_k is not None and k not in k_seen:
                            k_seen.add(k)
                            w = wmerge(w, rhs_ready_k[k])
                        start = (kp == 0 and i == 0)
                        stop = (kp == nkp - 1 and i == len(ks) - 1)
                        last_of_pass = (i == len(ks) - 1 and
                                        j_local == 3 and tt == ntt - 1)
                        lhsT = wring[:, (setid * 4 + i) * 512
                                     + j_local * 128:
                                     (setid * 4 + i) * 512
                                     + (j_local + 1) * 128]
                        v = kb.mm(slot.ap, lhsT, rhs_fn(k, tt), start,
                                  stop, waits=w, inc=(stop or last_of_pass))
                        first_of_pass = False
                        first_mm_of_proj = False
                        if v:
                            lastv = v
                        if stop:
                            slot.mm_done = v
            if lastv:
                kb.wring_lastmm[setid] = {kb.MM: lastv}
            if idx + 2 < len(passes):
                plan_load(idx + 2)
            if kp == nkp - 1:
                if ntt == 2:
                    for jl in range(4):
                        j = jb * 4 + jl
                        base = jl * 2
                        sl0, sl1 = kb.slots[base], kb.slots[base + 1]
                        pt = PA if base < 4 else PB
                        pap = pt[:, (base % 4) * 512:(base % 4) * 512 + 1024]
                        mmw = {kb.MM: max(sl0.mm_done, sl1.mm_done)}
                        done = evac_pair_fn(j, None, pap, mmw)
                        sl0.last_evac = done
                        sl1.last_evac = done
                else:
                    for jl in (0, 2):
                        base = jl if slots4 else (jb % 2) * 4 + jl
                        sl0, sl1 = kb.slots[base], kb.slots[base + 1]
                        pt = PA if base < 4 else PB
                        pap = pt[:, (base % 4) * 512:(base % 4) * 512 + 1024]
                        mmw = {kb.MM: max(sl0.mm_done, sl1.mm_done)}
                        done = evac_pair_fn(jb * 4 + jl, jb * 4 + jl + 1,
                                            pap, mmw)
                        sl0.last_evac = done
                        sl1.last_evac = done
        for b in rhs_bufs:
            b.used(kb.MM, kb.MM.n)

    def ln_stats(x_ap_fn, x_buf, th, sqp, sq_gate, x_ready_g=None):
        ntt = th // 512
        sqv = 0
        for tt in range(ntt):
            for g in range(G):
                idx = (g * ntt + tt) * 512
                w = wmerge(x_ready_g[g] if x_ready_g else x_buf.ready,
                           CONSTS,
                           sq_gate if (tt == 0 and g == 0) else None)
                sqv = kb.act(sqp[:, idx:idx + 512], x_ap_fn(g, tt),
                             AF.Square, waits=w)
        sx, sq = kb.slots[4], kb.slots[5]
        for tt in range(ntt):
            for g in range(G):
                idx = (g * ntt + tt) * 512
                w0 = wmerge(x_buf.ready, sx.last_evac if g == 0 else None,
                            CONSTS)
                kb.mm(sx.ap[0:1, :], onesc[:], x_ap_fn(g, tt),
                      start=(g == 0), stop=(g == G - 1), waits=w0, inc=True)
                w1 = wmerge({kb.SC: sqv} if (tt == 0 and g == 0) else None,
                            sq.last_evac if g == 0 else None)
                v = kb.mm(sq.ap[0:1, :], onesc[:], sqp[:, idx:idx + 512],
                          start=(g == 0), stop=(g == G - 1), waits=w1,
                          inc=True)
                sq.mm_done = v
            sxv = kb.MM.n - 1
            c = slice(tt * 512, (tt + 1) * 512)
            v1 = kb.scop(lambda e, c=c: e.mul(stats_m[0:1, c], sx.ap[0:1, :],
                                              1.0 / D), waits={kb.MM: sxv})
            v2 = kb.scop(lambda e, c=c: e.mul(stats_t[0:1, c], sq.ap[0:1, :],
                                              1.0 / D), waits={kb.MM: kb.MM.n})
            sx.last_evac = {kb.SC: v1}
            sq.last_evac = {kb.SC: v2}
            c2 = slice(1024 + c.start, 1024 + c.stop)
            kb.vec(lambda e, c=c, c2=c2: e.tensor_tensor(
                stats_t[0:1, c2], stats_m[0:1, c], stats_m[0:1, c], ALU.mult),
                waits={kb.SC: v2})
            v4 = kb.vec(lambda e, c=c, c2=c2: e.tensor_tensor(
                stats_t[0:1, c], stats_t[0:1, c], stats_t[0:1, c2],
                ALU.subtract))
            v5 = kb.scop(lambda e, c=c: e.activation(
                out=stats_t[0:1, c], in_=stats_t[0:1, c], func=AF.Sqrt,
                bias=epsc[0:1, :]), waits={kb.VE: v4})
            kb.vec(lambda e, c=c: e.reciprocal(stats_r[0:1, c],
                                               stats_t[0:1, c]),
                   waits={kb.SC: v5})
        x_buf.used(kb.MM, kb.MM.n)
        for row, srct, dstt, bufb in ((0, stats_m, mbt, B_mb),
                                      (1, stats_r, rbt, B_rb)):
            bw = bufb.begin_write()
            for tt in range(ntt):
                sl = kb.slots[6 + row]
                c = slice(tt * 512, (tt + 1) * 512)
                w = wmerge({kb.VE: kb.VE.n, kb.SC: kb.SC.n}, sl.last_evac,
                           bw, CONSTS)
                kb.mm(sl.ap, onesk[:], srct[0:1, c], True, True,
                      waits=w, inc=True)
                vv = kb.scop(lambda e, sl=sl, dstt=dstt, c=c:
                             e.copy(dstt[:, c], sl.ap),
                             waits={kb.MM: kb.MM.n})
                sl.last_evac = {kb.SC: vv}
                bufb.set_ready(kb.SC, vv)

    def pslice(pool, g, tt=None):
        if tt is None:
            return pool[:, g * TH:(g + 1) * TH]
        return pool[:, g * TH + tt * 512: g * TH + (tt + 1) * 512]

    def hslab(hb, g, h, tt=None):
        off = hb.offset + g * 128 * T + h * TH
        if tt is None:
            return bass.AP(hb.tensor, off, [[T, 128], [1, TH]])
        return bass.AP(hb.tensor, off + tt * 512, [[T, 128], [1, 512]])

    def tap(name, src_pool, h, buf):
        if name in tap_out:
            for g in range(G):
                v = kb.store(hslab(tap_out[name], g, h),
                             pslice(src_pool, g), waits=buf.ready)
                buf.used_d(v)

    def plan_ln1(h, xr, sqp, sqb):
        ln_stats(lambda g, tt: pslice(pool1, g, tt), B_p1, TH,
                 sqp, sqb.begin_write(), x_ready_g=xr)
        sqb.set_ready(kb.MM, kb.MM.n)
        xn_rg = []
        for g in range(G):
            w = wmerge(B_mb.ready, B_rb.ready, CONSTS, B_p1.last_use,
                       {kb.MM: kb.MM.n} if g == 0 else None)
            kb.vec(lambda e, g=g: e.tensor_tensor(
                pslice(pool1, g), pslice(pool1, g), mbt[:], ALU.subtract),
                waits=w)
            kb.vec(lambda e, g=g: e.tensor_tensor(
                pslice(pool1, g), pslice(pool1, g), rbt[:], ALU.mult))
            vv = kb.vec(lambda e, g=g: e.tensor_scalar(
                pslice(pool1, g), pslice(pool1, g),
                bv("norm_g", g), bv("norm_b", g), ALU.mult, ALU.add))
            xn_rg.append({kb.VE: vv})
        B_p1.ready = {kb.VE: kb.VE.n}
        B_mb.used(kb.VE, kb.VE.n)
        B_rb.used(kb.VE, kb.VE.n)
        tap("xn", pool1, h, B_p1)
        return xn_rg

    xg_ready = {}
    prefetched = [None]
    pre_ln1 = [None]
    for h in range(2):
        if h == 1 and prefetched[0] is not None:
            B_p1.ready = prefetched[0]
            xr = xg_ready[1]
        else:
            gate = B_p1.begin_write()
            xr = []
            for g in range(G):
                lv = kb.load(pslice(pool1, g), hslab(x_fm, g, h),
                             waits=gate if g == 0 else None)
                xr.append(lv)
            B_p1.set_ready_d(kb.lw_all())
            xg_ready[h] = xr

        if h == 1 and pre_ln1[0] is not None:
            xn_rg = pre_ln1[0]
        else:
            xn_rg = plan_ln1(h, xr, pool4, B_p4)

        p2gate = B_p2.begin_write()
        p3gate = B_p3.begin_write()

        low_rg, high_rg = [None] * G, [None] * G

        def fr_evac(j, _u, pap, mmw):
            w = wmerge(mmw, B_tb.last_use)
            sv = kb.act(tb, pap, AF.Sigmoid, bias=bv("fr_b", j), waits=w)
            B_tb.set_ready(kb.SC, sv)
            v1 = kb.vec(lambda e, j=j: e.tensor_tensor(
                pslice(pool2, j), pslice(pool1, j), tb, ALU.mult),
                waits=wmerge({kb.SC: sv}, p2gate if j == 0 else None))
            v2 = kb.vec(lambda e, j=j: e.tensor_tensor(
                pslice(pool3, j), pslice(pool1, j), pslice(pool2, j),
                ALU.subtract),
                waits=p3gate if j == 0 else None)
            B_tb.used(kb.VE, v1)
            B_p2.set_ready(kb.VE, v2)
            B_p3.set_ready(kb.VE, v2)
            low_rg[j] = {kb.VE: v1}
            high_rg[j] = {kb.VE: v2}
            return {kb.VE: v1}

        proj(fr_w, 8, 1024, lambda k, tt: pslice(pool1, k, tt), [B_p1],
             fr_evac, ntt=2, rhs_ready_k=xn_rg)
        tap("low", pool2, h, B_p2)
        tap("high", pool3, h, B_p3)

        def cp_evac(j, _u, pap, mmw):
            i = j % 2
            w = wmerge(mmw, stdb_use[i])
            sv = kb.act(stdb[i], pap, AF.Tanh, bias=bv("cp_b", j), waits=w)
            spv = kb.store(hslab(c_hbm, j, h), stdb[i], waits={kb.SC: sv})
            stdb_use[i] = spv
            B_chbm.set_ready_d(spv)
            return {kb.SC: sv}

        proj(cp_w, 8, 1024, lambda k, tt: pslice(pool3, k, tt), [B_p3],
             cp_evac, ntt=2, rhs_ready_k=high_rg)

        p3gate2 = B_p3.begin_write()

        def dp_evac(j, _u, pap, mmw):
            w = wmerge(mmw, p3gate2 if j == 0 else None)
            sv = kb.act(pslice(pool3, j), pap, AF.Sigmoid,
                        bias=bv("dp_b", j), waits=w)
            B_p3.set_ready(kb.SC, sv)
            return {kb.SC: sv}

        proj(dp_w, 8, 1024, lambda k, tt: pslice(pool2, k, tt), [B_p2],
             dp_evac, ntt=2, rhs_ready_k=low_rg)
        tap("a", pool3, h, B_p3)

        def bp_evac(g, _u, pap, mmw, h=h):
            i = g % 2
            w = wmerge(mmw, B_tb.last_use)
            sv = kb.act(tb, pap, AF.Tanh, bias=bv("bp_b", g), waits=w)
            B_tb.set_ready(kb.SC, sv)
            w2 = wmerge({kb.SC: sv}, udb_use[i], B_p3.ready)
            uv = kb.vec(lambda e, g=g, i=i: e.tensor_tensor(
                udb[i], tb, pslice(pool1, g), ALU.mult), waits=w2)
            B_tb.used(kb.VE, uv)
            w3 = wmerge({kb.VE: uv}, stdb_use[i])
            stv = kb.vec(lambda e, g=g, i=i: e.tensor_tensor_scan(
                stdb[i], pslice(pool3, g), udb[i], Sinit[:, g:g + 1],
                ALU.mult, ALU.add), waits=w3)
            kb.vec(lambda e, g=g, i=i: e.tensor_copy(
                Sinit[:, g:g + 1], stdb[i][:, TH - 1:TH]))
            spv = kb.store(hslab(st_hbm, g, h), stdb[i],
                           waits={kb.VE: kb.VE.n})
            stdb_use[i] = spv
            B_sthbm.set_ready_d(spv)
            zv = kb.vec(lambda e: e.memset(tb, 0.0),
                        waits=wmerge({kb.VE: stv}, B_tb.last_use))
            B_tb.set_ready(kb.VE, zv)
            av = kb.vec(lambda e, g=g, i=i: e.tensor_tensor_scan(
                udb[i], pslice(pool3, g), tb, Ainit[:, g:g + 1],
                ALU.mult, ALU.add), waits={kb.VE: zv})
            B_p3.used(kb.VE, av)
            B_tb.used(kb.VE, av)
            kb.vec(lambda e, g=g, i=i: e.tensor_copy(
                Ainit[:, g:g + 1], udb[i][:, TH - 1:TH]))
            spv2 = kb.store(hslab(A_hbm, g, h), udb[i],
                            waits={kb.VE: kb.VE.n})
            udb_use[i] = spv2
            B_Ahbm.set_ready_d(spv2)
            skv = kb.vec(lambda e, g=g: e.tensor_single_scalar(
                pslice(pool1, g), pslice(pool1, g), bv("skip", g), ALU.mult),
                waits=wmerge({kb.VE: uv}, B_p1.last_use))
            spv3 = kb.store(hslab(sk_hbm, g, h), pslice(pool1, g),
                            waits={kb.VE: skv})
            B_skhbm.set_ready_d(spv3)
            B_p1.used_d(spv3)
            return {kb.VE: av}

        proj(bp_w, 8, 1024, lambda k, tt: pslice(pool2, k, tt), [B_p2],
             bp_evac, ntt=2, rhs_ready_k=low_rg)

        if h == 0:
            gate = B_p1.begin_write()
            xg_ready[1] = []
            for g in range(G):
                lv = kb.load(pslice(pool1, g), hslab(x_fm, g, 1),
                             waits=gate if g == 0 else None)
                xg_ready[1].append(lv)
            B_p1.set_ready_d(kb.lw_all())
            prefetched[0] = dict(B_p1.ready)
            pre_ln1[0] = plan_ln1(1, xg_ready[1], pool3, B_p3)

        for tt in range(2):
            h1_rg = [None] * 16
            p4gate = B_p4.begin_write()

            def m1_evac(j0, j1, pap, mmw, p4gate=p4gate):
                sv = None
                for jj, off in ((j0, 0), (j1, 512)):
                    w = wmerge(mmw if off == 0 else None,
                               p4gate if jj == 0 else None)
                    sv = kb.act(pool4[:, jj * 512:(jj + 1) * 512],
                                pap[:, off:off + 512], AF.Gelu,
                                bias=bv("m1_b", jj), waits=w)
                    h1_rg[jj] = {kb.SC: sv}
                B_p4.set_ready(kb.SC, sv)
                return {kb.SC: sv}

            proj(m1_w, 8, 2048, lambda k, _t, tt=tt: pslice(pool2, k, tt),
                 [B_p2], m1_evac, ntt=1, rhs_ready_k=low_rg, slots4=TEST_SLOTS4)

            def m2_evac(j0, j1, pap, mmw, tt=tt, h=h):
                svs = []
                for jj, off in ((j0, 0), (j1, 512)):
                    i = jj % 2
                    w = wmerge(mmw if off == 0 else None, B_tb.last_use)
                    sv = kb.scop(lambda e, jj=jj, off=off: e.add(
                        tbv[jj % 2], pap[:, off:off + 512], bv("m2_b", jj)),
                        waits=w)
                    B_tb.set_ready(kb.SC, sv)
                    spv = kb.store(hslab(mo_hbm, jj, h, tt), tbv[jj % 2],
                                   waits={kb.SC: sv})
                    B_tb.used_d(spv)
                    svs.append(sv)
                B_mohbm.set_ready_d(kb.sp_all())
                return {kb.SC: max(svs)}

            proj(m2_w, 16, 1024,
                 lambda k, _t: pool4[:, k * 512:(k + 1) * 512], [B_p4],
                 m2_evac, ntt=1, rhs_ready_k=h1_rg, slots4=TEST_SLOTS4)

    gv = kb.store(gin_h.ap(), Sinit[:], waits={kb.VE: kb.VE.n})
    ccv = kb.op("gpsimd", lambda e: e.collective_compute(
        "AllGather", ALU.bypass, replica_groups=[list(range(8))],
        ins=[gin_h.ap().opt()], outs=[gout_h.ap().opt()]),
        waits=gv, inc=kb.CC)
    glv = kb.gdma(gath[:],
                  bass.AP(gout_h, 0, [[8, 128], [1024, 8], [1, 8]]),
                  waits={kb.CC: ccv})
    cv1 = kb.vec(lambda e: e.tensor_tensor(
        bass.AP(gath, 0, [[64, 128], [1, 8], [8, 8]]),
        bass.AP(gath, 0, [[64, 128], [1, 8], [8, 8]]),
        bass.AP(selt, 0, [[8, 128], [0, 8], [1, 8]]), ALU.mult),
        waits=wmerge(glv, CONSTS))
    cv2 = kb.vec(lambda e: e.tensor_reduce(
        carry[:], bass.AP(gath, 0, [[64, 128], [1, 8], [8, 8]]),
        AX.X, ALU.add), waits={kb.VE: cv1})
    carry_ready = {kb.VE: cv2}

    defer_x2tm = []
    for h in range(2):
        g2 = B_p2.begin_write()
        g3 = B_p3.begin_write()
        g4 = B_p4.begin_write()
        mo_rg = []
        for g in range(G):
            lvm = kb.load(pslice(pool2, g), hslab(mo_hbm, g, h),
                          waits=wmerge(B_mohbm.ready, g2) if g == 0 else None)
            mo_rg.append(lvm)
            kb.load(pslice(pool3, g), hslab(c_hbm, g, h),
                    waits=wmerge(B_chbm.ready, g3) if g == 0 else None)
            kb.load(pslice(pool4, g), hslab(sk_hbm, g, h),
                    waits=wmerge(B_skhbm.ready, g4) if g == 0 else None)
        if h == 0:
            B_chbm.ready = {}
            B_skhbm.ready = {}
            B_mohbm.ready = {}
        B_p2.set_ready_d(kb.lw_all())
        B_p3.set_ready_d(kb.lw_all())
        B_p4.set_ready_d(kb.lw_all())

        ssd_rg = []
        g1 = B_p1.begin_write()
        for g in range(G):
            i = g % 2
            l1 = kb.load(stdb[i], hslab(st_hbm, g, h),
                         waits=wmerge(B_sthbm.ready, stdb_use[i],
                                      g1 if g == 0 else None))
            l2 = kb.load(udb[i], hslab(A_hbm, g, h),
                         waits=wmerge(B_Ahbm.ready, udb_use[i]))
            fv = kb.vec(lambda e, g=g, i=i: e.scalar_tensor_tensor(
                stdb[i], udb[i], carry[:, g:g + 1], stdb[i],
                ALU.mult, ALU.add),
                waits=wmerge(l1, l2, carry_ready))
            udb_use[i] = {kb.VE: fv}
            kb.vec(lambda e, g=g, i=i: e.tensor_tensor(
                stdb[i], stdb[i], pslice(pool3, g), ALU.mult),
                waits=B_p3.ready)
            sv = kb.vec(lambda e, g=g, i=i: e.tensor_tensor(
                pslice(pool1, g), stdb[i], pslice(pool4, g), ALU.add),
                waits=B_p4.ready)
            stdb_use[i] = {kb.VE: sv}
            ssd_rg.append({kb.VE: sv})
            B_p1.set_ready(kb.VE, sv)
        tap("ssd", pool1, h, B_p1)

        g3b = B_p3.begin_write()

        def gate_evac(j, _u, pap, mmw, g3b=g3b):
            w = wmerge(mmw, g3b if j == 0 else None)
            sv = kb.act(pslice(pool3, j), pap, AF.Sigmoid,
                        bias=bv("gate_b", j), waits=w)
            B_p3.set_ready(kb.SC, sv)
            return {kb.SC: sv}

        gate_rk = [ssd_rg[k] if k < 8 else mo_rg[k - 8] for k in range(16)]
        proj(gate_w, 16, 1024,
             lambda k, tt: pslice(pool1, k, tt) if k < 8
             else pslice(pool2, k - 8, tt),
             [B_p1, B_p2], gate_evac, ntt=2, rhs_ready_k=gate_rk,
             korder=[8, 9, 10, 11, 12, 13, 14, 15, 0, 1, 2, 3, 4, 5, 6, 7])

        for g in range(G):
            w = wmerge(B_p3.ready, B_tb.last_use,
                       B_p1.last_use if g == 0 else None)
            v0 = kb.vec(lambda e, g=g: e.tensor_tensor(
                tb, pslice(pool1, g), pslice(pool2, g), ALU.subtract),
                waits=w)
            B_tb.set_ready(kb.VE, v0)
            kb.vec(lambda e, g=g: e.tensor_tensor(
                tb, tb, pslice(pool3, g), ALU.mult))
            v = kb.vec(lambda e, g=g: e.tensor_tensor(
                pslice(pool1, g), tb, pslice(pool2, g), ALU.add))
            B_tb.used(kb.VE, v)
        B_p1.ready = {kb.VE: kb.VE.n}
        B_p2.used(kb.VE, kb.VE.n)
        B_p3.used(kb.VE, kb.VE.n)

        for g in range(G):
            i = g % 2
            lv = kb.load(stdb[i], hslab(x_fm, g, h), waits=stdb_use[i])
            v = kb.vec(lambda e, g=g, i=i: e.tensor_tensor(
                pslice(pool1, g), pslice(pool1, g), stdb[i], ALU.add),
                waits=wmerge(lv, B_p1.last_use if g == 0 else None))
            stdb_use[i] = {kb.VE: v}
        B_p1.ready = {kb.VE: kb.VE.n}
        tap("y", pool1, h, B_p1)

        ln_stats(lambda g, tt: pslice(pool1, g, tt), B_p1, TH,
                 pool4, B_p4.begin_write())
        B_p4.set_ready(kb.MM, kb.MM.n)
        p2g = B_p2.begin_write()
        yn_rg = []
        for g in range(G):
            w = wmerge(B_mb.ready, B_rb.ready, p2g if g == 0 else None)
            kb.vec(lambda e, g=g: e.tensor_tensor(
                pslice(pool2, g), pslice(pool1, g), mbt[:], ALU.subtract),
                waits=w)
            kb.vec(lambda e, g=g: e.tensor_tensor(
                pslice(pool2, g), pslice(pool2, g), rbt[:], ALU.mult))
            vv = kb.vec(lambda e, g=g: e.tensor_scalar(
                pslice(pool2, g), pslice(pool2, g),
                bv("ffn_g", g), bv("ffn_bn", g), ALU.mult, ALU.add))
            yn_rg.append({kb.VE: vv})
        B_p2.ready = {kb.VE: kb.VE.n}
        B_mb.used(kb.VE, kb.VE.n)
        B_rb.used(kb.VE, kb.VE.n)

        for tt in range(2):
            h2_rg = [None] * 32
            p34g = wmerge(B_p3.begin_write(), B_p4.begin_write())

            def f1_evac(j0, j1, pap, mmw, p34g=p34g):
                sv = None
                for jj, off in ((j0, 0), (j1, 512)):
                    dst = (pool3[:, jj * 512:(jj + 1) * 512] if jj < 16
                           else pool4[:, (jj - 16) * 512:(jj - 15) * 512])
                    w = wmerge(mmw if off == 0 else None,
                               p34g if jj == 0 else None)
                    sv = kb.act(dst, pap[:, off:off + 512], AF.Gelu,
                                bias=bv("ffn1_b", jj), waits=w)
                    h2_rg[jj] = {kb.SC: sv}
                B_p3.set_ready(kb.SC, sv)
                B_p4.set_ready(kb.SC, sv)
                return {kb.SC: sv}

            proj(ffn1_w, 8, 4096, lambda k, _t, tt=tt: pslice(pool2, k, tt),
                 [B_p2], f1_evac, ntt=1, rhs_ready_k=yn_rg)

            def f2_evac(j0, j1, pap, mmw, tt=tt):
                vv = None
                for jj, off in ((j0, 0), (j1, 512)):
                    w = wmerge(mmw, B_p1.last_use) if off == 0 else None
                    vv = kb.vec(lambda e, jj=jj, off=off, tt=tt:
                                e.scalar_tensor_tensor(
                                    pslice(pool1, jj, tt),
                                    pap[:, off:off + 512], bv("ffn2_b", jj),
                                    pslice(pool1, jj, tt), ALU.add, ALU.add),
                                waits=w)
                return {kb.VE: vv}

            proj(ffn2_w, 32, 1024,
                 lambda k, _t: (pool3[:, k * 512:(k + 1) * 512] if k < 16
                                else pool4[:, (k - 16) * 512:(k - 15) * 512]),
                 [B_p3, B_p4], f2_evac, ntt=1, rhs_ready_k=h2_rg)
        B_p1.ready = {kb.VE: kb.VE.n}

        for g in range(G):
            i = g % 2
            lv = kb.load(stdb[i], hslab(x_fm, g, h), waits=stdb_use[i])
            v = kb.vec(lambda e, g=g, i=i: e.tensor_tensor(
                pslice(pool1, g), pslice(pool1, g), stdb[i], ALU.add),
                waits=wmerge(lv, B_p1.last_use if g == 0 else None))
            stdb_use[i] = {kb.VE: v}
        B_p1.ready = {kb.VE: kb.VE.n}
        tap("y3", pool1, h, B_p1)

        ln_stats(lambda g, tt: pslice(pool1, g, tt), B_p1, TH,
                 pool4, B_p4.begin_write())
        B_p4.set_ready(kb.MM, kb.MM.n)
        for g in range(G):
            w = wmerge(B_mb.ready, B_rb.ready,
                       B_p1.last_use if g == 0 else None)
            kb.vec(lambda e, g=g: e.tensor_tensor(
                pslice(pool1, g), pslice(pool1, g), mbt[:], ALU.subtract),
                waits=w)
            v = kb.vec(lambda e, g=g: e.tensor_tensor(
                pslice(pool1, g), pslice(pool1, g), rbt[:], ALU.mult))
        B_p1.ready = {kb.VE: kb.VE.n}
        B_mb.used(kb.VE, kb.VE.n)
        B_rb.used(kb.VE, kb.VE.n)
        tap("x2", pool1, h, B_p1)
        p4g2 = B_p4.begin_write()
        st_prev = [{} for _ in range(4)]
        def x2tm_transposes(h=h, p4g2=p4g2, st_prev=st_prev):
          for g in range(G):
            for tc in range(8):
                tsl = kb.slots[6 + ((g * 8 + tc) % 2)]
                w = wmerge(B_p1.ready if (g == 0 and tc == 0) else None,
                           tsl.last_evac, CONSTS)
                mv = kb.op("tensor", lambda e, g=g, tc=tc, tsl=tsl:
                           e.transpose(tsl.ap[:, 0:128],
                                       pool1[:, g * TH + tc * 128:
                                             g * TH + (tc + 1) * 128],
                                       ident[:]),
                           waits=w, inc=kb.MM)
                b = (g * 8 + tc) % 4
                cv = kb.scop(lambda e, tsl=tsl, b=b: e.copy(
                    pool4[:, b * 128:(b + 1) * 128], tsl.ap[:, 0:128]),
                    waits=wmerge({kb.MM: mv}, st_prev[b],
                                 p4g2 if (g == 0 and tc == 0) else None))
                tsl.last_evac = {kb.SC: cv}
                spv = kb.store(
                    bass.AP(x2tm.tensor,
                            x2tm.offset + (h * 1024 + tc * 128) * 1032
                            + g * 128,
                            [[1032, 128], [1, 128]]),
                    pool4[:, b * 128:(b + 1) * 128], waits={kb.SC: cv})
                st_prev[b] = spv
                B_p4.used_d(spv)
                B_x2tm.set_ready_d(spv)
          B_p4.set_ready(kb.SC, kb.SC.n)
          B_p1.used(kb.MM, kb.MM.n)
        if h == 0:
            x2tm_transposes()
        else:
            defer_x2tm.append(x2tm_transposes)

        lsl = kb.slots[4]
        for tc in range(8):
            for k in range(G):
                w = wmerge(B_p1.ready if (tc == 0 and k == 0) else None,
                           lsl.last_evac if k == 0 else None, CONSTS)
                lhsT = pool1[:, k * TH + tc * 128: k * TH + (tc + 1) * 128]
                kb.mm(lsl.ap[0:128, 0:8], lhsT, gwTt[:, k * 8:(k + 1) * 8],
                      start=(k == 0), stop=(k == G - 1), waits=w,
                      inc=(k == G - 1))
            j = h * 8 + tc
            sv = kb.scop(lambda e, j=j: e.copy(probs[:, j * 8:(j + 1) * 8],
                                               lsl.ap[0:128, 0:8]),
                         waits={kb.MM: kb.MM.n})
            lsl.last_evac = {kb.SC: sv}
        B_p1.used(kb.MM, kb.MM.n)
    B_probs.set_ready(kb.SC, kb.SC.n)

    pv = bass.AP(probs, 0, [[128, 128], [8, 16], [1, 8]])
    SCRSTR = 5280
    gt4 = lambda: bass.AP(scr, 0, [[SCRSTR, 128], [64, 16], [8, 8], [1, 8]])
    w44 = lambda: bass.AP(scr, 1024, [[SCRSTR, 128], [64, 16], [8, 8], [1, 8]])
    gt128 = scr[:, 0:128]
    w128 = scr[:, 1024:1152]
    cnt8 = scr[:, 1152:1160]

    v = kb.vec(lambda e: e.tensor_reduce(
        bass.AP(bis, 32, [[96, 128], [1, 16]]), pv, AX.X, ALU.max),
        waits=wmerge({kb.SC: kb.SC.n}, B_probs.ready))
    v = kb.vec(lambda e: e.tensor_tensor(
        pv, pv, bass.AP(bis, 32, [[96, 128], [1, 16], [0, 8]]),
        ALU.subtract))
    v = kb.scop(lambda e: e.activation(out=probs[:], in_=probs[:],
                                       func=AF.Exp),
                waits={kb.VE: kb.VE.n})
    v = kb.vec(lambda e: e.tensor_reduce(
        bass.AP(bis, 32, [[96, 128], [1, 16]]), pv, AX.X, ALU.add),
        waits={kb.SC: v})
    v = kb.vec(lambda e: e.reciprocal(bass.AP(bis, 48, [[96, 128], [1, 16]]),
                                      bass.AP(bis, 32, [[96, 128], [1, 16]])))
    v = kb.vec(lambda e: e.tensor_tensor(
        pv, pv, bass.AP(bis, 48, [[96, 128], [1, 16], [0, 8]]), ALU.mult))
    v = kb.vec(lambda e: e.tensor_tensor(
        gt4(),
        bass.AP(probs, 0, [[128, 128], [8, 16], [0, 8], [1, 8]]),
        bass.AP(probs, 0, [[128, 128], [8, 16], [1, 8], [0, 8]]),
        ALU.is_gt), waits=wmerge({kb.VE: v}, B_tb.last_use, stdb_use[0],
                                 stdb_use[1], udb_use[0], udb_use[1]))
    v = kb.vec(lambda e: e.tensor_tensor(
        w44(), gt4(),
        bass.AP(probs, 0, [[128, 128], [8, 16], [0, 8], [1, 8]]), ALU.mult))
    v = kb.vec(lambda e: e.tensor_reduce(
        bass.AP(s_sb, 0, [[128, 128], [8, 16], [1, 8]]), w44(), AX.X,
        ALU.add))
    v = kb.vec(lambda e: e.tensor_reduce(
        bass.AP(scr, 1024, [[SCRSTR, 128], [8, 16], [1, 8]]), gt4(), AX.X,
        ALU.add))
    v = kb.vec(lambda e: e.tensor_single_scalar(assn[:], s_sb[:], THR,
                                                ALU.is_lt))
    v = kb.vec(lambda e: e.tensor_single_scalar(gt128, probs[:], 1.0,
                                                ALU.subtract))
    v = kb.vec(lambda e: e.tensor_tensor(gt128, gt128, w128, ALU.subtract))
    kb.vec(lambda e: e.tensor_scalar(w128, assn[:], 1e9, -1e9,
                                     ALU.mult, ALU.add))
    kb.vec(lambda e: e.tensor_tensor(gt128, gt128, assn[:], ALU.mult))
    rv = kb.vec(lambda e: e.tensor_tensor(s_sb[:], gt128, w128, ALU.add))
    rtapv = {}
    if "sm_R" in small_taps:
        rtapv = kb.store(small_taps["sm_R"], s_sb[:], waits={kb.VE: rv})
    rsp = kb.store(rin_h.ap(), s_sb[:], waits={kb.VE: rv})
    ccv2 = kb.op("gpsimd", lambda e: e.collective_compute(
        "AllGather", ALU.bypass, replica_groups=[list(range(8))],
        ins=[rin_h.ap().opt()], outs=[rout_h.ap().opt()]),
        waits=wmerge(rsp, {kb.CC: 1}), inc=kb.CC)
    for fn_ in defer_x2tm:
        fn_()
    sbcg = B_sbc.begin_write()
    rlv = kb.gdma(sbc[:, 0:1024],
                  bass.AP(rout_h, 0, [[128, 128], [16384, 8], [1, 128]]),
                  waits=wmerge({kb.CC: ccv2}, sbcg))
    B_sbc.set_ready_d(rlv)

    kb.vec(lambda e: e.memset(bis[:, 24:32], -8.5), waits=rlv)
    kb.vec(lambda e: e.memset(bis[:, 32:40], 0.5))
    kb.vec(lambda e: e.memset(bis[:, 40:48], -2e9))
    kb.vec(lambda e: e.memset(bis[:, 48:56], -9e8))
    kb.vec(lambda e: e.tensor_copy(bis[:, 16:24], bis[:, 24:32]))

    Rall = lambda: bass.AP(sbc, 0, [[2048, 128], [1, 8], [128, 8], [8, 16]])
    gtb = lambda: bass.AP(scr, 0, [[SCRSTR, 128], [1, 8], [128, 8], [8, 16]])
    midb = lambda: bass.AP(bis, 16, [[96, 128], [1, 8], [0, 8], [0, 16]])
    bsl = kb.slots[6]

    def bis_round(first):
        v1 = kb.vec(lambda e: e.tensor_tensor(gtb(), Rall(), midb(),
                                              ALU.is_ge),
                    waits=wmerge({kb.VE: kb.VE.n}, B_sbc.ready))
        v2 = kb.vec(lambda e: e.tensor_reduce(
            bass.AP(scr, 0, [[SCRSTR, 128], [1, 8], [128, 8]]), gtb(),
            AX.X, ALU.add))
        v3 = kb.vec(lambda e: e.tensor_reduce(
            bass.AP(scr, 1152, [[SCRSTR, 128], [1, 8]]),
            bass.AP(scr, 0, [[SCRSTR, 128], [1, 8], [128, 8]]),
            AX.X, ALU.add))
        mv = kb.mm(bsl.ap[0:128, 0:8], ones128[:], cnt8, True, True,
                   waits=wmerge({kb.VE: v3}, bsl.last_evac), inc=True)
        v4 = kb.vec(lambda e: e.tensor_single_scalar(
            bis[:, 56:64], bsl.ap[0:128, 0:8], CAP, ALU.is_ge),
            waits={kb.MM: mv})
        bsl.last_evac = {kb.VE: v4}
        kb.vec(lambda e: e.tensor_scalar(bis[:, 64:72], bis[:, 56:64],
                                         -1.0, 1.0, ALU.mult, ALU.add))
        if first:
            kb.vec(lambda e: e.tensor_single_scalar(bis[:, 72:80],
                                                    bis[:, 56:64], -8.5,
                                                    ALU.mult))
            kb.vec(lambda e: e.tensor_single_scalar(bis[:, 80:88],
                                                    bis[:, 64:72], -2e9,
                                                    ALU.mult))
            kb.vec(lambda e: e.tensor_tensor(bis[:, 0:8], bis[:, 72:80],
                                             bis[:, 80:88], ALU.add))
            kb.vec(lambda e: e.tensor_single_scalar(bis[:, 72:80],
                                                    bis[:, 56:64], 0.5,
                                                    ALU.mult))
            kb.vec(lambda e: e.tensor_single_scalar(bis[:, 80:88],
                                                    bis[:, 64:72], -9e8,
                                                    ALU.mult))
            kb.vec(lambda e: e.tensor_tensor(bis[:, 8:16], bis[:, 72:80],
                                             bis[:, 80:88], ALU.add))
        else:
            kb.vec(lambda e: e.tensor_tensor(bis[:, 72:80], bis[:, 16:24],
                                             bis[:, 56:64], ALU.mult))
            kb.vec(lambda e: e.tensor_tensor(bis[:, 80:88], bis[:, 0:8],
                                             bis[:, 64:72], ALU.mult))
            kb.vec(lambda e: e.tensor_tensor(bis[:, 0:8], bis[:, 72:80],
                                             bis[:, 80:88], ALU.add))
            kb.vec(lambda e: e.tensor_tensor(bis[:, 72:80], bis[:, 8:16],
                                             bis[:, 56:64], ALU.mult))
            kb.vec(lambda e: e.tensor_tensor(bis[:, 80:88], bis[:, 16:24],
                                             bis[:, 64:72], ALU.mult))
            kb.vec(lambda e: e.tensor_tensor(bis[:, 8:16], bis[:, 72:80],
                                             bis[:, 80:88], ALU.add))
        kb.vec(lambda e: e.tensor_tensor(bis[:, 16:24], bis[:, 0:8],
                                         bis[:, 8:16], ALU.add))
        kb.vec(lambda e: e.tensor_single_scalar(bis[:, 16:24], bis[:, 16:24],
                                                0.5, ALU.mult))

    bis_round(True)
    for _ in range(41):
        bis_round(False)
    B_sbc.used(kb.VE, kb.VE.n)

    v = kb.vec(lambda e: e.tensor_tensor(
        bass.AP(scr, 0, [[SCRSTR, 128], [8, 16], [1, 8]]),
        bass.AP(s_sb, 0, [[128, 128], [8, 16], [1, 8]]),
        bass.AP(bis, 0, [[96, 128], [0, 16], [1, 8]]), ALU.is_ge),
        waits={kb.VE: kb.VE.n})
    v = kb.vec(lambda e: e.tensor_tensor(s_sb[:], probs[:], gt128,
                                         ALU.mult),
               waits=wmerge(rtapv, rsp))
    sv_done = kb.vec(lambda e: e.tensor_tensor(s_sb[:], s_sb[:], assn[:],
                                               ALU.mult))
    ssp = kb.store(bass.AP(x2tm.tensor, x2tm.offset + 1024,
                           [[1032, 128], [1032 * 128, 16], [1, 8]]),
                   s_sb[:], waits={kb.VE: sv_done})
    B_x2tm.set_ready_d(ssp)

    m8 = scr[0:8, 1024:3072]
    p8 = scr[0:8, 3072:5120]
    for j in range(16):
        tsl = kb.slots[6 + (j % 2)]
        w = wmerge({kb.VE: sv_done} if j == 0 else None, tsl.last_evac,
                   CONSTS)
        mv = kb.op("tensor", lambda e, j=j, tsl=tsl: e.transpose(
            tsl.ap[0:8, 0:128], s_sb[:, j * 8:(j + 1) * 8], ident[:]),
            waits=w, inc=kb.MM)
        cv = kb.scop(lambda e, j=j, tsl=tsl: e.copy(
            m8[:, j * 128:(j + 1) * 128], tsl.ap[0:8, 0:128]),
            waits={kb.MM: mv})
        tsl.last_evac = {kb.SC: cv}
    v = kb.vec(lambda e: e.tensor_single_scalar(m8, m8, 0.0, ALU.is_gt),
               waits={kb.SC: kb.SC.n})
    p8v = kb.vec(lambda e: e.tensor_tensor_scan(p8, m8, m8, 0.0,
                                                ALU.add, ALU.bypass))
    kb.vec(lambda e: e.tensor_copy(mbt[0:8, :], p8[:, 0:1024]),
           waits=wmerge(B_mb.begin_write(), {kb.VE: p8v}))
    p8v = kb.vec(lambda e: e.tensor_copy(rbt[0:8, :], p8[:, 1024:2048]),
                 waits=B_rb.begin_write())
    pos_ready = {kb.VE: p8v}

    zv2 = kb.vec(lambda e: e.memset(tb, 0.0), waits=B_tb.last_use)
    B_tb.set_ready(kb.VE, zv2)
    for tc in range(16):
        sp = kb.store(bass.AP(ymtm.tensor, ymtm.offset + tc * 128 * D,
                              [[D, 128], [1, D]]), tb, waits={kb.VE: zv2})
        B_tb.used_d(sp)
        B_ymtm.set_ready_d(sp)
    sp = kb.store(bass.AP(ymtm.tensor, ymtm.offset + T * D, [[D, 1], [1, D]]),
                  tb[0:1, :], waits={kb.VE: zv2})
    B_tb.used_d(sp)
    B_ymtm.set_ready_d(sp)
    ymtm_zero = dict(B_ymtm.ready)
    sp = kb.store(bass.AP(x2tm.tensor, x2tm.offset + T * 1032,
                          [[1032, 1], [1, 1024]]), tb[0:1, :],
                  waits={kb.VE: zv2})
    B_tb.used_d(sp)
    B_x2tm.set_ready_d(sp)
    sp = kb.store(bass.AP(x2tm.tensor, x2tm.offset + T * 1032 + 1024,
                          [[8, 1], [1, 8]]), tb[0:1, 0:8], waits={kb.VE: zv2})
    B_tb.used_d(sp)
    B_x2tm.set_ready_d(sp)

    if "sm_probs" in small_taps:
        kb.store(small_taps["sm_probs"], probs[:], waits={kb.VE: kb.VE.n})
    if "sm_s" in small_taps:
        kb.store(small_taps["sm_s"], s_sb[:], waits={kb.VE: kb.VE.n})
    if "sm_bis" in small_taps:
        kb.store(small_taps["sm_bis"], bis[:], waits={kb.VE: kb.VE.n})

    B_p4A, B_p4B = Buf(), Buf()
    scrS = [Buf() for _ in range(4)]
    for b_ in scrS:
        b_.used(kb.VE, kb.VE.n)
        b_.used(kb.MM, kb.MM.n)
    w1bufs = {}
    w2bufs = {}
    tokr = {}
    xgb = {}

    def plan_w1_loads(ee):
        w1b = Buf()
        g1 = B_p1.begin_write()
        g2 = B_p2.begin_write()
        for k in range(G):
            pool = pool1 if k < 4 else pool2
            w = (g1 if k == 0 else (g2 if k == 4 else None))
            kb.load(pool[:, (k % 4) * 1024:(k % 4 + 1) * 1024],
                    bass.AP(w1T.tensor, w1T.offset + ee * D * F + k * 128 * F,
                            [[F, 128], [1, F]]), waits=w)
        w1b.set_ready_d(kb.lw_all())
        w1bufs[ee] = w1b

    def plan_w2_loads(ee):
        w2b = Buf()
        g3 = B_p3.begin_write()
        for k in range(G):
            kb.load(pool3[:, k * 1024:(k + 1) * 1024],
                    bass.AP(w2T.tensor, w2T.offset + ee * F * D + k * 128 * D,
                            [[D, 128], [1, D]]), waits=g3 if k == 0 else None)
        w2b.set_ready_d(kb.lw_all())
        w2bufs[ee] = w2b

    def plan_lists(ee):
        sgate = B_sbc.begin_write()
        for tt2 in range(4):
            tsl = kb.slots[6 + (tt2 % 2)]
            w = wmerge(pos_ready if tt2 == 0 else None, tsl.last_evac,
                       CONSTS)
            psrc = mbt if tt2 < 2 else rbt
            mv = kb.mm(tsl.ap, sel8t[0:8, ee * 128:(ee + 1) * 128],
                       psrc[0:8, (tt2 % 2) * 512:(tt2 % 2 + 1) * 512],
                       True, True, waits=w, inc=True)
            cv = kb.scop(lambda e, tt2=tt2, tsl=tsl: e.copy(
                sbc[:, tt2 * 512:(tt2 + 1) * 512], tsl.ap),
                waits=wmerge({kb.MM: mv}, sgate if tt2 == 0 else None))
            tsl.last_evac = {kb.SC: cv}
        posb_ready = {kb.SC: kb.SC.n}
        cmp = tb
        for sg in range(4):
            w = wmerge(posb_ready if sg == 0 else None, B_tb.last_use)
            v = kb.vec(lambda e, sg=sg: e.tensor_scalar(
                scr[0:128, 0:1024], sbc[:, 0:1024], misct[:, sg:sg + 1],
                None, ALU.is_le, ALU.bypass), waits=w)
            B_tb.set_ready(kb.VE, v)
            v = kb.vec(lambda e, sg=sg, ee=ee: e.tensor_reduce(
                tokcf[:, ee * 4 + sg:ee * 4 + sg + 1], scr[0:128, 0:1024],
                AX.X, ALU.add))
            v = kb.vec(lambda e, sg=sg: e.tensor_scalar(
                scr[0:128, 0:1024], sbc[:, 1024:2048], misct[:, sg:sg + 1],
                None, ALU.is_le, ALU.bypass))
            v = kb.vec(lambda e, sg=sg, ee=ee: e.tensor_reduce(
                bis[:, 8:9], scr[0:128, 0:1024], AX.X, ALU.add))
            v = kb.vec(lambda e, sg=sg, ee=ee: e.tensor_tensor(
                tokcf[:, ee * 4 + sg:ee * 4 + sg + 1],
                tokcf[:, ee * 4 + sg:ee * 4 + sg + 1],
                bis[:, 8:9], ALU.add))
            B_tb.used(kb.VE, v)
        vtok = kb.vec(lambda e, ee=ee: e.tensor_copy(
            tokci[:, ee * 4:(ee + 1) * 4], tokcf[:, ee * 4:(ee + 1) * 4]))
        tokr[ee] = {kb.VE: vtok}
        B_sbc.used(kb.VE, vtok)

    def plan_gathers(ee, first):
        b = Buf()
        for sg in range(4):
            w = wmerge(B_x2tm.ready if first and sg == 0 else None,
                       tokr[ee] if sg == 0 else None,
                       scrS[sg].begin_write())
            gv = kb.store(scr[:, 1024 + sg * 1032:1024 + (sg + 1) * 1032],
                          misct[:, 0:1], waits=w)
            wd, fn, inc, add = kb.ops["gpsimd"].pop()
            idxap = tokci[:, ee * 4 + sg:ee * 4 + sg + 1]
            fn2 = (lambda e, sg=sg, idxap=idxap: e.indirect_dma_start(
                scr[:, 1024 + sg * 1032:1024 + (sg + 1) * 1032], None,
                x2tm, IndirectOffsetOnAxis(idxap, 0)))
            kb.ops["gpsimd"].append((wd, fn2, inc, add))
            scrS[sg].set_ready_d(gv)
        xgb[ee] = b

    plan_w1_loads(0)
    plan_w2_loads(0)
    plan_lists(0)
    plan_gathers(0, True)
    for ee in range(NE):
        w1b, w2b = w1bufs[ee], w2bufs[ee]
        XO = 0 if ee % 2 == 0 else 4096
        HO = 4096 - XO
        B_xf = B_p4A if ee % 2 == 0 else B_p4B
        B_h = B_p4B if ee % 2 == 0 else B_p4A
        p4lo_gate = B_xf.begin_write()
        p4lo_g = [None] * G
        for g in range(G):
            for sg in range(4):
                tsl = kb.slots[(g * 4 + sg) % 4]
                w = wmerge(scrS[sg].ready if g == 0 else None,
                           p4lo_gate if (sg == 0 and g == 0) else None,
                           tsl.last_evac)
                mv = kb.op("tensor", lambda e, sg=sg, g=g, tsl=tsl:
                           e.transpose(tsl.ap[:, 0:128],
                                       scr[:, 1024 + sg * 1032 + g * 128:
                                           1024 + sg * 1032 + (g + 1) * 128],
                                       ident[:]), waits=w, inc=kb.MM)
                cv = kb.scop(lambda e, sg=sg, g=g, tsl=tsl, XO=XO: e.copy(
                    pool4[:, XO + g * 512 + sg * 128:
                          XO + g * 512 + (sg + 1) * 128],
                    tsl.ap[:, 0:128]), waits={kb.MM: mv})
                tsl.last_evac = {kb.SC: cv}
                scrS[sg].used(kb.MM, mv)
            p4lo_g[g] = {kb.SC: kb.SC.n}
        B_xf.set_ready(kb.SC, kb.SC.n)
        srow = sbc[0:1, 1536:2048]
        sbc_gate = B_sbc.begin_write()
        for sg in range(4):
            tsl = kb.slots[6 + (sg % 2)]
            w = wmerge(scrS[sg].ready, tsl.last_evac,
                       sbc_gate if sg == 0 else None)
            mv = kb.op("tensor", lambda e, sg=sg, tsl=tsl, ee=ee:
                       e.transpose(tsl.ap[0:1, 0:128],
                                   scr[:, 1024 + sg * 1032 + 1024 + ee:
                                       1024 + sg * 1032 + 1024 + ee + 1],
                                   ident[:]), waits=w, inc=kb.MM)
            cv = kb.scop(lambda e, sg=sg, tsl=tsl: e.copy(
                srow[0:1, sg * 128:(sg + 1) * 128], tsl.ap[0:1, 0:128]),
                waits={kb.MM: mv})
            tsl.last_evac = {kb.SC: cv}
            scrS[sg].used(kb.MM, mv)
        bsl2 = kb.slots[6]
        mv = kb.mm(bsl2.ap, onesk[:], srow, True, True,
                   waits=wmerge({kb.SC: kb.SC.n}, bsl2.last_evac), inc=True)
        scv0 = kb.scop(lambda e, bsl2=bsl2: e.copy(sbc[:, 0:512], bsl2.ap),
                       waits={kb.MM: mv})
        bsl2.last_evac = {kb.SC: scv0}
        B_sbc.set_ready(kb.SC, scv0)

        hsl = [kb.slots[0], kb.slots[1]]
        p4hi_gate = B_h.begin_write()
        gelu_done = []
        for jf in range(8):
            sl = hsl[jf % 2]
            mv = None
            for k in range(G):
                pool = pool1 if k < 4 else pool2
                w = wmerge(w1b.ready if (jf == 0 and k == 0) else None,
                           p4lo_g[k] if jf == 0 else None,
                           sl.last_evac if k == 0 else None)
                lhsT = pool[:, (k % 4) * 1024 + jf * 128:
                            (k % 4) * 1024 + (jf + 1) * 128]
                mv = kb.mm(sl.ap, lhsT,
                           pool4[:, XO + k * 512:XO + (k + 1) * 512],
                           start=(k == 0), stop=(k == G - 1), waits=w,
                           inc=(k == G - 1))
                B_xf.used(kb.MM, mv)
            gvv = kb.act(pool4[:, HO + jf * 512:HO + (jf + 1) * 512],
                         sl.ap, AF.Gelu,
                         waits=wmerge({kb.MM: mv},
                                      p4hi_gate if jf == 0 else None))
            gelu_done.append(gvv)
            sl.last_evac = {kb.SC: gvv}
        hready = 0
        for jf in range(8):
            scv = kb.vec(lambda e, jf=jf, HO=HO: e.tensor_tensor(
                pool4[:, HO + jf * 512:HO + (jf + 1) * 512],
                pool4[:, HO + jf * 512:HO + (jf + 1) * 512],
                sbc[:, 0:512], ALU.mult),
                waits=wmerge({kb.SC: gelu_done[jf]},
                             B_sbc.ready if jf == 0 else None))
            hready = scv
            kb.slots[jf % 2].last_evac = wmerge(
                kb.slots[jf % 2].last_evac, {kb.VE: scv})
        B_sbc.used(kb.VE, hready)

        B_p1.used(kb.MM, kb.MM.n)
        B_p2.used(kb.MM, kb.MM.n)
        if ee + 1 < NE:
            plan_w1_loads(ee + 1)
            plan_lists(ee + 1)

        ysl = [kb.slots[4], kb.slots[5]]
        for j2 in range(8):
            sl = ysl[j2 % 2]
            mv = None
            for k in range(G):
                w = wmerge(w2b.ready if (j2 == 0 and k == 0) else None,
                           {kb.VE: hready} if (j2 == 0 and k == 0) else None,
                           sl.last_evac if k == 0 else None)
                lhsT = pool3[:, k * 1024 + j2 * 128:
                             k * 1024 + (j2 + 1) * 128]
                mv = kb.mm(sl.ap, lhsT,
                           pool4[:, HO + k * 512:HO + (k + 1) * 512],
                           start=(k == 0), stop=(k == G - 1), waits=w,
                           inc=(k == G - 1))
                B_h.used(kb.MM, mv)
            cv = kb.scop(lambda e, j2=j2, sl=sl: e.copy(tbv[j2 % 2], sl.ap),
                         waits=wmerge({kb.MM: mv}, B_tb.last_use))
            B_tb.set_ready(kb.SC, cv)
            sl.last_evac = {kb.SC: cv}
            for sg in range(4):
                tsl = kb.slots[2 + (sg % 2)]
                w = wmerge({kb.SC: cv}, tsl.last_evac,
                           B_xf.begin_write() if j2 == 0 and sg == 0
                           else None)
                mv2 = kb.op("tensor", lambda e, j2=j2, sg=sg, tsl=tsl:
                            e.transpose(tsl.ap[:, 0:128],
                                        tb[:, (j2 % 2) * 512 + sg * 128:
                                           (j2 % 2) * 512 + (sg + 1) * 128],
                                        ident[:]), waits=w, inc=kb.MM)
                cv2 = kb.scop(lambda e, j2=j2, sg=sg, tsl=tsl, XO=XO: e.copy(
                    pool4[:, XO + sg * 1024 + j2 * 128:
                          XO + sg * 1024 + (j2 + 1) * 128],
                    tsl.ap[:, 0:128]), waits={kb.MM: mv2})
                tsl.last_evac = {kb.SC: cv2}
                B_tb.used(kb.MM, mv2)
        for sg in range(4):
            w = wmerge({kb.SC: kb.SC.n},
                       ymtm_zero if ee == 0 and sg == 0 else None)
            sp = kb.store(pool4[:, XO + sg * 1024:XO + (sg + 1) * 1024],
                          misct[:, 0:1], waits=w, sem_i=3)
            wd, fn, inc, add = kb.ops["gpsimd"].pop()
            idxap = tokci[:, ee * 4 + sg:ee * 4 + sg + 1]
            fn2 = (lambda e, sg=sg, idxap=idxap, XO=XO: e.indirect_dma_start(
                ymtm, IndirectOffsetOnAxis(idxap, 0),
                pool4[:, XO + sg * 1024:XO + (sg + 1) * 1024], None,
                compute_op=ALU.add))
            kb.ops["gpsimd"].append((wd, fn2, inc, add))
            B_xf.used_d(sp)
            B_ymtm.set_ready_d(sp)
        B_p3.used(kb.MM, kb.MM.n)
        if ee + 1 < NE:
            plan_w2_loads(ee + 1)
            plan_gathers(ee + 1, False)

    if "sm_ymoe" in small_taps:
        for tc in range(16):
            kb.store(bass.AP(small_taps["sm_ymoe"].tensor,
                             small_taps["sm_ymoe"].offset + tc * 128 * D,
                             [[D, 128], [1, D]]),
                     misct[:, 0:1], waits=kb.sp_all())
            wd, fn, inc, add = kb.ops["gpsimd"].pop()
            fn2 = (lambda e, tc=tc: e.dma_start(
                out=bass.AP(small_taps["sm_ymoe"].tensor,
                            small_taps["sm_ymoe"].offset + tc * 128 * D,
                            [[D, 128], [1, D]]),
                in_=bass.AP(ymtm.tensor, ymtm.offset + tc * 128 * D,
                            [[D, 128], [1, D]])))
            kb.ops["gpsimd"].append((wd, fn2, inc, add))

    g4f = wmerge(B_p4.begin_write(), B_p4A.begin_write(),
                 B_p4B.begin_write())
    ch_use = [{}, {}]
    sq_use = [{}, {}]
    for tc in range(16):
        q = tc % 2
        x2o, ymo = q * 2048, q * 2048 + 1024
        w = wmerge(g4f if tc == 0 else None, ch_use[q],
                   B_x2tm.ready if tc == 0 else None)
        l1 = kb.load(pool4[:, x2o:x2o + 1024],
                     bass.AP(x2tm.tensor, x2tm.offset + tc * 128 * 1032,
                             [[1032, 128], [1, 1024]]), waits=w)
        l2 = kb.load(pool4[:, ymo:ymo + 1024],
                     bass.AP(ymtm.tensor, ymtm.offset + tc * 128 * D,
                             [[D, 128], [1, D]]),
                     waits=wmerge(B_ymtm.ready if tc == 0 else None))
        v = kb.vec(lambda e, x2o=x2o, ymo=ymo: e.tensor_tensor(
            pool4[:, x2o:x2o + 1024], pool4[:, x2o:x2o + 1024],
            pool4[:, ymo:ymo + 1024], ALU.add), waits=wmerge(l1, l2))
        c0 = q * 8
        sqo = 1024 + q * 1024
        s1 = kb.op("scalar", lambda e, x2o=x2o, c0=c0, sqo=sqo:
                   e.activation(out=scr[:, sqo:sqo + 1024],
                                in_=pool4[:, x2o:x2o + 1024], func=AF.Square,
                                accum_out=bis[:, c0 + 1:c0 + 2]),
                   waits=wmerge({kb.VE: v}, sq_use[q],
                                {kb.SC: kb.SC.n}), inc=kb.SC)
        s2 = kb.op("scalar", lambda e, x2o=x2o, c0=c0, sqo=sqo:
                   e.activation(out=scr[:, sqo:sqo + 1024],
                                in_=pool4[:, x2o:x2o + 1024], func=AF.Copy,
                                accum_out=bis[:, c0:c0 + 1]),
                   waits={kb.SC: kb.SC.n}, inc=kb.SC)
        sq_use[q] = {kb.SC: s2}
        v = kb.vec(lambda e, c0=c0: e.tensor_single_scalar(
            bis[:, c0:c0 + 1], bis[:, c0:c0 + 1], 1.0 / D, ALU.mult),
            waits={kb.SC: s2})
        v = kb.vec(lambda e, c0=c0: e.tensor_single_scalar(
            bis[:, c0 + 1:c0 + 2], bis[:, c0 + 1:c0 + 2], 1.0 / D, ALU.mult))
        v = kb.vec(lambda e, c0=c0: e.tensor_tensor(
            bis[:, c0 + 2:c0 + 3], bis[:, c0:c0 + 1], bis[:, c0:c0 + 1],
            ALU.mult))
        v = kb.vec(lambda e, c0=c0: e.tensor_tensor(
            bis[:, c0 + 1:c0 + 2], bis[:, c0 + 1:c0 + 2],
            bis[:, c0 + 2:c0 + 3], ALU.subtract))
        v5 = kb.scop(lambda e, c0=c0: e.activation(
            out=bis[:, c0 + 1:c0 + 2], in_=bis[:, c0 + 1:c0 + 2],
            func=AF.Sqrt, bias=epsc128[:]), waits={kb.VE: v})
        v6 = kb.vec(lambda e, c0=c0: e.reciprocal(
            bis[:, c0 + 2:c0 + 3], bis[:, c0 + 1:c0 + 2]),
            waits={kb.SC: v5})
        v = kb.vec(lambda e, x2o=x2o, c0=c0: e.tensor_scalar(
            pool4[:, x2o:x2o + 1024], pool4[:, x2o:x2o + 1024],
            bis[:, c0:c0 + 1], None, ALU.subtract, ALU.bypass))
        v = kb.vec(lambda e, x2o=x2o, c0=c0: e.tensor_scalar(
            pool4[:, x2o:x2o + 1024], pool4[:, x2o:x2o + 1024],
            bis[:, c0 + 2:c0 + 3], None, ALU.mult, ALU.bypass))
        sp = kb.store(bass.AP(out_tm.tensor, out_tm.offset + tc * 128 * D,
                              [[D, 128], [1, D]]), pool4[:, x2o:x2o + 1024],
                      waits={kb.VE: v})
        ch_use[q] = sp

    kb.op("gpsimd", lambda e: e.engine_nop(), waits=kb.sp_all())

    def emit(engine, name):
        seen = {}
        for w, fn, inc, add in kb.ops[name]:
            for s, v in w.items():
                if v and seen.get(s, 0) < v:
                    engine.wait_ge(s.h, v)
                    seen[s] = v
            instr = fn(engine)
            if inc is not None:
                instr.then_inc(inc.h, add)

    @block.sync
    def _(sync):
        emit(sync, "sync")

    @block.gpsimd
    def _(gpsimd):
        emit(gpsimd, "gpsimd")

    @block.scalar
    def _(scalar):
        emit(scalar, "scalar")

    @block.vector
    def _(vector):
        emit(vector, "vector")

    @block.tensor
    def _(tensor):
        emit(tensor, "tensor")

    es.close()
    return nc


def _pack_biases(kw):
    src = {"fr_b": "L0_attn_fr_b", "dp_b": "L0_attn_dp_b",
           "bp_b": "L0_attn_bp_b", "cp_b": "L0_attn_cp_b",
           "m1_b": "L0_attn_m1_b", "m2_b": "L0_attn_m2_b",
           "gate_b": "L0_attn_gate_b", "ffn1_b": "L0_attn_ffn1_b",
           "ffn2_b": "L0_attn_ffn2_b", "norm_g": "L0_attn_norm_g",
           "norm_b": "L0_attn_norm_b", "ffn_g": "L0_attn_ffn_g",
           "ffn_bn": "L0_attn_ffn_bn", "skip": "L0_attn_skip"}
    cols = []
    for name, _n in BIAS_ORDER:
        v = np.asarray(kw[src[name]], dtype=np.float32)
        cols.append(v.reshape(-1, 128).T)
    return np.ascontiguousarray(np.concatenate(cols, axis=1))


def kernel(**kw):
    x = np.asarray(kw["L0_attn_x"], dtype=np.float32)
    B, L, _ = x.shape
    biases = _pack_biases(kw)
    ident = np.eye(128, dtype=np.float32)
    misc = np.zeros((128, 20), dtype=np.float32)
    for tc in range(16):
        misc[:, tc] = tc * 128 + np.arange(128)
    misc[0:8, 16] = np.arange(8) * 513.0
    sel8 = np.zeros((8, 1024), dtype=np.float32)
    for ee in range(8):
        sel8[ee, ee * 128:(ee + 1) * 128] = 1.0
    w1T = np.ascontiguousarray(
        np.asarray(kw["L0_ffn_w1"], np.float32).transpose(0, 2, 1))
    w2T = np.ascontiguousarray(
        np.asarray(kw["L0_ffn_w2"], np.float32).transpose(0, 2, 1))
    gwT = np.ascontiguousarray(np.asarray(kw["L0_ffn_gate_w"], np.float32).T)

    shared = {
        "fr_w": np.ascontiguousarray(kw["L0_attn_fr_w"], np.float32),
        "dp_w": np.ascontiguousarray(kw["L0_attn_dp_w"], np.float32),
        "bp_w": np.ascontiguousarray(kw["L0_attn_bp_w"], np.float32),
        "cp_w": np.ascontiguousarray(kw["L0_attn_cp_w"], np.float32),
        "m1_w": np.ascontiguousarray(kw["L0_attn_m1_w"], np.float32),
        "m2_w": np.ascontiguousarray(kw["L0_attn_m2_w"], np.float32),
        "gate_w": np.ascontiguousarray(kw["L0_attn_gate_w"], np.float32),
        "ffn1_w": np.ascontiguousarray(kw["L0_attn_ffn1_w"], np.float32),
        "ffn2_w": np.ascontiguousarray(kw["L0_attn_ffn2_w"], np.float32),
        "gwT": gwT, "w1T": w1T, "w2T": w2T,
        "biases_in": biases, "ident_in": ident,
        "misc_in": misc, "sel8_in": sel8,
    }
    in_maps = []
    for c in range(8):
        b, h = c // 2, c % 2
        xs = np.ascontiguousarray(x[b, h * T:(h + 1) * T, :].T)
        sel = np.zeros((128, 8), dtype=np.float32)
        if h == 1:
            sel[:, c - 1] = 1.0
        m = dict(shared)
        m["x_fm"] = xs
        m["sel_in"] = sel
        in_maps.append(m)

    nc = build_nc()
    res = run_bass_kernel_spmd(nc, in_maps, core_ids=list(range(8)))
    out = np.empty((B, L, D), dtype=np.float32)
    for c in range(8):
        b, h = c // 2, c % 2
        out[b, h * T:(h + 1) * T, :] = res.results[c]["out_tm"]
    return out


# version 12
# speedup vs baseline: 1.0431x; 1.0359x over previous
import numpy as np
import jax
jax.config.update('jax_compilation_cache_dir', '/root/.bass_jax_cache')
jax.config.update('jax_persistent_cache_min_compile_time_secs', 1.0)
jax.config.update('jax_persistent_cache_min_entry_size_bytes', 0)
import concourse.bass as bass
import concourse.mybir as mybir
from concourse.bass_utils import run_bass_kernel_spmd
from concourse.bass_isa import ReduceOp
from concourse import library_config
from concourse.bass_core_rust import IndirectOffsetOnAxis

F32 = mybir.dt.float32
ALU = mybir.AluOpType
AF = mybir.ActivationFunctionType
AX = mybir.AxisListType

D = 1024
G = 8
T = 2048
TH = 1024
NE = 8
F = 1024
CAP = 2048.0
THR = 0.9
EPS = 1e-5
ENGS = ("sync", "gpsimd", "scalar", "vector", "tensor")

BIAS_ORDER = [("fr_b", 8), ("dp_b", 8), ("bp_b", 8), ("cp_b", 8),
              ("m1_b", 16), ("m2_b", 8), ("gate_b", 8), ("ffn1_b", 32),
              ("ffn2_b", 8), ("norm_g", 8), ("norm_b", 8), ("ffn_g", 8),
              ("ffn_bn", 8), ("skip", 8)]
NBIAS = sum(n for _, n in BIAS_ORDER)


class Sem:
    __slots__ = ("name", "n", "h")
    def __init__(self, name):
        self.name, self.n, self.h = name, 0, None


def wmerge(*dicts):
    out = {}
    for d in dicts:
        if not d:
            continue
        for s, v in d.items():
            if v and out.get(s, 0) < v:
                out[s] = v
    return out


class Buf:
    __slots__ = ("ready", "last_use")
    def __init__(self):
        self.ready = {}
        self.last_use = {}

    def set_ready(self, sem, val):
        if val and self.ready.get(sem, 0) < val:
            self.ready[sem] = val

    def set_ready_d(self, d):
        self.ready = wmerge(self.ready, d)

    def used(self, sem, val):
        if val and self.last_use.get(sem, 0) < val:
            self.last_use[sem] = val

    def used_d(self, d):
        self.last_use = wmerge(self.last_use, d)

    def begin_write(self):
        w = wmerge(self.ready, self.last_use)
        self.ready = {}
        self.last_use = {}
        return w


class Slot:
    __slots__ = ("ap", "last_evac", "mm_done")
    def __init__(self, ap):
        self.ap = ap
        self.last_evac = {}
        self.mm_done = 0


class KB:
    def __init__(self, nc):
        self.nc = nc
        self.ops = {e: [] for e in ENGS}
        self.LWs = [Sem("LW%d" % i) for i in range(6)]
        self.SPs = [Sem("SP%d" % i) for i in range(4)]
        self.MM = Sem("MM")
        self.GP = Sem("GP")
        self.SC = Sem("SC")
        self.VE = Sem("VE")
        self.CC = Sem("CC")
        self.sems = self.LWs + self.SPs + [self.MM, self.GP, self.SC,
                                           self.VE, self.CC]
        self.lw_rot = 0
        self.sp_rot = 0
        self.slots = []
        self.wring_lastmm = [{}, {}]
        self.wring_ctr = 0

    def lw_all(self):
        return {s: s.n for s in self.LWs if s.n}

    def sp_all(self):
        return {s: s.n for s in self.SPs if s.n}

    def op(self, eng, fn, waits=None, inc=None, add=1):
        w = dict(waits) if waits else {}
        v = None
        if inc is not None:
            inc.n += add
            v = inc.n
        self.ops[eng].append((w, fn, inc, add))
        return v

    def load(self, out_ap, in_ap, waits=None):
        sem = self.LWs[self.lw_rot % len(self.LWs)]
        self.lw_rot += 1
        w = wmerge(waits, {sem: sem.n})
        v = self.op("sync", lambda e: e.dma_start(out=out_ap, in_=in_ap),
                    w, sem, 16)
        return {sem: v}

    def store(self, out_ap, in_ap, waits=None, accum=False, sem_i=None):
        if accum:
            fn = lambda e: e.dma_start(out=out_ap, in_=in_ap, accum_op=ALU.add)
        else:
            fn = lambda e: e.dma_start(out=out_ap, in_=in_ap)
        if sem_i is None:
            sem = self.SPs[self.sp_rot % len(self.SPs)]
            self.sp_rot += 1
        else:
            sem = self.SPs[sem_i]
        w = wmerge(waits, {sem: sem.n})
        v = self.op("gpsimd", fn, w, sem, 16)
        return {sem: v}

    def gdma(self, out_ap, in_ap, waits=None):
        return self.store(out_ap, in_ap, waits=waits)

    def mm(self, out, lhsT, rhs, start, stop, waits=None, inc=False):
        return self.op(
            "tensor",
            lambda e: e.matmul(out, lhsT, rhs, start=start, stop=stop),
            waits, self.MM if inc else None)

    def act(self, out, in_, func, bias=0.0, scale=1.0, waits=None):
        w = wmerge(waits, {self.SC: self.SC.n})
        return self.op(
            "scalar",
            lambda e: e.activation(out=out, in_=in_, func=func, bias=bias,
                                   scale=scale),
            w, self.SC)

    def scop(self, fn, waits=None):
        w = wmerge(waits, {self.SC: self.SC.n})
        return self.op("scalar", fn, w, self.SC)

    def vec(self, fn, waits=None):
        w = wmerge(waits, {self.VE: self.VE.n})
        return self.op("vector", fn, w, self.VE)


TEST_SLOTS4 = False


def build_nc(taps=()):
    nc = bass.Bass("TRN2", target_bir_lowering=False)
    kb = KB(nc)

    def dinp(name, shape):
        return nc.dram_tensor(name, shape, F32, kind="ExternalInput").ap()

    x_fm = dinp("x_fm", [D, T])
    fr_w = dinp("fr_w", [D, D]); dp_w = dinp("dp_w", [D, D])
    bp_w = dinp("bp_w", [D, D]); cp_w = dinp("cp_w", [D, D])
    m1_w = dinp("m1_w", [D, 2 * D]); m2_w = dinp("m2_w", [2 * D, D])
    gate_w = dinp("gate_w", [2 * D, D])
    ffn1_w = dinp("ffn1_w", [D, 4 * D]); ffn2_w = dinp("ffn2_w", [4 * D, D])
    gwT = dinp("gwT", [D, NE])
    w1T = dinp("w1T", [NE, D, F])
    w2T = dinp("w2T", [NE, F, D])
    biases_in = dinp("biases_in", [128, NBIAS])
    misc_in = dinp("misc_in", [128, 20])
    sel8_in = dinp("sel8_in", [8, 1024])
    ident_in = dinp("ident_in", [128, 128])
    sel_in = dinp("sel_in", [128, 8])

    out_tm = nc.dram_tensor("out_tm", [T, D], F32, kind="ExternalOutput").ap()
    tap_out = {}
    small_taps = {}
    for tname in taps:
        if tname.startswith("sm_"):
            shp = {"sm_probs": [128, 128], "sm_s": [128, 128],
                   "sm_R": [128, 128], "sm_bis": [128, 96],
                   "sm_ymoe": [T, D], "sm_tok": [8, 513],
                   "sm_cnt": [1, 8]}[tname]
            small_taps[tname] = nc.dram_tensor("tap_" + tname, shp, F32,
                                               kind="ExternalOutput").ap()
        else:
            tap_out[tname] = nc.dram_tensor("tap_" + tname, [D, T], F32,
                                            kind="ExternalOutput").ap()

    c_hbm = nc.dram_tensor("c_hbm", [D, T], F32).ap()
    sk_hbm = nc.dram_tensor("sk_hbm", [D, T], F32).ap()
    st_hbm = nc.dram_tensor("st_hbm", [D, T], F32).ap()
    A_hbm = nc.dram_tensor("A_hbm", [D, T], F32).ap()
    mo_hbm = nc.dram_tensor("mo_hbm", [D, T], F32).ap()
    x2tm = nc.dram_tensor("x2tm", [T + 1, 1032], F32).ap()
    ymtm = nc.dram_tensor("ymtm", [T + 1, D], F32).ap()
    gin_h = nc.dram_tensor("gin_h", [128, 8], F32)
    gout_h = nc.dram_tensor("gout_h", [8 * 128, 8], F32)
    rin_h = nc.dram_tensor("rin_h", [128, 128], F32)
    rout_h = nc.dram_tensor("rout_h", [8 * 128, 128], F32)

    from contextlib import ExitStack
    es = ExitStack()
    e = es.enter_context

    pool1 = e(nc.sbuf_tensor("pool1", [128, 8192], F32))
    pool2 = e(nc.sbuf_tensor("pool2", [128, 8192], F32))
    pool3 = e(nc.sbuf_tensor("pool3", [128, 8192], F32))
    pool4 = e(nc.sbuf_tensor("pool4", [128, 8192], F32))
    wring = e(nc.sbuf_tensor("wring", [128, 4096], F32))
    scr = e(nc.sbuf_tensor("scr", [128, 5280], F32))
    mbt = e(nc.sbuf_tensor("mbt", [128, 1024], F32))
    rbt = e(nc.sbuf_tensor("rbt", [128, 1024], F32))
    stats_m = e(nc.sbuf_tensor("stats_m", [1, 1024], F32))
    stats_r = e(nc.sbuf_tensor("stats_r", [1, 1024], F32))
    stats_t = e(nc.sbuf_tensor("stats_t", [1, 2048], F32))
    sbias = e(nc.sbuf_tensor("sbias", [128, NBIAS], F32))
    ident = e(nc.sbuf_tensor("ident", [128, 128], F32))
    selt = e(nc.sbuf_tensor("selt", [128, 8], F32))
    gwTt = e(nc.sbuf_tensor("gwTt", [128, 64], F32))
    onesc = e(nc.sbuf_tensor("onesc", [128, 1], F32))
    onesk = e(nc.sbuf_tensor("onesk", [1, 128], F32))
    ones128 = e(nc.sbuf_tensor("ones128", [128, 128], F32))
    Sinit = e(nc.sbuf_tensor("Sinit", [128, 8], F32))
    Ainit = e(nc.sbuf_tensor("Ainit", [128, 8], F32))
    gath = e(nc.sbuf_tensor("gath", [128, 64], F32))
    carry = e(nc.sbuf_tensor("carry", [128, 8], F32))
    probs = e(nc.sbuf_tensor("probs", [128, 128], F32))
    assn = e(nc.sbuf_tensor("assn", [128, 128], F32))
    s_sb = e(nc.sbuf_tensor("s_sb", [128, 128], F32))
    sbc = e(nc.sbuf_tensor("sbc", [128, 2048], F32))
    bis = e(nc.sbuf_tensor("bis", [128, 96], F32))
    epsc = e(nc.sbuf_tensor("epsc", [1, 1], F32))
    epsc128 = e(nc.sbuf_tensor("epsc128", [128, 1], F32))
    ones11 = e(nc.sbuf_tensor("ones11", [1, 1], F32))
    misct = e(nc.sbuf_tensor("misct", [128, 20], F32))
    sel8t = e(nc.sbuf_tensor("sel8t", [8, 1024], F32))
    tokcf = e(nc.sbuf_tensor("tokcf", [128, 32], F32))
    tokci = e(nc.sbuf_tensor("tokci", [128, 32], mybir.dt.int32))
    neg9 = e(nc.sbuf_tensor("neg9", [128, 128], F32))

    PA = e(nc.psum_tensor("PA", [128, 2048], F32))
    PB = e(nc.psum_tensor("PB", [128, 2048], F32))

    for s in kb.sems:
        s.h = e(nc.semaphore(s.name))
    block = e(nc.Block())

    for i in range(4):
        kb.slots.append(Slot(PA[:, i * 512:(i + 1) * 512]))
    for i in range(4):
        kb.slots.append(Slot(PB[:, i * 512:(i + 1) * 512]))

    tb = scr[:, 0:1024]
    tbv = [scr[:, 0:512], scr[:, 512:1024]]
    stdb = [scr[:, 1024:2048], scr[:, 2048:3072]]
    udb = [scr[:, 3072:4096], scr[:, 4096:5120]]
    B_tb = Buf()
    stdb_use = [{}, {}]
    udb_use = [{}, {}]

    bofs = {}
    cur = 0
    for nme, n in BIAS_ORDER:
        bofs[nme] = cur
        cur += n
    def bv(nme, g):
        return sbias[:, bofs[nme] + g: bofs[nme] + g + 1]

    B_p1, B_p2, B_p3, B_p4 = Buf(), Buf(), Buf(), Buf()
    B_chbm, B_skhbm, B_sthbm, B_Ahbm = Buf(), Buf(), Buf(), Buf()
    B_mohbm, B_x2tm, B_ymtm = Buf(), Buf(), Buf()
    B_mb, B_rb, B_sbc, B_probs = Buf(), Buf(), Buf(), Buf()

    kb.load(sbias[:], biases_in)
    kb.load(ident[:], ident_in)
    kb.load(selt[:], sel_in)
    kb.load(gwTt[:], gwT.rearrange("(k p) e -> p k e", p=128))
    kb.load(misct[:], misc_in)
    kb.load(sel8t[:], sel8_in)
    const_lw = kb.lw_all()
    kb.vec(lambda e: e.memset(onesc[:], 1.0))
    kb.vec(lambda e: e.memset(onesk[:], 1.0))
    kb.vec(lambda e: e.memset(ones128[:], 1.0))
    kb.vec(lambda e: e.memset(Sinit[:], 0.0))
    kb.vec(lambda e: e.memset(Ainit[:], 1.0))
    kb.vec(lambda e: e.memset(neg9[:], -1e9))
    kb.vec(lambda e: e.memset(epsc[:], EPS))
    kb.vec(lambda e: e.memset(epsc128[:], EPS))
    kb.vec(lambda e: e.memset(ones11[:], 1.0))
    kb.vec(lambda e: e.memset(bis[:], 0.0))
    const_ve = kb.VE.n
    CONSTS = wmerge({kb.VE: const_ve}, const_lw)

    def proj(w_hbm, Kc, Nout, rhs_fn, rhs_bufs, evac_pair_fn, ntt,
             kpass=4, first_waits=None, rhs_ready_k=None, korder=None,
             slots4=False):
        nkp = (Kc + kpass - 1) // kpass
        rhs_ready = wmerge(*(b.ready for b in rhs_bufs))
        ks_all = korder if korder is not None else list(range(Kc))
        passes = []
        for jb in range(Nout // 512):
            for kp in range(nkp):
                passes.append((jb, kp,
                               ks_all[kp * kpass:
                                      min((kp + 1) * kpass, Kc)]))
        planned = {}
        k_seen = set()

        def plan_load(idx):
            jb, kp, ks = passes[idx]
            setid = kb.wring_ctr % 2
            kb.wring_ctr += 1
            i = 0
            lwv = {}
            first_load = True
            while i < len(ks):
                npair = 2 if (i + 1 < len(ks)
                              and ks[i + 1] == ks[i] + 1) else 1
                k0 = ks[i]
                dst = wring[:, (setid * 4 + i) * 512:
                            (setid * 4 + i + npair) * 512]
                if npair == 2:
                    ap = bass.AP(w_hbm.tensor,
                                 w_hbm.offset + k0 * 128 * Nout + jb * 512,
                                 [[Nout, 128], [128 * Nout, 2], [1, 512]])
                else:
                    ap = bass.AP(w_hbm.tensor,
                                 w_hbm.offset + k0 * 128 * Nout + jb * 512,
                                 [[Nout, 128], [1, 512]])
                w = kb.wring_lastmm[setid] if first_load else None
                lwv = wmerge(lwv, kb.load(dst, ap, waits=w))
                first_load = False
                i += npair
            planned[idx] = (setid, lwv)

        plan_load(0)
        if len(passes) > 1:
            plan_load(1)
        first_mm_of_proj = True
        for idx, (jb, kp, ks) in enumerate(passes):
            setid, lwv = planned.pop(idx)
            first_of_pass = True
            lastv = None
            for j_local in range(4):
                for tt in range(ntt):
                    s = (j_local * 2 + tt) if ntt == 2 \
                        else (jb % 2) * 4 + j_local
                    slot = kb.slots[s]
                    for i, k in enumerate(ks):
                        w = {}
                        if kp == 0 and i == 0:
                            w = wmerge(w, slot.last_evac)
                        if first_of_pass:
                            w = wmerge(w, lwv)
                            if first_mm_of_proj:
                                w = wmerge(w,
                                           None if rhs_ready_k else rhs_ready,
                                           first_waits)
                        if rhs_ready_k is not None and k not in k_seen:
                            k_seen.add(k)
                            w = wmerge(w, rhs_ready_k[k])
                        start = (kp == 0 and i == 0)
                        stop = (kp == nkp - 1 and i == len(ks) - 1)
                        last_of_pass = (i == len(ks) - 1 and
                                        j_local == 3 and tt == ntt - 1)
                        lhsT = wring[:, (setid * 4 + i) * 512
                                     + j_local * 128:
                                     (setid * 4 + i) * 512
                                     + (j_local + 1) * 128]
                        v = kb.mm(slot.ap, lhsT, rhs_fn(k, tt), start,
                                  stop, waits=w, inc=(stop or last_of_pass))
                        first_of_pass = False
                        first_mm_of_proj = False
                        if v:
                            lastv = v
                        if stop:
                            slot.mm_done = v
            if lastv:
                kb.wring_lastmm[setid] = {kb.MM: lastv}
            if idx + 2 < len(passes):
                plan_load(idx + 2)
            if kp == nkp - 1:
                if ntt == 2:
                    for jl in range(4):
                        j = jb * 4 + jl
                        base = jl * 2
                        sl0, sl1 = kb.slots[base], kb.slots[base + 1]
                        pt = PA if base < 4 else PB
                        pap = pt[:, (base % 4) * 512:(base % 4) * 512 + 1024]
                        mmw = {kb.MM: max(sl0.mm_done, sl1.mm_done)}
                        done = evac_pair_fn(j, None, pap, mmw)
                        sl0.last_evac = done
                        sl1.last_evac = done
                else:
                    for jl in (0, 2):
                        base = jl if slots4 else (jb % 2) * 4 + jl
                        sl0, sl1 = kb.slots[base], kb.slots[base + 1]
                        pt = PA if base < 4 else PB
                        pap = pt[:, (base % 4) * 512:(base % 4) * 512 + 1024]
                        mmw = {kb.MM: max(sl0.mm_done, sl1.mm_done)}
                        done = evac_pair_fn(jb * 4 + jl, jb * 4 + jl + 1,
                                            pap, mmw)
                        sl0.last_evac = done
                        sl1.last_evac = done
        for b in rhs_bufs:
            b.used(kb.MM, kb.MM.n)

    def ln_stats(x_ap_fn, x_buf, th, sqp, sq_gate, x_ready_g=None):
        ntt = th // 512
        sqv = 0
        for tt in range(ntt):
            for g in range(G):
                idx = (g * ntt + tt) * 512
                w = wmerge(x_ready_g[g] if x_ready_g else x_buf.ready,
                           CONSTS,
                           sq_gate if (tt == 0 and g == 0) else None)
                sqv = kb.act(sqp[:, idx:idx + 512], x_ap_fn(g, tt),
                             AF.Square, waits=w)
        sx, sq = kb.slots[4], kb.slots[5]
        for tt in range(ntt):
            for g in range(G):
                idx = (g * ntt + tt) * 512
                w0 = wmerge(x_buf.ready, sx.last_evac if g == 0 else None,
                            CONSTS)
                kb.mm(sx.ap[0:1, :], onesc[:], x_ap_fn(g, tt),
                      start=(g == 0), stop=(g == G - 1), waits=w0, inc=True)
                w1 = wmerge({kb.SC: sqv} if (tt == 0 and g == 0) else None,
                            sq.last_evac if g == 0 else None)
                v = kb.mm(sq.ap[0:1, :], onesc[:], sqp[:, idx:idx + 512],
                          start=(g == 0), stop=(g == G - 1), waits=w1,
                          inc=True)
                sq.mm_done = v
            sxv = kb.MM.n - 1
            c = slice(tt * 512, (tt + 1) * 512)
            v1 = kb.scop(lambda e, c=c: e.mul(stats_m[0:1, c], sx.ap[0:1, :],
                                              1.0 / D), waits={kb.MM: sxv})
            v2 = kb.scop(lambda e, c=c: e.mul(stats_t[0:1, c], sq.ap[0:1, :],
                                              1.0 / D), waits={kb.MM: kb.MM.n})
            sx.last_evac = {kb.SC: v1}
            sq.last_evac = {kb.SC: v2}
            c2 = slice(1024 + c.start, 1024 + c.stop)
            kb.vec(lambda e, c=c, c2=c2: e.tensor_tensor(
                stats_t[0:1, c2], stats_m[0:1, c], stats_m[0:1, c], ALU.mult),
                waits={kb.SC: v2})
            v4 = kb.vec(lambda e, c=c, c2=c2: e.tensor_tensor(
                stats_t[0:1, c], stats_t[0:1, c], stats_t[0:1, c2],
                ALU.subtract))
            v5 = kb.scop(lambda e, c=c: e.activation(
                out=stats_t[0:1, c], in_=stats_t[0:1, c], func=AF.Sqrt,
                bias=epsc[0:1, :]), waits={kb.VE: v4})
            kb.vec(lambda e, c=c: e.reciprocal(stats_r[0:1, c],
                                               stats_t[0:1, c]),
                   waits={kb.SC: v5})
        x_buf.used(kb.MM, kb.MM.n)
        for row, srct, dstt, bufb in ((0, stats_m, mbt, B_mb),
                                      (1, stats_r, rbt, B_rb)):
            bw = bufb.begin_write()
            for tt in range(ntt):
                sl = kb.slots[6 + row]
                c = slice(tt * 512, (tt + 1) * 512)
                w = wmerge({kb.VE: kb.VE.n, kb.SC: kb.SC.n}, sl.last_evac,
                           bw, CONSTS)
                kb.mm(sl.ap, onesk[:], srct[0:1, c], True, True,
                      waits=w, inc=True)
                vv = kb.scop(lambda e, sl=sl, dstt=dstt, c=c:
                             e.copy(dstt[:, c], sl.ap),
                             waits={kb.MM: kb.MM.n})
                sl.last_evac = {kb.SC: vv}
                bufb.set_ready(kb.SC, vv)

    def pslice(pool, g, tt=None):
        if tt is None:
            return pool[:, g * TH:(g + 1) * TH]
        return pool[:, g * TH + tt * 512: g * TH + (tt + 1) * 512]

    def hslab(hb, g, h, tt=None):
        off = hb.offset + g * 128 * T + h * TH
        if tt is None:
            return bass.AP(hb.tensor, off, [[T, 128], [1, TH]])
        return bass.AP(hb.tensor, off + tt * 512, [[T, 128], [1, 512]])

    def tap(name, src_pool, h, buf):
        if name in tap_out:
            for g in range(G):
                v = kb.store(hslab(tap_out[name], g, h),
                             pslice(src_pool, g), waits=buf.ready)
                buf.used_d(v)

    def plan_ln1(h, xr, sqp, sqb):
        ln_stats(lambda g, tt: pslice(pool1, g, tt), B_p1, TH,
                 sqp, sqb.begin_write(), x_ready_g=xr)
        sqb.set_ready(kb.MM, kb.MM.n)
        xn_rg = []
        for g in range(G):
            w = wmerge(B_mb.ready, B_rb.ready, CONSTS, B_p1.last_use,
                       {kb.MM: kb.MM.n} if g == 0 else None)
            kb.vec(lambda e, g=g: e.tensor_tensor(
                pslice(pool1, g), pslice(pool1, g), mbt[:], ALU.subtract),
                waits=w)
            kb.vec(lambda e, g=g: e.tensor_tensor(
                pslice(pool1, g), pslice(pool1, g), rbt[:], ALU.mult))
            vv = kb.vec(lambda e, g=g: e.tensor_scalar(
                pslice(pool1, g), pslice(pool1, g),
                bv("norm_g", g), bv("norm_b", g), ALU.mult, ALU.add))
            xn_rg.append({kb.VE: vv})
        B_p1.ready = {kb.VE: kb.VE.n}
        B_mb.used(kb.VE, kb.VE.n)
        B_rb.used(kb.VE, kb.VE.n)
        tap("xn", pool1, h, B_p1)
        return xn_rg

    xg_ready = {}
    prefetched = [None]
    pre_ln1 = [None]
    for h in range(2):
        if h == 1 and prefetched[0] is not None:
            B_p1.ready = prefetched[0]
            xr = xg_ready[1]
        else:
            gate = B_p1.begin_write()
            xr = []
            for g in range(G):
                lv = kb.load(pslice(pool1, g), hslab(x_fm, g, h),
                             waits=gate if g == 0 else None)
                xr.append(lv)
            B_p1.set_ready_d(kb.lw_all())
            xg_ready[h] = xr

        if h == 1 and pre_ln1[0] is not None:
            xn_rg = pre_ln1[0]
        else:
            xn_rg = plan_ln1(h, xr, pool4, B_p4)

        p2gate = B_p2.begin_write()
        p3gate = B_p3.begin_write()

        low_rg, high_rg = [None] * G, [None] * G

        def fr_evac(j, _u, pap, mmw):
            w = wmerge(mmw, B_tb.last_use)
            sv = kb.act(tb, pap, AF.Sigmoid, bias=bv("fr_b", j), waits=w)
            B_tb.set_ready(kb.SC, sv)
            v1 = kb.vec(lambda e, j=j: e.tensor_tensor(
                pslice(pool2, j), pslice(pool1, j), tb, ALU.mult),
                waits=wmerge({kb.SC: sv}, p2gate if j == 0 else None))
            v2 = kb.vec(lambda e, j=j: e.tensor_tensor(
                pslice(pool3, j), pslice(pool1, j), pslice(pool2, j),
                ALU.subtract),
                waits=p3gate if j == 0 else None)
            B_tb.used(kb.VE, v1)
            B_p2.set_ready(kb.VE, v2)
            B_p3.set_ready(kb.VE, v2)
            low_rg[j] = {kb.VE: v1}
            high_rg[j] = {kb.VE: v2}
            return {kb.VE: v1}

        proj(fr_w, 8, 1024, lambda k, tt: pslice(pool1, k, tt), [B_p1],
             fr_evac, ntt=2, rhs_ready_k=xn_rg)
        tap("low", pool2, h, B_p2)
        tap("high", pool3, h, B_p3)

        def cp_evac(j, _u, pap, mmw):
            i = j % 2
            w = wmerge(mmw, stdb_use[i])
            sv = kb.act(stdb[i], pap, AF.Tanh, bias=bv("cp_b", j), waits=w)
            spv = kb.store(hslab(c_hbm, j, h), stdb[i], waits={kb.SC: sv})
            stdb_use[i] = spv
            B_chbm.set_ready_d(spv)
            return {kb.SC: sv}

        proj(cp_w, 8, 1024, lambda k, tt: pslice(pool3, k, tt), [B_p3],
             cp_evac, ntt=2, rhs_ready_k=high_rg)

        p3gate2 = B_p3.begin_write()

        def dp_evac(j, _u, pap, mmw):
            w = wmerge(mmw, p3gate2 if j == 0 else None)
            sv = kb.act(pslice(pool3, j), pap, AF.Sigmoid,
                        bias=bv("dp_b", j), waits=w)
            B_p3.set_ready(kb.SC, sv)
            return {kb.SC: sv}

        proj(dp_w, 8, 1024, lambda k, tt: pslice(pool2, k, tt), [B_p2],
             dp_evac, ntt=2, rhs_ready_k=low_rg)
        tap("a", pool3, h, B_p3)

        def bp_evac(g, _u, pap, mmw, h=h):
            i = g % 2
            w = wmerge(mmw, B_tb.last_use)
            sv = kb.act(tb, pap, AF.Tanh, bias=bv("bp_b", g), waits=w)
            B_tb.set_ready(kb.SC, sv)
            w2 = wmerge({kb.SC: sv}, udb_use[i], B_p3.ready)
            uv = kb.vec(lambda e, g=g, i=i: e.tensor_tensor(
                udb[i], tb, pslice(pool1, g), ALU.mult), waits=w2)
            B_tb.used(kb.VE, uv)
            w3 = wmerge({kb.VE: uv}, stdb_use[i])
            stv = kb.vec(lambda e, g=g, i=i: e.tensor_tensor_scan(
                stdb[i], pslice(pool3, g), udb[i], Sinit[:, g:g + 1],
                ALU.mult, ALU.add), waits=w3)
            kb.vec(lambda e, g=g, i=i: e.tensor_copy(
                Sinit[:, g:g + 1], stdb[i][:, TH - 1:TH]))
            spv = kb.store(hslab(st_hbm, g, h), stdb[i],
                           waits={kb.VE: kb.VE.n})
            stdb_use[i] = spv
            B_sthbm.set_ready_d(spv)
            zv = kb.vec(lambda e: e.memset(tb, 0.0),
                        waits=wmerge({kb.VE: stv}, B_tb.last_use))
            B_tb.set_ready(kb.VE, zv)
            av = kb.vec(lambda e, g=g, i=i: e.tensor_tensor_scan(
                udb[i], pslice(pool3, g), tb, Ainit[:, g:g + 1],
                ALU.mult, ALU.add), waits={kb.VE: zv})
            B_p3.used(kb.VE, av)
            B_tb.used(kb.VE, av)
            kb.vec(lambda e, g=g, i=i: e.tensor_copy(
                Ainit[:, g:g + 1], udb[i][:, TH - 1:TH]))
            spv2 = kb.store(hslab(A_hbm, g, h), udb[i],
                            waits={kb.VE: kb.VE.n})
            udb_use[i] = spv2
            B_Ahbm.set_ready_d(spv2)
            skv = kb.vec(lambda e, g=g: e.tensor_single_scalar(
                pslice(pool1, g), pslice(pool1, g), bv("skip", g), ALU.mult),
                waits=wmerge({kb.VE: uv}, B_p1.last_use))
            spv3 = kb.store(hslab(sk_hbm, g, h), pslice(pool1, g),
                            waits={kb.VE: skv})
            B_skhbm.set_ready_d(spv3)
            B_p1.used_d(spv3)
            return {kb.VE: av}

        proj(bp_w, 8, 1024, lambda k, tt: pslice(pool2, k, tt), [B_p2],
             bp_evac, ntt=2, rhs_ready_k=low_rg)

        if h == 0:
            gate = B_p1.begin_write()
            xg_ready[1] = []
            for g in range(G):
                lv = kb.load(pslice(pool1, g), hslab(x_fm, g, 1),
                             waits=gate if g == 0 else None)
                xg_ready[1].append(lv)
            B_p1.set_ready_d(kb.lw_all())
            prefetched[0] = dict(B_p1.ready)
            pre_ln1[0] = plan_ln1(1, xg_ready[1], pool3, B_p3)

        for tt in range(2):
            h1_rg = [None] * 16
            p4gate = B_p4.begin_write()

            def m1_evac(j0, j1, pap, mmw, p4gate=p4gate):
                sv = None
                for jj, off in ((j0, 0), (j1, 512)):
                    w = wmerge(mmw if off == 0 else None,
                               p4gate if jj == 0 else None)
                    sv = kb.act(pool4[:, jj * 512:(jj + 1) * 512],
                                pap[:, off:off + 512], AF.Gelu,
                                bias=bv("m1_b", jj), waits=w)
                    h1_rg[jj] = {kb.SC: sv}
                B_p4.set_ready(kb.SC, sv)
                return {kb.SC: sv}

            proj(m1_w, 8, 2048, lambda k, _t, tt=tt: pslice(pool2, k, tt),
                 [B_p2], m1_evac, ntt=1, rhs_ready_k=low_rg, slots4=TEST_SLOTS4)

            def m2_evac(j0, j1, pap, mmw, tt=tt, h=h):
                svs = []
                for jj, off in ((j0, 0), (j1, 512)):
                    i = jj % 2
                    w = wmerge(mmw if off == 0 else None, B_tb.last_use)
                    sv = kb.scop(lambda e, jj=jj, off=off: e.add(
                        tbv[jj % 2], pap[:, off:off + 512], bv("m2_b", jj)),
                        waits=w)
                    B_tb.set_ready(kb.SC, sv)
                    spv = kb.store(hslab(mo_hbm, jj, h, tt), tbv[jj % 2],
                                   waits={kb.SC: sv})
                    B_tb.used_d(spv)
                    svs.append(sv)
                B_mohbm.set_ready_d(kb.sp_all())
                return {kb.SC: max(svs)}

            proj(m2_w, 16, 1024,
                 lambda k, _t: pool4[:, k * 512:(k + 1) * 512], [B_p4],
                 m2_evac, ntt=1, rhs_ready_k=h1_rg, slots4=TEST_SLOTS4)

    gv = kb.store(gin_h.ap(), Sinit[:], waits={kb.VE: kb.VE.n})
    ccv = kb.op("gpsimd", lambda e: e.collective_compute(
        "AllGather", ALU.bypass, replica_groups=[list(range(8))],
        ins=[gin_h.ap().opt()], outs=[gout_h.ap().opt()]),
        waits=gv, inc=kb.CC)
    glv = kb.gdma(gath[:],
                  bass.AP(gout_h, 0, [[8, 128], [1024, 8], [1, 8]]),
                  waits={kb.CC: ccv})
    cv1 = kb.vec(lambda e: e.tensor_tensor(
        bass.AP(gath, 0, [[64, 128], [1, 8], [8, 8]]),
        bass.AP(gath, 0, [[64, 128], [1, 8], [8, 8]]),
        bass.AP(selt, 0, [[8, 128], [0, 8], [1, 8]]), ALU.mult),
        waits=wmerge(glv, CONSTS))
    cv2 = kb.vec(lambda e: e.tensor_reduce(
        carry[:], bass.AP(gath, 0, [[64, 128], [1, 8], [8, 8]]),
        AX.X, ALU.add), waits={kb.VE: cv1})
    carry_ready = {kb.VE: cv2}

    defer_x2tm = []
    for h in range(2):
        g2 = B_p2.begin_write()
        g3 = B_p3.begin_write()
        g4 = B_p4.begin_write()
        mo_rg = []
        for g in range(G):
            lvm = kb.load(pslice(pool2, g), hslab(mo_hbm, g, h),
                          waits=wmerge(B_mohbm.ready, g2) if g == 0 else None)
            mo_rg.append(lvm)
            kb.load(pslice(pool3, g), hslab(c_hbm, g, h),
                    waits=wmerge(B_chbm.ready, g3) if g == 0 else None)
            kb.load(pslice(pool4, g), hslab(sk_hbm, g, h),
                    waits=wmerge(B_skhbm.ready, g4) if g == 0 else None)
        if h == 0:
            B_chbm.ready = {}
            B_skhbm.ready = {}
            B_mohbm.ready = {}
        B_p2.set_ready_d(kb.lw_all())
        B_p3.set_ready_d(kb.lw_all())
        B_p4.set_ready_d(kb.lw_all())

        ssd_rg = []
        g1 = B_p1.begin_write()
        for g in range(G):
            i = g % 2
            l1 = kb.load(stdb[i], hslab(st_hbm, g, h),
                         waits=wmerge(B_sthbm.ready, stdb_use[i],
                                      g1 if g == 0 else None))
            l2 = kb.load(udb[i], hslab(A_hbm, g, h),
                         waits=wmerge(B_Ahbm.ready, udb_use[i]))
            fv = kb.vec(lambda e, g=g, i=i: e.scalar_tensor_tensor(
                stdb[i], udb[i], carry[:, g:g + 1], stdb[i],
                ALU.mult, ALU.add),
                waits=wmerge(l1, l2, carry_ready))
            udb_use[i] = {kb.VE: fv}
            kb.vec(lambda e, g=g, i=i: e.tensor_tensor(
                stdb[i], stdb[i], pslice(pool3, g), ALU.mult),
                waits=B_p3.ready)
            sv = kb.vec(lambda e, g=g, i=i: e.tensor_tensor(
                pslice(pool1, g), stdb[i], pslice(pool4, g), ALU.add),
                waits=B_p4.ready)
            stdb_use[i] = {kb.VE: sv}
            ssd_rg.append({kb.VE: sv})
            B_p1.set_ready(kb.VE, sv)
        tap("ssd", pool1, h, B_p1)

        g3b = B_p3.begin_write()

        def gate_evac(j, _u, pap, mmw, g3b=g3b):
            w = wmerge(mmw, g3b if j == 0 else None)
            sv = kb.act(pslice(pool3, j), pap, AF.Sigmoid,
                        bias=bv("gate_b", j), waits=w)
            B_p3.set_ready(kb.SC, sv)
            return {kb.SC: sv}

        gate_rk = [ssd_rg[k] if k < 8 else mo_rg[k - 8] for k in range(16)]
        proj(gate_w, 16, 1024,
             lambda k, tt: pslice(pool1, k, tt) if k < 8
             else pslice(pool2, k - 8, tt),
             [B_p1, B_p2], gate_evac, ntt=2, rhs_ready_k=gate_rk,
             korder=[8, 9, 10, 11, 12, 13, 14, 15, 0, 1, 2, 3, 4, 5, 6, 7])

        for g in range(G):
            w = wmerge(B_p3.ready, B_tb.last_use,
                       B_p1.last_use if g == 0 else None)
            v0 = kb.vec(lambda e, g=g: e.tensor_tensor(
                tb, pslice(pool1, g), pslice(pool2, g), ALU.subtract),
                waits=w)
            B_tb.set_ready(kb.VE, v0)
            kb.vec(lambda e, g=g: e.tensor_tensor(
                tb, tb, pslice(pool3, g), ALU.mult))
            v = kb.vec(lambda e, g=g: e.tensor_tensor(
                pslice(pool1, g), tb, pslice(pool2, g), ALU.add))
            B_tb.used(kb.VE, v)
        B_p1.ready = {kb.VE: kb.VE.n}
        B_p2.used(kb.VE, kb.VE.n)
        B_p3.used(kb.VE, kb.VE.n)

        for g in range(G):
            i = g % 2
            lv = kb.load(stdb[i], hslab(x_fm, g, h), waits=stdb_use[i])
            v = kb.vec(lambda e, g=g, i=i: e.tensor_tensor(
                pslice(pool1, g), pslice(pool1, g), stdb[i], ALU.add),
                waits=wmerge(lv, B_p1.last_use if g == 0 else None))
            stdb_use[i] = {kb.VE: v}
        B_p1.ready = {kb.VE: kb.VE.n}
        tap("y", pool1, h, B_p1)

        ln_stats(lambda g, tt: pslice(pool1, g, tt), B_p1, TH,
                 pool4, B_p4.begin_write())
        B_p4.set_ready(kb.MM, kb.MM.n)
        p2g = B_p2.begin_write()
        yn_rg = []
        for g in range(G):
            w = wmerge(B_mb.ready, B_rb.ready, p2g if g == 0 else None)
            kb.vec(lambda e, g=g: e.tensor_tensor(
                pslice(pool2, g), pslice(pool1, g), mbt[:], ALU.subtract),
                waits=w)
            kb.vec(lambda e, g=g: e.tensor_tensor(
                pslice(pool2, g), pslice(pool2, g), rbt[:], ALU.mult))
            vv = kb.vec(lambda e, g=g: e.tensor_scalar(
                pslice(pool2, g), pslice(pool2, g),
                bv("ffn_g", g), bv("ffn_bn", g), ALU.mult, ALU.add))
            yn_rg.append({kb.VE: vv})
        B_p2.ready = {kb.VE: kb.VE.n}
        B_mb.used(kb.VE, kb.VE.n)
        B_rb.used(kb.VE, kb.VE.n)

        for tt in range(2):
            h2_rg = [None] * 32
            p34g = wmerge(B_p3.begin_write(), B_p4.begin_write())

            def f1_evac(j0, j1, pap, mmw, p34g=p34g):
                sv = None
                for jj, off in ((j0, 0), (j1, 512)):
                    dst = (pool3[:, jj * 512:(jj + 1) * 512] if jj < 16
                           else pool4[:, (jj - 16) * 512:(jj - 15) * 512])
                    w = wmerge(mmw if off == 0 else None,
                               p34g if jj == 0 else None)
                    sv = kb.act(dst, pap[:, off:off + 512], AF.Gelu,
                                bias=bv("ffn1_b", jj), waits=w)
                    h2_rg[jj] = {kb.SC: sv}
                B_p3.set_ready(kb.SC, sv)
                B_p4.set_ready(kb.SC, sv)
                return {kb.SC: sv}

            proj(ffn1_w, 8, 4096, lambda k, _t, tt=tt: pslice(pool2, k, tt),
                 [B_p2], f1_evac, ntt=1, rhs_ready_k=yn_rg)

            def f2_evac(j0, j1, pap, mmw, tt=tt):
                vv = None
                for jj, off in ((j0, 0), (j1, 512)):
                    w = wmerge(mmw, B_p1.last_use) if off == 0 else None
                    vv = kb.vec(lambda e, jj=jj, off=off, tt=tt:
                                e.scalar_tensor_tensor(
                                    pslice(pool1, jj, tt),
                                    pap[:, off:off + 512], bv("ffn2_b", jj),
                                    pslice(pool1, jj, tt), ALU.add, ALU.add),
                                waits=w)
                return {kb.VE: vv}

            proj(ffn2_w, 32, 1024,
                 lambda k, _t: (pool3[:, k * 512:(k + 1) * 512] if k < 16
                                else pool4[:, (k - 16) * 512:(k - 15) * 512]),
                 [B_p3, B_p4], f2_evac, ntt=1, rhs_ready_k=h2_rg)
        B_p1.ready = {kb.VE: kb.VE.n}

        for g in range(G):
            i = g % 2
            lv = kb.load(stdb[i], hslab(x_fm, g, h), waits=stdb_use[i])
            v = kb.vec(lambda e, g=g, i=i: e.tensor_tensor(
                pslice(pool1, g), pslice(pool1, g), stdb[i], ALU.add),
                waits=wmerge(lv, B_p1.last_use if g == 0 else None))
            stdb_use[i] = {kb.VE: v}
        B_p1.ready = {kb.VE: kb.VE.n}
        tap("y3", pool1, h, B_p1)

        ln_stats(lambda g, tt: pslice(pool1, g, tt), B_p1, TH,
                 pool4, B_p4.begin_write())
        B_p4.set_ready(kb.MM, kb.MM.n)
        for g in range(G):
            w = wmerge(B_mb.ready, B_rb.ready,
                       B_p1.last_use if g == 0 else None)
            kb.vec(lambda e, g=g: e.tensor_tensor(
                pslice(pool1, g), pslice(pool1, g), mbt[:], ALU.subtract),
                waits=w)
            v = kb.vec(lambda e, g=g: e.tensor_tensor(
                pslice(pool1, g), pslice(pool1, g), rbt[:], ALU.mult))
        B_p1.ready = {kb.VE: kb.VE.n}
        B_mb.used(kb.VE, kb.VE.n)
        B_rb.used(kb.VE, kb.VE.n)
        tap("x2", pool1, h, B_p1)
        p4g2 = B_p4.begin_write()
        st_prev = [{} for _ in range(4)]
        def x2tm_transposes(h=h, p4g2=p4g2, st_prev=st_prev):
          for g in range(G):
            for tc in range(8):
                tsl = kb.slots[6 + ((g * 8 + tc) % 2)]
                w = wmerge(B_p1.ready if (g == 0 and tc == 0) else None,
                           tsl.last_evac, CONSTS)
                mv = kb.op("tensor", lambda e, g=g, tc=tc, tsl=tsl:
                           e.transpose(tsl.ap[:, 0:128],
                                       pool1[:, g * TH + tc * 128:
                                             g * TH + (tc + 1) * 128],
                                       ident[:]),
                           waits=w, inc=kb.MM)
                b = (g * 8 + tc) % 4
                cv = kb.scop(lambda e, tsl=tsl, b=b: e.copy(
                    pool4[:, b * 128:(b + 1) * 128], tsl.ap[:, 0:128]),
                    waits=wmerge({kb.MM: mv}, st_prev[b],
                                 p4g2 if (g == 0 and tc == 0) else None))
                tsl.last_evac = {kb.SC: cv}
                spv = kb.store(
                    bass.AP(x2tm.tensor,
                            x2tm.offset + (h * 1024 + tc * 128) * 1032
                            + g * 128,
                            [[1032, 128], [1, 128]]),
                    pool4[:, b * 128:(b + 1) * 128], waits={kb.SC: cv})
                st_prev[b] = spv
                B_p4.used_d(spv)
                B_x2tm.set_ready_d(spv)
          B_p4.set_ready(kb.SC, kb.SC.n)
          B_p1.used(kb.MM, kb.MM.n)
        if h == 0:
            x2tm_transposes()
        else:
            defer_x2tm.append(x2tm_transposes)

        lsl = kb.slots[4]
        for tc in range(8):
            for k in range(G):
                w = wmerge(B_p1.ready if (tc == 0 and k == 0) else None,
                           lsl.last_evac if k == 0 else None, CONSTS)
                lhsT = pool1[:, k * TH + tc * 128: k * TH + (tc + 1) * 128]
                kb.mm(lsl.ap[0:128, 0:8], lhsT, gwTt[:, k * 8:(k + 1) * 8],
                      start=(k == 0), stop=(k == G - 1), waits=w,
                      inc=(k == G - 1))
            j = h * 8 + tc
            sv = kb.scop(lambda e, j=j: e.copy(probs[:, j * 8:(j + 1) * 8],
                                               lsl.ap[0:128, 0:8]),
                         waits={kb.MM: kb.MM.n})
            lsl.last_evac = {kb.SC: sv}
        B_p1.used(kb.MM, kb.MM.n)
    B_probs.set_ready(kb.SC, kb.SC.n)

    pv = bass.AP(probs, 0, [[128, 128], [8, 16], [1, 8]])
    SCRSTR = 5280
    gt4 = lambda: bass.AP(scr, 0, [[SCRSTR, 128], [64, 16], [8, 8], [1, 8]])
    w44 = lambda: bass.AP(scr, 1024, [[SCRSTR, 128], [64, 16], [8, 8], [1, 8]])
    gt128 = scr[:, 0:128]
    w128 = scr[:, 1024:1152]
    cnt8 = scr[:, 1152:1160]

    v = kb.vec(lambda e: e.tensor_reduce(
        bass.AP(bis, 32, [[96, 128], [1, 16]]), pv, AX.X, ALU.max),
        waits=wmerge({kb.SC: kb.SC.n}, B_probs.ready))
    v = kb.vec(lambda e: e.tensor_tensor(
        pv, pv, bass.AP(bis, 32, [[96, 128], [1, 16], [0, 8]]),
        ALU.subtract))
    v = kb.scop(lambda e: e.activation(out=probs[:], in_=probs[:],
                                       func=AF.Exp),
                waits={kb.VE: kb.VE.n})
    v = kb.vec(lambda e: e.tensor_reduce(
        bass.AP(bis, 32, [[96, 128], [1, 16]]), pv, AX.X, ALU.add),
        waits={kb.SC: v})
    v = kb.vec(lambda e: e.reciprocal(bass.AP(bis, 48, [[96, 128], [1, 16]]),
                                      bass.AP(bis, 32, [[96, 128], [1, 16]])))
    v = kb.vec(lambda e: e.tensor_tensor(
        pv, pv, bass.AP(bis, 48, [[96, 128], [1, 16], [0, 8]]), ALU.mult))
    v = kb.vec(lambda e: e.tensor_tensor(
        gt4(),
        bass.AP(probs, 0, [[128, 128], [8, 16], [0, 8], [1, 8]]),
        bass.AP(probs, 0, [[128, 128], [8, 16], [1, 8], [0, 8]]),
        ALU.is_gt), waits=wmerge({kb.VE: v}, B_tb.last_use, stdb_use[0],
                                 stdb_use[1], udb_use[0], udb_use[1]))
    v = kb.vec(lambda e: e.tensor_tensor(
        w44(), gt4(),
        bass.AP(probs, 0, [[128, 128], [8, 16], [0, 8], [1, 8]]), ALU.mult))
    v = kb.vec(lambda e: e.tensor_reduce(
        bass.AP(s_sb, 0, [[128, 128], [8, 16], [1, 8]]), w44(), AX.X,
        ALU.add))
    v = kb.vec(lambda e: e.tensor_reduce(
        bass.AP(scr, 1024, [[SCRSTR, 128], [8, 16], [1, 8]]), gt4(), AX.X,
        ALU.add))
    v = kb.vec(lambda e: e.tensor_single_scalar(assn[:], s_sb[:], THR,
                                                ALU.is_lt))
    v = kb.vec(lambda e: e.tensor_single_scalar(gt128, probs[:], 1.0,
                                                ALU.subtract))
    v = kb.vec(lambda e: e.tensor_tensor(gt128, gt128, w128, ALU.subtract))
    kb.vec(lambda e: e.tensor_scalar(w128, assn[:], 1e9, -1e9,
                                     ALU.mult, ALU.add))
    kb.vec(lambda e: e.tensor_tensor(gt128, gt128, assn[:], ALU.mult))
    rv = kb.vec(lambda e: e.tensor_tensor(s_sb[:], gt128, w128, ALU.add))
    rtapv = {}
    if "sm_R" in small_taps:
        rtapv = kb.store(small_taps["sm_R"], s_sb[:], waits={kb.VE: rv})
    rsp = kb.store(rin_h.ap(), s_sb[:], waits={kb.VE: rv})
    ccv2 = kb.op("gpsimd", lambda e: e.collective_compute(
        "AllGather", ALU.bypass, replica_groups=[list(range(8))],
        ins=[rin_h.ap().opt()], outs=[rout_h.ap().opt()]),
        waits=wmerge(rsp, {kb.CC: 1}), inc=kb.CC)
    for fn_ in defer_x2tm:
        fn_()
    sbcg = B_sbc.begin_write()
    rlv = kb.gdma(sbc[:, 0:1024],
                  bass.AP(rout_h, 0, [[128, 128], [16384, 8], [1, 128]]),
                  waits=wmerge({kb.CC: ccv2}, sbcg))
    B_sbc.set_ready_d(rlv)

    kb.vec(lambda e: e.memset(bis[:, 24:32], -8.5), waits=rlv)
    kb.vec(lambda e: e.memset(bis[:, 32:40], 0.5))
    kb.vec(lambda e: e.memset(bis[:, 40:48], -2e9))
    kb.vec(lambda e: e.memset(bis[:, 48:56], -9e8))
    kb.vec(lambda e: e.tensor_copy(bis[:, 16:24], bis[:, 24:32]))

    Rall = lambda: bass.AP(sbc, 0, [[2048, 128], [1, 8], [128, 8], [8, 16]])
    gtb = lambda: bass.AP(scr, 0, [[SCRSTR, 128], [1, 8], [128, 8], [8, 16]])
    midb = lambda: bass.AP(bis, 16, [[96, 128], [1, 8], [0, 8], [0, 16]])
    bsl = kb.slots[6]

    def bis_round(first):
        v1 = kb.vec(lambda e: e.tensor_tensor(gtb(), Rall(), midb(),
                                              ALU.is_ge),
                    waits=wmerge({kb.VE: kb.VE.n}, B_sbc.ready))
        v2 = kb.vec(lambda e: e.tensor_reduce(
            bass.AP(scr, 0, [[SCRSTR, 128], [1, 8], [128, 8]]), gtb(),
            AX.X, ALU.add))
        v3 = kb.vec(lambda e: e.tensor_reduce(
            bass.AP(scr, 1152, [[SCRSTR, 128], [1, 8]]),
            bass.AP(scr, 0, [[SCRSTR, 128], [1, 8], [128, 8]]),
            AX.X, ALU.add))
        mv = kb.mm(bsl.ap[0:128, 0:8], ones128[:], cnt8, True, True,
                   waits=wmerge({kb.VE: v3}, bsl.last_evac), inc=True)
        v4 = kb.vec(lambda e: e.tensor_single_scalar(
            bis[:, 56:64], bsl.ap[0:128, 0:8], CAP, ALU.is_ge),
            waits={kb.MM: mv})
        bsl.last_evac = {kb.VE: v4}
        kb.vec(lambda e: e.tensor_scalar(bis[:, 64:72], bis[:, 56:64],
                                         -1.0, 1.0, ALU.mult, ALU.add))
        if first:
            kb.vec(lambda e: e.tensor_single_scalar(bis[:, 72:80],
                                                    bis[:, 56:64], -8.5,
                                                    ALU.mult))
            kb.vec(lambda e: e.tensor_single_scalar(bis[:, 80:88],
                                                    bis[:, 64:72], -2e9,
                                                    ALU.mult))
            kb.vec(lambda e: e.tensor_tensor(bis[:, 0:8], bis[:, 72:80],
                                             bis[:, 80:88], ALU.add))
            kb.vec(lambda e: e.tensor_single_scalar(bis[:, 72:80],
                                                    bis[:, 56:64], 0.5,
                                                    ALU.mult))
            kb.vec(lambda e: e.tensor_single_scalar(bis[:, 80:88],
                                                    bis[:, 64:72], -9e8,
                                                    ALU.mult))
            kb.vec(lambda e: e.tensor_tensor(bis[:, 8:16], bis[:, 72:80],
                                             bis[:, 80:88], ALU.add))
        else:
            kb.vec(lambda e: e.tensor_tensor(bis[:, 72:80], bis[:, 16:24],
                                             bis[:, 56:64], ALU.mult))
            kb.vec(lambda e: e.tensor_tensor(bis[:, 80:88], bis[:, 0:8],
                                             bis[:, 64:72], ALU.mult))
            kb.vec(lambda e: e.tensor_tensor(bis[:, 0:8], bis[:, 72:80],
                                             bis[:, 80:88], ALU.add))
            kb.vec(lambda e: e.tensor_tensor(bis[:, 72:80], bis[:, 8:16],
                                             bis[:, 56:64], ALU.mult))
            kb.vec(lambda e: e.tensor_tensor(bis[:, 80:88], bis[:, 16:24],
                                             bis[:, 64:72], ALU.mult))
            kb.vec(lambda e: e.tensor_tensor(bis[:, 8:16], bis[:, 72:80],
                                             bis[:, 80:88], ALU.add))
        kb.vec(lambda e: e.tensor_tensor(bis[:, 16:24], bis[:, 0:8],
                                         bis[:, 8:16], ALU.add))
        kb.vec(lambda e: e.tensor_single_scalar(bis[:, 16:24], bis[:, 16:24],
                                                0.5, ALU.mult))

    bis_round(True)
    for _ in range(41):
        bis_round(False)
    B_sbc.used(kb.VE, kb.VE.n)

    v = kb.vec(lambda e: e.tensor_tensor(
        bass.AP(scr, 0, [[SCRSTR, 128], [8, 16], [1, 8]]),
        bass.AP(s_sb, 0, [[128, 128], [8, 16], [1, 8]]),
        bass.AP(bis, 0, [[96, 128], [0, 16], [1, 8]]), ALU.is_ge),
        waits={kb.VE: kb.VE.n})
    v = kb.vec(lambda e: e.tensor_tensor(s_sb[:], probs[:], gt128,
                                         ALU.mult),
               waits=wmerge(rtapv, rsp))
    sv_done = kb.vec(lambda e: e.tensor_tensor(s_sb[:], s_sb[:], assn[:],
                                               ALU.mult))
    ssp = kb.store(bass.AP(x2tm.tensor, x2tm.offset + 1024,
                           [[1032, 128], [1032 * 128, 16], [1, 8]]),
                   s_sb[:], waits={kb.VE: sv_done})
    B_x2tm.set_ready_d(ssp)

    m8 = scr[0:8, 1024:3072]
    p8 = scr[0:8, 3072:5120]
    for j in range(16):
        tsl = kb.slots[6 + (j % 2)]
        w = wmerge({kb.VE: sv_done} if j == 0 else None, tsl.last_evac,
                   CONSTS)
        mv = kb.op("tensor", lambda e, j=j, tsl=tsl: e.transpose(
            tsl.ap[0:8, 0:128], s_sb[:, j * 8:(j + 1) * 8], ident[:]),
            waits=w, inc=kb.MM)
        cv = kb.scop(lambda e, j=j, tsl=tsl: e.copy(
            m8[:, j * 128:(j + 1) * 128], tsl.ap[0:8, 0:128]),
            waits={kb.MM: mv})
        tsl.last_evac = {kb.SC: cv}
    v = kb.vec(lambda e: e.tensor_single_scalar(m8, m8, 0.0, ALU.is_gt),
               waits={kb.SC: kb.SC.n})
    p8v = kb.vec(lambda e: e.tensor_tensor_scan(p8, m8, m8, 0.0,
                                                ALU.add, ALU.bypass))
    kb.vec(lambda e: e.tensor_copy(mbt[0:8, :], p8[:, 0:1024]),
           waits=wmerge(B_mb.begin_write(), {kb.VE: p8v}))
    p8v = kb.vec(lambda e: e.tensor_copy(rbt[0:8, :], p8[:, 1024:2048]),
                 waits=B_rb.begin_write())
    pos_ready = {kb.VE: p8v}

    zv2 = kb.vec(lambda e: e.memset(tb, 0.0), waits=B_tb.last_use)
    B_tb.set_ready(kb.VE, zv2)
    for tc in range(16):
        sp = kb.store(bass.AP(ymtm.tensor, ymtm.offset + tc * 128 * D,
                              [[D, 128], [1, D]]), tb, waits={kb.VE: zv2})
        B_tb.used_d(sp)
        B_ymtm.set_ready_d(sp)
    sp = kb.store(bass.AP(ymtm.tensor, ymtm.offset + T * D, [[D, 1], [1, D]]),
                  tb[0:1, :], waits={kb.VE: zv2})
    B_tb.used_d(sp)
    B_ymtm.set_ready_d(sp)
    ymtm_zero = dict(B_ymtm.ready)
    sp = kb.store(bass.AP(x2tm.tensor, x2tm.offset + T * 1032,
                          [[1032, 1], [1, 1024]]), tb[0:1, :],
                  waits={kb.VE: zv2})
    B_tb.used_d(sp)
    B_x2tm.set_ready_d(sp)
    sp = kb.store(bass.AP(x2tm.tensor, x2tm.offset + T * 1032 + 1024,
                          [[8, 1], [1, 8]]), tb[0:1, 0:8], waits={kb.VE: zv2})
    B_tb.used_d(sp)
    B_x2tm.set_ready_d(sp)

    if "sm_probs" in small_taps:
        kb.store(small_taps["sm_probs"], probs[:], waits={kb.VE: kb.VE.n})
    if "sm_s" in small_taps:
        kb.store(small_taps["sm_s"], s_sb[:], waits={kb.VE: kb.VE.n})
    if "sm_bis" in small_taps:
        kb.store(small_taps["sm_bis"], bis[:], waits={kb.VE: kb.VE.n})

    B_p4A, B_p4B = Buf(), Buf()
    scrS = [Buf() for _ in range(4)]
    for b_ in scrS:
        b_.used(kb.VE, kb.VE.n)
        b_.used(kb.MM, kb.MM.n)
    w1bufs = {}
    w2bufs = {}
    tokr = {}
    xgb = {}

    def plan_w1_loads(ee):
        w1b = Buf()
        g1 = B_p1.begin_write()
        g2 = B_p2.begin_write()
        for k in range(G):
            pool = pool1 if k < 4 else pool2
            w = (g1 if k == 0 else (g2 if k == 4 else None))
            kb.load(pool[:, (k % 4) * 1024:(k % 4 + 1) * 1024],
                    bass.AP(w1T.tensor, w1T.offset + ee * D * F + k * 128 * F,
                            [[F, 128], [1, F]]), waits=w)
        w1b.set_ready_d(kb.lw_all())
        w1bufs[ee] = w1b

    def plan_w2_loads(ee):
        w2b = Buf()
        g3 = B_p3.begin_write()
        for k in range(G):
            kb.load(pool3[:, k * 1024:(k + 1) * 1024],
                    bass.AP(w2T.tensor, w2T.offset + ee * F * D + k * 128 * D,
                            [[D, 128], [1, D]]), waits=g3 if k == 0 else None)
        w2b.set_ready_d(kb.lw_all())
        w2bufs[ee] = w2b

    def plan_lists(ee):
        sgate = B_sbc.begin_write()
        for tt2 in range(4):
            tsl = kb.slots[6 + (tt2 % 2)]
            w = wmerge(pos_ready if tt2 == 0 else None, tsl.last_evac,
                       CONSTS)
            psrc = mbt if tt2 < 2 else rbt
            mv = kb.mm(tsl.ap, sel8t[0:8, ee * 128:(ee + 1) * 128],
                       psrc[0:8, (tt2 % 2) * 512:(tt2 % 2 + 1) * 512],
                       True, True, waits=w, inc=True)
            cv = kb.scop(lambda e, tt2=tt2, tsl=tsl: e.copy(
                sbc[:, tt2 * 512:(tt2 + 1) * 512], tsl.ap),
                waits=wmerge({kb.MM: mv}, sgate if tt2 == 0 else None))
            tsl.last_evac = {kb.SC: cv}
        posb_ready = {kb.SC: kb.SC.n}
        cmp = tb
        for sg in range(4):
            w = wmerge(posb_ready if sg == 0 else None, B_tb.last_use)
            v = kb.vec(lambda e, sg=sg: e.tensor_scalar(
                scr[0:128, 0:1024], sbc[:, 0:1024], misct[:, sg:sg + 1],
                None, ALU.is_le, ALU.bypass), waits=w)
            B_tb.set_ready(kb.VE, v)
            v = kb.vec(lambda e, sg=sg, ee=ee: e.tensor_reduce(
                tokcf[:, ee * 4 + sg:ee * 4 + sg + 1], scr[0:128, 0:1024],
                AX.X, ALU.add))
            v = kb.vec(lambda e, sg=sg: e.tensor_scalar(
                scr[0:128, 0:1024], sbc[:, 1024:2048], misct[:, sg:sg + 1],
                None, ALU.is_le, ALU.bypass))
            v = kb.vec(lambda e, sg=sg, ee=ee: e.tensor_reduce(
                bis[:, 8:9], scr[0:128, 0:1024], AX.X, ALU.add))
            v = kb.vec(lambda e, sg=sg, ee=ee: e.tensor_tensor(
                tokcf[:, ee * 4 + sg:ee * 4 + sg + 1],
                tokcf[:, ee * 4 + sg:ee * 4 + sg + 1],
                bis[:, 8:9], ALU.add))
            B_tb.used(kb.VE, v)
        vtok = kb.vec(lambda e, ee=ee: e.tensor_copy(
            tokci[:, ee * 4:(ee + 1) * 4], tokcf[:, ee * 4:(ee + 1) * 4]))
        tokr[ee] = {kb.VE: vtok}
        B_sbc.used(kb.VE, vtok)

    def plan_gathers(ee, first):
        b = Buf()
        for sg in range(4):
            w = wmerge(B_x2tm.ready if first and sg == 0 else None,
                       tokr[ee] if sg == 0 else None,
                       scrS[sg].begin_write())
            gv = kb.store(scr[:, 1024 + sg * 1032:1024 + (sg + 1) * 1032],
                          misct[:, 0:1], waits=w)
            wd, fn, inc, add = kb.ops["gpsimd"].pop()
            idxap = tokci[:, ee * 4 + sg:ee * 4 + sg + 1]
            fn2 = (lambda e, sg=sg, idxap=idxap: e.indirect_dma_start(
                scr[:, 1024 + sg * 1032:1024 + (sg + 1) * 1032], None,
                x2tm, IndirectOffsetOnAxis(idxap, 0)))
            kb.ops["gpsimd"].append((wd, fn2, inc, add))
            scrS[sg].set_ready_d(gv)
        xgb[ee] = b

    plan_w1_loads(0)
    plan_w2_loads(0)
    plan_lists(0)
    plan_gathers(0, True)
    for ee in range(NE):
        w1b, w2b = w1bufs[ee], w2bufs[ee]
        XO = 0 if ee % 2 == 0 else 4096
        HO = 4096 - XO
        B_xf = B_p4A if ee % 2 == 0 else B_p4B
        B_h = B_p4B if ee % 2 == 0 else B_p4A
        p4lo_gate = B_xf.begin_write()
        p4lo_g = [None] * G
        for g in range(G):
            for sg in range(4):
                tsl = kb.slots[(g * 4 + sg) % 4]
                w = wmerge(scrS[sg].ready if g == 0 else None,
                           p4lo_gate if (sg == 0 and g == 0) else None,
                           tsl.last_evac)
                mv = kb.op("tensor", lambda e, sg=sg, g=g, tsl=tsl:
                           e.transpose(tsl.ap[:, 0:128],
                                       scr[:, 1024 + sg * 1032 + g * 128:
                                           1024 + sg * 1032 + (g + 1) * 128],
                                       ident[:]), waits=w, inc=kb.MM)
                cv = kb.scop(lambda e, sg=sg, g=g, tsl=tsl, XO=XO: e.copy(
                    pool4[:, XO + g * 512 + sg * 128:
                          XO + g * 512 + (sg + 1) * 128],
                    tsl.ap[:, 0:128]), waits={kb.MM: mv})
                tsl.last_evac = {kb.SC: cv}
                scrS[sg].used(kb.MM, mv)
            p4lo_g[g] = {kb.SC: kb.SC.n}
        B_xf.set_ready(kb.SC, kb.SC.n)
        srow = sbc[0:1, 1536:2048]
        sbc_gate = B_sbc.begin_write()
        for sg in range(4):
            tsl = kb.slots[6 + (sg % 2)]
            w = wmerge(scrS[sg].ready, tsl.last_evac,
                       sbc_gate if sg == 0 else None)
            mv = kb.op("tensor", lambda e, sg=sg, tsl=tsl, ee=ee:
                       e.transpose(tsl.ap[0:1, 0:128],
                                   scr[:, 1024 + sg * 1032 + 1024 + ee:
                                       1024 + sg * 1032 + 1024 + ee + 1],
                                   ident[:]), waits=w, inc=kb.MM)
            cv = kb.scop(lambda e, sg=sg, tsl=tsl: e.copy(
                srow[0:1, sg * 128:(sg + 1) * 128], tsl.ap[0:1, 0:128]),
                waits={kb.MM: mv})
            tsl.last_evac = {kb.SC: cv}
            scrS[sg].used(kb.MM, mv)
        bsl2 = kb.slots[6]
        mv = kb.mm(bsl2.ap, onesk[:], srow, True, True,
                   waits=wmerge({kb.SC: kb.SC.n}, bsl2.last_evac), inc=True)
        scv0 = kb.scop(lambda e, bsl2=bsl2: e.copy(sbc[:, 0:512], bsl2.ap),
                       waits={kb.MM: mv})
        bsl2.last_evac = {kb.SC: scv0}
        B_sbc.set_ready(kb.SC, scv0)

        hsl = [kb.slots[0], kb.slots[1]]
        p4hi_gate = B_h.begin_write()
        gelu_done = []
        for jf in range(8):
            sl = hsl[jf % 2]
            mv = None
            for k in range(G):
                pool = pool1 if k < 4 else pool2
                w = wmerge(w1b.ready if (jf == 0 and k == 0) else None,
                           p4lo_g[k] if jf == 0 else None,
                           sl.last_evac if k == 0 else None)
                lhsT = pool[:, (k % 4) * 1024 + jf * 128:
                            (k % 4) * 1024 + (jf + 1) * 128]
                mv = kb.mm(sl.ap, lhsT,
                           pool4[:, XO + k * 512:XO + (k + 1) * 512],
                           start=(k == 0), stop=(k == G - 1), waits=w,
                           inc=(k == G - 1))
                B_xf.used(kb.MM, mv)
            gvv = kb.act(pool4[:, HO + jf * 512:HO + (jf + 1) * 512],
                         sl.ap, AF.Gelu,
                         waits=wmerge({kb.MM: mv},
                                      p4hi_gate if jf == 0 else None))
            gelu_done.append(gvv)
            sl.last_evac = {kb.SC: gvv}
        hready = 0
        for jf in range(8):
            scv = kb.vec(lambda e, jf=jf, HO=HO: e.tensor_tensor(
                pool4[:, HO + jf * 512:HO + (jf + 1) * 512],
                pool4[:, HO + jf * 512:HO + (jf + 1) * 512],
                sbc[:, 0:512], ALU.mult),
                waits=wmerge({kb.SC: gelu_done[jf]},
                             B_sbc.ready if jf == 0 else None))
            hready = scv
            kb.slots[jf % 2].last_evac = wmerge(
                kb.slots[jf % 2].last_evac, {kb.VE: scv})
        B_sbc.used(kb.VE, hready)

        B_p1.used(kb.MM, kb.MM.n)
        B_p2.used(kb.MM, kb.MM.n)
        if ee + 1 < NE:
            plan_w1_loads(ee + 1)
            plan_lists(ee + 1)
            plan_gathers(ee + 1, False)

        ysl = [kb.slots[4], kb.slots[5]]
        for j2 in range(8):
            sl = ysl[j2 % 2]
            mv = None
            for k in range(G):
                w = wmerge(w2b.ready if (j2 == 0 and k == 0) else None,
                           {kb.VE: hready} if (j2 == 0 and k == 0) else None,
                           sl.last_evac if k == 0 else None)
                lhsT = pool3[:, k * 1024 + j2 * 128:
                             k * 1024 + (j2 + 1) * 128]
                mv = kb.mm(sl.ap, lhsT,
                           pool4[:, HO + k * 512:HO + (k + 1) * 512],
                           start=(k == 0), stop=(k == G - 1), waits=w,
                           inc=(k == G - 1))
                B_h.used(kb.MM, mv)
            cv = kb.scop(lambda e, j2=j2, sl=sl: e.copy(tbv[j2 % 2], sl.ap),
                         waits=wmerge({kb.MM: mv}, B_tb.last_use))
            B_tb.set_ready(kb.SC, cv)
            sl.last_evac = {kb.SC: cv}
            for sg in range(4):
                tsl = kb.slots[2 + (sg % 2)]
                w = wmerge({kb.SC: cv}, tsl.last_evac,
                           B_xf.begin_write() if j2 == 0 and sg == 0
                           else None)
                mv2 = kb.op("tensor", lambda e, j2=j2, sg=sg, tsl=tsl:
                            e.transpose(tsl.ap[:, 0:128],
                                        tb[:, (j2 % 2) * 512 + sg * 128:
                                           (j2 % 2) * 512 + (sg + 1) * 128],
                                        ident[:]), waits=w, inc=kb.MM)
                cv2 = kb.scop(lambda e, j2=j2, sg=sg, tsl=tsl, XO=XO: e.copy(
                    pool4[:, XO + sg * 1024 + j2 * 128:
                          XO + sg * 1024 + (j2 + 1) * 128],
                    tsl.ap[:, 0:128]), waits={kb.MM: mv2})
                tsl.last_evac = {kb.SC: cv2}
                B_tb.used(kb.MM, mv2)
        for sg in range(4):
            w = wmerge({kb.SC: kb.SC.n},
                       ymtm_zero if ee == 0 and sg == 0 else None)
            sp = kb.store(pool4[:, XO + sg * 1024:XO + (sg + 1) * 1024],
                          misct[:, 0:1], waits=w, sem_i=3)
            wd, fn, inc, add = kb.ops["gpsimd"].pop()
            idxap = tokci[:, ee * 4 + sg:ee * 4 + sg + 1]
            fn2 = (lambda e, sg=sg, idxap=idxap, XO=XO: e.indirect_dma_start(
                ymtm, IndirectOffsetOnAxis(idxap, 0),
                pool4[:, XO + sg * 1024:XO + (sg + 1) * 1024], None,
                compute_op=ALU.add))
            kb.ops["gpsimd"].append((wd, fn2, inc, add))
            B_xf.used_d(sp)
            B_ymtm.set_ready_d(sp)
        B_p3.used(kb.MM, kb.MM.n)
        if ee + 1 < NE:
            plan_w2_loads(ee + 1)

    if "sm_ymoe" in small_taps:
        for tc in range(16):
            kb.store(bass.AP(small_taps["sm_ymoe"].tensor,
                             small_taps["sm_ymoe"].offset + tc * 128 * D,
                             [[D, 128], [1, D]]),
                     misct[:, 0:1], waits=kb.sp_all())
            wd, fn, inc, add = kb.ops["gpsimd"].pop()
            fn2 = (lambda e, tc=tc: e.dma_start(
                out=bass.AP(small_taps["sm_ymoe"].tensor,
                            small_taps["sm_ymoe"].offset + tc * 128 * D,
                            [[D, 128], [1, D]]),
                in_=bass.AP(ymtm.tensor, ymtm.offset + tc * 128 * D,
                            [[D, 128], [1, D]])))
            kb.ops["gpsimd"].append((wd, fn2, inc, add))

    g4f = wmerge(B_p4.begin_write(), B_p4A.begin_write(),
                 B_p4B.begin_write())
    ch_use = [{}, {}]
    sq_use = [{}, {}]
    for tc in range(16):
        q = tc % 2
        x2o, ymo = q * 2048, q * 2048 + 1024
        w = wmerge(g4f if tc == 0 else None, ch_use[q],
                   B_x2tm.ready if tc == 0 else None)
        l1 = kb.load(pool4[:, x2o:x2o + 1024],
                     bass.AP(x2tm.tensor, x2tm.offset + tc * 128 * 1032,
                             [[1032, 128], [1, 1024]]), waits=w)
        l2 = kb.load(pool4[:, ymo:ymo + 1024],
                     bass.AP(ymtm.tensor, ymtm.offset + tc * 128 * D,
                             [[D, 128], [1, D]]),
                     waits=wmerge(B_ymtm.ready if tc == 0 else None))
        v = kb.vec(lambda e, x2o=x2o, ymo=ymo: e.tensor_tensor(
            pool4[:, x2o:x2o + 1024], pool4[:, x2o:x2o + 1024],
            pool4[:, ymo:ymo + 1024], ALU.add), waits=wmerge(l1, l2))
        c0 = q * 8
        sqo = 1024 + q * 1024
        s1 = kb.op("scalar", lambda e, x2o=x2o, c0=c0, sqo=sqo:
                   e.activation(out=scr[:, sqo:sqo + 1024],
                                in_=pool4[:, x2o:x2o + 1024], func=AF.Square,
                                accum_out=bis[:, c0 + 1:c0 + 2]),
                   waits=wmerge({kb.VE: v}, sq_use[q],
                                {kb.SC: kb.SC.n}), inc=kb.SC)
        s2 = kb.op("scalar", lambda e, x2o=x2o, c0=c0, sqo=sqo:
                   e.activation(out=scr[:, sqo:sqo + 1024],
                                in_=pool4[:, x2o:x2o + 1024], func=AF.Copy,
                                accum_out=bis[:, c0:c0 + 1]),
                   waits={kb.SC: kb.SC.n}, inc=kb.SC)
        sq_use[q] = {kb.SC: s2}
        v = kb.vec(lambda e, c0=c0: e.tensor_single_scalar(
            bis[:, c0:c0 + 1], bis[:, c0:c0 + 1], 1.0 / D, ALU.mult),
            waits={kb.SC: s2})
        v = kb.vec(lambda e, c0=c0: e.tensor_single_scalar(
            bis[:, c0 + 1:c0 + 2], bis[:, c0 + 1:c0 + 2], 1.0 / D, ALU.mult))
        v = kb.vec(lambda e, c0=c0: e.tensor_tensor(
            bis[:, c0 + 2:c0 + 3], bis[:, c0:c0 + 1], bis[:, c0:c0 + 1],
            ALU.mult))
        v = kb.vec(lambda e, c0=c0: e.tensor_tensor(
            bis[:, c0 + 1:c0 + 2], bis[:, c0 + 1:c0 + 2],
            bis[:, c0 + 2:c0 + 3], ALU.subtract))
        v5 = kb.scop(lambda e, c0=c0: e.activation(
            out=bis[:, c0 + 1:c0 + 2], in_=bis[:, c0 + 1:c0 + 2],
            func=AF.Sqrt, bias=epsc128[:]), waits={kb.VE: v})
        v6 = kb.vec(lambda e, c0=c0: e.reciprocal(
            bis[:, c0 + 2:c0 + 3], bis[:, c0 + 1:c0 + 2]),
            waits={kb.SC: v5})
        v = kb.vec(lambda e, x2o=x2o, c0=c0: e.tensor_scalar(
            pool4[:, x2o:x2o + 1024], pool4[:, x2o:x2o + 1024],
            bis[:, c0:c0 + 1], None, ALU.subtract, ALU.bypass))
        v = kb.vec(lambda e, x2o=x2o, c0=c0: e.tensor_scalar(
            pool4[:, x2o:x2o + 1024], pool4[:, x2o:x2o + 1024],
            bis[:, c0 + 2:c0 + 3], None, ALU.mult, ALU.bypass))
        sp = kb.store(bass.AP(out_tm.tensor, out_tm.offset + tc * 128 * D,
                              [[D, 128], [1, D]]), pool4[:, x2o:x2o + 1024],
                      waits={kb.VE: v})
        ch_use[q] = sp

    kb.op("gpsimd", lambda e: e.engine_nop(), waits=kb.sp_all())

    def emit(engine, name):
        seen = {}
        for w, fn, inc, add in kb.ops[name]:
            for s, v in w.items():
                if v and seen.get(s, 0) < v:
                    engine.wait_ge(s.h, v)
                    seen[s] = v
            instr = fn(engine)
            if inc is not None:
                instr.then_inc(inc.h, add)

    @block.sync
    def _(sync):
        emit(sync, "sync")

    @block.gpsimd
    def _(gpsimd):
        emit(gpsimd, "gpsimd")

    @block.scalar
    def _(scalar):
        emit(scalar, "scalar")

    @block.vector
    def _(vector):
        emit(vector, "vector")

    @block.tensor
    def _(tensor):
        emit(tensor, "tensor")

    es.close()
    return nc


def _pack_biases(kw):
    src = {"fr_b": "L0_attn_fr_b", "dp_b": "L0_attn_dp_b",
           "bp_b": "L0_attn_bp_b", "cp_b": "L0_attn_cp_b",
           "m1_b": "L0_attn_m1_b", "m2_b": "L0_attn_m2_b",
           "gate_b": "L0_attn_gate_b", "ffn1_b": "L0_attn_ffn1_b",
           "ffn2_b": "L0_attn_ffn2_b", "norm_g": "L0_attn_norm_g",
           "norm_b": "L0_attn_norm_b", "ffn_g": "L0_attn_ffn_g",
           "ffn_bn": "L0_attn_ffn_bn", "skip": "L0_attn_skip"}
    cols = []
    for name, _n in BIAS_ORDER:
        v = np.asarray(kw[src[name]], dtype=np.float32)
        cols.append(v.reshape(-1, 128).T)
    return np.ascontiguousarray(np.concatenate(cols, axis=1))


def kernel(**kw):
    x = np.asarray(kw["L0_attn_x"], dtype=np.float32)
    B, L, _ = x.shape
    biases = _pack_biases(kw)
    ident = np.eye(128, dtype=np.float32)
    misc = np.zeros((128, 20), dtype=np.float32)
    for tc in range(16):
        misc[:, tc] = tc * 128 + np.arange(128)
    misc[0:8, 16] = np.arange(8) * 513.0
    sel8 = np.zeros((8, 1024), dtype=np.float32)
    for ee in range(8):
        sel8[ee, ee * 128:(ee + 1) * 128] = 1.0
    w1T = np.ascontiguousarray(
        np.asarray(kw["L0_ffn_w1"], np.float32).transpose(0, 2, 1))
    w2T = np.ascontiguousarray(
        np.asarray(kw["L0_ffn_w2"], np.float32).transpose(0, 2, 1))
    gwT = np.ascontiguousarray(np.asarray(kw["L0_ffn_gate_w"], np.float32).T)

    shared = {
        "fr_w": np.ascontiguousarray(kw["L0_attn_fr_w"], np.float32),
        "dp_w": np.ascontiguousarray(kw["L0_attn_dp_w"], np.float32),
        "bp_w": np.ascontiguousarray(kw["L0_attn_bp_w"], np.float32),
        "cp_w": np.ascontiguousarray(kw["L0_attn_cp_w"], np.float32),
        "m1_w": np.ascontiguousarray(kw["L0_attn_m1_w"], np.float32),
        "m2_w": np.ascontiguousarray(kw["L0_attn_m2_w"], np.float32),
        "gate_w": np.ascontiguousarray(kw["L0_attn_gate_w"], np.float32),
        "ffn1_w": np.ascontiguousarray(kw["L0_attn_ffn1_w"], np.float32),
        "ffn2_w": np.ascontiguousarray(kw["L0_attn_ffn2_w"], np.float32),
        "gwT": gwT, "w1T": w1T, "w2T": w2T,
        "biases_in": biases, "ident_in": ident,
        "misc_in": misc, "sel8_in": sel8,
    }
    in_maps = []
    for c in range(8):
        b, h = c // 2, c % 2
        xs = np.ascontiguousarray(x[b, h * T:(h + 1) * T, :].T)
        sel = np.zeros((128, 8), dtype=np.float32)
        if h == 1:
            sel[:, c - 1] = 1.0
        m = dict(shared)
        m["x_fm"] = xs
        m["sel_in"] = sel
        in_maps.append(m)

    nc = build_nc()
    res = run_bass_kernel_spmd(nc, in_maps, core_ids=list(range(8)))
    out = np.empty((B, L, D), dtype=np.float32)
    for c in range(8):
        b, h = c // 2, c % 2
        out[b, h * T:(h + 1) * T, :] = res.results[c]["out_tm"]
    return out


# version 14
# speedup vs baseline: 1.0516x; 1.0082x over previous
import numpy as np
import jax
jax.config.update('jax_compilation_cache_dir', '/root/.bass_jax_cache')
jax.config.update('jax_persistent_cache_min_compile_time_secs', 1.0)
jax.config.update('jax_persistent_cache_min_entry_size_bytes', 0)
import concourse.bass as bass
import concourse.mybir as mybir
from concourse.bass_utils import run_bass_kernel_spmd
from concourse.bass_isa import ReduceOp
from concourse import library_config
from concourse.bass_core_rust import IndirectOffsetOnAxis

F32 = mybir.dt.float32
ALU = mybir.AluOpType
AF = mybir.ActivationFunctionType
AX = mybir.AxisListType

D = 1024
G = 8
T = 2048
TH = 1024
NE = 8
F = 1024
CAP = 2048.0
THR = 0.9
EPS = 1e-5
ENGS = ("sync", "gpsimd", "scalar", "vector", "tensor")

BIAS_ORDER = [("fr_b", 8), ("dp_b", 8), ("bp_b", 8), ("cp_b", 8),
              ("m1_b", 16), ("m2_b", 8), ("gate_b", 8), ("ffn1_b", 32),
              ("ffn2_b", 8), ("norm_g", 8), ("norm_b", 8), ("ffn_g", 8),
              ("ffn_bn", 8), ("skip", 8)]
NBIAS = sum(n for _, n in BIAS_ORDER)


class Sem:
    __slots__ = ("name", "n", "h")
    def __init__(self, name):
        self.name, self.n, self.h = name, 0, None


def wmerge(*dicts):
    out = {}
    for d in dicts:
        if not d:
            continue
        for s, v in d.items():
            if v and out.get(s, 0) < v:
                out[s] = v
    return out


class Buf:
    __slots__ = ("ready", "last_use")
    def __init__(self):
        self.ready = {}
        self.last_use = {}

    def set_ready(self, sem, val):
        if val and self.ready.get(sem, 0) < val:
            self.ready[sem] = val

    def set_ready_d(self, d):
        self.ready = wmerge(self.ready, d)

    def used(self, sem, val):
        if val and self.last_use.get(sem, 0) < val:
            self.last_use[sem] = val

    def used_d(self, d):
        self.last_use = wmerge(self.last_use, d)

    def begin_write(self):
        w = wmerge(self.ready, self.last_use)
        self.ready = {}
        self.last_use = {}
        return w


class Slot:
    __slots__ = ("ap", "last_evac", "mm_done")
    def __init__(self, ap):
        self.ap = ap
        self.last_evac = {}
        self.mm_done = 0


class KB:
    def __init__(self, nc):
        self.nc = nc
        self.ops = {e: [] for e in ENGS}
        self.LWs = [Sem("LW%d" % i) for i in range(6)]
        self.SPs = [Sem("SP%d" % i) for i in range(4)]
        self.MM = Sem("MM")
        self.GP = Sem("GP")
        self.SC = Sem("SC")
        self.VE = Sem("VE")
        self.CC = Sem("CC")
        self.sems = self.LWs + self.SPs + [self.MM, self.GP, self.SC,
                                           self.VE, self.CC]
        self.lw_rot = 0
        self.sp_rot = 0
        self.slots = []
        self.wring_lastmm = [{}, {}]
        self.wring_ctr = 0

    def lw_all(self):
        return {s: s.n for s in self.LWs if s.n}

    def sp_all(self):
        return {s: s.n for s in self.SPs if s.n}

    def op(self, eng, fn, waits=None, inc=None, add=1):
        w = dict(waits) if waits else {}
        v = None
        if inc is not None:
            inc.n += add
            v = inc.n
        self.ops[eng].append((w, fn, inc, add))
        return v

    def load(self, out_ap, in_ap, waits=None):
        sem = self.LWs[self.lw_rot % len(self.LWs)]
        self.lw_rot += 1
        w = wmerge(waits, {sem: sem.n})
        v = self.op("sync", lambda e: e.dma_start(out=out_ap, in_=in_ap),
                    w, sem, 16)
        return {sem: v}

    def store(self, out_ap, in_ap, waits=None, accum=False, sem_i=None):
        if accum:
            fn = lambda e: e.dma_start(out=out_ap, in_=in_ap, accum_op=ALU.add)
        else:
            fn = lambda e: e.dma_start(out=out_ap, in_=in_ap)
        if sem_i is None:
            sem = self.SPs[self.sp_rot % len(self.SPs)]
            self.sp_rot += 1
        else:
            sem = self.SPs[sem_i]
        w = wmerge(waits, {sem: sem.n})
        v = self.op("gpsimd", fn, w, sem, 16)
        return {sem: v}

    def gdma(self, out_ap, in_ap, waits=None):
        return self.store(out_ap, in_ap, waits=waits)

    def mm(self, out, lhsT, rhs, start, stop, waits=None, inc=False):
        return self.op(
            "tensor",
            lambda e: e.matmul(out, lhsT, rhs, start=start, stop=stop),
            waits, self.MM if inc else None)

    def act(self, out, in_, func, bias=0.0, scale=1.0, waits=None):
        w = wmerge(waits, {self.SC: self.SC.n})
        return self.op(
            "scalar",
            lambda e: e.activation(out=out, in_=in_, func=func, bias=bias,
                                   scale=scale),
            w, self.SC)

    def scop(self, fn, waits=None):
        w = wmerge(waits, {self.SC: self.SC.n})
        return self.op("scalar", fn, w, self.SC)

    def vec(self, fn, waits=None):
        w = wmerge(waits, {self.VE: self.VE.n})
        return self.op("vector", fn, w, self.VE)


TEST_SLOTS4 = False


def build_nc(taps=()):
    nc = bass.Bass("TRN2", target_bir_lowering=False)
    kb = KB(nc)

    def dinp(name, shape):
        return nc.dram_tensor(name, shape, F32, kind="ExternalInput").ap()

    x_fm = dinp("x_fm", [D, T])
    fr_w = dinp("fr_w", [D, D]); dp_w = dinp("dp_w", [D, D])
    bp_w = dinp("bp_w", [D, D]); cp_w = dinp("cp_w", [D, D])
    m1_w = dinp("m1_w", [D, 2 * D]); m2_w = dinp("m2_w", [2 * D, D])
    gate_w = dinp("gate_w", [2 * D, D])
    ffn1_w = dinp("ffn1_w", [D, 4 * D]); ffn2_w = dinp("ffn2_w", [4 * D, D])
    gwT = dinp("gwT", [D, NE])
    w1T = dinp("w1T", [NE, D, F])
    w2T = dinp("w2T", [NE, F, D])
    biases_in = dinp("biases_in", [128, NBIAS])
    misc_in = dinp("misc_in", [128, 20])
    sel8_in = dinp("sel8_in", [8, 1024])
    ident_in = dinp("ident_in", [128, 128])
    sel_in = dinp("sel_in", [128, 8])

    out_tm = nc.dram_tensor("out_tm", [T, D], F32, kind="ExternalOutput").ap()
    tap_out = {}
    small_taps = {}
    for tname in taps:
        if tname.startswith("sm_"):
            shp = {"sm_probs": [128, 128], "sm_s": [128, 128],
                   "sm_R": [128, 128], "sm_bis": [128, 96],
                   "sm_ymoe": [T, D], "sm_tok": [8, 513],
                   "sm_cnt": [1, 8]}[tname]
            small_taps[tname] = nc.dram_tensor("tap_" + tname, shp, F32,
                                               kind="ExternalOutput").ap()
        else:
            tap_out[tname] = nc.dram_tensor("tap_" + tname, [D, T], F32,
                                            kind="ExternalOutput").ap()

    c_hbm = nc.dram_tensor("c_hbm", [D, T], F32).ap()
    sk_hbm = nc.dram_tensor("sk_hbm", [D, T], F32).ap()
    st_hbm = nc.dram_tensor("st_hbm", [D, T], F32).ap()
    A_hbm = nc.dram_tensor("A_hbm", [D, T], F32).ap()
    mo_hbm = nc.dram_tensor("mo_hbm", [D, T], F32).ap()
    x2tm = nc.dram_tensor("x2tm", [T + 4, 1032], F32).ap()
    ymtm = nc.dram_tensor("ymtm", [T + 4, D], F32).ap()
    gin_h = nc.dram_tensor("gin_h", [128, 8], F32)
    gout_h = nc.dram_tensor("gout_h", [8 * 128, 8], F32)
    rin_h = nc.dram_tensor("rin_h", [128, 128], F32)
    rout_h = nc.dram_tensor("rout_h", [8 * 128, 128], F32)

    from contextlib import ExitStack
    es = ExitStack()
    e = es.enter_context

    pool1 = e(nc.sbuf_tensor("pool1", [128, 8192], F32))
    pool2 = e(nc.sbuf_tensor("pool2", [128, 8192], F32))
    pool3 = e(nc.sbuf_tensor("pool3", [128, 8192], F32))
    pool4 = e(nc.sbuf_tensor("pool4", [128, 8192], F32))
    wring = e(nc.sbuf_tensor("wring", [128, 4096], F32))
    scr = e(nc.sbuf_tensor("scr", [128, 5280], F32))
    mbt = e(nc.sbuf_tensor("mbt", [128, 1024], F32))
    rbt = e(nc.sbuf_tensor("rbt", [128, 1024], F32))
    stats_m = e(nc.sbuf_tensor("stats_m", [1, 1024], F32))
    stats_r = e(nc.sbuf_tensor("stats_r", [1, 1024], F32))
    stats_t = e(nc.sbuf_tensor("stats_t", [1, 2048], F32))
    sbias = e(nc.sbuf_tensor("sbias", [128, NBIAS], F32))
    ident = e(nc.sbuf_tensor("ident", [128, 128], F32))
    selt = e(nc.sbuf_tensor("selt", [128, 8], F32))
    gwTt = e(nc.sbuf_tensor("gwTt", [128, 64], F32))
    onesc = e(nc.sbuf_tensor("onesc", [128, 1], F32))
    onesk = e(nc.sbuf_tensor("onesk", [1, 128], F32))
    ones128 = e(nc.sbuf_tensor("ones128", [128, 128], F32))
    Sinit = e(nc.sbuf_tensor("Sinit", [128, 8], F32))
    Ainit = e(nc.sbuf_tensor("Ainit", [128, 8], F32))
    gath = e(nc.sbuf_tensor("gath", [128, 64], F32))
    carry = e(nc.sbuf_tensor("carry", [128, 8], F32))
    probs = e(nc.sbuf_tensor("probs", [128, 128], F32))
    assn = e(nc.sbuf_tensor("assn", [128, 128], F32))
    s_sb = e(nc.sbuf_tensor("s_sb", [128, 128], F32))
    sbc = e(nc.sbuf_tensor("sbc", [128, 2048], F32))
    bis = e(nc.sbuf_tensor("bis", [128, 96], F32))
    epsc = e(nc.sbuf_tensor("epsc", [1, 1], F32))
    epsc128 = e(nc.sbuf_tensor("epsc128", [128, 1], F32))
    ones11 = e(nc.sbuf_tensor("ones11", [1, 1], F32))
    misct = e(nc.sbuf_tensor("misct", [128, 20], F32))
    sel8t = e(nc.sbuf_tensor("sel8t", [8, 1024], F32))
    tokcf = e(nc.sbuf_tensor("tokcf", [128, 32], F32))
    tokci = e(nc.sbuf_tensor("tokci", [128, 32], mybir.dt.int32))
    neg9 = e(nc.sbuf_tensor("neg9", [128, 128], F32))

    PA = e(nc.psum_tensor("PA", [128, 2048], F32))
    PB = e(nc.psum_tensor("PB", [128, 2048], F32))

    for s in kb.sems:
        s.h = e(nc.semaphore(s.name))
    block = e(nc.Block())

    for i in range(4):
        kb.slots.append(Slot(PA[:, i * 512:(i + 1) * 512]))
    for i in range(4):
        kb.slots.append(Slot(PB[:, i * 512:(i + 1) * 512]))

    tb = scr[:, 0:1024]
    tbv = [scr[:, 0:512], scr[:, 512:1024]]
    stdb = [scr[:, 1024:2048], scr[:, 2048:3072]]
    udb = [scr[:, 3072:4096], scr[:, 4096:5120]]
    B_tb = Buf()
    stdb_use = [{}, {}]
    udb_use = [{}, {}]

    bofs = {}
    cur = 0
    for nme, n in BIAS_ORDER:
        bofs[nme] = cur
        cur += n
    def bv(nme, g):
        return sbias[:, bofs[nme] + g: bofs[nme] + g + 1]

    B_p1, B_p2, B_p3, B_p4 = Buf(), Buf(), Buf(), Buf()
    B_chbm, B_skhbm, B_sthbm, B_Ahbm = Buf(), Buf(), Buf(), Buf()
    B_mohbm, B_x2tm, B_ymtm = Buf(), Buf(), Buf()
    B_mb, B_rb, B_sbc, B_probs = Buf(), Buf(), Buf(), Buf()

    kb.load(sbias[:], biases_in)
    kb.load(ident[:], ident_in)
    kb.load(selt[:], sel_in)
    kb.load(gwTt[:], gwT.rearrange("(k p) e -> p k e", p=128))
    kb.load(misct[:], misc_in)
    kb.load(sel8t[:], sel8_in)
    const_lw = kb.lw_all()
    kb.vec(lambda e: e.memset(onesc[:], 1.0))
    kb.vec(lambda e: e.memset(onesk[:], 1.0))
    kb.vec(lambda e: e.memset(ones128[:], 1.0))
    kb.vec(lambda e: e.memset(Sinit[:], 0.0))
    kb.vec(lambda e: e.memset(Ainit[:], 1.0))
    kb.vec(lambda e: e.memset(neg9[:], -1e9))
    kb.vec(lambda e: e.memset(epsc[:], EPS))
    kb.vec(lambda e: e.memset(epsc128[:], EPS))
    kb.vec(lambda e: e.memset(ones11[:], 1.0))
    kb.vec(lambda e: e.memset(bis[:], 0.0))
    const_ve = kb.VE.n
    CONSTS = wmerge({kb.VE: const_ve}, const_lw)

    def proj(w_hbm, Kc, Nout, rhs_fn, rhs_bufs, evac_pair_fn, ntt,
             kpass=4, first_waits=None, rhs_ready_k=None, korder=None,
             slots4=False):
        nkp = (Kc + kpass - 1) // kpass
        rhs_ready = wmerge(*(b.ready for b in rhs_bufs))
        ks_all = korder if korder is not None else list(range(Kc))
        passes = []
        for jb in range(Nout // 512):
            for kp in range(nkp):
                passes.append((jb, kp,
                               ks_all[kp * kpass:
                                      min((kp + 1) * kpass, Kc)]))
        planned = {}
        k_seen = set()

        def plan_load(idx):
            jb, kp, ks = passes[idx]
            setid = kb.wring_ctr % 2
            kb.wring_ctr += 1
            i = 0
            lwv = {}
            first_load = True
            while i < len(ks):
                npair = 2 if (i + 1 < len(ks)
                              and ks[i + 1] == ks[i] + 1) else 1
                k0 = ks[i]
                dst = wring[:, (setid * 4 + i) * 512:
                            (setid * 4 + i + npair) * 512]
                if npair == 2:
                    ap = bass.AP(w_hbm.tensor,
                                 w_hbm.offset + k0 * 128 * Nout + jb * 512,
                                 [[Nout, 128], [128 * Nout, 2], [1, 512]])
                else:
                    ap = bass.AP(w_hbm.tensor,
                                 w_hbm.offset + k0 * 128 * Nout + jb * 512,
                                 [[Nout, 128], [1, 512]])
                w = kb.wring_lastmm[setid] if first_load else None
                lwv = wmerge(lwv, kb.load(dst, ap, waits=w))
                first_load = False
                i += npair
            planned[idx] = (setid, lwv)

        plan_load(0)
        if len(passes) > 1:
            plan_load(1)
        first_mm_of_proj = True
        for idx, (jb, kp, ks) in enumerate(passes):
            setid, lwv = planned.pop(idx)
            first_of_pass = True
            lastv = None
            for j_local in range(4):
                for tt in range(ntt):
                    s = (j_local * 2 + tt) if ntt == 2 \
                        else (jb % 2) * 4 + j_local
                    slot = kb.slots[s]
                    for i, k in enumerate(ks):
                        w = {}
                        if kp == 0 and i == 0:
                            w = wmerge(w, slot.last_evac)
                        if first_of_pass:
                            w = wmerge(w, lwv)
                            if first_mm_of_proj:
                                w = wmerge(w,
                                           None if rhs_ready_k else rhs_ready,
                                           first_waits)
                        if rhs_ready_k is not None and k not in k_seen:
                            k_seen.add(k)
                            w = wmerge(w, rhs_ready_k[k])
                        start = (kp == 0 and i == 0)
                        stop = (kp == nkp - 1 and i == len(ks) - 1)
                        last_of_pass = (i == len(ks) - 1 and
                                        j_local == 3 and tt == ntt - 1)
                        lhsT = wring[:, (setid * 4 + i) * 512
                                     + j_local * 128:
                                     (setid * 4 + i) * 512
                                     + (j_local + 1) * 128]
                        v = kb.mm(slot.ap, lhsT, rhs_fn(k, tt), start,
                                  stop, waits=w, inc=(stop or last_of_pass))
                        first_of_pass = False
                        first_mm_of_proj = False
                        if v:
                            lastv = v
                        if stop:
                            slot.mm_done = v
            if lastv:
                kb.wring_lastmm[setid] = {kb.MM: lastv}
            if idx + 2 < len(passes):
                plan_load(idx + 2)
            if kp == nkp - 1:
                if ntt == 2:
                    for jl in range(4):
                        j = jb * 4 + jl
                        base = jl * 2
                        sl0, sl1 = kb.slots[base], kb.slots[base + 1]
                        pt = PA if base < 4 else PB
                        pap = pt[:, (base % 4) * 512:(base % 4) * 512 + 1024]
                        mmw = {kb.MM: max(sl0.mm_done, sl1.mm_done)}
                        done = evac_pair_fn(j, None, pap, mmw)
                        sl0.last_evac = done
                        sl1.last_evac = done
                else:
                    for jl in (0, 2):
                        base = jl if slots4 else (jb % 2) * 4 + jl
                        sl0, sl1 = kb.slots[base], kb.slots[base + 1]
                        pt = PA if base < 4 else PB
                        pap = pt[:, (base % 4) * 512:(base % 4) * 512 + 1024]
                        mmw = {kb.MM: max(sl0.mm_done, sl1.mm_done)}
                        done = evac_pair_fn(jb * 4 + jl, jb * 4 + jl + 1,
                                            pap, mmw)
                        sl0.last_evac = done
                        sl1.last_evac = done
        for b in rhs_bufs:
            b.used(kb.MM, kb.MM.n)

    def ln_stats(x_ap_fn, x_buf, th, sqp, sq_gate, x_ready_g=None):
        ntt = th // 512
        sqv = 0
        for tt in range(ntt):
            for g in range(G):
                idx = (g * ntt + tt) * 512
                w = wmerge(x_ready_g[g] if x_ready_g else x_buf.ready,
                           CONSTS,
                           sq_gate if (tt == 0 and g == 0) else None)
                sqv = kb.act(sqp[:, idx:idx + 512], x_ap_fn(g, tt),
                             AF.Square, waits=w)
        sx, sq = kb.slots[4], kb.slots[5]
        for tt in range(ntt):
            for g in range(G):
                idx = (g * ntt + tt) * 512
                w0 = wmerge(x_buf.ready, sx.last_evac if g == 0 else None,
                            CONSTS)
                kb.mm(sx.ap[0:1, :], onesc[:], x_ap_fn(g, tt),
                      start=(g == 0), stop=(g == G - 1), waits=w0, inc=True)
                w1 = wmerge({kb.SC: sqv} if (tt == 0 and g == 0) else None,
                            sq.last_evac if g == 0 else None)
                v = kb.mm(sq.ap[0:1, :], onesc[:], sqp[:, idx:idx + 512],
                          start=(g == 0), stop=(g == G - 1), waits=w1,
                          inc=True)
                sq.mm_done = v
            sxv = kb.MM.n - 1
            c = slice(tt * 512, (tt + 1) * 512)
            v1 = kb.scop(lambda e, c=c: e.mul(stats_m[0:1, c], sx.ap[0:1, :],
                                              1.0 / D), waits={kb.MM: sxv})
            v2 = kb.scop(lambda e, c=c: e.mul(stats_t[0:1, c], sq.ap[0:1, :],
                                              1.0 / D), waits={kb.MM: kb.MM.n})
            sx.last_evac = {kb.SC: v1}
            sq.last_evac = {kb.SC: v2}
            c2 = slice(1024 + c.start, 1024 + c.stop)
            kb.vec(lambda e, c=c, c2=c2: e.tensor_tensor(
                stats_t[0:1, c2], stats_m[0:1, c], stats_m[0:1, c], ALU.mult),
                waits={kb.SC: v2})
            v4 = kb.vec(lambda e, c=c, c2=c2: e.tensor_tensor(
                stats_t[0:1, c], stats_t[0:1, c], stats_t[0:1, c2],
                ALU.subtract))
            v5 = kb.scop(lambda e, c=c: e.activation(
                out=stats_t[0:1, c], in_=stats_t[0:1, c], func=AF.Sqrt,
                bias=epsc[0:1, :]), waits={kb.VE: v4})
            kb.vec(lambda e, c=c: e.reciprocal(stats_r[0:1, c],
                                               stats_t[0:1, c]),
                   waits={kb.SC: v5})
        x_buf.used(kb.MM, kb.MM.n)
        for row, srct, dstt, bufb in ((0, stats_m, mbt, B_mb),
                                      (1, stats_r, rbt, B_rb)):
            bw = bufb.begin_write()
            for tt in range(ntt):
                sl = kb.slots[6 + row]
                c = slice(tt * 512, (tt + 1) * 512)
                w = wmerge({kb.VE: kb.VE.n, kb.SC: kb.SC.n}, sl.last_evac,
                           bw, CONSTS)
                kb.mm(sl.ap, onesk[:], srct[0:1, c], True, True,
                      waits=w, inc=True)
                vv = kb.scop(lambda e, sl=sl, dstt=dstt, c=c:
                             e.copy(dstt[:, c], sl.ap),
                             waits={kb.MM: kb.MM.n})
                sl.last_evac = {kb.SC: vv}
                bufb.set_ready(kb.SC, vv)

    def pslice(pool, g, tt=None):
        if tt is None:
            return pool[:, g * TH:(g + 1) * TH]
        return pool[:, g * TH + tt * 512: g * TH + (tt + 1) * 512]

    def hslab(hb, g, h, tt=None):
        off = hb.offset + g * 128 * T + h * TH
        if tt is None:
            return bass.AP(hb.tensor, off, [[T, 128], [1, TH]])
        return bass.AP(hb.tensor, off + tt * 512, [[T, 128], [1, 512]])

    def tap(name, src_pool, h, buf):
        if name in tap_out:
            for g in range(G):
                v = kb.store(hslab(tap_out[name], g, h),
                             pslice(src_pool, g), waits=buf.ready)
                buf.used_d(v)

    def plan_ln1(h, xr, sqp, sqb):
        ln_stats(lambda g, tt: pslice(pool1, g, tt), B_p1, TH,
                 sqp, sqb.begin_write(), x_ready_g=xr)
        sqb.set_ready(kb.MM, kb.MM.n)
        xn_rg = []
        for g in range(G):
            w = wmerge(B_mb.ready, B_rb.ready, CONSTS, B_p1.last_use,
                       {kb.MM: kb.MM.n} if g == 0 else None)
            kb.vec(lambda e, g=g: e.tensor_tensor(
                pslice(pool1, g), pslice(pool1, g), mbt[:], ALU.subtract),
                waits=w)
            kb.vec(lambda e, g=g: e.tensor_tensor(
                pslice(pool1, g), pslice(pool1, g), rbt[:], ALU.mult))
            vv = kb.vec(lambda e, g=g: e.tensor_scalar(
                pslice(pool1, g), pslice(pool1, g),
                bv("norm_g", g), bv("norm_b", g), ALU.mult, ALU.add))
            xn_rg.append({kb.VE: vv})
        B_p1.ready = {kb.VE: kb.VE.n}
        B_mb.used(kb.VE, kb.VE.n)
        B_rb.used(kb.VE, kb.VE.n)
        tap("xn", pool1, h, B_p1)
        return xn_rg

    xg_ready = {}
    prefetched = [None]
    pre_ln1 = [None]
    for h in range(2):
        if h == 1 and prefetched[0] is not None:
            B_p1.ready = prefetched[0]
            xr = xg_ready[1]
        else:
            gate = B_p1.begin_write()
            xr = []
            for g in range(G):
                lv = kb.load(pslice(pool1, g), hslab(x_fm, g, h),
                             waits=gate if g == 0 else None)
                xr.append(lv)
            B_p1.set_ready_d(kb.lw_all())
            xg_ready[h] = xr

        if h == 1 and pre_ln1[0] is not None:
            xn_rg = pre_ln1[0]
        else:
            xn_rg = plan_ln1(h, xr, pool4, B_p4)

        p2gate = B_p2.begin_write()
        p3gate = B_p3.begin_write()

        low_rg, high_rg = [None] * G, [None] * G

        def fr_evac(j, _u, pap, mmw):
            w = wmerge(mmw, B_tb.last_use)
            sv = kb.act(tb, pap, AF.Sigmoid, bias=bv("fr_b", j), waits=w)
            B_tb.set_ready(kb.SC, sv)
            v1 = kb.vec(lambda e, j=j: e.tensor_tensor(
                pslice(pool2, j), pslice(pool1, j), tb, ALU.mult),
                waits=wmerge({kb.SC: sv}, p2gate if j == 0 else None))
            v2 = kb.vec(lambda e, j=j: e.tensor_tensor(
                pslice(pool3, j), pslice(pool1, j), pslice(pool2, j),
                ALU.subtract),
                waits=p3gate if j == 0 else None)
            B_tb.used(kb.VE, v1)
            B_p2.set_ready(kb.VE, v2)
            B_p3.set_ready(kb.VE, v2)
            low_rg[j] = {kb.VE: v1}
            high_rg[j] = {kb.VE: v2}
            return {kb.VE: v1}

        proj(fr_w, 8, 1024, lambda k, tt: pslice(pool1, k, tt), [B_p1],
             fr_evac, ntt=2, rhs_ready_k=xn_rg)
        tap("low", pool2, h, B_p2)
        tap("high", pool3, h, B_p3)

        def cp_evac(j, _u, pap, mmw):
            i = j % 2
            w = wmerge(mmw, stdb_use[i])
            sv = kb.act(stdb[i], pap, AF.Tanh, bias=bv("cp_b", j), waits=w)
            spv = kb.store(hslab(c_hbm, j, h), stdb[i], waits={kb.SC: sv})
            stdb_use[i] = spv
            B_chbm.set_ready_d(spv)
            return {kb.SC: sv}

        proj(cp_w, 8, 1024, lambda k, tt: pslice(pool3, k, tt), [B_p3],
             cp_evac, ntt=2, rhs_ready_k=high_rg)

        p3gate2 = B_p3.begin_write()

        def dp_evac(j, _u, pap, mmw):
            w = wmerge(mmw, p3gate2 if j == 0 else None)
            sv = kb.act(pslice(pool3, j), pap, AF.Sigmoid,
                        bias=bv("dp_b", j), waits=w)
            B_p3.set_ready(kb.SC, sv)
            return {kb.SC: sv}

        proj(dp_w, 8, 1024, lambda k, tt: pslice(pool2, k, tt), [B_p2],
             dp_evac, ntt=2, rhs_ready_k=low_rg)
        tap("a", pool3, h, B_p3)

        def bp_evac(g, _u, pap, mmw, h=h):
            i = g % 2
            w = wmerge(mmw, B_tb.last_use)
            sv = kb.act(tb, pap, AF.Tanh, bias=bv("bp_b", g), waits=w)
            B_tb.set_ready(kb.SC, sv)
            w2 = wmerge({kb.SC: sv}, udb_use[i], B_p3.ready)
            uv = kb.vec(lambda e, g=g, i=i: e.tensor_tensor(
                udb[i], tb, pslice(pool1, g), ALU.mult), waits=w2)
            B_tb.used(kb.VE, uv)
            w3 = wmerge({kb.VE: uv}, stdb_use[i])
            stv = kb.vec(lambda e, g=g, i=i: e.tensor_tensor_scan(
                stdb[i], pslice(pool3, g), udb[i], Sinit[:, g:g + 1],
                ALU.mult, ALU.add), waits=w3)
            kb.vec(lambda e, g=g, i=i: e.tensor_copy(
                Sinit[:, g:g + 1], stdb[i][:, TH - 1:TH]))
            spv = kb.store(hslab(st_hbm, g, h), stdb[i],
                           waits={kb.VE: kb.VE.n})
            stdb_use[i] = spv
            B_sthbm.set_ready_d(spv)
            zv = kb.vec(lambda e: e.memset(tb, 0.0),
                        waits=wmerge({kb.VE: stv}, B_tb.last_use))
            B_tb.set_ready(kb.VE, zv)
            av = kb.vec(lambda e, g=g, i=i: e.tensor_tensor_scan(
                udb[i], pslice(pool3, g), tb, Ainit[:, g:g + 1],
                ALU.mult, ALU.add), waits={kb.VE: zv})
            B_p3.used(kb.VE, av)
            B_tb.used(kb.VE, av)
            kb.vec(lambda e, g=g, i=i: e.tensor_copy(
                Ainit[:, g:g + 1], udb[i][:, TH - 1:TH]))
            spv2 = kb.store(hslab(A_hbm, g, h), udb[i],
                            waits={kb.VE: kb.VE.n})
            udb_use[i] = spv2
            B_Ahbm.set_ready_d(spv2)
            skv = kb.vec(lambda e, g=g: e.tensor_single_scalar(
                pslice(pool1, g), pslice(pool1, g), bv("skip", g), ALU.mult),
                waits=wmerge({kb.VE: uv}, B_p1.last_use))
            spv3 = kb.store(hslab(sk_hbm, g, h), pslice(pool1, g),
                            waits={kb.VE: skv})
            B_skhbm.set_ready_d(spv3)
            B_p1.used_d(spv3)
            return {kb.VE: av}

        proj(bp_w, 8, 1024, lambda k, tt: pslice(pool2, k, tt), [B_p2],
             bp_evac, ntt=2, rhs_ready_k=low_rg)

        if h == 0:
            gate = B_p1.begin_write()
            xg_ready[1] = []
            for g in range(G):
                lv = kb.load(pslice(pool1, g), hslab(x_fm, g, 1),
                             waits=gate if g == 0 else None)
                xg_ready[1].append(lv)
            B_p1.set_ready_d(kb.lw_all())
            prefetched[0] = dict(B_p1.ready)
            pre_ln1[0] = plan_ln1(1, xg_ready[1], pool3, B_p3)

        for tt in range(2):
            h1_rg = [None] * 16
            p4gate = B_p4.begin_write()

            def m1_evac(j0, j1, pap, mmw, p4gate=p4gate):
                sv = None
                for jj, off in ((j0, 0), (j1, 512)):
                    w = wmerge(mmw if off == 0 else None,
                               p4gate if jj == 0 else None)
                    sv = kb.act(pool4[:, jj * 512:(jj + 1) * 512],
                                pap[:, off:off + 512], AF.Gelu,
                                bias=bv("m1_b", jj), waits=w)
                    h1_rg[jj] = {kb.SC: sv}
                B_p4.set_ready(kb.SC, sv)
                return {kb.SC: sv}

            proj(m1_w, 8, 2048, lambda k, _t, tt=tt: pslice(pool2, k, tt),
                 [B_p2], m1_evac, ntt=1, rhs_ready_k=low_rg, slots4=TEST_SLOTS4)

            def m2_evac(j0, j1, pap, mmw, tt=tt, h=h):
                svs = []
                for jj, off in ((j0, 0), (j1, 512)):
                    i = jj % 2
                    w = wmerge(mmw if off == 0 else None, B_tb.last_use)
                    sv = kb.scop(lambda e, jj=jj, off=off: e.add(
                        tbv[jj % 2], pap[:, off:off + 512], bv("m2_b", jj)),
                        waits=w)
                    B_tb.set_ready(kb.SC, sv)
                    spv = kb.store(hslab(mo_hbm, jj, h, tt), tbv[jj % 2],
                                   waits={kb.SC: sv})
                    B_tb.used_d(spv)
                    svs.append(sv)
                B_mohbm.set_ready_d(kb.sp_all())
                return {kb.SC: max(svs)}

            proj(m2_w, 16, 1024,
                 lambda k, _t: pool4[:, k * 512:(k + 1) * 512], [B_p4],
                 m2_evac, ntt=1, rhs_ready_k=h1_rg, slots4=TEST_SLOTS4)

    gv = kb.store(gin_h.ap(), Sinit[:], waits={kb.VE: kb.VE.n})
    ccv = kb.op("gpsimd", lambda e: e.collective_compute(
        "AllGather", ALU.bypass, replica_groups=[list(range(8))],
        ins=[gin_h.ap().opt()], outs=[gout_h.ap().opt()]),
        waits=gv, inc=kb.CC)
    glv = kb.gdma(gath[:],
                  bass.AP(gout_h, 0, [[8, 128], [1024, 8], [1, 8]]),
                  waits={kb.CC: ccv})
    cv1 = kb.vec(lambda e: e.tensor_tensor(
        bass.AP(gath, 0, [[64, 128], [1, 8], [8, 8]]),
        bass.AP(gath, 0, [[64, 128], [1, 8], [8, 8]]),
        bass.AP(selt, 0, [[8, 128], [0, 8], [1, 8]]), ALU.mult),
        waits=wmerge(glv, CONSTS))
    cv2 = kb.vec(lambda e: e.tensor_reduce(
        carry[:], bass.AP(gath, 0, [[64, 128], [1, 8], [8, 8]]),
        AX.X, ALU.add), waits={kb.VE: cv1})
    carry_ready = {kb.VE: cv2}

    defer_x2tm = []
    for h in range(2):
        g2 = B_p2.begin_write()
        g3 = B_p3.begin_write()
        g4 = B_p4.begin_write()
        mo_rg = []
        for g in range(G):
            lvm = kb.load(pslice(pool2, g), hslab(mo_hbm, g, h),
                          waits=wmerge(B_mohbm.ready, g2) if g == 0 else None)
            mo_rg.append(lvm)
            kb.gdma(pslice(pool3, g), hslab(c_hbm, g, h),
                    waits=wmerge(B_chbm.ready, g3) if g == 0 else None)
            kb.gdma(pslice(pool4, g), hslab(sk_hbm, g, h),
                    waits=wmerge(B_skhbm.ready, g4) if g == 0 else None)
        if h == 0:
            B_chbm.ready = {}
            B_skhbm.ready = {}
            B_mohbm.ready = {}
        B_p2.set_ready_d(kb.lw_all())
        B_p3.set_ready_d(kb.sp_all())
        B_p4.set_ready_d(kb.sp_all())

        ssd_rg = []
        g1 = B_p1.begin_write()
        for g in range(G):
            i = g % 2
            l1 = kb.load(stdb[i], hslab(st_hbm, g, h),
                         waits=wmerge(B_sthbm.ready, stdb_use[i],
                                      g1 if g == 0 else None))
            l2 = kb.gdma(udb[i], hslab(A_hbm, g, h),
                         waits=wmerge(B_Ahbm.ready, udb_use[i]))
            fv = kb.vec(lambda e, g=g, i=i: e.scalar_tensor_tensor(
                stdb[i], udb[i], carry[:, g:g + 1], stdb[i],
                ALU.mult, ALU.add),
                waits=wmerge(l1, l2, carry_ready))
            udb_use[i] = {kb.VE: fv}
            kb.vec(lambda e, g=g, i=i: e.tensor_tensor(
                stdb[i], stdb[i], pslice(pool3, g), ALU.mult),
                waits=B_p3.ready)
            sv = kb.vec(lambda e, g=g, i=i: e.tensor_tensor(
                pslice(pool1, g), stdb[i], pslice(pool4, g), ALU.add),
                waits=B_p4.ready)
            stdb_use[i] = {kb.VE: sv}
            ssd_rg.append({kb.VE: sv})
            B_p1.set_ready(kb.VE, sv)
        tap("ssd", pool1, h, B_p1)

        g3b = B_p3.begin_write()

        def gate_evac(j, _u, pap, mmw, g3b=g3b):
            w = wmerge(mmw, g3b if j == 0 else None)
            sv = kb.act(pslice(pool3, j), pap, AF.Sigmoid,
                        bias=bv("gate_b", j), waits=w)
            B_p3.set_ready(kb.SC, sv)
            return {kb.SC: sv}

        gate_rk = [ssd_rg[k] if k < 8 else mo_rg[k - 8] for k in range(16)]
        proj(gate_w, 16, 1024,
             lambda k, tt: pslice(pool1, k, tt) if k < 8
             else pslice(pool2, k - 8, tt),
             [B_p1, B_p2], gate_evac, ntt=2, rhs_ready_k=gate_rk,
             korder=[8, 9, 10, 11, 12, 13, 14, 15, 0, 1, 2, 3, 4, 5, 6, 7])

        for g in range(G):
            w = wmerge(B_p3.ready, B_tb.last_use,
                       B_p1.last_use if g == 0 else None)
            v0 = kb.vec(lambda e, g=g: e.tensor_tensor(
                tb, pslice(pool1, g), pslice(pool2, g), ALU.subtract),
                waits=w)
            B_tb.set_ready(kb.VE, v0)
            kb.vec(lambda e, g=g: e.tensor_tensor(
                tb, tb, pslice(pool3, g), ALU.mult))
            v = kb.vec(lambda e, g=g: e.tensor_tensor(
                pslice(pool1, g), tb, pslice(pool2, g), ALU.add))
            B_tb.used(kb.VE, v)
        B_p1.ready = {kb.VE: kb.VE.n}
        B_p2.used(kb.VE, kb.VE.n)
        B_p3.used(kb.VE, kb.VE.n)

        for g in range(G):
            i = g % 2
            lv = kb.load(stdb[i], hslab(x_fm, g, h), waits=stdb_use[i])
            v = kb.vec(lambda e, g=g, i=i: e.tensor_tensor(
                pslice(pool1, g), pslice(pool1, g), stdb[i], ALU.add),
                waits=wmerge(lv, B_p1.last_use if g == 0 else None))
            stdb_use[i] = {kb.VE: v}
        B_p1.ready = {kb.VE: kb.VE.n}
        tap("y", pool1, h, B_p1)

        ln_stats(lambda g, tt: pslice(pool1, g, tt), B_p1, TH,
                 pool4, B_p4.begin_write())
        B_p4.set_ready(kb.MM, kb.MM.n)
        p2g = B_p2.begin_write()
        yn_rg = []
        for g in range(G):
            w = wmerge(B_mb.ready, B_rb.ready, p2g if g == 0 else None)
            kb.vec(lambda e, g=g: e.tensor_tensor(
                pslice(pool2, g), pslice(pool1, g), mbt[:], ALU.subtract),
                waits=w)
            kb.vec(lambda e, g=g: e.tensor_tensor(
                pslice(pool2, g), pslice(pool2, g), rbt[:], ALU.mult))
            vv = kb.vec(lambda e, g=g: e.tensor_scalar(
                pslice(pool2, g), pslice(pool2, g),
                bv("ffn_g", g), bv("ffn_bn", g), ALU.mult, ALU.add))
            yn_rg.append({kb.VE: vv})
        B_p2.ready = {kb.VE: kb.VE.n}
        B_mb.used(kb.VE, kb.VE.n)
        B_rb.used(kb.VE, kb.VE.n)

        for tt in range(2):
            h2_rg = [None] * 32
            p34g = wmerge(B_p3.begin_write(), B_p4.begin_write())

            def f1_evac(j0, j1, pap, mmw, p34g=p34g):
                sv = None
                for jj, off in ((j0, 0), (j1, 512)):
                    dst = (pool3[:, jj * 512:(jj + 1) * 512] if jj < 16
                           else pool4[:, (jj - 16) * 512:(jj - 15) * 512])
                    w = wmerge(mmw if off == 0 else None,
                               p34g if jj == 0 else None)
                    sv = kb.act(dst, pap[:, off:off + 512], AF.Gelu,
                                bias=bv("ffn1_b", jj), waits=w)
                    h2_rg[jj] = {kb.SC: sv}
                B_p3.set_ready(kb.SC, sv)
                B_p4.set_ready(kb.SC, sv)
                return {kb.SC: sv}

            proj(ffn1_w, 8, 4096, lambda k, _t, tt=tt: pslice(pool2, k, tt),
                 [B_p2], f1_evac, ntt=1, rhs_ready_k=yn_rg)

            def f2_evac(j0, j1, pap, mmw, tt=tt):
                vv = None
                for jj, off in ((j0, 0), (j1, 512)):
                    w = wmerge(mmw, B_p1.last_use) if off == 0 else None
                    vv = kb.vec(lambda e, jj=jj, off=off, tt=tt:
                                e.scalar_tensor_tensor(
                                    pslice(pool1, jj, tt),
                                    pap[:, off:off + 512], bv("ffn2_b", jj),
                                    pslice(pool1, jj, tt), ALU.add, ALU.add),
                                waits=w)
                return {kb.VE: vv}

            proj(ffn2_w, 32, 1024,
                 lambda k, _t: (pool3[:, k * 512:(k + 1) * 512] if k < 16
                                else pool4[:, (k - 16) * 512:(k - 15) * 512]),
                 [B_p3, B_p4], f2_evac, ntt=1, rhs_ready_k=h2_rg)
        B_p1.ready = {kb.VE: kb.VE.n}

        for g in range(G):
            i = g % 2
            lv = kb.load(stdb[i], hslab(x_fm, g, h), waits=stdb_use[i])
            v = kb.vec(lambda e, g=g, i=i: e.tensor_tensor(
                pslice(pool1, g), pslice(pool1, g), stdb[i], ALU.add),
                waits=wmerge(lv, B_p1.last_use if g == 0 else None))
            stdb_use[i] = {kb.VE: v}
        B_p1.ready = {kb.VE: kb.VE.n}
        tap("y3", pool1, h, B_p1)

        ln_stats(lambda g, tt: pslice(pool1, g, tt), B_p1, TH,
                 pool4, B_p4.begin_write())
        B_p4.set_ready(kb.MM, kb.MM.n)
        for g in range(G):
            w = wmerge(B_mb.ready, B_rb.ready,
                       B_p1.last_use if g == 0 else None)
            kb.vec(lambda e, g=g: e.tensor_tensor(
                pslice(pool1, g), pslice(pool1, g), mbt[:], ALU.subtract),
                waits=w)
            v = kb.vec(lambda e, g=g: e.tensor_tensor(
                pslice(pool1, g), pslice(pool1, g), rbt[:], ALU.mult))
        B_p1.ready = {kb.VE: kb.VE.n}
        B_mb.used(kb.VE, kb.VE.n)
        B_rb.used(kb.VE, kb.VE.n)
        tap("x2", pool1, h, B_p1)
        p4g2 = B_p4.begin_write()
        st_prev = [{} for _ in range(4)]
        def x2tm_transposes(h=h, p4g2=p4g2, st_prev=st_prev):
          for g in range(G):
            for tc in range(8):
                tsl = kb.slots[6 + ((g * 8 + tc) % 2)]
                w = wmerge(B_p1.ready if (g == 0 and tc == 0) else None,
                           tsl.last_evac, CONSTS)
                mv = kb.op("tensor", lambda e, g=g, tc=tc, tsl=tsl:
                           e.transpose(tsl.ap[:, 0:128],
                                       pool1[:, g * TH + tc * 128:
                                             g * TH + (tc + 1) * 128],
                                       ident[:]),
                           waits=w, inc=kb.MM)
                b = (g * 8 + tc) % 4
                cv = kb.scop(lambda e, tsl=tsl, b=b: e.copy(
                    pool4[:, b * 128:(b + 1) * 128], tsl.ap[:, 0:128]),
                    waits=wmerge({kb.MM: mv}, st_prev[b],
                                 p4g2 if (g == 0 and tc == 0) else None))
                tsl.last_evac = {kb.SC: cv}
                spv = kb.store(
                    bass.AP(x2tm.tensor,
                            x2tm.offset + (h * 1024 + tc * 128) * 1032
                            + g * 128,
                            [[1032, 128], [1, 128]]),
                    pool4[:, b * 128:(b + 1) * 128], waits={kb.SC: cv})
                st_prev[b] = spv
                B_p4.used_d(spv)
                B_x2tm.set_ready_d(spv)
          B_p4.set_ready(kb.SC, kb.SC.n)
          B_p1.used(kb.MM, kb.MM.n)
        if h == 0:
            x2tm_transposes()
        else:
            defer_x2tm.append(x2tm_transposes)

        lsl = kb.slots[4]
        for tc in range(8):
            for k in range(G):
                w = wmerge(B_p1.ready if (tc == 0 and k == 0) else None,
                           lsl.last_evac if k == 0 else None, CONSTS)
                lhsT = pool1[:, k * TH + tc * 128: k * TH + (tc + 1) * 128]
                kb.mm(lsl.ap[0:128, 0:8], lhsT, gwTt[:, k * 8:(k + 1) * 8],
                      start=(k == 0), stop=(k == G - 1), waits=w,
                      inc=(k == G - 1))
            j = h * 8 + tc
            sv = kb.scop(lambda e, j=j: e.copy(probs[:, j * 8:(j + 1) * 8],
                                               lsl.ap[0:128, 0:8]),
                         waits={kb.MM: kb.MM.n})
            lsl.last_evac = {kb.SC: sv}
        B_p1.used(kb.MM, kb.MM.n)
    B_probs.set_ready(kb.SC, kb.SC.n)

    pv = bass.AP(probs, 0, [[128, 128], [8, 16], [1, 8]])
    SCRSTR = 5280
    gt4 = lambda: bass.AP(scr, 0, [[SCRSTR, 128], [64, 16], [8, 8], [1, 8]])
    w44 = lambda: bass.AP(scr, 1024, [[SCRSTR, 128], [64, 16], [8, 8], [1, 8]])
    gt128 = scr[:, 0:128]
    w128 = scr[:, 1024:1152]
    cnt8 = scr[:, 1152:1160]

    v = kb.vec(lambda e: e.tensor_reduce(
        bass.AP(bis, 32, [[96, 128], [1, 16]]), pv, AX.X, ALU.max),
        waits=wmerge({kb.SC: kb.SC.n}, B_probs.ready))
    v = kb.vec(lambda e: e.tensor_tensor(
        pv, pv, bass.AP(bis, 32, [[96, 128], [1, 16], [0, 8]]),
        ALU.subtract))
    v = kb.scop(lambda e: e.activation(out=probs[:], in_=probs[:],
                                       func=AF.Exp),
                waits={kb.VE: kb.VE.n})
    v = kb.vec(lambda e: e.tensor_reduce(
        bass.AP(bis, 32, [[96, 128], [1, 16]]), pv, AX.X, ALU.add),
        waits={kb.SC: v})
    v = kb.vec(lambda e: e.reciprocal(bass.AP(bis, 48, [[96, 128], [1, 16]]),
                                      bass.AP(bis, 32, [[96, 128], [1, 16]])))
    v = kb.vec(lambda e: e.tensor_tensor(
        pv, pv, bass.AP(bis, 48, [[96, 128], [1, 16], [0, 8]]), ALU.mult))
    v = kb.vec(lambda e: e.tensor_tensor(
        gt4(),
        bass.AP(probs, 0, [[128, 128], [8, 16], [0, 8], [1, 8]]),
        bass.AP(probs, 0, [[128, 128], [8, 16], [1, 8], [0, 8]]),
        ALU.is_gt), waits=wmerge({kb.VE: v}, B_tb.last_use, stdb_use[0],
                                 stdb_use[1], udb_use[0], udb_use[1]))
    v = kb.vec(lambda e: e.tensor_tensor(
        w44(), gt4(),
        bass.AP(probs, 0, [[128, 128], [8, 16], [0, 8], [1, 8]]), ALU.mult))
    v = kb.vec(lambda e: e.tensor_reduce(
        bass.AP(s_sb, 0, [[128, 128], [8, 16], [1, 8]]), w44(), AX.X,
        ALU.add))
    v = kb.vec(lambda e: e.tensor_reduce(
        bass.AP(scr, 1024, [[SCRSTR, 128], [8, 16], [1, 8]]), gt4(), AX.X,
        ALU.add))
    v = kb.vec(lambda e: e.tensor_single_scalar(assn[:], s_sb[:], THR,
                                                ALU.is_lt))
    v = kb.vec(lambda e: e.tensor_single_scalar(gt128, probs[:], 1.0,
                                                ALU.subtract))
    v = kb.vec(lambda e: e.tensor_tensor(gt128, gt128, w128, ALU.subtract))
    kb.vec(lambda e: e.tensor_scalar(w128, assn[:], 1e9, -1e9,
                                     ALU.mult, ALU.add))
    kb.vec(lambda e: e.tensor_tensor(gt128, gt128, assn[:], ALU.mult))
    rv = kb.vec(lambda e: e.tensor_tensor(s_sb[:], gt128, w128, ALU.add))
    rtapv = {}
    if "sm_R" in small_taps:
        rtapv = kb.store(small_taps["sm_R"], s_sb[:], waits={kb.VE: rv})
    rsp = kb.store(rin_h.ap(), s_sb[:], waits={kb.VE: rv})
    ccv2 = kb.op("gpsimd", lambda e: e.collective_compute(
        "AllGather", ALU.bypass, replica_groups=[list(range(8))],
        ins=[rin_h.ap().opt()], outs=[rout_h.ap().opt()]),
        waits=wmerge(rsp, {kb.CC: 1}), inc=kb.CC)
    for fn_ in defer_x2tm:
        fn_()
    sbcg = B_sbc.begin_write()
    rlv = kb.gdma(sbc[:, 0:1024],
                  bass.AP(rout_h, 0, [[128, 128], [16384, 8], [1, 128]]),
                  waits=wmerge({kb.CC: ccv2}, sbcg))
    B_sbc.set_ready_d(rlv)

    kb.vec(lambda e: e.memset(bis[:, 24:32], -8.5), waits=rlv)
    kb.vec(lambda e: e.memset(bis[:, 32:40], 0.5))
    kb.vec(lambda e: e.memset(bis[:, 40:48], -2e9))
    kb.vec(lambda e: e.memset(bis[:, 48:56], -9e8))
    kb.vec(lambda e: e.tensor_copy(bis[:, 16:24], bis[:, 24:32]))

    Rall = lambda: bass.AP(sbc, 0, [[2048, 128], [1, 8], [128, 8], [8, 16]])
    gtb = lambda: bass.AP(scr, 0, [[SCRSTR, 128], [1, 8], [128, 8], [8, 16]])
    midb = lambda: bass.AP(bis, 16, [[96, 128], [1, 8], [0, 8], [0, 16]])
    bsl = kb.slots[6]

    def bis_round(first):
        v1 = kb.vec(lambda e: e.tensor_tensor(gtb(), Rall(), midb(),
                                              ALU.is_ge),
                    waits=wmerge({kb.VE: kb.VE.n}, B_sbc.ready))
        v2 = kb.vec(lambda e: e.tensor_reduce(
            bass.AP(scr, 0, [[SCRSTR, 128], [1, 8], [128, 8]]), gtb(),
            AX.X, ALU.add))
        v3 = kb.vec(lambda e: e.tensor_reduce(
            bass.AP(scr, 1152, [[SCRSTR, 128], [1, 8]]),
            bass.AP(scr, 0, [[SCRSTR, 128], [1, 8], [128, 8]]),
            AX.X, ALU.add))
        mv = kb.mm(bsl.ap[0:128, 0:8], ones128[:], cnt8, True, True,
                   waits=wmerge({kb.VE: v3}, bsl.last_evac), inc=True)
        v4 = kb.vec(lambda e: e.tensor_single_scalar(
            bis[:, 56:64], bsl.ap[0:128, 0:8], CAP, ALU.is_ge),
            waits={kb.MM: mv})
        bsl.last_evac = {kb.VE: v4}
        kb.vec(lambda e: e.tensor_scalar(bis[:, 64:72], bis[:, 56:64],
                                         -1.0, 1.0, ALU.mult, ALU.add))
        if first:
            kb.vec(lambda e: e.tensor_single_scalar(bis[:, 72:80],
                                                    bis[:, 56:64], -8.5,
                                                    ALU.mult))
            kb.vec(lambda e: e.tensor_single_scalar(bis[:, 80:88],
                                                    bis[:, 64:72], -2e9,
                                                    ALU.mult))
            kb.vec(lambda e: e.tensor_tensor(bis[:, 0:8], bis[:, 72:80],
                                             bis[:, 80:88], ALU.add))
            kb.vec(lambda e: e.tensor_single_scalar(bis[:, 72:80],
                                                    bis[:, 56:64], 0.5,
                                                    ALU.mult))
            kb.vec(lambda e: e.tensor_single_scalar(bis[:, 80:88],
                                                    bis[:, 64:72], -9e8,
                                                    ALU.mult))
            kb.vec(lambda e: e.tensor_tensor(bis[:, 8:16], bis[:, 72:80],
                                             bis[:, 80:88], ALU.add))
        else:
            kb.vec(lambda e: e.tensor_tensor(bis[:, 72:80], bis[:, 16:24],
                                             bis[:, 56:64], ALU.mult))
            kb.vec(lambda e: e.tensor_tensor(bis[:, 80:88], bis[:, 0:8],
                                             bis[:, 64:72], ALU.mult))
            kb.vec(lambda e: e.tensor_tensor(bis[:, 0:8], bis[:, 72:80],
                                             bis[:, 80:88], ALU.add))
            kb.vec(lambda e: e.tensor_tensor(bis[:, 72:80], bis[:, 8:16],
                                             bis[:, 56:64], ALU.mult))
            kb.vec(lambda e: e.tensor_tensor(bis[:, 80:88], bis[:, 16:24],
                                             bis[:, 64:72], ALU.mult))
            kb.vec(lambda e: e.tensor_tensor(bis[:, 8:16], bis[:, 72:80],
                                             bis[:, 80:88], ALU.add))
        kb.vec(lambda e: e.tensor_tensor(bis[:, 16:24], bis[:, 0:8],
                                         bis[:, 8:16], ALU.add))
        kb.vec(lambda e: e.tensor_single_scalar(bis[:, 16:24], bis[:, 16:24],
                                                0.5, ALU.mult))

    bis_round(True)
    for _ in range(41):
        bis_round(False)
    B_sbc.used(kb.VE, kb.VE.n)

    v = kb.vec(lambda e: e.tensor_tensor(
        bass.AP(scr, 0, [[SCRSTR, 128], [8, 16], [1, 8]]),
        bass.AP(s_sb, 0, [[128, 128], [8, 16], [1, 8]]),
        bass.AP(bis, 0, [[96, 128], [0, 16], [1, 8]]), ALU.is_ge),
        waits={kb.VE: kb.VE.n})
    v = kb.vec(lambda e: e.tensor_tensor(s_sb[:], probs[:], gt128,
                                         ALU.mult),
               waits=wmerge(rtapv, rsp))
    sv_done = kb.vec(lambda e: e.tensor_tensor(s_sb[:], s_sb[:], assn[:],
                                               ALU.mult))
    ssp = kb.store(bass.AP(x2tm.tensor, x2tm.offset + 1024,
                           [[1032, 128], [1032 * 128, 16], [1, 8]]),
                   s_sb[:], waits={kb.VE: sv_done})
    B_x2tm.set_ready_d(ssp)

    m8 = scr[0:8, 1024:3072]
    p8 = scr[0:8, 3072:5120]
    for j in range(16):
        tsl = kb.slots[6 + (j % 2)]
        w = wmerge({kb.VE: sv_done} if j == 0 else None, tsl.last_evac,
                   CONSTS)
        mv = kb.op("tensor", lambda e, j=j, tsl=tsl: e.transpose(
            tsl.ap[0:8, 0:128], s_sb[:, j * 8:(j + 1) * 8], ident[:]),
            waits=w, inc=kb.MM)
        cv = kb.scop(lambda e, j=j, tsl=tsl: e.copy(
            m8[:, j * 128:(j + 1) * 128], tsl.ap[0:8, 0:128]),
            waits={kb.MM: mv})
        tsl.last_evac = {kb.SC: cv}
    v = kb.vec(lambda e: e.tensor_single_scalar(m8, m8, 0.0, ALU.is_gt),
               waits={kb.SC: kb.SC.n})
    p8v = kb.vec(lambda e: e.tensor_tensor_scan(p8, m8, m8, 0.0,
                                                ALU.add, ALU.bypass))
    kb.vec(lambda e: e.tensor_copy(mbt[0:8, :], p8[:, 0:1024]),
           waits=wmerge(B_mb.begin_write(), {kb.VE: p8v}))
    p8v = kb.vec(lambda e: e.tensor_copy(rbt[0:8, :], p8[:, 1024:2048]),
                 waits=B_rb.begin_write())
    pos_ready = {kb.VE: p8v}

    zv2 = kb.vec(lambda e: e.memset(tb, 0.0), waits=B_tb.last_use)
    B_tb.set_ready(kb.VE, zv2)
    for tc in range(16):
        sp = kb.store(bass.AP(ymtm.tensor, ymtm.offset + tc * 128 * D,
                              [[D, 128], [1, D]]), tb, waits={kb.VE: zv2})
        B_tb.used_d(sp)
        B_ymtm.set_ready_d(sp)
    sp = kb.store(bass.AP(ymtm.tensor, ymtm.offset + T * D, [[D, 4], [1, D]]),
                  tb[0:4, :], waits={kb.VE: zv2})
    B_tb.used_d(sp)
    B_ymtm.set_ready_d(sp)
    ymtm_zero = dict(B_ymtm.ready)
    sp = kb.store(bass.AP(x2tm.tensor, x2tm.offset + T * 1032,
                          [[1032, 4], [1, 1024]]), tb[0:4, :],
                  waits={kb.VE: zv2})
    B_tb.used_d(sp)
    B_x2tm.set_ready_d(sp)
    sp = kb.store(bass.AP(x2tm.tensor, x2tm.offset + T * 1032 + 1024,
                          [[1032, 4], [1, 8]]), tb[0:4, 0:8],
                  waits={kb.VE: zv2})
    B_tb.used_d(sp)
    B_x2tm.set_ready_d(sp)

    if "sm_probs" in small_taps:
        kb.store(small_taps["sm_probs"], probs[:], waits={kb.VE: kb.VE.n})
    if "sm_s" in small_taps:
        kb.store(small_taps["sm_s"], s_sb[:], waits={kb.VE: kb.VE.n})
    if "sm_bis" in small_taps:
        kb.store(small_taps["sm_bis"], bis[:], waits={kb.VE: kb.VE.n})

    B_p4A, B_p4B = Buf(), Buf()
    B_ytb = Buf()
    scrS = [Buf() for _ in range(4)]
    for b_ in scrS:
        b_.used(kb.VE, kb.VE.n)
        b_.used(kb.MM, kb.MM.n)
    w1bufs = {}
    w2bufs = {}
    tokr = {}
    xgb = {}

    def plan_w1_loads(ee):
        w1b = Buf()
        g1 = B_p1.begin_write()
        g2 = B_p2.begin_write()
        for k in range(G):
            pool = pool1 if k < 4 else pool2
            w = (g1 if k == 0 else (g2 if k == 4 else None))
            kb.load(pool[:, (k % 4) * 1024:(k % 4 + 1) * 1024],
                    bass.AP(w1T.tensor, w1T.offset + ee * D * F + k * 128 * F,
                            [[F, 128], [1, F]]), waits=w)
        w1b.set_ready_d(kb.lw_all())
        w1bufs[ee] = w1b

    def plan_w2_loads(ee):
        w2b = Buf()
        g3 = B_p3.begin_write()
        for k in range(G):
            kb.load(pool3[:, k * 1024:(k + 1) * 1024],
                    bass.AP(w2T.tensor, w2T.offset + ee * F * D + k * 128 * D,
                            [[D, 128], [1, D]]), waits=g3 if k == 0 else None)
        w2b.set_ready_d(kb.lw_all())
        w2bufs[ee] = w2b

    def plan_lists(ee):
        sgate = B_sbc.begin_write()
        for tt2 in range(4):
            tsl = kb.slots[6 + (tt2 % 2)]
            w = wmerge(pos_ready if tt2 == 0 else None, tsl.last_evac,
                       CONSTS)
            psrc = mbt if tt2 < 2 else rbt
            mv = kb.mm(tsl.ap, sel8t[0:8, ee * 128:(ee + 1) * 128],
                       psrc[0:8, (tt2 % 2) * 512:(tt2 % 2 + 1) * 512],
                       True, True, waits=w, inc=True)
            cv = kb.scop(lambda e, tt2=tt2, tsl=tsl: e.copy(
                sbc[:, tt2 * 512:(tt2 + 1) * 512], tsl.ap),
                waits=wmerge({kb.MM: mv}, sgate if tt2 == 0 else None))
            tsl.last_evac = {kb.SC: cv}
        posb_ready = {kb.SC: kb.SC.n}
        cmp = tb
        for sg in range(4):
            w = wmerge(posb_ready if sg == 0 else None, B_tb.last_use)
            v = kb.vec(lambda e, sg=sg: e.tensor_scalar(
                scr[0:128, 0:1024], sbc[:, 0:1024], misct[:, sg:sg + 1],
                None, ALU.is_le, ALU.bypass), waits=w)
            B_tb.set_ready(kb.VE, v)
            v = kb.vec(lambda e, sg=sg, ee=ee: e.tensor_reduce(
                tokcf[:, ee * 4 + sg:ee * 4 + sg + 1], scr[0:128, 0:1024],
                AX.X, ALU.add))
            v = kb.vec(lambda e, sg=sg: e.tensor_scalar(
                scr[0:128, 0:1024], sbc[:, 1024:2048], misct[:, sg:sg + 1],
                None, ALU.is_le, ALU.bypass))
            v = kb.vec(lambda e, sg=sg, ee=ee: e.tensor_reduce(
                bis[:, 8:9], scr[0:128, 0:1024], AX.X, ALU.add))
            v = kb.vec(lambda e, sg=sg, ee=ee: e.tensor_tensor(
                tokcf[:, ee * 4 + sg:ee * 4 + sg + 1],
                tokcf[:, ee * 4 + sg:ee * 4 + sg + 1],
                bis[:, 8:9], ALU.add))
            B_tb.used(kb.VE, v)
        for sg in range(1, 4):
            kb.vec(lambda e, sg=sg, ee=ee: e.tensor_scalar(
                bis[:, 8:9], tokcf[:, ee * 4 + sg:ee * 4 + sg + 1],
                float(T), float(sg), ALU.is_ge, ALU.mult))
            kb.vec(lambda e, sg=sg, ee=ee: e.tensor_tensor(
                tokcf[:, ee * 4 + sg:ee * 4 + sg + 1],
                tokcf[:, ee * 4 + sg:ee * 4 + sg + 1],
                bis[:, 8:9], ALU.add))
        vtok = kb.vec(lambda e, ee=ee: e.tensor_copy(
            tokci[:, ee * 4:(ee + 1) * 4], tokcf[:, ee * 4:(ee + 1) * 4]))
        tokr[ee] = {kb.VE: vtok}
        B_sbc.used(kb.VE, vtok)

    def plan_gathers(ee, first):
        b = Buf()
        for sg in range(4):
            w = wmerge(B_x2tm.ready if first and sg == 0 else None,
                       tokr[ee] if sg == 0 else None,
                       scrS[sg].begin_write())
            gv = kb.store(scr[:, 1024 + sg * 1032:1024 + (sg + 1) * 1032],
                          misct[:, 0:1], waits=w)
            wd, fn, inc, add = kb.ops["gpsimd"].pop()
            idxap = tokci[:, ee * 4 + sg:ee * 4 + sg + 1]
            fn2 = (lambda e, sg=sg, idxap=idxap: e.indirect_dma_start(
                scr[:, 1024 + sg * 1032:1024 + (sg + 1) * 1032], None,
                x2tm, IndirectOffsetOnAxis(idxap, 0)))
            kb.ops["gpsimd"].append((wd, fn2, inc, add))
            scrS[sg].set_ready_d(gv)
        xgb[ee] = b

    ytb = wring
    x2p_rdy = []
    plan_w1_loads(0)
    plan_w2_loads(0)
    plan_lists(0)
    plan_gathers(0, True)
    for ee in range(NE):
        w1b, w2b = w1bufs[ee], w2bufs[ee]
        XO = 0 if ee % 2 == 0 else 4096
        HO = 4096 - XO
        B_xf = B_p4A if ee % 2 == 0 else B_p4B
        B_h = B_p4B if ee % 2 == 0 else B_p4A
        p4lo_gate = B_xf.begin_write()
        p4lo_g = [None] * G
        for g in range(G):
            for sg in range(4):
                tsl = kb.slots[(g * 4 + sg) % 4]
                w = wmerge(scrS[sg].ready if g == 0 else None,
                           p4lo_gate if (sg == 0 and g == 0) else None,
                           tsl.last_evac)
                mv = kb.op("tensor", lambda e, sg=sg, g=g, tsl=tsl:
                           e.transpose(tsl.ap[:, 0:128],
                                       scr[:, 1024 + sg * 1032 + g * 128:
                                           1024 + sg * 1032 + (g + 1) * 128],
                                       ident[:]), waits=w, inc=kb.MM)
                cv = kb.scop(lambda e, sg=sg, g=g, tsl=tsl, XO=XO: e.copy(
                    pool4[:, XO + g * 512 + sg * 128:
                          XO + g * 512 + (sg + 1) * 128],
                    tsl.ap[:, 0:128]), waits={kb.MM: mv})
                tsl.last_evac = {kb.SC: cv}
                scrS[sg].used(kb.MM, mv)
            p4lo_g[g] = {kb.SC: kb.SC.n}
        B_xf.set_ready(kb.SC, kb.SC.n)
        srow = sbc[0:1, 1536:2048]
        sbc_gate = B_sbc.begin_write()
        for sg in range(4):
            tsl = kb.slots[6 + (sg % 2)]
            w = wmerge(scrS[sg].ready, tsl.last_evac,
                       sbc_gate if sg == 0 else None)
            mv = kb.op("tensor", lambda e, sg=sg, tsl=tsl, ee=ee:
                       e.transpose(tsl.ap[0:1, 0:128],
                                   scr[:, 1024 + sg * 1032 + 1024 + ee:
                                       1024 + sg * 1032 + 1024 + ee + 1],
                                   ident[:]), waits=w, inc=kb.MM)
            cv = kb.scop(lambda e, sg=sg, tsl=tsl: e.copy(
                srow[0:1, sg * 128:(sg + 1) * 128], tsl.ap[0:1, 0:128]),
                waits={kb.MM: mv})
            tsl.last_evac = {kb.SC: cv}
            scrS[sg].used(kb.MM, mv)
        bsl2 = kb.slots[6]
        mv = kb.mm(bsl2.ap, onesk[:], srow, True, True,
                   waits=wmerge({kb.SC: kb.SC.n}, bsl2.last_evac), inc=True)
        scv0 = kb.scop(lambda e, bsl2=bsl2: e.copy(sbc[:, 0:512], bsl2.ap),
                       waits={kb.MM: mv})
        bsl2.last_evac = {kb.SC: scv0}
        B_sbc.set_ready(kb.SC, scv0)

        hsl = [kb.slots[0], kb.slots[1]]
        p4hi_gate = B_h.begin_write()
        gelu_done = []
        for jf in range(8):
            sl = hsl[jf % 2]
            mv = None
            for k in range(G):
                pool = pool1 if k < 4 else pool2
                w = wmerge(w1b.ready if (jf == 0 and k == 0) else None,
                           p4lo_g[k] if jf == 0 else None,
                           sl.last_evac if k == 0 else None)
                lhsT = pool[:, (k % 4) * 1024 + jf * 128:
                            (k % 4) * 1024 + (jf + 1) * 128]
                mv = kb.mm(sl.ap, lhsT,
                           pool4[:, XO + k * 512:XO + (k + 1) * 512],
                           start=(k == 0), stop=(k == G - 1), waits=w,
                           inc=(k == G - 1))
                B_xf.used(kb.MM, mv)
            gvv = kb.act(pool4[:, HO + jf * 512:HO + (jf + 1) * 512],
                         sl.ap, AF.Gelu,
                         waits=wmerge({kb.MM: mv},
                                      p4hi_gate if jf == 0 else None))
            gelu_done.append(gvv)
            sl.last_evac = {kb.SC: gvv}
        hready = 0
        for jf in range(8):
            scv = kb.vec(lambda e, jf=jf, HO=HO: e.tensor_tensor(
                pool4[:, HO + jf * 512:HO + (jf + 1) * 512],
                pool4[:, HO + jf * 512:HO + (jf + 1) * 512],
                sbc[:, 0:512], ALU.mult),
                waits=wmerge({kb.SC: gelu_done[jf]},
                             B_sbc.ready if jf == 0 else None))
            hready = scv
            kb.slots[jf % 2].last_evac = wmerge(
                kb.slots[jf % 2].last_evac, {kb.VE: scv})
        B_sbc.used(kb.VE, hready)

        B_p1.used(kb.MM, kb.MM.n)
        B_p2.used(kb.MM, kb.MM.n)
        if ee + 1 < NE:
            plan_w1_loads(ee + 1)
            plan_lists(ee + 1)
            plan_gathers(ee + 1, False)

        ysl = [kb.slots[4], kb.slots[5]]
        for j2 in range(8):
            sl = ysl[j2 % 2]
            mv = None
            for k in range(G):
                w = wmerge(w2b.ready if (j2 == 0 and k == 0) else None,
                           {kb.VE: hready} if (j2 == 0 and k == 0) else None,
                           sl.last_evac if k == 0 else None)
                lhsT = pool3[:, k * 1024 + j2 * 128:
                             k * 1024 + (j2 + 1) * 128]
                mv = kb.mm(sl.ap, lhsT,
                           pool4[:, HO + k * 512:HO + (k + 1) * 512],
                           start=(k == 0), stop=(k == G - 1), waits=w,
                           inc=(k == G - 1))
                B_h.used(kb.MM, mv)
            cv = kb.scop(lambda e, j2=j2, sl=sl: e.copy(tbv[j2 % 2], sl.ap),
                         waits=wmerge({kb.MM: mv}, B_tb.last_use))
            B_tb.set_ready(kb.SC, cv)
            sl.last_evac = {kb.SC: cv}
            for sg in range(4):
                tsl = kb.slots[2 + (sg % 2)]
                w = wmerge({kb.SC: cv}, tsl.last_evac)
                mv2 = kb.op("tensor", lambda e, j2=j2, sg=sg, tsl=tsl:
                            e.transpose(tsl.ap[:, 0:128],
                                        tb[:, (j2 % 2) * 512 + sg * 128:
                                           (j2 % 2) * 512 + (sg + 1) * 128],
                                        ident[:]), waits=w, inc=kb.MM)
                cv2 = kb.scop(lambda e, j2=j2, sg=sg, tsl=tsl: e.copy(
                    ytb[:, sg * 1024 + j2 * 128:sg * 1024 + (j2 + 1) * 128],
                    tsl.ap[:, 0:128]),
                    waits=wmerge({kb.MM: mv2},
                                 B_ytb.begin_write()
                                 if j2 == 0 and sg == 0 else None))
                tsl.last_evac = {kb.SC: cv2}
                B_tb.used(kb.MM, mv2)
        scat_base = kb.SPs[3].n
        for sg in range(4):
            w = wmerge({kb.SC: kb.SC.n},
                       ymtm_zero if ee == 0 and sg == 0 else None,
                       {kb.SPs[3]: scat_base} if scat_base else None)
            idxap = tokci[:, ee * 4 + sg:ee * 4 + sg + 1]
            spv = kb.op("gpsimd",
                        lambda e, sg=sg, idxap=idxap: e.indirect_dma_start(
                            ymtm, IndirectOffsetOnAxis(idxap, 0),
                            ytb[:, sg * 1024:(sg + 1) * 1024], None,
                            compute_op=ALU.add),
                        waits=w, inc=kb.SPs[3], add=16)
            sp = {kb.SPs[3]: spv}
            B_ytb.used_d(sp)
            B_ymtm.set_ready_d(sp)
        B_p3.used(kb.MM, kb.MM.n)
        if ee + 1 < NE:
            plan_w2_loads(ee + 1)
        if ee == 7:
            x2p_g1 = B_p1.begin_write()
            x2p_g2 = B_p2.begin_write()
            for tc in range(16):
                xpool = pool1 if tc < 8 else pool2
                w = (x2p_g1 if tc == 0 else (x2p_g2 if tc == 8 else None))
                lv = kb.load(xpool[:, (tc % 8) * 1024:(tc % 8 + 1) * 1024],
                             bass.AP(x2tm.tensor,
                                     x2tm.offset + tc * 128 * 1032,
                                     [[1032, 128], [1, 1024]]),
                             waits=wmerge(w, B_x2tm.ready
                                          if tc == 0 else None))
                x2p_rdy.append(lv)

    if "sm_ymoe" in small_taps:
        for tc in range(16):
            kb.store(bass.AP(small_taps["sm_ymoe"].tensor,
                             small_taps["sm_ymoe"].offset + tc * 128 * D,
                             [[D, 128], [1, D]]),
                     misct[:, 0:1], waits=kb.sp_all())
            wd, fn, inc, add = kb.ops["gpsimd"].pop()
            fn2 = (lambda e, tc=tc: e.dma_start(
                out=bass.AP(small_taps["sm_ymoe"].tensor,
                            small_taps["sm_ymoe"].offset + tc * 128 * D,
                            [[D, 128], [1, D]]),
                in_=bass.AP(ymtm.tensor, ymtm.offset + tc * 128 * D,
                            [[D, 128], [1, D]])))
            kb.ops["gpsimd"].append((wd, fn2, inc, add))

    g4f = wmerge(B_p4.begin_write(), B_p4A.begin_write(),
                 B_p4B.begin_write())
    ld_use = [None] * 16
    st_use = [None] * 16

    def ym_load(tc):
        ymo = (tc % 8) * 1024
        ym_ap = bass.AP(ymtm.tensor, ymtm.offset + tc * 128 * D,
                        [[D, 128], [1, D]])
        w = wmerge(g4f if tc == 0 else None,
                   B_ymtm.ready if tc < 2 else None,
                   st_use[tc - 8] if tc >= 8 else None)
        if tc % 2 == 0:
            ld_use[tc] = kb.load(pool4[:, ymo:ymo + 1024], ym_ap, waits=w)
        else:
            ld_use[tc] = kb.gdma(pool4[:, ymo:ymo + 1024], ym_ap, waits=w)

    for tc in range(8):
        ym_load(tc)
    for tc in range(16):
        ymo = (tc % 8) * 1024
        xpool = pool1 if tc < 8 else pool2
        xap = xpool[:, (tc % 8) * 1024:(tc % 8 + 1) * 1024]
        v = kb.vec(lambda e, ymo=ymo, xap=xap: e.tensor_tensor(
            pool4[:, ymo:ymo + 1024], pool4[:, ymo:ymo + 1024],
            xap, ALU.add), waits=wmerge(ld_use[tc], x2p_rdy[tc]))
        cm, cvr, cs = 16 + tc, 64 + tc, 32 + tc
        sqo = 1024 + (tc % 4) * 1024
        kb.op("scalar", lambda e, ymo=ymo, cvr=cvr, sqo=sqo:
              e.activation(out=scr[:, sqo:sqo + 1024],
                           in_=pool4[:, ymo:ymo + 1024], func=AF.Square,
                           scale=1.0 / 32.0,
                           accum_out=bis[:, cvr:cvr + 1]),
              waits=wmerge({kb.VE: v}, {kb.SC: kb.SC.n}), inc=kb.SC)
        s2 = kb.op("scalar", lambda e, ymo=ymo, cm=cm, sqo=sqo:
                   e.activation(out=scr[:, sqo:sqo + 1024],
                                in_=pool4[:, ymo:ymo + 1024], func=AF.Copy,
                                scale=1.0 / 1024.0,
                                accum_out=bis[:, cm:cm + 1]),
                   waits={kb.SC: kb.SC.n}, inc=kb.SC)
        kb.vec(lambda e, cm=cm, cs=cs: e.tensor_tensor(
            bis[:, cs:cs + 1], bis[:, cm:cm + 1], bis[:, cm:cm + 1],
            ALU.mult), waits={kb.SC: s2})
        vv = kb.vec(lambda e, cvr=cvr, cs=cs: e.tensor_tensor(
            bis[:, cvr:cvr + 1], bis[:, cvr:cvr + 1], bis[:, cs:cs + 1],
            ALU.subtract))
        if tc % 4 == 3:
            b0 = 64 + tc - 3
            sv = kb.scop(lambda e, b0=b0: e.activation(
                out=bis[:, b0:b0 + 4], in_=bis[:, b0:b0 + 4],
                func=AF.Sqrt, bias=epsc128[:]), waits={kb.VE: vv})
            for t2 in range(tc - 3, tc + 1):
                ymo2 = (t2 % 8) * 1024
                cm2, cv2_, cs2 = 16 + t2, 64 + t2, 32 + t2
                kb.vec(lambda e, cv2_=cv2_, cs2=cs2: e.reciprocal(
                    bis[:, cs2:cs2 + 1], bis[:, cv2_:cv2_ + 1]),
                    waits={kb.SC: sv} if t2 == tc - 3 else None)
                kb.vec(lambda e, cm2=cm2, cv2_=cv2_, cs2=cs2: e.tensor_tensor(
                    bis[:, cv2_:cv2_ + 1], bis[:, cm2:cm2 + 1],
                    bis[:, cs2:cs2 + 1], ALU.mult))
                v8 = kb.vec(lambda e, ymo2=ymo2, cv2_=cv2_, cs2=cs2:
                            e.tensor_scalar(
                                pool4[:, ymo2:ymo2 + 1024],
                                pool4[:, ymo2:ymo2 + 1024],
                                bis[:, cs2:cs2 + 1], bis[:, cv2_:cv2_ + 1],
                                ALU.mult, ALU.subtract))
                out_ap = bass.AP(out_tm.tensor, out_tm.offset + t2 * 128 * D,
                                 [[D, 128], [1, D]])
                if t2 % 2 == 0:
                    st_use[t2] = kb.store(out_ap, pool4[:, ymo2:ymo2 + 1024],
                                          waits={kb.VE: v8})
                else:
                    st_use[t2] = kb.load(out_ap, pool4[:, ymo2:ymo2 + 1024],
                                         waits={kb.VE: v8})
                if t2 + 8 < 16:
                    ym_load(t2 + 8)

    kb.op("gpsimd", lambda e: e.engine_nop(),
          waits=wmerge(kb.sp_all(), kb.lw_all()))

    def emit(engine, name):
        seen = {}
        for w, fn, inc, add in kb.ops[name]:
            for s, v in w.items():
                if v and seen.get(s, 0) < v:
                    engine.wait_ge(s.h, v)
                    seen[s] = v
            instr = fn(engine)
            if inc is not None:
                instr.then_inc(inc.h, add)

    @block.sync
    def _(sync):
        emit(sync, "sync")

    @block.gpsimd
    def _(gpsimd):
        emit(gpsimd, "gpsimd")

    @block.scalar
    def _(scalar):
        emit(scalar, "scalar")

    @block.vector
    def _(vector):
        emit(vector, "vector")

    @block.tensor
    def _(tensor):
        emit(tensor, "tensor")

    es.close()
    return nc


def _pack_biases(kw):
    src = {"fr_b": "L0_attn_fr_b", "dp_b": "L0_attn_dp_b",
           "bp_b": "L0_attn_bp_b", "cp_b": "L0_attn_cp_b",
           "m1_b": "L0_attn_m1_b", "m2_b": "L0_attn_m2_b",
           "gate_b": "L0_attn_gate_b", "ffn1_b": "L0_attn_ffn1_b",
           "ffn2_b": "L0_attn_ffn2_b", "norm_g": "L0_attn_norm_g",
           "norm_b": "L0_attn_norm_b", "ffn_g": "L0_attn_ffn_g",
           "ffn_bn": "L0_attn_ffn_bn", "skip": "L0_attn_skip"}
    cols = []
    for name, _n in BIAS_ORDER:
        v = np.asarray(kw[src[name]], dtype=np.float32)
        cols.append(v.reshape(-1, 128).T)
    return np.ascontiguousarray(np.concatenate(cols, axis=1))


def kernel(**kw):
    x = np.asarray(kw["L0_attn_x"], dtype=np.float32)
    B, L, _ = x.shape
    biases = _pack_biases(kw)
    ident = np.eye(128, dtype=np.float32)
    misc = np.zeros((128, 20), dtype=np.float32)
    for tc in range(16):
        misc[:, tc] = tc * 128 + np.arange(128)
    misc[0:8, 16] = np.arange(8) * 513.0
    sel8 = np.zeros((8, 1024), dtype=np.float32)
    for ee in range(8):
        sel8[ee, ee * 128:(ee + 1) * 128] = 1.0
    w1T = np.ascontiguousarray(
        np.asarray(kw["L0_ffn_w1"], np.float32).transpose(0, 2, 1))
    w2T = np.ascontiguousarray(
        np.asarray(kw["L0_ffn_w2"], np.float32).transpose(0, 2, 1))
    gwT = np.ascontiguousarray(np.asarray(kw["L0_ffn_gate_w"], np.float32).T)

    shared = {
        "fr_w": np.ascontiguousarray(kw["L0_attn_fr_w"], np.float32),
        "dp_w": np.ascontiguousarray(kw["L0_attn_dp_w"], np.float32),
        "bp_w": np.ascontiguousarray(kw["L0_attn_bp_w"], np.float32),
        "cp_w": np.ascontiguousarray(kw["L0_attn_cp_w"], np.float32),
        "m1_w": np.ascontiguousarray(kw["L0_attn_m1_w"], np.float32),
        "m2_w": np.ascontiguousarray(kw["L0_attn_m2_w"], np.float32),
        "gate_w": np.ascontiguousarray(kw["L0_attn_gate_w"], np.float32),
        "ffn1_w": np.ascontiguousarray(kw["L0_attn_ffn1_w"], np.float32),
        "ffn2_w": np.ascontiguousarray(kw["L0_attn_ffn2_w"], np.float32),
        "gwT": gwT, "w1T": w1T, "w2T": w2T,
        "biases_in": biases, "ident_in": ident,
        "misc_in": misc, "sel8_in": sel8,
    }
    in_maps = []
    for c in range(8):
        b, h = c // 2, c % 2
        xs = np.ascontiguousarray(x[b, h * T:(h + 1) * T, :].T)
        sel = np.zeros((128, 8), dtype=np.float32)
        if h == 1:
            sel[:, c - 1] = 1.0
        m = dict(shared)
        m["x_fm"] = xs
        m["sel_in"] = sel
        in_maps.append(m)

    nc = build_nc()
    res = run_bass_kernel_spmd(nc, in_maps, core_ids=list(range(8)))
    out = np.empty((B, L, D), dtype=np.float32)
    for c in range(8):
        b, h = c // 2, c % 2
        out[b, h * T:(h + 1) * T, :] = res.results[c]["out_tm"]
    return out


# version 15
# speedup vs baseline: 1.0632x; 1.0110x over previous
import numpy as np
import jax
jax.config.update('jax_compilation_cache_dir', '/root/.bass_jax_cache')
jax.config.update('jax_persistent_cache_min_compile_time_secs', 1.0)
jax.config.update('jax_persistent_cache_min_entry_size_bytes', 0)
import concourse.bass as bass
import concourse.mybir as mybir
from concourse.bass_utils import run_bass_kernel_spmd
from concourse.bass_isa import ReduceOp
from concourse import library_config
from concourse.bass_core_rust import IndirectOffsetOnAxis

F32 = mybir.dt.float32
ALU = mybir.AluOpType
AF = mybir.ActivationFunctionType
AX = mybir.AxisListType

D = 1024
G = 8
T = 2048
TH = 1024
NE = 8
F = 1024
CAP = 2048.0
THR = 0.9
EPS = 1e-5
ENGS = ("sync", "gpsimd", "scalar", "vector", "tensor")

BIAS_ORDER = [("fr_b", 8), ("dp_b", 8), ("bp_b", 8), ("cp_b", 8),
              ("m1_b", 16), ("m2_b", 8), ("gate_b", 8), ("ffn1_b", 32),
              ("ffn2_b", 8), ("norm_g", 8), ("norm_b", 8), ("ffn_g", 8),
              ("ffn_bn", 8), ("skip", 8)]
NBIAS = sum(n for _, n in BIAS_ORDER)


class Sem:
    __slots__ = ("name", "n", "h")
    def __init__(self, name):
        self.name, self.n, self.h = name, 0, None


def wmerge(*dicts):
    out = {}
    for d in dicts:
        if not d:
            continue
        for s, v in d.items():
            if v and out.get(s, 0) < v:
                out[s] = v
    return out


class Buf:
    __slots__ = ("ready", "last_use")
    def __init__(self):
        self.ready = {}
        self.last_use = {}

    def set_ready(self, sem, val):
        if val and self.ready.get(sem, 0) < val:
            self.ready[sem] = val

    def set_ready_d(self, d):
        self.ready = wmerge(self.ready, d)

    def used(self, sem, val):
        if val and self.last_use.get(sem, 0) < val:
            self.last_use[sem] = val

    def used_d(self, d):
        self.last_use = wmerge(self.last_use, d)

    def begin_write(self):
        w = wmerge(self.ready, self.last_use)
        self.ready = {}
        self.last_use = {}
        return w


class Slot:
    __slots__ = ("ap", "last_evac", "mm_done")
    def __init__(self, ap):
        self.ap = ap
        self.last_evac = {}
        self.mm_done = 0


class KB:
    def __init__(self, nc):
        self.nc = nc
        self.ops = {e: [] for e in ENGS}
        self.LWs = [Sem("LW%d" % i) for i in range(6)]
        self.SPs = [Sem("SP%d" % i) for i in range(4)]
        self.MM = Sem("MM")
        self.GP = Sem("GP")
        self.SC = Sem("SC")
        self.VE = Sem("VE")
        self.CC = Sem("CC")
        self.sems = self.LWs + self.SPs + [self.MM, self.GP, self.SC,
                                           self.VE, self.CC]
        self.lw_rot = 0
        self.sp_rot = 0
        self.slots = []
        self.wring_lastmm = [{}, {}]
        self.wring_ctr = 0

    def lw_all(self):
        return {s: s.n for s in self.LWs if s.n}

    def sp_all(self):
        return {s: s.n for s in self.SPs if s.n}

    def op(self, eng, fn, waits=None, inc=None, add=1):
        w = dict(waits) if waits else {}
        v = None
        if inc is not None:
            inc.n += add
            v = inc.n
        self.ops[eng].append((w, fn, inc, add))
        return v

    def load(self, out_ap, in_ap, waits=None):
        sem = self.LWs[self.lw_rot % len(self.LWs)]
        self.lw_rot += 1
        w = wmerge(waits, {sem: sem.n})
        v = self.op("sync", lambda e: e.dma_start(out=out_ap, in_=in_ap),
                    w, sem, 16)
        return {sem: v}

    def store(self, out_ap, in_ap, waits=None, accum=False, sem_i=None):
        if accum:
            fn = lambda e: e.dma_start(out=out_ap, in_=in_ap, accum_op=ALU.add)
        else:
            fn = lambda e: e.dma_start(out=out_ap, in_=in_ap)
        if sem_i is None:
            sem = self.SPs[self.sp_rot % len(self.SPs)]
            self.sp_rot += 1
        else:
            sem = self.SPs[sem_i]
        w = wmerge(waits, {sem: sem.n})
        v = self.op("gpsimd", fn, w, sem, 16)
        return {sem: v}

    def gdma(self, out_ap, in_ap, waits=None):
        return self.store(out_ap, in_ap, waits=waits)

    def mm(self, out, lhsT, rhs, start, stop, waits=None, inc=False):
        return self.op(
            "tensor",
            lambda e: e.matmul(out, lhsT, rhs, start=start, stop=stop),
            waits, self.MM if inc else None)

    def act(self, out, in_, func, bias=0.0, scale=1.0, waits=None):
        w = wmerge(waits, {self.SC: self.SC.n})
        return self.op(
            "scalar",
            lambda e: e.activation(out=out, in_=in_, func=func, bias=bias,
                                   scale=scale),
            w, self.SC)

    def scop(self, fn, waits=None):
        w = wmerge(waits, {self.SC: self.SC.n})
        return self.op("scalar", fn, w, self.SC)

    def vec(self, fn, waits=None):
        w = wmerge(waits, {self.VE: self.VE.n})
        return self.op("vector", fn, w, self.VE)


TEST_SLOTS4 = False


def build_nc(taps=()):
    nc = bass.Bass("TRN2", target_bir_lowering=False)
    kb = KB(nc)

    def dinp(name, shape):
        return nc.dram_tensor(name, shape, F32, kind="ExternalInput").ap()

    x_fm = dinp("x_fm", [D, T])
    fr_w = dinp("fr_w", [D, D]); dp_w = dinp("dp_w", [D, D])
    bp_w = dinp("bp_w", [D, D]); cp_w = dinp("cp_w", [D, D])
    m1_w = dinp("m1_w", [D, 2 * D]); m2_w = dinp("m2_w", [2 * D, D])
    gate_w = dinp("gate_w", [2 * D, D])
    ffn1_w = dinp("ffn1_w", [D, 4 * D]); ffn2_w = dinp("ffn2_w", [4 * D, D])
    gwT = dinp("gwT", [D, NE])
    w1T = dinp("w1T", [NE, D, F])
    w2T = dinp("w2T", [NE, F, D])
    biases_in = dinp("biases_in", [128, NBIAS])
    misc_in = dinp("misc_in", [128, 20])
    sel8_in = dinp("sel8_in", [8, 1024])
    ident_in = dinp("ident_in", [128, 128])
    sel_in = dinp("sel_in", [128, 8])

    out_tm = nc.dram_tensor("out_tm", [T, D], F32, kind="ExternalOutput").ap()
    tap_out = {}
    small_taps = {}
    for tname in taps:
        if tname.startswith("sm_"):
            shp = {"sm_probs": [128, 128], "sm_s": [128, 128],
                   "sm_R": [128, 128], "sm_bis": [128, 96],
                   "sm_ymoe": [T, D], "sm_tok": [8, 513],
                   "sm_cnt": [1, 8]}[tname]
            small_taps[tname] = nc.dram_tensor("tap_" + tname, shp, F32,
                                               kind="ExternalOutput").ap()
        else:
            tap_out[tname] = nc.dram_tensor("tap_" + tname, [D, T], F32,
                                            kind="ExternalOutput").ap()

    c_hbm = nc.dram_tensor("c_hbm", [D, T], F32).ap()
    sk_hbm = nc.dram_tensor("sk_hbm", [D, T], F32).ap()
    st_hbm = nc.dram_tensor("st_hbm", [D, T], F32).ap()
    A_hbm = nc.dram_tensor("A_hbm", [D, T], F32).ap()
    mo_hbm = nc.dram_tensor("mo_hbm", [D, T], F32).ap()
    x2tm = nc.dram_tensor("x2tm", [T + 4, 1032], F32).ap()
    ymtm = nc.dram_tensor("ymtm", [T + 4, D], F32).ap()
    gin_h = nc.dram_tensor("gin_h", [128, 8], F32)
    gout_h = nc.dram_tensor("gout_h", [8 * 128, 8], F32)
    rin_h = nc.dram_tensor("rin_h", [128, 128], F32)
    rout_h = nc.dram_tensor("rout_h", [8 * 128, 128], F32)

    from contextlib import ExitStack
    es = ExitStack()
    e = es.enter_context

    pool1 = e(nc.sbuf_tensor("pool1", [128, 8192], F32))
    pool2 = e(nc.sbuf_tensor("pool2", [128, 8192], F32))
    pool3 = e(nc.sbuf_tensor("pool3", [128, 8192], F32))
    pool4 = e(nc.sbuf_tensor("pool4", [128, 8192], F32))
    wring = e(nc.sbuf_tensor("wring", [128, 4096], F32))
    scr = e(nc.sbuf_tensor("scr", [128, 5280], F32))
    mbt = e(nc.sbuf_tensor("mbt", [128, 1024], F32))
    rbt = e(nc.sbuf_tensor("rbt", [128, 1024], F32))
    stats_m = e(nc.sbuf_tensor("stats_m", [1, 1024], F32))
    stats_r = e(nc.sbuf_tensor("stats_r", [1, 1024], F32))
    stats_t = e(nc.sbuf_tensor("stats_t", [1, 2048], F32))
    sbias = e(nc.sbuf_tensor("sbias", [128, NBIAS], F32))
    ident = e(nc.sbuf_tensor("ident", [128, 128], F32))
    selt = e(nc.sbuf_tensor("selt", [128, 8], F32))
    gwTt = e(nc.sbuf_tensor("gwTt", [128, 64], F32))
    onesc = e(nc.sbuf_tensor("onesc", [128, 1], F32))
    onesk = e(nc.sbuf_tensor("onesk", [1, 128], F32))
    ones128 = e(nc.sbuf_tensor("ones128", [128, 128], F32))
    Sinit = e(nc.sbuf_tensor("Sinit", [128, 8], F32))
    Ainit = e(nc.sbuf_tensor("Ainit", [128, 8], F32))
    gath = e(nc.sbuf_tensor("gath", [128, 64], F32))
    carry = e(nc.sbuf_tensor("carry", [128, 8], F32))
    probs = e(nc.sbuf_tensor("probs", [128, 128], F32))
    assn = e(nc.sbuf_tensor("assn", [128, 128], F32))
    s_sb = e(nc.sbuf_tensor("s_sb", [128, 128], F32))
    sbc = e(nc.sbuf_tensor("sbc", [128, 2048], F32))
    bis = e(nc.sbuf_tensor("bis", [128, 96], F32))
    epsc = e(nc.sbuf_tensor("epsc", [1, 1], F32))
    epsc128 = e(nc.sbuf_tensor("epsc128", [128, 1], F32))
    ones11 = e(nc.sbuf_tensor("ones11", [1, 1], F32))
    misct = e(nc.sbuf_tensor("misct", [128, 20], F32))
    sel8t = e(nc.sbuf_tensor("sel8t", [8, 1024], F32))
    tokcf = e(nc.sbuf_tensor("tokcf", [128, 32], F32))
    tokci = e(nc.sbuf_tensor("tokci", [128, 32], mybir.dt.int32))
    neg9 = e(nc.sbuf_tensor("neg9", [128, 128], F32))

    PA = e(nc.psum_tensor("PA", [128, 2048], F32))
    PB = e(nc.psum_tensor("PB", [128, 2048], F32))

    for s in kb.sems:
        s.h = e(nc.semaphore(s.name))
    block = e(nc.Block())

    for i in range(4):
        kb.slots.append(Slot(PA[:, i * 512:(i + 1) * 512]))
    for i in range(4):
        kb.slots.append(Slot(PB[:, i * 512:(i + 1) * 512]))

    tb = scr[:, 0:1024]
    tbv = [scr[:, 0:512], scr[:, 512:1024]]
    stdb = [scr[:, 1024:2048], scr[:, 2048:3072]]
    udb = [scr[:, 3072:4096], scr[:, 4096:5120]]
    B_tb = Buf()
    stdb_use = [{}, {}]
    udb_use = [{}, {}]

    bofs = {}
    cur = 0
    for nme, n in BIAS_ORDER:
        bofs[nme] = cur
        cur += n
    def bv(nme, g):
        return sbias[:, bofs[nme] + g: bofs[nme] + g + 1]

    B_p1, B_p2, B_p3, B_p4 = Buf(), Buf(), Buf(), Buf()
    B_chbm, B_skhbm, B_sthbm, B_Ahbm = Buf(), Buf(), Buf(), Buf()
    B_mohbm, B_x2tm, B_ymtm = Buf(), Buf(), Buf()
    B_mb, B_rb, B_sbc, B_probs = Buf(), Buf(), Buf(), Buf()

    kb.load(sbias[:], biases_in)
    kb.load(ident[:], ident_in)
    kb.load(selt[:], sel_in)
    kb.load(gwTt[:], gwT.rearrange("(k p) e -> p k e", p=128))
    kb.load(misct[:], misc_in)
    kb.load(sel8t[:], sel8_in)
    const_lw = kb.lw_all()
    kb.vec(lambda e: e.memset(onesc[:], 1.0))
    kb.vec(lambda e: e.memset(onesk[:], 1.0))
    kb.vec(lambda e: e.memset(ones128[:], 1.0))
    kb.vec(lambda e: e.memset(Sinit[:], 0.0))
    kb.vec(lambda e: e.memset(Ainit[:], 1.0))
    kb.vec(lambda e: e.memset(neg9[:], -1e9))
    kb.vec(lambda e: e.memset(epsc[:], EPS))
    kb.vec(lambda e: e.memset(epsc128[:], EPS))
    kb.vec(lambda e: e.memset(ones11[:], 1.0))
    kb.vec(lambda e: e.memset(bis[:], 0.0))
    const_ve = kb.VE.n
    CONSTS = wmerge({kb.VE: const_ve}, const_lw)

    def proj(w_hbm, Kc, Nout, rhs_fn, rhs_bufs, evac_pair_fn, ntt,
             kpass=4, first_waits=None, rhs_ready_k=None, korder=None,
             slots4=False):
        nkp = (Kc + kpass - 1) // kpass
        rhs_ready = wmerge(*(b.ready for b in rhs_bufs))
        ks_all = korder if korder is not None else list(range(Kc))
        passes = []
        for jb in range(Nout // 512):
            for kp in range(nkp):
                passes.append((jb, kp,
                               ks_all[kp * kpass:
                                      min((kp + 1) * kpass, Kc)]))
        planned = {}
        k_seen = set()

        def plan_load(idx):
            jb, kp, ks = passes[idx]
            setid = kb.wring_ctr % 2
            kb.wring_ctr += 1
            i = 0
            lwv = {}
            first_load = True
            while i < len(ks):
                npair = 2 if (i + 1 < len(ks)
                              and ks[i + 1] == ks[i] + 1) else 1
                k0 = ks[i]
                dst = wring[:, (setid * 4 + i) * 512:
                            (setid * 4 + i + npair) * 512]
                if npair == 2:
                    ap = bass.AP(w_hbm.tensor,
                                 w_hbm.offset + k0 * 128 * Nout + jb * 512,
                                 [[Nout, 128], [128 * Nout, 2], [1, 512]])
                else:
                    ap = bass.AP(w_hbm.tensor,
                                 w_hbm.offset + k0 * 128 * Nout + jb * 512,
                                 [[Nout, 128], [1, 512]])
                w = kb.wring_lastmm[setid] if first_load else None
                lwv = wmerge(lwv, kb.load(dst, ap, waits=w))
                first_load = False
                i += npair
            planned[idx] = (setid, lwv)

        plan_load(0)
        if len(passes) > 1:
            plan_load(1)
        first_mm_of_proj = True
        for idx, (jb, kp, ks) in enumerate(passes):
            setid, lwv = planned.pop(idx)
            first_of_pass = True
            lastv = None
            for j_local in range(4):
                for tt in range(ntt):
                    s = (j_local * 2 + tt) if ntt == 2 \
                        else (jb % 2) * 4 + j_local
                    slot = kb.slots[s]
                    for i, k in enumerate(ks):
                        w = {}
                        if kp == 0 and i == 0:
                            w = wmerge(w, slot.last_evac)
                        if first_of_pass:
                            w = wmerge(w, lwv)
                            if first_mm_of_proj:
                                w = wmerge(w,
                                           None if rhs_ready_k else rhs_ready,
                                           first_waits)
                        if rhs_ready_k is not None and k not in k_seen:
                            k_seen.add(k)
                            w = wmerge(w, rhs_ready_k[k])
                        start = (kp == 0 and i == 0)
                        stop = (kp == nkp - 1 and i == len(ks) - 1)
                        last_of_pass = (i == len(ks) - 1 and
                                        j_local == 3 and tt == ntt - 1)
                        lhsT = wring[:, (setid * 4 + i) * 512
                                     + j_local * 128:
                                     (setid * 4 + i) * 512
                                     + (j_local + 1) * 128]
                        v = kb.mm(slot.ap, lhsT, rhs_fn(k, tt), start,
                                  stop, waits=w, inc=(stop or last_of_pass))
                        first_of_pass = False
                        first_mm_of_proj = False
                        if v:
                            lastv = v
                        if stop:
                            slot.mm_done = v
            if lastv:
                kb.wring_lastmm[setid] = {kb.MM: lastv}
            if idx + 2 < len(passes):
                plan_load(idx + 2)
            if kp == nkp - 1:
                if ntt == 2:
                    for jl in range(4):
                        j = jb * 4 + jl
                        base = jl * 2
                        sl0, sl1 = kb.slots[base], kb.slots[base + 1]
                        pt = PA if base < 4 else PB
                        pap = pt[:, (base % 4) * 512:(base % 4) * 512 + 1024]
                        mmw = {kb.MM: max(sl0.mm_done, sl1.mm_done)}
                        done = evac_pair_fn(j, None, pap, mmw)
                        sl0.last_evac = done
                        sl1.last_evac = done
                else:
                    for jl in (0, 2):
                        base = jl if slots4 else (jb % 2) * 4 + jl
                        sl0, sl1 = kb.slots[base], kb.slots[base + 1]
                        pt = PA if base < 4 else PB
                        pap = pt[:, (base % 4) * 512:(base % 4) * 512 + 1024]
                        mmw = {kb.MM: max(sl0.mm_done, sl1.mm_done)}
                        done = evac_pair_fn(jb * 4 + jl, jb * 4 + jl + 1,
                                            pap, mmw)
                        sl0.last_evac = done
                        sl1.last_evac = done
        for b in rhs_bufs:
            b.used(kb.MM, kb.MM.n)

    def ln_stats(x_ap_fn, x_buf, th, sqp, sq_gate, x_ready_g=None):
        ntt = th // 512
        sqv = 0
        for tt in range(ntt):
            for g in range(G):
                idx = (g * ntt + tt) * 512
                w = wmerge(x_ready_g[g] if x_ready_g else x_buf.ready,
                           CONSTS,
                           sq_gate if (tt == 0 and g == 0) else None)
                sqv = kb.act(sqp[:, idx:idx + 512], x_ap_fn(g, tt),
                             AF.Square, waits=w)
        sx, sq = kb.slots[4], kb.slots[5]
        for tt in range(ntt):
            for g in range(G):
                idx = (g * ntt + tt) * 512
                w0 = wmerge(x_buf.ready, sx.last_evac if g == 0 else None,
                            CONSTS)
                kb.mm(sx.ap[0:1, :], onesc[:], x_ap_fn(g, tt),
                      start=(g == 0), stop=(g == G - 1), waits=w0, inc=True)
                w1 = wmerge({kb.SC: sqv} if (tt == 0 and g == 0) else None,
                            sq.last_evac if g == 0 else None)
                v = kb.mm(sq.ap[0:1, :], onesc[:], sqp[:, idx:idx + 512],
                          start=(g == 0), stop=(g == G - 1), waits=w1,
                          inc=True)
                sq.mm_done = v
            sxv = kb.MM.n - 1
            c = slice(tt * 512, (tt + 1) * 512)
            v1 = kb.scop(lambda e, c=c: e.mul(stats_m[0:1, c], sx.ap[0:1, :],
                                              1.0 / D), waits={kb.MM: sxv})
            v2 = kb.scop(lambda e, c=c: e.mul(stats_t[0:1, c], sq.ap[0:1, :],
                                              1.0 / D), waits={kb.MM: kb.MM.n})
            sx.last_evac = {kb.SC: v1}
            sq.last_evac = {kb.SC: v2}
            c2 = slice(1024 + c.start, 1024 + c.stop)
            kb.vec(lambda e, c=c, c2=c2: e.tensor_tensor(
                stats_t[0:1, c2], stats_m[0:1, c], stats_m[0:1, c], ALU.mult),
                waits={kb.SC: v2})
            v4 = kb.vec(lambda e, c=c, c2=c2: e.tensor_tensor(
                stats_t[0:1, c], stats_t[0:1, c], stats_t[0:1, c2],
                ALU.subtract))
            v5 = kb.scop(lambda e, c=c: e.activation(
                out=stats_t[0:1, c], in_=stats_t[0:1, c], func=AF.Sqrt,
                bias=epsc[0:1, :]), waits={kb.VE: v4})
            kb.vec(lambda e, c=c: e.reciprocal(stats_r[0:1, c],
                                               stats_t[0:1, c]),
                   waits={kb.SC: v5})
        x_buf.used(kb.MM, kb.MM.n)
        for row, srct, dstt, bufb in ((0, stats_m, mbt, B_mb),
                                      (1, stats_r, rbt, B_rb)):
            bw = bufb.begin_write()
            for tt in range(ntt):
                sl = kb.slots[6 + row]
                c = slice(tt * 512, (tt + 1) * 512)
                w = wmerge({kb.VE: kb.VE.n, kb.SC: kb.SC.n}, sl.last_evac,
                           bw, CONSTS)
                kb.mm(sl.ap, onesk[:], srct[0:1, c], True, True,
                      waits=w, inc=True)
                vv = kb.scop(lambda e, sl=sl, dstt=dstt, c=c:
                             e.copy(dstt[:, c], sl.ap),
                             waits={kb.MM: kb.MM.n})
                sl.last_evac = {kb.SC: vv}
                bufb.set_ready(kb.SC, vv)

    def pslice(pool, g, tt=None):
        if tt is None:
            return pool[:, g * TH:(g + 1) * TH]
        return pool[:, g * TH + tt * 512: g * TH + (tt + 1) * 512]

    def hslab(hb, g, h, tt=None):
        off = hb.offset + g * 128 * T + h * TH
        if tt is None:
            return bass.AP(hb.tensor, off, [[T, 128], [1, TH]])
        return bass.AP(hb.tensor, off + tt * 512, [[T, 128], [1, 512]])

    def tap(name, src_pool, h, buf):
        if name in tap_out:
            for g in range(G):
                v = kb.store(hslab(tap_out[name], g, h),
                             pslice(src_pool, g), waits=buf.ready)
                buf.used_d(v)

    def plan_ln1(h, xr, sqp, sqb):
        ln_stats(lambda g, tt: pslice(pool1, g, tt), B_p1, TH,
                 sqp, sqb.begin_write(), x_ready_g=xr)
        sqb.set_ready(kb.MM, kb.MM.n)
        xn_rg = []
        for g in range(G):
            w = wmerge(B_mb.ready, B_rb.ready, CONSTS, B_p1.last_use,
                       {kb.MM: kb.MM.n} if g == 0 else None)
            kb.vec(lambda e, g=g: e.tensor_tensor(
                pslice(pool1, g), pslice(pool1, g), mbt[:], ALU.subtract),
                waits=w)
            kb.vec(lambda e, g=g: e.tensor_tensor(
                pslice(pool1, g), pslice(pool1, g), rbt[:], ALU.mult))
            vv = kb.vec(lambda e, g=g: e.tensor_scalar(
                pslice(pool1, g), pslice(pool1, g),
                bv("norm_g", g), bv("norm_b", g), ALU.mult, ALU.add))
            xn_rg.append({kb.VE: vv})
        B_p1.ready = {kb.VE: kb.VE.n}
        B_mb.used(kb.VE, kb.VE.n)
        B_rb.used(kb.VE, kb.VE.n)
        tap("xn", pool1, h, B_p1)
        return xn_rg

    xg_ready = {}
    prefetched = [None]
    pre_ln1 = [None]
    for h in range(2):
        if h == 1 and prefetched[0] is not None:
            B_p1.ready = prefetched[0]
            xr = xg_ready[1]
        else:
            gate = B_p1.begin_write()
            xr = []
            for g in range(G):
                lv = kb.load(pslice(pool1, g), hslab(x_fm, g, h),
                             waits=gate if g == 0 else None)
                xr.append(lv)
            B_p1.set_ready_d(kb.lw_all())
            xg_ready[h] = xr

        if h == 1 and pre_ln1[0] is not None:
            xn_rg = pre_ln1[0]
        else:
            xn_rg = plan_ln1(h, xr, pool4, B_p4)

        p2gate = B_p2.begin_write()
        p3gate = B_p3.begin_write()

        low_rg, high_rg = [None] * G, [None] * G

        def fr_evac(j, _u, pap, mmw):
            w = wmerge(mmw, B_tb.last_use)
            sv = kb.act(tb, pap, AF.Sigmoid, bias=bv("fr_b", j), waits=w)
            B_tb.set_ready(kb.SC, sv)
            v1 = kb.vec(lambda e, j=j: e.tensor_tensor(
                pslice(pool2, j), pslice(pool1, j), tb, ALU.mult),
                waits=wmerge({kb.SC: sv}, p2gate if j == 0 else None))
            v2 = kb.vec(lambda e, j=j: e.tensor_tensor(
                pslice(pool3, j), pslice(pool1, j), pslice(pool2, j),
                ALU.subtract),
                waits=p3gate if j == 0 else None)
            B_tb.used(kb.VE, v1)
            B_p2.set_ready(kb.VE, v2)
            B_p3.set_ready(kb.VE, v2)
            low_rg[j] = {kb.VE: v1}
            high_rg[j] = {kb.VE: v2}
            return {kb.VE: v1}

        proj(fr_w, 8, 1024, lambda k, tt: pslice(pool1, k, tt), [B_p1],
             fr_evac, ntt=2, rhs_ready_k=xn_rg)
        tap("low", pool2, h, B_p2)
        tap("high", pool3, h, B_p3)

        c_rg = [None] * G
        p4gate_c = B_p4.begin_write()

        def cp_evac(j, _u, pap, mmw, p4gate_c=p4gate_c):
            w = wmerge(mmw, p4gate_c if j == 0 else None)
            sv = kb.act(pslice(pool4, j), pap, AF.Tanh, bias=bv("cp_b", j),
                        waits=w)
            B_p4.set_ready(kb.SC, sv)
            c_rg[j] = {kb.SC: sv}
            return {kb.SC: sv}

        proj(cp_w, 8, 1024, lambda k, tt: pslice(pool3, k, tt), [B_p3],
             cp_evac, ntt=2, rhs_ready_k=high_rg)

        p3gate2 = B_p3.begin_write()

        def dp_evac(j, _u, pap, mmw):
            w = wmerge(mmw, p3gate2 if j == 0 else None)
            sv = kb.act(pslice(pool3, j), pap, AF.Sigmoid,
                        bias=bv("dp_b", j), waits=w)
            B_p3.set_ready(kb.SC, sv)
            return {kb.SC: sv}

        proj(dp_w, 8, 1024, lambda k, tt: pslice(pool2, k, tt), [B_p2],
             dp_evac, ntt=2, rhs_ready_k=low_rg)
        tap("a", pool3, h, B_p3)

        def bp_evac(g, _u, pap, mmw, h=h, c_rg=c_rg):
            i = g % 2
            w = wmerge(mmw, B_tb.last_use)
            sv = kb.act(tb, pap, AF.Tanh, bias=bv("bp_b", g), waits=w)
            B_tb.set_ready(kb.SC, sv)
            w2 = wmerge({kb.SC: sv}, udb_use[i], B_p3.ready)
            uv = kb.vec(lambda e, g=g, i=i: e.tensor_tensor(
                udb[i], tb, pslice(pool1, g), ALU.mult), waits=w2)
            B_tb.used(kb.VE, uv)
            w3 = wmerge({kb.VE: uv}, stdb_use[i])
            stv = kb.vec(lambda e, g=g, i=i: e.tensor_tensor_scan(
                stdb[i], pslice(pool3, g), udb[i], Sinit[:, g:g + 1],
                ALU.mult, ALU.add), waits=w3)
            kb.vec(lambda e, g=g, i=i: e.tensor_copy(
                Sinit[:, g:g + 1], stdb[i][:, TH - 1:TH]))
            vm = kb.vec(lambda e, g=g, i=i: e.tensor_tensor(
                stdb[i], stdb[i], pslice(pool4, g), ALU.mult),
                waits=c_rg[g])
            B_p4.used(kb.VE, vm)
            vs = kb.vec(lambda e, g=g, i=i: e.scalar_tensor_tensor(
                stdb[i], pslice(pool1, g), bv("skip", g), stdb[i],
                ALU.mult, ALU.add))
            B_p1.used(kb.VE, vs)
            spv = kb.store(hslab(st_hbm, g, h), stdb[i],
                           waits={kb.VE: kb.VE.n})
            stdb_use[i] = spv
            B_sthbm.set_ready_d(spv)
            zv = kb.vec(lambda e: e.memset(tb, 0.0),
                        waits=wmerge({kb.VE: stv}, B_tb.last_use))
            B_tb.set_ready(kb.VE, zv)
            av = kb.vec(lambda e, g=g, i=i: e.tensor_tensor_scan(
                udb[i], pslice(pool3, g), tb, Ainit[:, g:g + 1],
                ALU.mult, ALU.add), waits={kb.VE: zv})
            B_p3.used(kb.VE, av)
            B_tb.used(kb.VE, av)
            kb.vec(lambda e, g=g, i=i: e.tensor_copy(
                Ainit[:, g:g + 1], udb[i][:, TH - 1:TH]))
            vf = kb.vec(lambda e, g=g, i=i: e.tensor_tensor(
                udb[i], udb[i], pslice(pool4, g), ALU.mult))
            B_p4.used(kb.VE, vf)
            spv2 = kb.store(hslab(A_hbm, g, h), udb[i],
                            waits={kb.VE: kb.VE.n})
            udb_use[i] = spv2
            B_Ahbm.set_ready_d(spv2)
            return {kb.VE: av}

        proj(bp_w, 8, 1024, lambda k, tt: pslice(pool2, k, tt), [B_p2],
             bp_evac, ntt=2, rhs_ready_k=low_rg)

        if h == 0:
            gate = B_p1.begin_write()
            xg_ready[1] = []
            for g in range(G):
                lv = kb.load(pslice(pool1, g), hslab(x_fm, g, 1),
                             waits=gate if g == 0 else None)
                xg_ready[1].append(lv)
            B_p1.set_ready_d(kb.lw_all())
            prefetched[0] = dict(B_p1.ready)
            pre_ln1[0] = plan_ln1(1, xg_ready[1], pool3, B_p3)

        for tt in range(2):
            h1_rg = [None] * 16
            p4gate = B_p4.begin_write()

            def m1_evac(j0, j1, pap, mmw, p4gate=p4gate):
                sv = None
                for jj, off in ((j0, 0), (j1, 512)):
                    w = wmerge(mmw if off == 0 else None,
                               p4gate if jj == 0 else None)
                    sv = kb.act(pool4[:, jj * 512:(jj + 1) * 512],
                                pap[:, off:off + 512], AF.Gelu,
                                bias=bv("m1_b", jj), waits=w)
                    h1_rg[jj] = {kb.SC: sv}
                B_p4.set_ready(kb.SC, sv)
                return {kb.SC: sv}

            proj(m1_w, 8, 2048, lambda k, _t, tt=tt: pslice(pool2, k, tt),
                 [B_p2], m1_evac, ntt=1, rhs_ready_k=low_rg, slots4=TEST_SLOTS4)

            def m2_evac(j0, j1, pap, mmw, tt=tt, h=h):
                svs = []
                for jj, off in ((j0, 0), (j1, 512)):
                    i = jj % 2
                    w = wmerge(mmw if off == 0 else None, B_tb.last_use)
                    sv = kb.scop(lambda e, jj=jj, off=off: e.add(
                        tbv[jj % 2], pap[:, off:off + 512], bv("m2_b", jj)),
                        waits=w)
                    B_tb.set_ready(kb.SC, sv)
                    spv = kb.store(hslab(mo_hbm, jj, h, tt), tbv[jj % 2],
                                   waits={kb.SC: sv})
                    B_tb.used_d(spv)
                    svs.append(sv)
                B_mohbm.set_ready_d(kb.sp_all())
                return {kb.SC: max(svs)}

            proj(m2_w, 16, 1024,
                 lambda k, _t: pool4[:, k * 512:(k + 1) * 512], [B_p4],
                 m2_evac, ntt=1, rhs_ready_k=h1_rg, slots4=TEST_SLOTS4)

    gv = kb.store(gin_h.ap(), Sinit[:], waits={kb.VE: kb.VE.n})
    ccv = kb.op("gpsimd", lambda e: e.collective_compute(
        "AllGather", ALU.bypass, replica_groups=[list(range(8))],
        ins=[gin_h.ap().opt()], outs=[gout_h.ap().opt()]),
        waits=gv, inc=kb.CC)
    glv = kb.gdma(gath[:],
                  bass.AP(gout_h, 0, [[8, 128], [1024, 8], [1, 8]]),
                  waits={kb.CC: ccv})
    cv1 = kb.vec(lambda e: e.tensor_tensor(
        bass.AP(gath, 0, [[64, 128], [1, 8], [8, 8]]),
        bass.AP(gath, 0, [[64, 128], [1, 8], [8, 8]]),
        bass.AP(selt, 0, [[8, 128], [0, 8], [1, 8]]), ALU.mult),
        waits=wmerge(glv, CONSTS))
    cv2 = kb.vec(lambda e: e.tensor_reduce(
        carry[:], bass.AP(gath, 0, [[64, 128], [1, 8], [8, 8]]),
        AX.X, ALU.add), waits={kb.VE: cv1})
    carry_ready = {kb.VE: cv2}

    defer_x2tm = []
    for h in range(2):
        g2 = B_p2.begin_write()
        g1 = B_p1.begin_write()
        g3 = B_p3.begin_write()
        mo_rg, sl_rg, f2_rg = [], [], []
        for g in range(G):
            lvm = kb.load(pslice(pool2, g), hslab(mo_hbm, g, h),
                          waits=wmerge(B_mohbm.ready, g2) if g == 0 else None)
            mo_rg.append(lvm)
            lsl = kb.load(pslice(pool1, g), hslab(st_hbm, g, h),
                          waits=wmerge(B_sthbm.ready, g1) if g == 0 else None)
            sl_rg.append(lsl)
            lf2 = kb.gdma(pslice(pool3, g), hslab(A_hbm, g, h),
                          waits=wmerge(B_Ahbm.ready, g3) if g == 0 else None)
            f2_rg.append(lf2)
        if h == 0:
            B_mohbm.ready = {}
            B_sthbm.ready = {}
            B_Ahbm.ready = {}
        B_p2.set_ready_d(kb.lw_all())
        B_p1.set_ready_d(kb.lw_all())
        B_p3.set_ready_d(kb.sp_all())

        ssd_rg = []
        for g in range(G):
            sv = kb.vec(lambda e, g=g: e.scalar_tensor_tensor(
                pslice(pool1, g), pslice(pool3, g), carry[:, g:g + 1],
                pslice(pool1, g), ALU.mult, ALU.add),
                waits=wmerge(sl_rg[g], f2_rg[g], carry_ready))
            ssd_rg.append({kb.VE: sv})
            B_p1.set_ready(kb.VE, sv)
            B_p3.used(kb.VE, sv)
        tap("ssd", pool1, h, B_p1)

        g3b = B_p3.begin_write()

        def gate_evac(j, _u, pap, mmw, g3b=g3b):
            w = wmerge(mmw, g3b if j == 0 else None)
            sv = kb.act(pslice(pool3, j), pap, AF.Sigmoid,
                        bias=bv("gate_b", j), waits=w)
            B_p3.set_ready(kb.SC, sv)
            return {kb.SC: sv}

        gate_rk = [ssd_rg[k] if k < 8 else mo_rg[k - 8] for k in range(16)]
        proj(gate_w, 16, 1024,
             lambda k, tt: pslice(pool1, k, tt) if k < 8
             else pslice(pool2, k - 8, tt),
             [B_p1, B_p2], gate_evac, ntt=2, rhs_ready_k=gate_rk,
             korder=[8, 9, 10, 11, 12, 13, 14, 15, 0, 1, 2, 3, 4, 5, 6, 7])

        for g in range(G):
            w = wmerge(B_p3.ready, B_tb.last_use,
                       B_p1.last_use if g == 0 else None)
            v0 = kb.vec(lambda e, g=g: e.tensor_tensor(
                tb, pslice(pool1, g), pslice(pool2, g), ALU.subtract),
                waits=w)
            B_tb.set_ready(kb.VE, v0)
            kb.vec(lambda e, g=g: e.tensor_tensor(
                tb, tb, pslice(pool3, g), ALU.mult))
            v = kb.vec(lambda e, g=g: e.tensor_tensor(
                pslice(pool1, g), tb, pslice(pool2, g), ALU.add))
            B_tb.used(kb.VE, v)
        B_p1.ready = {kb.VE: kb.VE.n}
        B_p2.used(kb.VE, kb.VE.n)
        B_p3.used(kb.VE, kb.VE.n)

        for g in range(G):
            i = g % 2
            lv = kb.load(stdb[i], hslab(x_fm, g, h), waits=stdb_use[i])
            v = kb.vec(lambda e, g=g, i=i: e.tensor_tensor(
                pslice(pool1, g), pslice(pool1, g), stdb[i], ALU.add),
                waits=wmerge(lv, B_p1.last_use if g == 0 else None))
            stdb_use[i] = {kb.VE: v}
        B_p1.ready = {kb.VE: kb.VE.n}
        tap("y", pool1, h, B_p1)

        ln_stats(lambda g, tt: pslice(pool1, g, tt), B_p1, TH,
                 pool4, B_p4.begin_write())
        B_p4.set_ready(kb.MM, kb.MM.n)
        p2g = B_p2.begin_write()
        yn_rg = []
        for g in range(G):
            w = wmerge(B_mb.ready, B_rb.ready, p2g if g == 0 else None)
            kb.vec(lambda e, g=g: e.tensor_tensor(
                pslice(pool2, g), pslice(pool1, g), mbt[:], ALU.subtract),
                waits=w)
            kb.vec(lambda e, g=g: e.tensor_tensor(
                pslice(pool2, g), pslice(pool2, g), rbt[:], ALU.mult))
            vv = kb.vec(lambda e, g=g: e.tensor_scalar(
                pslice(pool2, g), pslice(pool2, g),
                bv("ffn_g", g), bv("ffn_bn", g), ALU.mult, ALU.add))
            yn_rg.append({kb.VE: vv})
        B_p2.ready = {kb.VE: kb.VE.n}
        B_mb.used(kb.VE, kb.VE.n)
        B_rb.used(kb.VE, kb.VE.n)

        for tt in range(2):
            h2_rg = [None] * 32
            p34g = wmerge(B_p3.begin_write(), B_p4.begin_write())

            def f1_evac(j0, j1, pap, mmw, p34g=p34g):
                sv = None
                for jj, off in ((j0, 0), (j1, 512)):
                    dst = (pool3[:, jj * 512:(jj + 1) * 512] if jj < 16
                           else pool4[:, (jj - 16) * 512:(jj - 15) * 512])
                    w = wmerge(mmw if off == 0 else None,
                               p34g if jj == 0 else None)
                    sv = kb.act(dst, pap[:, off:off + 512], AF.Gelu,
                                bias=bv("ffn1_b", jj), waits=w)
                    h2_rg[jj] = {kb.SC: sv}
                B_p3.set_ready(kb.SC, sv)
                B_p4.set_ready(kb.SC, sv)
                return {kb.SC: sv}

            proj(ffn1_w, 8, 4096, lambda k, _t, tt=tt: pslice(pool2, k, tt),
                 [B_p2], f1_evac, ntt=1, rhs_ready_k=yn_rg)

            def f2_evac(j0, j1, pap, mmw, tt=tt):
                vv = None
                for jj, off in ((j0, 0), (j1, 512)):
                    w = wmerge(mmw, B_p1.last_use) if off == 0 else None
                    vv = kb.vec(lambda e, jj=jj, off=off, tt=tt:
                                e.scalar_tensor_tensor(
                                    pslice(pool1, jj, tt),
                                    pap[:, off:off + 512], bv("ffn2_b", jj),
                                    pslice(pool1, jj, tt), ALU.add, ALU.add),
                                waits=w)
                return {kb.VE: vv}

            proj(ffn2_w, 32, 1024,
                 lambda k, _t: (pool3[:, k * 512:(k + 1) * 512] if k < 16
                                else pool4[:, (k - 16) * 512:(k - 15) * 512]),
                 [B_p3, B_p4], f2_evac, ntt=1, rhs_ready_k=h2_rg)
        B_p1.ready = {kb.VE: kb.VE.n}

        for g in range(G):
            i = g % 2
            lv = kb.load(stdb[i], hslab(x_fm, g, h), waits=stdb_use[i])
            v = kb.vec(lambda e, g=g, i=i: e.tensor_tensor(
                pslice(pool1, g), pslice(pool1, g), stdb[i], ALU.add),
                waits=wmerge(lv, B_p1.last_use if g == 0 else None))
            stdb_use[i] = {kb.VE: v}
        B_p1.ready = {kb.VE: kb.VE.n}
        tap("y3", pool1, h, B_p1)

        ln_stats(lambda g, tt: pslice(pool1, g, tt), B_p1, TH,
                 pool4, B_p4.begin_write())
        B_p4.set_ready(kb.MM, kb.MM.n)
        for g in range(G):
            w = wmerge(B_mb.ready, B_rb.ready,
                       B_p1.last_use if g == 0 else None)
            kb.vec(lambda e, g=g: e.tensor_tensor(
                pslice(pool1, g), pslice(pool1, g), mbt[:], ALU.subtract),
                waits=w)
            v = kb.vec(lambda e, g=g: e.tensor_tensor(
                pslice(pool1, g), pslice(pool1, g), rbt[:], ALU.mult))
        B_p1.ready = {kb.VE: kb.VE.n}
        B_mb.used(kb.VE, kb.VE.n)
        B_rb.used(kb.VE, kb.VE.n)
        tap("x2", pool1, h, B_p1)
        p4g2 = B_p4.begin_write()
        st_prev = [{} for _ in range(4)]
        def x2tm_transposes(h=h, p4g2=p4g2, st_prev=st_prev):
          for g in range(G):
            for tc in range(8):
                tsl = kb.slots[6 + ((g * 8 + tc) % 2)]
                w = wmerge(B_p1.ready if (g == 0 and tc == 0) else None,
                           tsl.last_evac, CONSTS)
                mv = kb.op("tensor", lambda e, g=g, tc=tc, tsl=tsl:
                           e.transpose(tsl.ap[:, 0:128],
                                       pool1[:, g * TH + tc * 128:
                                             g * TH + (tc + 1) * 128],
                                       ident[:]),
                           waits=w, inc=kb.MM)
                b = (g * 8 + tc) % 4
                cv = kb.scop(lambda e, tsl=tsl, b=b: e.copy(
                    pool4[:, b * 128:(b + 1) * 128], tsl.ap[:, 0:128]),
                    waits=wmerge({kb.MM: mv}, st_prev[b],
                                 p4g2 if (g == 0 and tc == 0) else None))
                tsl.last_evac = {kb.SC: cv}
                spv = kb.store(
                    bass.AP(x2tm.tensor,
                            x2tm.offset + (h * 1024 + tc * 128) * 1032
                            + g * 128,
                            [[1032, 128], [1, 128]]),
                    pool4[:, b * 128:(b + 1) * 128], waits={kb.SC: cv})
                st_prev[b] = spv
                B_p4.used_d(spv)
                B_x2tm.set_ready_d(spv)
          B_p4.set_ready(kb.SC, kb.SC.n)
          B_p1.used(kb.MM, kb.MM.n)
        if h == 0:
            x2tm_transposes()
        else:
            defer_x2tm.append(x2tm_transposes)

        lsl = kb.slots[4]
        for tc in range(8):
            for k in range(G):
                w = wmerge(B_p1.ready if (tc == 0 and k == 0) else None,
                           lsl.last_evac if k == 0 else None, CONSTS)
                lhsT = pool1[:, k * TH + tc * 128: k * TH + (tc + 1) * 128]
                kb.mm(lsl.ap[0:128, 0:8], lhsT, gwTt[:, k * 8:(k + 1) * 8],
                      start=(k == 0), stop=(k == G - 1), waits=w,
                      inc=(k == G - 1))
            j = h * 8 + tc
            sv = kb.scop(lambda e, j=j: e.copy(probs[:, j * 8:(j + 1) * 8],
                                               lsl.ap[0:128, 0:8]),
                         waits={kb.MM: kb.MM.n})
            lsl.last_evac = {kb.SC: sv}
        B_p1.used(kb.MM, kb.MM.n)
    B_probs.set_ready(kb.SC, kb.SC.n)

    pv = bass.AP(probs, 0, [[128, 128], [8, 16], [1, 8]])
    SCRSTR = 5280
    gt4 = lambda: bass.AP(scr, 0, [[SCRSTR, 128], [64, 16], [8, 8], [1, 8]])
    w44 = lambda: bass.AP(scr, 1024, [[SCRSTR, 128], [64, 16], [8, 8], [1, 8]])
    gt128 = scr[:, 0:128]
    w128 = scr[:, 1024:1152]
    cnt8 = scr[:, 1152:1160]

    v = kb.vec(lambda e: e.tensor_reduce(
        bass.AP(bis, 32, [[96, 128], [1, 16]]), pv, AX.X, ALU.max),
        waits=wmerge({kb.SC: kb.SC.n}, B_probs.ready))
    v = kb.vec(lambda e: e.tensor_tensor(
        pv, pv, bass.AP(bis, 32, [[96, 128], [1, 16], [0, 8]]),
        ALU.subtract))
    v = kb.scop(lambda e: e.activation(out=probs[:], in_=probs[:],
                                       func=AF.Exp),
                waits={kb.VE: kb.VE.n})
    v = kb.vec(lambda e: e.tensor_reduce(
        bass.AP(bis, 32, [[96, 128], [1, 16]]), pv, AX.X, ALU.add),
        waits={kb.SC: v})
    v = kb.vec(lambda e: e.reciprocal(bass.AP(bis, 48, [[96, 128], [1, 16]]),
                                      bass.AP(bis, 32, [[96, 128], [1, 16]])))
    v = kb.vec(lambda e: e.tensor_tensor(
        pv, pv, bass.AP(bis, 48, [[96, 128], [1, 16], [0, 8]]), ALU.mult))
    v = kb.vec(lambda e: e.tensor_tensor(
        gt4(),
        bass.AP(probs, 0, [[128, 128], [8, 16], [0, 8], [1, 8]]),
        bass.AP(probs, 0, [[128, 128], [8, 16], [1, 8], [0, 8]]),
        ALU.is_gt), waits=wmerge({kb.VE: v}, B_tb.last_use, stdb_use[0],
                                 stdb_use[1], udb_use[0], udb_use[1]))
    v = kb.vec(lambda e: e.tensor_tensor(
        w44(), gt4(),
        bass.AP(probs, 0, [[128, 128], [8, 16], [0, 8], [1, 8]]), ALU.mult))
    v = kb.vec(lambda e: e.tensor_reduce(
        bass.AP(s_sb, 0, [[128, 128], [8, 16], [1, 8]]), w44(), AX.X,
        ALU.add))
    v = kb.vec(lambda e: e.tensor_reduce(
        bass.AP(scr, 1024, [[SCRSTR, 128], [8, 16], [1, 8]]), gt4(), AX.X,
        ALU.add))
    v = kb.vec(lambda e: e.tensor_single_scalar(assn[:], s_sb[:], THR,
                                                ALU.is_lt))
    v = kb.vec(lambda e: e.tensor_single_scalar(gt128, probs[:], 1.0,
                                                ALU.subtract))
    v = kb.vec(lambda e: e.tensor_tensor(gt128, gt128, w128, ALU.subtract))
    kb.vec(lambda e: e.tensor_scalar(w128, assn[:], 1e9, -1e9,
                                     ALU.mult, ALU.add))
    kb.vec(lambda e: e.tensor_tensor(gt128, gt128, assn[:], ALU.mult))
    rv = kb.vec(lambda e: e.tensor_tensor(s_sb[:], gt128, w128, ALU.add))
    rtapv = {}
    if "sm_R" in small_taps:
        rtapv = kb.store(small_taps["sm_R"], s_sb[:], waits={kb.VE: rv})
    rsp = kb.store(rin_h.ap(), s_sb[:], waits={kb.VE: rv})
    ccv2 = kb.op("gpsimd", lambda e: e.collective_compute(
        "AllGather", ALU.bypass, replica_groups=[list(range(8))],
        ins=[rin_h.ap().opt()], outs=[rout_h.ap().opt()]),
        waits=wmerge(rsp, {kb.CC: 1}), inc=kb.CC)
    for fn_ in defer_x2tm:
        fn_()
    sbcg = B_sbc.begin_write()
    rlv = kb.gdma(sbc[:, 0:1024],
                  bass.AP(rout_h, 0, [[128, 128], [16384, 8], [1, 128]]),
                  waits=wmerge({kb.CC: ccv2}, sbcg))
    B_sbc.set_ready_d(rlv)

    kb.vec(lambda e: e.memset(bis[:, 24:32], -8.5), waits=rlv)
    kb.vec(lambda e: e.memset(bis[:, 32:40], 0.5))
    kb.vec(lambda e: e.memset(bis[:, 40:48], -2e9))
    kb.vec(lambda e: e.memset(bis[:, 48:56], -9e8))
    kb.vec(lambda e: e.tensor_copy(bis[:, 16:24], bis[:, 24:32]))

    Rall = lambda: bass.AP(sbc, 0, [[2048, 128], [1, 8], [128, 8], [8, 16]])
    gtb = lambda: bass.AP(scr, 0, [[SCRSTR, 128], [1, 8], [128, 8], [8, 16]])
    midb = lambda: bass.AP(bis, 16, [[96, 128], [1, 8], [0, 8], [0, 16]])
    bsl = kb.slots[6]

    def bis_round(first):
        v1 = kb.vec(lambda e: e.tensor_tensor(gtb(), Rall(), midb(),
                                              ALU.is_ge),
                    waits=wmerge({kb.VE: kb.VE.n}, B_sbc.ready))
        v2 = kb.vec(lambda e: e.tensor_reduce(
            bass.AP(scr, 0, [[SCRSTR, 128], [1, 8], [128, 8]]), gtb(),
            AX.X, ALU.add))
        v3 = kb.vec(lambda e: e.tensor_reduce(
            bass.AP(scr, 1152, [[SCRSTR, 128], [1, 8]]),
            bass.AP(scr, 0, [[SCRSTR, 128], [1, 8], [128, 8]]),
            AX.X, ALU.add))
        mv = kb.mm(bsl.ap[0:128, 0:8], ones128[:], cnt8, True, True,
                   waits=wmerge({kb.VE: v3}, bsl.last_evac), inc=True)
        v4 = kb.vec(lambda e: e.tensor_single_scalar(
            bis[:, 56:64], bsl.ap[0:128, 0:8], CAP, ALU.is_ge),
            waits={kb.MM: mv})
        bsl.last_evac = {kb.VE: v4}
        kb.vec(lambda e: e.tensor_scalar(bis[:, 64:72], bis[:, 56:64],
                                         -1.0, 1.0, ALU.mult, ALU.add))
        if first:
            kb.vec(lambda e: e.tensor_single_scalar(bis[:, 72:80],
                                                    bis[:, 56:64], -8.5,
                                                    ALU.mult))
            kb.vec(lambda e: e.tensor_single_scalar(bis[:, 80:88],
                                                    bis[:, 64:72], -2e9,
                                                    ALU.mult))
            kb.vec(lambda e: e.tensor_tensor(bis[:, 0:8], bis[:, 72:80],
                                             bis[:, 80:88], ALU.add))
            kb.vec(lambda e: e.tensor_single_scalar(bis[:, 72:80],
                                                    bis[:, 56:64], 0.5,
                                                    ALU.mult))
            kb.vec(lambda e: e.tensor_single_scalar(bis[:, 80:88],
                                                    bis[:, 64:72], -9e8,
                                                    ALU.mult))
            kb.vec(lambda e: e.tensor_tensor(bis[:, 8:16], bis[:, 72:80],
                                             bis[:, 80:88], ALU.add))
        else:
            kb.vec(lambda e: e.tensor_tensor(bis[:, 72:80], bis[:, 16:24],
                                             bis[:, 56:64], ALU.mult))
            kb.vec(lambda e: e.tensor_tensor(bis[:, 80:88], bis[:, 0:8],
                                             bis[:, 64:72], ALU.mult))
            kb.vec(lambda e: e.tensor_tensor(bis[:, 0:8], bis[:, 72:80],
                                             bis[:, 80:88], ALU.add))
            kb.vec(lambda e: e.tensor_tensor(bis[:, 72:80], bis[:, 8:16],
                                             bis[:, 56:64], ALU.mult))
            kb.vec(lambda e: e.tensor_tensor(bis[:, 80:88], bis[:, 16:24],
                                             bis[:, 64:72], ALU.mult))
            kb.vec(lambda e: e.tensor_tensor(bis[:, 8:16], bis[:, 72:80],
                                             bis[:, 80:88], ALU.add))
        kb.vec(lambda e: e.tensor_tensor(bis[:, 16:24], bis[:, 0:8],
                                         bis[:, 8:16], ALU.add))
        kb.vec(lambda e: e.tensor_single_scalar(bis[:, 16:24], bis[:, 16:24],
                                                0.5, ALU.mult))

    bis_round(True)
    for _ in range(41):
        bis_round(False)
    B_sbc.used(kb.VE, kb.VE.n)

    v = kb.vec(lambda e: e.tensor_tensor(
        bass.AP(scr, 0, [[SCRSTR, 128], [8, 16], [1, 8]]),
        bass.AP(s_sb, 0, [[128, 128], [8, 16], [1, 8]]),
        bass.AP(bis, 0, [[96, 128], [0, 16], [1, 8]]), ALU.is_ge),
        waits={kb.VE: kb.VE.n})
    v = kb.vec(lambda e: e.tensor_tensor(s_sb[:], probs[:], gt128,
                                         ALU.mult),
               waits=wmerge(rtapv, rsp))
    sv_done = kb.vec(lambda e: e.tensor_tensor(s_sb[:], s_sb[:], assn[:],
                                               ALU.mult))
    ssp = kb.store(bass.AP(x2tm.tensor, x2tm.offset + 1024,
                           [[1032, 128], [1032 * 128, 16], [1, 8]]),
                   s_sb[:], waits={kb.VE: sv_done})
    B_x2tm.set_ready_d(ssp)

    m8 = scr[0:8, 1024:3072]
    p8 = scr[0:8, 3072:5120]
    for j in range(16):
        tsl = kb.slots[6 + (j % 2)]
        w = wmerge({kb.VE: sv_done} if j == 0 else None, tsl.last_evac,
                   CONSTS)
        mv = kb.op("tensor", lambda e, j=j, tsl=tsl: e.transpose(
            tsl.ap[0:8, 0:128], s_sb[:, j * 8:(j + 1) * 8], ident[:]),
            waits=w, inc=kb.MM)
        cv = kb.scop(lambda e, j=j, tsl=tsl: e.copy(
            m8[:, j * 128:(j + 1) * 128], tsl.ap[0:8, 0:128]),
            waits={kb.MM: mv})
        tsl.last_evac = {kb.SC: cv}
    v = kb.vec(lambda e: e.tensor_single_scalar(m8, m8, 0.0, ALU.is_gt),
               waits={kb.SC: kb.SC.n})
    p8v = kb.vec(lambda e: e.tensor_tensor_scan(p8, m8, m8, 0.0,
                                                ALU.add, ALU.bypass))
    kb.vec(lambda e: e.tensor_copy(mbt[0:8, :], p8[:, 0:1024]),
           waits=wmerge(B_mb.begin_write(), {kb.VE: p8v}))
    p8v = kb.vec(lambda e: e.tensor_copy(rbt[0:8, :], p8[:, 1024:2048]),
                 waits=B_rb.begin_write())
    pos_ready = {kb.VE: p8v}

    zv2 = kb.vec(lambda e: e.memset(tb, 0.0), waits=B_tb.last_use)
    B_tb.set_ready(kb.VE, zv2)
    for tc in range(16):
        sp = kb.store(bass.AP(ymtm.tensor, ymtm.offset + tc * 128 * D,
                              [[D, 128], [1, D]]), tb, waits={kb.VE: zv2})
        B_tb.used_d(sp)
        B_ymtm.set_ready_d(sp)
    sp = kb.store(bass.AP(ymtm.tensor, ymtm.offset + T * D, [[D, 4], [1, D]]),
                  tb[0:4, :], waits={kb.VE: zv2})
    B_tb.used_d(sp)
    B_ymtm.set_ready_d(sp)
    ymtm_zero = dict(B_ymtm.ready)
    sp = kb.store(bass.AP(x2tm.tensor, x2tm.offset + T * 1032,
                          [[1032, 4], [1, 1024]]), tb[0:4, :],
                  waits={kb.VE: zv2})
    B_tb.used_d(sp)
    B_x2tm.set_ready_d(sp)
    sp = kb.store(bass.AP(x2tm.tensor, x2tm.offset + T * 1032 + 1024,
                          [[1032, 4], [1, 8]]), tb[0:4, 0:8],
                  waits={kb.VE: zv2})
    B_tb.used_d(sp)
    B_x2tm.set_ready_d(sp)

    if "sm_probs" in small_taps:
        kb.store(small_taps["sm_probs"], probs[:], waits={kb.VE: kb.VE.n})
    if "sm_s" in small_taps:
        kb.store(small_taps["sm_s"], s_sb[:], waits={kb.VE: kb.VE.n})
    if "sm_bis" in small_taps:
        kb.store(small_taps["sm_bis"], bis[:], waits={kb.VE: kb.VE.n})

    B_p4A, B_p4B = Buf(), Buf()
    B_ytb = Buf()
    scrS = [Buf() for _ in range(4)]
    for b_ in scrS:
        b_.used(kb.VE, kb.VE.n)
        b_.used(kb.MM, kb.MM.n)
    w1bufs = {}
    w2bufs = {}
    tokr = {}
    xgb = {}

    def plan_w1_loads(ee):
        w1b = Buf()
        g1 = B_p1.begin_write()
        g2 = B_p2.begin_write()
        for k in range(G):
            pool = pool1 if k < 4 else pool2
            w = (g1 if k == 0 else (g2 if k == 4 else None))
            kb.load(pool[:, (k % 4) * 1024:(k % 4 + 1) * 1024],
                    bass.AP(w1T.tensor, w1T.offset + ee * D * F + k * 128 * F,
                            [[F, 128], [1, F]]), waits=w)
        w1b.set_ready_d(kb.lw_all())
        w1bufs[ee] = w1b

    def plan_w2_loads(ee):
        w2b = Buf()
        g3 = B_p3.begin_write()
        for k in range(G):
            kb.load(pool3[:, k * 1024:(k + 1) * 1024],
                    bass.AP(w2T.tensor, w2T.offset + ee * F * D + k * 128 * D,
                            [[D, 128], [1, D]]), waits=g3 if k == 0 else None)
        w2b.set_ready_d(kb.lw_all())
        w2bufs[ee] = w2b

    def plan_lists(ee):
        sgate = B_sbc.begin_write()
        for tt2 in range(4):
            tsl = kb.slots[6 + (tt2 % 2)]
            w = wmerge(pos_ready if tt2 == 0 else None, tsl.last_evac,
                       CONSTS)
            psrc = mbt if tt2 < 2 else rbt
            mv = kb.mm(tsl.ap, sel8t[0:8, ee * 128:(ee + 1) * 128],
                       psrc[0:8, (tt2 % 2) * 512:(tt2 % 2 + 1) * 512],
                       True, True, waits=w, inc=True)
            cv = kb.scop(lambda e, tt2=tt2, tsl=tsl: e.copy(
                sbc[:, tt2 * 512:(tt2 + 1) * 512], tsl.ap),
                waits=wmerge({kb.MM: mv}, sgate if tt2 == 0 else None))
            tsl.last_evac = {kb.SC: cv}
        posb_ready = {kb.SC: kb.SC.n}
        cmp = tb
        for sg in range(4):
            w = wmerge(posb_ready if sg == 0 else None, B_tb.last_use)
            v = kb.vec(lambda e, sg=sg: e.tensor_scalar(
                scr[0:128, 0:1024], sbc[:, 0:1024], misct[:, sg:sg + 1],
                None, ALU.is_le, ALU.bypass), waits=w)
            B_tb.set_ready(kb.VE, v)
            v = kb.vec(lambda e, sg=sg, ee=ee: e.tensor_reduce(
                tokcf[:, ee * 4 + sg:ee * 4 + sg + 1], scr[0:128, 0:1024],
                AX.X, ALU.add))
            v = kb.vec(lambda e, sg=sg: e.tensor_scalar(
                scr[0:128, 0:1024], sbc[:, 1024:2048], misct[:, sg:sg + 1],
                None, ALU.is_le, ALU.bypass))
            v = kb.vec(lambda e, sg=sg, ee=ee: e.tensor_reduce(
                bis[:, 8:9], scr[0:128, 0:1024], AX.X, ALU.add))
            v = kb.vec(lambda e, sg=sg, ee=ee: e.tensor_tensor(
                tokcf[:, ee * 4 + sg:ee * 4 + sg + 1],
                tokcf[:, ee * 4 + sg:ee * 4 + sg + 1],
                bis[:, 8:9], ALU.add))
            B_tb.used(kb.VE, v)
        for sg in range(1, 4):
            kb.vec(lambda e, sg=sg, ee=ee: e.tensor_scalar(
                bis[:, 8:9], tokcf[:, ee * 4 + sg:ee * 4 + sg + 1],
                float(T), float(sg), ALU.is_ge, ALU.mult))
            kb.vec(lambda e, sg=sg, ee=ee: e.tensor_tensor(
                tokcf[:, ee * 4 + sg:ee * 4 + sg + 1],
                tokcf[:, ee * 4 + sg:ee * 4 + sg + 1],
                bis[:, 8:9], ALU.add))
        vtok = kb.vec(lambda e, ee=ee: e.tensor_copy(
            tokci[:, ee * 4:(ee + 1) * 4], tokcf[:, ee * 4:(ee + 1) * 4]))
        tokr[ee] = {kb.VE: vtok}
        B_sbc.used(kb.VE, vtok)

    def plan_gathers(ee, first):
        b = Buf()
        for sg in range(4):
            w = wmerge(B_x2tm.ready if first and sg == 0 else None,
                       tokr[ee] if sg == 0 else None,
                       scrS[sg].begin_write())
            gv = kb.store(scr[:, 1024 + sg * 1032:1024 + (sg + 1) * 1032],
                          misct[:, 0:1], waits=w)
            wd, fn, inc, add = kb.ops["gpsimd"].pop()
            idxap = tokci[:, ee * 4 + sg:ee * 4 + sg + 1]
            fn2 = (lambda e, sg=sg, idxap=idxap: e.indirect_dma_start(
                scr[:, 1024 + sg * 1032:1024 + (sg + 1) * 1032], None,
                x2tm, IndirectOffsetOnAxis(idxap, 0)))
            kb.ops["gpsimd"].append((wd, fn2, inc, add))
            scrS[sg].set_ready_d(gv)
        xgb[ee] = b

    ytb = wring
    x2p_rdy = []
    plan_w1_loads(0)
    plan_w2_loads(0)
    plan_lists(0)
    plan_gathers(0, True)
    for ee in range(NE):
        w1b, w2b = w1bufs[ee], w2bufs[ee]
        XO = 0 if ee % 2 == 0 else 4096
        HO = 4096 - XO
        B_xf = B_p4A if ee % 2 == 0 else B_p4B
        B_h = B_p4B if ee % 2 == 0 else B_p4A
        p4lo_gate = B_xf.begin_write()
        p4lo_g = [None] * G
        for g in range(G):
            for sg in range(4):
                tsl = kb.slots[(g * 4 + sg) % 4]
                w = wmerge(scrS[sg].ready if g == 0 else None,
                           p4lo_gate if (sg == 0 and g == 0) else None,
                           tsl.last_evac)
                mv = kb.op("tensor", lambda e, sg=sg, g=g, tsl=tsl:
                           e.transpose(tsl.ap[:, 0:128],
                                       scr[:, 1024 + sg * 1032 + g * 128:
                                           1024 + sg * 1032 + (g + 1) * 128],
                                       ident[:]), waits=w, inc=kb.MM)
                cv = kb.scop(lambda e, sg=sg, g=g, tsl=tsl, XO=XO: e.copy(
                    pool4[:, XO + g * 512 + sg * 128:
                          XO + g * 512 + (sg + 1) * 128],
                    tsl.ap[:, 0:128]), waits={kb.MM: mv})
                tsl.last_evac = {kb.SC: cv}
                scrS[sg].used(kb.MM, mv)
            p4lo_g[g] = {kb.SC: kb.SC.n}
        B_xf.set_ready(kb.SC, kb.SC.n)
        srow = sbc[0:1, 1536:2048]
        sbc_gate = B_sbc.begin_write()
        for sg in range(4):
            tsl = kb.slots[6 + (sg % 2)]
            w = wmerge(scrS[sg].ready, tsl.last_evac,
                       sbc_gate if sg == 0 else None)
            mv = kb.op("tensor", lambda e, sg=sg, tsl=tsl, ee=ee:
                       e.transpose(tsl.ap[0:1, 0:128],
                                   scr[:, 1024 + sg * 1032 + 1024 + ee:
                                       1024 + sg * 1032 + 1024 + ee + 1],
                                   ident[:]), waits=w, inc=kb.MM)
            cv = kb.scop(lambda e, sg=sg, tsl=tsl: e.copy(
                srow[0:1, sg * 128:(sg + 1) * 128], tsl.ap[0:1, 0:128]),
                waits={kb.MM: mv})
            tsl.last_evac = {kb.SC: cv}
            scrS[sg].used(kb.MM, mv)
        bsl2 = kb.slots[6]
        mv = kb.mm(bsl2.ap, onesk[:], srow, True, True,
                   waits=wmerge({kb.SC: kb.SC.n}, bsl2.last_evac), inc=True)
        scv0 = kb.scop(lambda e, bsl2=bsl2: e.copy(sbc[:, 0:512], bsl2.ap),
                       waits={kb.MM: mv})
        bsl2.last_evac = {kb.SC: scv0}
        B_sbc.set_ready(kb.SC, scv0)

        hsl = [kb.slots[0], kb.slots[1]]
        p4hi_gate = B_h.begin_write()
        gelu_done = []
        for jf in range(8):
            sl = hsl[jf % 2]
            mv = None
            for k in range(G):
                pool = pool1 if k < 4 else pool2
                w = wmerge(w1b.ready if (jf == 0 and k == 0) else None,
                           p4lo_g[k] if jf == 0 else None,
                           sl.last_evac if k == 0 else None)
                lhsT = pool[:, (k % 4) * 1024 + jf * 128:
                            (k % 4) * 1024 + (jf + 1) * 128]
                mv = kb.mm(sl.ap, lhsT,
                           pool4[:, XO + k * 512:XO + (k + 1) * 512],
                           start=(k == 0), stop=(k == G - 1), waits=w,
                           inc=(k == G - 1))
                B_xf.used(kb.MM, mv)
            gvv = kb.act(pool4[:, HO + jf * 512:HO + (jf + 1) * 512],
                         sl.ap, AF.Gelu,
                         waits=wmerge({kb.MM: mv},
                                      p4hi_gate if jf == 0 else None))
            gelu_done.append(gvv)
            sl.last_evac = {kb.SC: gvv}
        hready = 0
        for jf in range(8):
            scv = kb.vec(lambda e, jf=jf, HO=HO: e.tensor_tensor(
                pool4[:, HO + jf * 512:HO + (jf + 1) * 512],
                pool4[:, HO + jf * 512:HO + (jf + 1) * 512],
                sbc[:, 0:512], ALU.mult),
                waits=wmerge({kb.SC: gelu_done[jf]},
                             B_sbc.ready if jf == 0 else None))
            hready = scv
            kb.slots[jf % 2].last_evac = wmerge(
                kb.slots[jf % 2].last_evac, {kb.VE: scv})
        B_sbc.used(kb.VE, hready)

        B_p1.used(kb.MM, kb.MM.n)
        B_p2.used(kb.MM, kb.MM.n)
        if ee + 1 < NE:
            plan_w1_loads(ee + 1)
            plan_lists(ee + 1)
            plan_gathers(ee + 1, False)

        ysl = [kb.slots[4], kb.slots[5]]
        for j2 in range(8):
            sl = ysl[j2 % 2]
            mv = None
            for k in range(G):
                w = wmerge(w2b.ready if (j2 == 0 and k == 0) else None,
                           {kb.VE: hready} if (j2 == 0 and k == 0) else None,
                           sl.last_evac if k == 0 else None)
                lhsT = pool3[:, k * 1024 + j2 * 128:
                             k * 1024 + (j2 + 1) * 128]
                mv = kb.mm(sl.ap, lhsT,
                           pool4[:, HO + k * 512:HO + (k + 1) * 512],
                           start=(k == 0), stop=(k == G - 1), waits=w,
                           inc=(k == G - 1))
                B_h.used(kb.MM, mv)
            cv = kb.scop(lambda e, j2=j2, sl=sl: e.copy(tbv[j2 % 2], sl.ap),
                         waits=wmerge({kb.MM: mv}, B_tb.last_use))
            B_tb.set_ready(kb.SC, cv)
            sl.last_evac = {kb.SC: cv}
            for sg in range(4):
                tsl = kb.slots[2 + (sg % 2)]
                w = wmerge({kb.SC: cv}, tsl.last_evac)
                mv2 = kb.op("tensor", lambda e, j2=j2, sg=sg, tsl=tsl:
                            e.transpose(tsl.ap[:, 0:128],
                                        tb[:, (j2 % 2) * 512 + sg * 128:
                                           (j2 % 2) * 512 + (sg + 1) * 128],
                                        ident[:]), waits=w, inc=kb.MM)
                cv2 = kb.scop(lambda e, j2=j2, sg=sg, tsl=tsl: e.copy(
                    ytb[:, sg * 1024 + j2 * 128:sg * 1024 + (j2 + 1) * 128],
                    tsl.ap[:, 0:128]),
                    waits=wmerge({kb.MM: mv2},
                                 B_ytb.begin_write()
                                 if j2 == 0 and sg == 0 else None))
                tsl.last_evac = {kb.SC: cv2}
                B_tb.used(kb.MM, mv2)
        scat_base = kb.SPs[3].n
        for sg in range(4):
            w = wmerge({kb.SC: kb.SC.n},
                       ymtm_zero if ee == 0 and sg == 0 else None,
                       {kb.SPs[3]: scat_base} if scat_base else None)
            idxap = tokci[:, ee * 4 + sg:ee * 4 + sg + 1]
            spv = kb.op("gpsimd",
                        lambda e, sg=sg, idxap=idxap: e.indirect_dma_start(
                            ymtm, IndirectOffsetOnAxis(idxap, 0),
                            ytb[:, sg * 1024:(sg + 1) * 1024], None,
                            compute_op=ALU.add),
                        waits=w, inc=kb.SPs[3], add=16)
            sp = {kb.SPs[3]: spv}
            B_ytb.used_d(sp)
            B_ymtm.set_ready_d(sp)
        B_p3.used(kb.MM, kb.MM.n)
        if ee + 1 < NE:
            plan_w2_loads(ee + 1)
        if ee == 7:
            x2p_g1 = B_p1.begin_write()
            x2p_g2 = B_p2.begin_write()
            for tc in range(16):
                xpool = pool1 if tc < 8 else pool2
                w = (x2p_g1 if tc == 0 else (x2p_g2 if tc == 8 else None))
                lv = kb.load(xpool[:, (tc % 8) * 1024:(tc % 8 + 1) * 1024],
                             bass.AP(x2tm.tensor,
                                     x2tm.offset + tc * 128 * 1032,
                                     [[1032, 128], [1, 1024]]),
                             waits=wmerge(w, B_x2tm.ready
                                          if tc == 0 else None))
                x2p_rdy.append(lv)

    if "sm_ymoe" in small_taps:
        for tc in range(16):
            kb.store(bass.AP(small_taps["sm_ymoe"].tensor,
                             small_taps["sm_ymoe"].offset + tc * 128 * D,
                             [[D, 128], [1, D]]),
                     misct[:, 0:1], waits=kb.sp_all())
            wd, fn, inc, add = kb.ops["gpsimd"].pop()
            fn2 = (lambda e, tc=tc: e.dma_start(
                out=bass.AP(small_taps["sm_ymoe"].tensor,
                            small_taps["sm_ymoe"].offset + tc * 128 * D,
                            [[D, 128], [1, D]]),
                in_=bass.AP(ymtm.tensor, ymtm.offset + tc * 128 * D,
                            [[D, 128], [1, D]])))
            kb.ops["gpsimd"].append((wd, fn2, inc, add))

    g4f = wmerge(B_p4.begin_write(), B_p4A.begin_write(),
                 B_p4B.begin_write())
    ld_use = [None] * 16
    st_use = [None] * 16

    def ym_load(tc):
        ymo = (tc % 8) * 1024
        ym_ap = bass.AP(ymtm.tensor, ymtm.offset + tc * 128 * D,
                        [[D, 128], [1, D]])
        w = wmerge(g4f if tc == 0 else None,
                   B_ymtm.ready if tc < 2 else None,
                   st_use[tc - 8] if tc >= 8 else None)
        if tc % 2 == 0:
            ld_use[tc] = kb.load(pool4[:, ymo:ymo + 1024], ym_ap, waits=w)
        else:
            ld_use[tc] = kb.gdma(pool4[:, ymo:ymo + 1024], ym_ap, waits=w)

    for tc in range(8):
        ym_load(tc)
    for tc in range(16):
        ymo = (tc % 8) * 1024
        xpool = pool1 if tc < 8 else pool2
        xap = xpool[:, (tc % 8) * 1024:(tc % 8 + 1) * 1024]
        v = kb.vec(lambda e, ymo=ymo, xap=xap: e.tensor_tensor(
            pool4[:, ymo:ymo + 1024], pool4[:, ymo:ymo + 1024],
            xap, ALU.add), waits=wmerge(ld_use[tc], x2p_rdy[tc]))
        cm, cvr, cs = 16 + tc, 64 + tc, 32 + tc
        sqo = 1024 + (tc % 4) * 1024
        kb.op("scalar", lambda e, ymo=ymo, cvr=cvr, sqo=sqo:
              e.activation(out=scr[:, sqo:sqo + 1024],
                           in_=pool4[:, ymo:ymo + 1024], func=AF.Square,
                           scale=1.0 / 32.0,
                           accum_out=bis[:, cvr:cvr + 1]),
              waits=wmerge({kb.VE: v}, {kb.SC: kb.SC.n}), inc=kb.SC)
        s2 = kb.op("scalar", lambda e, ymo=ymo, cm=cm, sqo=sqo:
                   e.activation(out=scr[:, sqo:sqo + 1024],
                                in_=pool4[:, ymo:ymo + 1024], func=AF.Copy,
                                scale=1.0 / 1024.0,
                                accum_out=bis[:, cm:cm + 1]),
                   waits={kb.SC: kb.SC.n}, inc=kb.SC)
        kb.vec(lambda e, cm=cm, cs=cs: e.tensor_tensor(
            bis[:, cs:cs + 1], bis[:, cm:cm + 1], bis[:, cm:cm + 1],
            ALU.mult), waits={kb.SC: s2})
        vv = kb.vec(lambda e, cvr=cvr, cs=cs: e.tensor_tensor(
            bis[:, cvr:cvr + 1], bis[:, cvr:cvr + 1], bis[:, cs:cs + 1],
            ALU.subtract))
        if tc % 4 == 3:
            b0 = 64 + tc - 3
            sv = kb.scop(lambda e, b0=b0: e.activation(
                out=bis[:, b0:b0 + 4], in_=bis[:, b0:b0 + 4],
                func=AF.Sqrt, bias=epsc128[:]), waits={kb.VE: vv})
            for t2 in range(tc - 3, tc + 1):
                ymo2 = (t2 % 8) * 1024
                cm2, cv2_, cs2 = 16 + t2, 64 + t2, 32 + t2
                kb.vec(lambda e, cv2_=cv2_, cs2=cs2: e.reciprocal(
                    bis[:, cs2:cs2 + 1], bis[:, cv2_:cv2_ + 1]),
                    waits={kb.SC: sv} if t2 == tc - 3 else None)
                kb.vec(lambda e, cm2=cm2, cv2_=cv2_, cs2=cs2: e.tensor_tensor(
                    bis[:, cv2_:cv2_ + 1], bis[:, cm2:cm2 + 1],
                    bis[:, cs2:cs2 + 1], ALU.mult))
                v8 = kb.vec(lambda e, ymo2=ymo2, cv2_=cv2_, cs2=cs2:
                            e.tensor_scalar(
                                pool4[:, ymo2:ymo2 + 1024],
                                pool4[:, ymo2:ymo2 + 1024],
                                bis[:, cs2:cs2 + 1], bis[:, cv2_:cv2_ + 1],
                                ALU.mult, ALU.subtract))
                out_ap = bass.AP(out_tm.tensor, out_tm.offset + t2 * 128 * D,
                                 [[D, 128], [1, D]])
                if t2 % 2 == 0:
                    st_use[t2] = kb.store(out_ap, pool4[:, ymo2:ymo2 + 1024],
                                          waits={kb.VE: v8})
                else:
                    st_use[t2] = kb.load(out_ap, pool4[:, ymo2:ymo2 + 1024],
                                         waits={kb.VE: v8})
                if t2 + 8 < 16:
                    ym_load(t2 + 8)

    kb.op("gpsimd", lambda e: e.engine_nop(),
          waits=wmerge(kb.sp_all(), kb.lw_all()))

    def emit(engine, name):
        seen = {}
        for w, fn, inc, add in kb.ops[name]:
            for s, v in w.items():
                if v and seen.get(s, 0) < v:
                    engine.wait_ge(s.h, v)
                    seen[s] = v
            instr = fn(engine)
            if inc is not None:
                instr.then_inc(inc.h, add)

    @block.sync
    def _(sync):
        emit(sync, "sync")

    @block.gpsimd
    def _(gpsimd):
        emit(gpsimd, "gpsimd")

    @block.scalar
    def _(scalar):
        emit(scalar, "scalar")

    @block.vector
    def _(vector):
        emit(vector, "vector")

    @block.tensor
    def _(tensor):
        emit(tensor, "tensor")

    es.close()
    return nc


def _pack_biases(kw):
    src = {"fr_b": "L0_attn_fr_b", "dp_b": "L0_attn_dp_b",
           "bp_b": "L0_attn_bp_b", "cp_b": "L0_attn_cp_b",
           "m1_b": "L0_attn_m1_b", "m2_b": "L0_attn_m2_b",
           "gate_b": "L0_attn_gate_b", "ffn1_b": "L0_attn_ffn1_b",
           "ffn2_b": "L0_attn_ffn2_b", "norm_g": "L0_attn_norm_g",
           "norm_b": "L0_attn_norm_b", "ffn_g": "L0_attn_ffn_g",
           "ffn_bn": "L0_attn_ffn_bn", "skip": "L0_attn_skip"}
    cols = []
    for name, _n in BIAS_ORDER:
        v = np.asarray(kw[src[name]], dtype=np.float32)
        cols.append(v.reshape(-1, 128).T)
    return np.ascontiguousarray(np.concatenate(cols, axis=1))


def kernel(**kw):
    x = np.asarray(kw["L0_attn_x"], dtype=np.float32)
    B, L, _ = x.shape
    biases = _pack_biases(kw)
    ident = np.eye(128, dtype=np.float32)
    misc = np.zeros((128, 20), dtype=np.float32)
    for tc in range(16):
        misc[:, tc] = tc * 128 + np.arange(128)
    misc[0:8, 16] = np.arange(8) * 513.0
    sel8 = np.zeros((8, 1024), dtype=np.float32)
    for ee in range(8):
        sel8[ee, ee * 128:(ee + 1) * 128] = 1.0
    w1T = np.ascontiguousarray(
        np.asarray(kw["L0_ffn_w1"], np.float32).transpose(0, 2, 1))
    w2T = np.ascontiguousarray(
        np.asarray(kw["L0_ffn_w2"], np.float32).transpose(0, 2, 1))
    gwT = np.ascontiguousarray(np.asarray(kw["L0_ffn_gate_w"], np.float32).T)

    shared = {
        "fr_w": np.ascontiguousarray(kw["L0_attn_fr_w"], np.float32),
        "dp_w": np.ascontiguousarray(kw["L0_attn_dp_w"], np.float32),
        "bp_w": np.ascontiguousarray(kw["L0_attn_bp_w"], np.float32),
        "cp_w": np.ascontiguousarray(kw["L0_attn_cp_w"], np.float32),
        "m1_w": np.ascontiguousarray(kw["L0_attn_m1_w"], np.float32),
        "m2_w": np.ascontiguousarray(kw["L0_attn_m2_w"], np.float32),
        "gate_w": np.ascontiguousarray(kw["L0_attn_gate_w"], np.float32),
        "ffn1_w": np.ascontiguousarray(kw["L0_attn_ffn1_w"], np.float32),
        "ffn2_w": np.ascontiguousarray(kw["L0_attn_ffn2_w"], np.float32),
        "gwT": gwT, "w1T": w1T, "w2T": w2T,
        "biases_in": biases, "ident_in": ident,
        "misc_in": misc, "sel8_in": sel8,
    }
    in_maps = []
    for c in range(8):
        b, h = c // 2, c % 2
        xs = np.ascontiguousarray(x[b, h * T:(h + 1) * T, :].T)
        sel = np.zeros((128, 8), dtype=np.float32)
        if h == 1:
            sel[:, c - 1] = 1.0
        m = dict(shared)
        m["x_fm"] = xs
        m["sel_in"] = sel
        in_maps.append(m)

    nc = build_nc()
    res = run_bass_kernel_spmd(nc, in_maps, core_ids=list(range(8)))
    out = np.empty((B, L, D), dtype=np.float32)
    for c in range(8):
        b, h = c // 2, c % 2
        out[b, h * T:(h + 1) * T, :] = res.results[c]["out_tm"]
    return out
